# version 06
# baseline (speedup 1.0000x reference)
import numpy as np
import ml_dtypes
from contextlib import ExitStack
import concourse.bass as bass
import concourse.mybir as mybir
from concourse.bass_utils import run_bass_kernel_spmd

F32 = mybir.dt.float32
BF16 = mybir.dt.bfloat16
AF = mybir.ActivationFunctionType
ALU = mybir.AluOpType
AX = mybir.AxisListType

D = 1024
DC = 8
T = 2048
H = 16
HD = 64
NTT = 4
TT = 512
EPS_LN = 1e-5
BF = ml_dtypes.bfloat16

(C_EPS, C_EPS6, C_ONE, C_NEG1, C_I1024, C_PROBE, C_ZERO, C_I8, C_Z0,
 C_CARRY, C_SCRA, C_SCRB, C_NHALF, C_HALF) = range(14)

DBG = None


def f32(x):
    return np.ascontiguousarray(x, dtype=np.float32)


def _pack_pv(vecs):
    out = np.zeros((128, 8 * len(vecs)), dtype=np.float32)
    for j, v in enumerate(vecs):
        out[:, j * 8:(j + 1) * 8] = f32(v).reshape(8, 128).T
    return out


def _rows(vecs, dtype):
    return np.concatenate([np.asarray(v).reshape(-1) for v in vecs])[None, :].astype(dtype)


def prep_core_inputs(ins, core):
    b, half = core // 2, core % 2
    s0 = half * T
    x = np.asarray(ins["L0_attn_x"])
    d = {}
    d["x_fm"] = f32(x[b, s0:s0 + T, :].T)
    d["ident_bf"] = np.eye(128, dtype=BF)
    d["ones_col_f32"] = np.ones((128, 1), dtype=np.float32)
    d["ones_col_bf"] = np.ones((128, 1), dtype=BF)
    d["ones_row_f32"] = np.ones((1, 128), dtype=np.float32)
    d["ones_row_bf"] = np.ones((1, 128), dtype=BF)
    z0 = 0.0 if half == 0 else -1e30
    carry = 0.0 if half == 0 else 1.0
    d["consts"] = np.tile(np.array([[EPS_LN, 1e-6, 1.0, -1.0, 1.0 / 1024,
                                     1.0 / (4096 * 8.0), 0.0, 0.125, z0, carry,
                                     0.0, 0.0, -0.5, 0.5, 0.0, 0.0]],
                                   dtype=np.float32), (128, 1))
    for nm in ["Wq", "Wk", "Wv", "Wfq", "Wfk"]:
        d["l0a_" + nm] = f32(ins["L0_attn_" + nm])
    W1 = f32(ins["L0_attn_W1"])
    d["l0a_W1sw"] = np.ascontiguousarray(
        W1.reshape(8, 128, 8, 4, 128).transpose(2, 1, 3, 0, 4)).reshape(8, 128, 4096)
    W2 = f32(ins["L0_attn_W2"])
    d["l0a_W2sw"] = np.ascontiguousarray(
        W2.reshape(8, 4, 128, 1024).transpose(0, 2, 1, 3)).reshape(8, 128, 4096)
    pvm = _pack_pv([ins["L0_attn_" + n] for n in
                    ["gq", "betaq", "gk", "betak", "bfk", "ga", "ba",
                     "bq", "bk", "bfq", "b2", "gv", "betav"]])
    bfkh = np.zeros((128, 16), dtype=np.float32)
    bfk_ = f32(ins["L0_attn_bfk"]).reshape(16, 64)
    bfkh[0:64, :] = bfk_.T
    d["l0a_pv"] = np.concatenate([pvm, bfkh], axis=1)
    d["l0a_b1fm"] = f32(ins["L0_attn_b1"]).reshape(32, 128).T.copy()
    d["l0a_rows"] = _rows([ins["L0_attn_" + n] for n in ["bv", "bfk"]], BF)
    d["l0f_wqkv"] = f32(np.asarray(ins["L0_ffn_in_proj_w"]).T)
    d["l0f_wout"] = f32(np.asarray(ins["L0_ffn_out_proj_w"]).T)
    bi = f32(ins["L0_ffn_in_proj_b"])
    d["l0f_pv"] = _pack_pv([bi[0:1024], bi[1024:2048], bi[2048:3072],
                            ins["L0_ffn_out_proj_b"], ins["L0_ffn_ln1_g"],
                            ins["L0_ffn_ln1_b"]])
    w1 = f32(ins["L0_ffn_w1"])
    d["l0f_w1sw"] = np.ascontiguousarray(
        w1.reshape(8, 8, 128, 4, 4, 128).transpose(0, 3, 2, 4, 1, 5)
    ).reshape(8, 4, 128, 4096)
    w2m = f32(ins["L0_ffn_w2"])
    d["l0f_w2sw"] = np.ascontiguousarray(
        w2m.reshape(8, 4, 4, 128, 1024).transpose(0, 1, 3, 2, 4)
    ).reshape(8, 4, 128, 4096)
    d["l0f_wge"] = np.concatenate(
        [f32(ins["L0_ffn_w_group"]),
         f32(ins["L0_ffn_w_expert"]).transpose(1, 0, 2).reshape(1024, 8)], axis=1)
    d["l0f_bge"] = _rows([ins["L0_ffn_b_group"],
                          f32(ins["L0_ffn_b_expert"]).reshape(8)], BF)
    d["l0f_b2"] = f32(ins["L0_ffn_b2"])
    d["l0f_b1fm"] = np.ascontiguousarray(
        f32(ins["L0_ffn_b1"]).reshape(8, 16, 128).transpose(2, 0, 1)).reshape(128, 128)
    d["l0f_pv3"] = _pack_pv([ins["L0_ffn_ln2_g"], ins["L0_ffn_ln2_b"]])
    basis8 = np.zeros((8, 1024), dtype=BF)
    for e in range(8):
        basis8[e, e * 128:(e + 1) * 128] = 1
    d["basis8"] = basis8
    d["l1_wqkv"] = f32(ins["L1_attn_W_qkv"])
    d["l1_wout"] = f32(ins["L1_attn_W_out"])
    d["tril"] = np.triu(np.ones((128, 128), dtype=np.float32)).astype(BF)
    ff = np.arange(4096)
    oo, ii = ff // 128, ff % 128
    sigma = (ii // 4) * 128 + (ii % 4) * 32 + oo
    inv = np.empty(4096, np.int64)
    inv[sigma] = ff
    WEXP = f32(ins["L1_ffn_W_exp"])
    d["l1f_wexpsw"] = np.ascontiguousarray(
        WEXP.reshape(8, 128, 8, 4, 128).transpose(2, 1, 3, 0, 4)).reshape(8, 128, 4096)
    WCON = f32(ins["L1_ffn_W_con"])[inv, :]
    d["l1f_wconsw"] = np.ascontiguousarray(
        WCON.reshape(32, 128, 8, 128).transpose(2, 1, 0, 3)).reshape(8, 128, 4096)
    def stage_ab(wst, s2):
        al = np.empty(4096, np.float32)
        bt = np.empty(4096, np.float32)
        w = f32(wst).reshape(4096 // (2 * s2), s2, 2, 2)
        for f in range(4096):
            nb_i = f // (2 * s2)
            s_i = f % s2
            if (f // s2) % 2 == 0:
                al[f] = w[nb_i, s_i, 0, 0]
                bt[f] = w[nb_i, s_i, 0, 1]
            else:
                al[f] = w[nb_i, s_i, 1, 1]
                bt[f] = w[nb_i, s_i, 1, 0]
        btp = bt[np.bitwise_xor(np.arange(4096), s2)]
        return al, btp
    ab_cols = []
    for wname in ["up_weights", "down_weights"]:
        wt = f32(ins["L1_ffn_" + wname])
        for st in range(7):
            al, btp = stage_ab(wt[st], 2 ** st)
            ab_cols.append((al, btp))
        for st in range(7, 12):
            al, btp = stage_ab(wt[st], 2 ** st)
            ab_cols.append((al[inv], btp[inv]))
    n_inst = len(ab_cols)
    ab = np.zeros((128, n_inst * 64), dtype=np.float32)
    for j, (al, btp) in enumerate(ab_cols):
        ab[:, j * 64: j * 64 + 32] = al.reshape(32, 128).T
        ab[:, j * 64 + 32: j * 64 + 64] = btp.reshape(32, 128).T
    d["l1f_ab"] = ab
    xorm = np.zeros((7, 128, 128), dtype=BF)
    for sj, s2 in enumerate([1, 2, 4, 8, 16, 32, 64]):
        idx = np.arange(128)
        xorm[sj, idx, np.bitwise_xor(idx, s2)] = 1
    d["l1f_xorm"] = xorm
    ub = f32(ins["L1_ffn_up_bias"])[sigma.argsort()] if False else f32(ins["L1_ffn_up_bias"])[inv]
    d["l1f_pv"] = np.concatenate(
        [ub.reshape(32, 128).T,
         f32(ins["L1_ffn_b_con"]).reshape(8, 128).T,
         f32(ins["L1_ffn_down_bias"]).reshape(8, 128).T], axis=1)
    return d


(PV_GQ, PV_BETAQ, PV_GK, PV_BETAK, PV_BFK, PV_GA, PV_BA,
 PV_BQ, PV_BK, PV_BFQ, PV_B2, PV_GV, PV_BETAV) = [j * 8 for j in range(13)]
PV_BFKH = 104
(PV2_BQ, PV2_BK, PV2_BV, PV2_BO, PV2_LN1G, PV2_LN1B) = [j * 8 for j in range(6)]
PV3_LN2G, PV3_LN2B = 0, 8
R_BV, R_BFK = 0, 1024


_PASSTHRU = {"wait_ge", "nop", "dma_start", "dma_start_transpose", "then_inc",
             "collective_compute", "indirect_dma_start"}


class SerialEng:

    def __init__(self, C, eng, ev):
        self._C = C
        self._e = eng
        self._ev = ev

    def __getattr__(self, name):
        f = getattr(self._e, name)
        if name in _PASSTHRU or name.startswith("_"):
            return f
        C, eng, ev = self._C, self._e, self._ev

        def call(*a, **k):
            C.wait(eng, ev)
            ins = f(*a, **k)
            C.inc(ins, ev, 1)
            return ins
        return call


class RecIns:
    __slots__ = ("calls",)

    def __init__(self):
        self.calls = []

    def then_inc(self, sem, k):
        self.calls.append(("then_inc", (sem, k), {}))
        return self


class RecEng:

    def __init__(self, log):
        self._log = log

    def __getattr__(self, name):
        if name.startswith("_"):
            raise AttributeError(name)
        log = self._log

        def call(*a, **k):
            ri = RecIns()
            log.append((name, a, k, ri))
            return ri
        return call


class Ctx:
    def __init__(self, nc, block):
        self.nc = nc
        self.block = block
        self.evs = {}
        self.banks = {}
        self.dumped = False
        self.logs = {n: [] for n in ("vector", "scalar", "gpsimd",
                                     "tensor", "sync")}

    def sem(self, name, handle):
        self.evs[name] = [handle, 0]

    def dgrp(self, q):
        name = self._dsems[self._dptr % len(self._dsems)]
        self._dptr += 1
        if self.count(name) > 0:
            self.wait(q, name)
        return name

    def add_bank(self, key, ten):
        b = type("B", (), {})()
        b.h, b.mark = ten, None
        self.banks[key] = b

    def inc(self, instr, name, k=1):
        ev = self.evs[name]
        instr.then_inc(ev[0], k)
        ev[1] += k
        return instr

    def wait(self, eng, name):
        ev = self.evs[name]
        if ev[1] > 0:
            eng.wait_ge(ev[0], ev[1])

    def wait_at(self, eng, name, n):
        if n > 0:
            eng.wait_ge(self.evs[name][0], n)

    def count(self, name):
        return self.evs[name][1]

    def mark(self, name):
        return (name, self.evs[name][1])

    def wmark(self, eng, mk):
        if mk is not None:
            self.wait_at(eng, mk[0], mk[1])

    def bacq(self, eng, key):
        b = self.banks[key]
        self.wmark(eng, b.mark)
        return b.h

    def bh(self, key):
        return self.banks[key].h

    def brel(self, key, ev):
        self.banks[key].mark = (ev, self.evs[ev][1])

    def vec(self, fn):
        fn(SerialEng(self, RecEng(self.logs["vector"]), "ve"))

    def scal(self, fn):
        fn(SerialEng(self, RecEng(self.logs["scalar"]), "se"))

    def gps(self, fn):
        fn(SerialEng(self, RecEng(self.logs["gpsimd"]), "gc"))

    def ten(self, fn):
        fn(RecEng(self.logs["tensor"]))

    def syn(self, fn):
        fn(RecEng(self.logs["sync"]))

    def flush(self):
        for eng_attr in ("vector", "scalar", "gpsimd", "tensor", "sync"):
            log = self.logs[eng_attr]

            @getattr(self.block, eng_attr)
            def _(eng, log=log):
                for name, a, k, ri in log:
                    ins = getattr(eng, name)(*a, **k)
                    for m, aa, kk in ri.calls:
                        getattr(ins, m)(*aa, **kk)


def expand_view(ap, dims):
    return bass.AP(ap.tensor, offset=ap.offset,
                   ap=[list(ap.ap[0])] + [list(dd) for dd in dims])


def fm_ln(C, S, src_chunk, ttw, dst_chunk=None, gamma_beta=None,
          src_f32=False, residual_into=None, src_ready=None, nchunk=DC,
          inv_n_col=C_I1024, pv_src="pv"):
    cst = S["consts"]
    sqf = S["sqf"]
    rows = S["rows"]
    mm_entry = C.count("mm")
    se_entry = C.count("se")
    ve_entry = C.count("ve")
    RB = 3
    nround = (nchunk + RB - 1) // RB

    @C.vec
    def _(v):
        C.wait_at(v, "se", se_entry)
        for r in range(nround):
            if r > 0:
                C.wait_at(v, "mm", mm_entry + RB * r)
            for kc in range(RB * r, min(RB * r + RB, nchunk)):
                if src_ready is not None:
                    C.wmark(v, src_ready(kc))
                v.tensor_tensor(sqf[:, (kc % RB) * 512:(kc % RB) * 512 + ttw],
                                src_chunk(kc), src_chunk(kc), ALU.mult)

    @C.ten
    def _(t):
        for r in range(nround):
            hi = min(RB * r + RB, nchunk)
            C.wait_at(t, "ve", ve_entry + hi)
            for kc in range(RB * r, hi):
                pb2 = C.bacq(t, 2) if kc == 0 else C.bh(2)
                pb3 = C.bacq(t, 3) if kc == 0 else C.bh(3)
                if src_ready is not None:
                    C.wmark(t, src_ready(kc))
                ones = S["onec_f"] if src_f32 else S["onec_b"]
                t.matmul(pb2[0:1, 0:ttw], ones[:, 0:1], src_chunk(kc),
                         start=(kc == 0), stop=(kc == nchunk - 1))
                C.inc(t.matmul(pb3[0:1, 0:ttw], S["onec_f"][:, 0:1],
                               sqf[:, (kc % RB) * 512:(kc % RB) * 512 + ttw],
                               start=(kc == 0), stop=(kc == nchunk - 1)), "mm")
    stats_mm = C.count("mm")

    @C.vec
    def _(v):
        C.wait_at(v, "mm", stats_mm)
        v.tensor_scalar(rows[0:1, 0:ttw], C.bh(2)[0:1, 0:ttw],
                        cst[0:1, inv_n_col:inv_n_col + 1], None, ALU.mult)
        v.tensor_scalar(rows[0:1, 512:512 + ttw], C.bh(3)[0:1, 0:ttw],
                        cst[0:1, inv_n_col:inv_n_col + 1], None, ALU.mult)
        C.brel(3, "ve")
        v.tensor_tensor(C.bh(2)[0:1, 0:ttw], rows[0:1, 0:ttw],
                        rows[0:1, 0:ttw], ALU.mult)
        v.tensor_tensor(rows[0:1, 512:512 + ttw], rows[0:1, 512:512 + ttw],
                        C.bh(2)[0:1, 0:ttw], ALU.subtract)
        C.brel(2, "ve")
    ve_var = C.count("ve")

    @C.scal
    def _(sc):
        C.wait_at(sc, "ve", ve_var)
        sc.activation(rows[0:1, 512:512 + ttw], rows[0:1, 512:512 + ttw],
                      AF.Ln, bias=cst[0:1, C_EPS:C_EPS + 1])
        sc.activation(rows[0:1, 512:512 + ttw], rows[0:1, 512:512 + ttw],
                      AF.Exp, scale=-0.5)
    se_std = C.count("se")

    @C.vec
    def _(v):
        C.wait_at(v, "se", se_std)
        v.tensor_tensor(rows[0:1, 0:ttw], rows[0:1, 0:ttw],
                        rows[0:1, 512:512 + ttw], ALU.mult)
    ve_rows = C.count("ve")

    @C.ten
    def _(t):
        C.wait_at(t, "ve", ve_rows)
        pb2 = C.bacq(t, 2)
        t.matmul(pb2[:, 0:ttw], S["oner_f"][0:1, :], rows[0:1, 512:512 + ttw],
                 start=True, stop=True)
        pb3 = C.bacq(t, 3)
        C.inc(t.matmul(pb3[:, 0:ttw], S["oner_f"][0:1, :], rows[0:1, 0:ttw],
                       start=True, stop=True), "mm")
    bc_mm = C.count("mm")

    @C.vec
    def _(v):
        C.wait_at(v, "mm", bc_mm)
        for kc in range(nchunk):
            v.tensor_tensor(sqf[:, 0:ttw], src_chunk(kc), C.bh(2)[:, 0:ttw], ALU.mult)
            if kc == nchunk - 1:
                C.brel(2, "ve")
            v.tensor_tensor(sqf[:, 0:ttw], sqf[:, 0:ttw],
                            C.bh(3)[:, 0:ttw], ALU.subtract)
            if kc == nchunk - 1:
                C.brel(3, "ve")
            if residual_into is not None:
                v.tensor_tensor(residual_into(kc), residual_into(kc),
                                sqf[:, 0:ttw], ALU.add)
            elif gamma_beta is not None:
                gcol, bcol = gamma_beta
                v.tensor_scalar(dst_chunk(kc), sqf[:, 0:ttw],
                                S[pv_src][:, gcol + kc:gcol + kc + 1],
                                S[pv_src][:, bcol + kc:bcol + kc + 1],
                                ALU.mult, ALU.add)
            else:
                v.tensor_copy(dst_chunk(kc), sqf[:, 0:ttw])
    return C.count("ve")


def fm_proj(C, S, wslot_fn, src_chunk, ttw, out_cb, banks=(0, 1), n_oc=DC,
            src_ready=None, w_ready=None, nkc=DC):
    for oc in range(n_oc):
        bk = banks[oc % len(banks)]

        @C.ten
        def _(t, oc=oc, bk=bk):
            p = C.bacq(t, bk)
            if w_ready is not None:
                C.wait_at(t, *w_ready)
            if src_ready is not None:
                C.wmark(t, src_ready(oc))
            for kc in range(nkc):
                ins = t.matmul(p[:, 0:ttw], wslot_fn(kc, oc), src_chunk(kc),
                               start=(kc == 0), stop=(kc == nkc - 1))
            C.inc(ins, "mm")
        out_cb(oc, bk, C.count("mm"))


def build_program(core):
    half = core % 2
    nc = bass.Bass("TRN2", target_bir_lowering=False)
    g = {}

    def din(name, shape, dt=F32):
        g[name] = nc.dram_tensor(name, shape, dt, kind="ExternalInput").ap()

    def dout(name, shape, dt=F32):
        g[name] = nc.dram_tensor(name, shape, dt, kind="ExternalOutput").ap()

    def dint(name, shape, dt=F32):
        g[name] = nc.dram_tensor(name, shape, dt).ap()

    din("x_fm", [D, T])
    dout("out_fm", [D, T])
    din("ident_bf", [128, 128], BF16)
    din("ones_col_f32", [128, 1]); din("ones_col_bf", [128, 1], BF16)
    din("ones_row_f32", [1, 128]); din("ones_row_bf", [1, 128], BF16)
    din("consts", [128, 16])
    for nm in ["Wq", "Wk", "Wv", "Wfq", "Wfk"]:
        din("l0a_" + nm, [D, D])
    din("l0a_W1sw", [8, 128, 4096]); din("l0a_W2sw", [8, 128, 4096])
    din("l0a_pv", [128, 120]); din("l0a_b1fm", [128, 32])
    din("l0a_rows", [1, 2 * 1024], BF16)
    dint("pq_spill", [D, T], BF16)
    dint("w1_bf", [8, 128, 4096], BF16)
    dint("w2_bf", [8, 128, 4096], BF16)
    for nm in ["Wq", "Wk", "Wv", "Wfq", "Wfk"]:
        dint(nm + "_bf", [D, D], BF16)
    din("l0f_wqkv", [D, 3 * D]); din("l0f_wout", [D, D])
    din("l0f_pv", [128, 48])
    dint("wqkv_bf", [D, 3 * D], BF16)
    dint("wout_bf", [D, D], BF16)
    dint("kv_buf", [2048, T], BF16)
    dint("q_buf", [D, T], BF16)
    dint("kv_gath", [4096, T], BF16)
    din("l0f_w1sw", [8, 4, 128, 4096]); din("l0f_w2sw", [8, 4, 128, 4096])
    din("l0f_wge", [D, 12]); din("l0f_bge", [1, 12], BF16)
    din("l0f_b2", [8, D]); din("l0f_b1fm", [128, 128])
    din("l0f_pv3", [128, 16]); din("basis8", [8, 1024], BF16)
    dint("moew1_bf", [8, 4, 128, 4096], BF16)
    dint("moew2_bf", [8, 4, 128, 4096], BF16)
    dint("wge_bf", [D, 12], BF16)
    din("l1_wqkv", [D, 3 * D]); din("l1_wout", [D, D])
    din("tril", [128, 128], BF16)
    dint("l1wqkv_bf", [D, 3 * D], BF16)
    dint("l1wout_bf", [D, D], BF16)
    dint("pkfm_buf", [D, T], BF16)
    dint("pktm_buf", [T, D], BF16)
    dint("v_buf", [T, D], BF16)
    dint("carry_buf", [64, 1040])
    dint("carry_gath", [128, 1040])
    din("l1f_wexpsw", [8, 128, 4096]); din("l1f_wconsw", [8, 128, 4096])
    din("l1f_ab", [128, 1536]); din("l1f_xorm", [7, 128, 128], BF16)
    din("l1f_pv", [128, 48])
    dint("wexp_bf", [8, 128, 4096], BF16)
    dint("wcon_bf", [8, 128, 4096], BF16)
    dint("scr", [4096, T], BF16)
    dint("scr2", [4096, T], BF16)
    dint("scr3", [4096, T], BF16)
    dint("probe_buf", [128, 8])
    dint("probe_red", [128, 8])
    dint("kvu_buf", [66, 1040])
    dint("kvu_red", [66, 1040])
    dint("mmx_buf", [1, 16])
    dint("mmx_red", [1, 16])

    es = ExitStack()
    sbt = lambda n, s, dt: es.enter_context(nc.sbuf_tensor(n, s, dt))
    pst = lambda n, s, dt: es.enter_context(nc.psum_tensor(n, s, dt))
    sem = lambda n: es.enter_context(nc.semaphore(n))

    S = {}
    S["xs"] = sbt("xs", [128, DC * T], F32)
    S["yb"] = sbt("yb", [128, DC * T], BF16)
    S["kb"] = sbt("kb", [128, DC * T], BF16)
    S["vb"] = sbt("vb", [128, DC * T], BF16)
    S["slotA"] = sbt("slotA", [128, DC * TT], BF16)
    S["slotB"] = sbt("slotB", [128, DC * TT], BF16)
    S["sqf"] = sbt("sqf", [128, 1536], F32)
    S["rows"] = sbt("rows", [128, 1024], F32)
    S["smf"] = sbt("smf", [128, 2048], F32)
    S["kvu"] = sbt("kvu", [64, 1040], F32)
    S["kvb"] = sbt("kvb", [128, 1040], BF16)
    S["scoreT"] = sbt("scoreT", [128, 256], BF16)
    S["probe"] = sbt("probe", [128, 8], F32)
    S["pbd"] = sbt("pbd", [128, 16], BF16)
    S["consts"] = sbt("consts_sb", [128, 16], F32)
    S["pv"] = sbt("pv", [128, 120], F32)
    S["pv2"] = sbt("pv2", [128, 48], F32)
    S["pv3"] = sbt("pv3", [128, 16], F32)
    S["pv5"] = sbt("pv5", [128, 48], F32)
    S["b1fm3"] = sbt("b1fm3", [128, 128], F32)
    S["bge"] = sbt("bge", [1, 12], BF16)
    S["b1fm"] = sbt("b1fm", [128, 32], F32)
    S["rowsbf"] = sbt("rowsbf", [1, 2 * 1024], BF16)
    S["onec_f"] = sbt("onec_f", [128, 1], F32)
    S["onec_b"] = sbt("onec_b", [128, 1], BF16)
    S["oner_f"] = sbt("oner_f", [1, 128], F32)
    S["oner_b"] = sbt("oner_b", [1, 128], BF16)
    S["identb"] = sbt("identb", [128, 128], BF16)

    block = es.enter_context(nc.Block())
    C = Ctx(nc, block)
    for n in ["wl", "al", "mm", "ve", "se", "gc", "st", "cc", "pc", "pc2"]:
        C.sem(n, sem("s_" + n))
    C._dsems = []
    C._dptr = 0
    for i in range(16):
        nm = f"d{i}"
        C.sem(nm, sem("s_" + nm))
        C._dsems.append(nm)
    for i in range(7):
        C.add_bank(i, pst(f"p{i}", [128, 512], F32))
    C.add_bank("T", pst("pT", [128, 1024], BF16))
    C.scrA = S["consts"][0:1, 10:11]
    C.scrB = S["consts"][0:1, 11:12]

    emit_preamble(C, g, S)
    emit_phase1(C, g, S, half)
    if not C.dumped:
        emit_phase2(C, g, S)
    if not C.dumped:
        emit_phase3(C, g, S)
    if not C.dumped:
        emit_phase4(C, g, S)
    if not C.dumped:
        emit_phase5(C, g, S)
    if not C.dumped:
        emit_epilogue(C, g, S)
    C.flush()
    es.close()
    return nc, g


def emit_preamble(C, g, S):
    @C.syn
    def _(sy):
        grp = C.dgrp(sy)
        for kc in range(DC):
            C.inc(sy.dma_start(out=S["xs"][:, kc * T:(kc + 1) * T],
                               in_=g["x_fm"][kc * 128:(kc + 1) * 128, :]), grp, 16)
        for nm, src in [("consts", "consts"), ("pv", "l0a_pv"), ("b1fm", "l0a_b1fm"),
                        ("pv2", "l0f_pv"), ("pv3", "l0f_pv3"), ("b1fm3", "l0f_b1fm"),
                        ("bge", "l0f_bge"), ("pv5", "l1f_pv"),
                        ("rowsbf", "l0a_rows"), ("onec_f", "ones_col_f32"),
                        ("onec_b", "ones_col_bf"), ("oner_f", "ones_row_f32"),
                        ("oner_b", "ones_row_bf"), ("identb", "ident_bf")]:
            C.inc(sy.dma_start(out=S[nm][:], in_=g[src][:]), grp, 16)
        C.pre_al = (grp, C.count(grp))

    @C.gps
    def _(gp):
        for nm in ["Wq", "Wfq", "Wk", "Wv", "Wfk"]:
            C.inc(gp.dma_start(out=g[nm + "_bf"][:], in_=g["l0a_" + nm][:]), "pc", 16)
        C.inc(gp.dma_start(out=g["w1_bf"][:], in_=g["l0a_W1sw"][:]), "pc", 16)
        C.inc(gp.dma_start(out=g["w2_bf"][:], in_=g["l0a_W2sw"][:]), "pc", 16)
        C.inc(gp.dma_start(out=g["wqkv_bf"][:], in_=g["l0f_wqkv"][:]), "pc", 16)
        C.inc(gp.dma_start(out=g["wout_bf"][:], in_=g["l0f_wout"][:]), "pc", 16)
        C.inc(gp.dma_start(out=g["wge_bf"][:], in_=g["l0f_wge"][:]), "pc", 16)
        C.inc(gp.dma_start(out=g["moew1_bf"][:], in_=g["l0f_w1sw"][:]), "pc2", 16)
        C.inc(gp.dma_start(out=g["moew2_bf"][:], in_=g["l0f_w2sw"][:]), "pc2", 16)
        C.inc(gp.dma_start(out=g["l1wqkv_bf"][:], in_=g["l1_wqkv"][:]), "pc2", 16)
        C.inc(gp.dma_start(out=g["l1wout_bf"][:], in_=g["l1_wout"][:]), "pc2", 16)
        C.inc(gp.dma_start(out=g["wexp_bf"][:], in_=g["l1f_wexpsw"][:]), "pc2", 16)
        C.inc(gp.dma_start(out=g["wcon_bf"][:], in_=g["l1f_wconsw"][:]), "pc2", 16)
    C.precast_pc = C.count("pc")
    C.precast_pc2 = C.count("pc2")


def emit_epilogue(C, g, S, src=None):
    if src is None:
        src = S["xs"]

    @C.syn
    def _(sy):
        for nm in ["ve", "se", "mm", "gc"]:
            C.wait(sy, nm)
        for kc in range(DC):
            C.inc(sy.dma_start(out=g["out_fm"][kc * 128:(kc + 1) * 128, :],
                               in_=src[:, kc * T:(kc + 1) * T]), "st", 16)
        C.wait(sy, "st")


def maybe_dump(C, g, S, tag, buf):
    if DBG != tag:
        return False

    @C.gps
    def _(gp):
        for nm in ["ve", "se", "mm", "al", "wl", "st", "cc"]:
            C.wait(gp, nm)
        for kc in range(DC):
            C.inc(gp.dma_start(out=g["out_fm"][kc * 128:(kc + 1) * 128, :],
                               in_=buf[:, kc * T:(kc + 1) * T]), "st", 16)
        C.wait(gp, "st")
    C.dumped = True
    return True


def emit_phase1(C, g, S, half):
    xs, yb, kb, vb = S["xs"], S["yb"], S["kb"], S["vb"]
    slotA, slotB, sqf, rows, smf = S["slotA"], S["slotB"], S["sqf"], S["rows"], S["smf"]
    cst, pv = S["consts"], S["pv"]
    WS0, WS1 = 0, 8192

    def wsl(slot, kc, c0, cn):
        base = WS0 if slot == 0 else WS1
        return vb[:, base + kc * 1024 + c0: base + kc * 1024 + c0 + cn]

    wl_marks = {}

    def load_wfull(sy, wname, slot, gate_mm=0):
        grp = C.dgrp(sy)
        if gate_mm:
            C.wait_at(sy, "mm", gate_mm)
        C.wait_at(sy, "pc", C.precast_pc)
        src_bf = g[wname.replace("l0a_", "") + "_bf"]
        for kc in range(DC):
            C.inc(sy.dma_start(out=wsl(slot, kc, 0, 1024),
                               in_=src_bf[kc * 128:(kc + 1) * 128, :]), grp, 16)
        wl_marks[wname] = (grp, C.count(grp))

    @C.syn
    def _(sy):
        load_wfull(sy, "l0a_Wq", 0)
        load_wfull(sy, "l0a_Wfq", 1)

    p1_marks = []
    for tt in range(NTT):
        c0 = tt * TT
        m = fm_ln(C, S,
                  src_chunk=lambda kc, c0=c0: xs[:, kc * T + c0: kc * T + c0 + TT],
                  ttw=TT,
                  dst_chunk=lambda kc, c0=c0: yb[:, kc * T + c0: kc * T + c0 + TT],
                  src_f32=True,
                  src_ready=(lambda kc: C.pre_al) if tt == 0 else None)
        p1_marks.append(m)
    p1_done = p1_marks[-1]
    maybe_dump(C, g, S, "p1_y", yb)
    if C.dumped:
        return

    q_mm_done = 0
    pq_mm_done = 0
    slotA_free = 0
    slotB_free = None
    for tt in range(NTT):
        c0 = tt * TT
        evac_marks = []
        saf_now = slotA_free

        def q_out(oc, bk, mmk, evac_marks=evac_marks, saf=saf_now):
            @C.scal
            def _(sc):
                C.wait_at(sc, "mm", mmk)
                C.wait_at(sc, "ve", saf)
                sc.activation(S["slotA"][:, oc * TT:oc * TT + TT], C.bh(bk)[:, 0:TT],
                              AF.Identity, bias=pv[:, PV_BQ + oc:PV_BQ + oc + 1])
                C.brel(bk, "se")
            evac_marks.append(("se", C.count("se")))

        fm_proj(C, S, lambda kc, oc: wsl(0, kc, oc * 128, 128),
                lambda kc, c0=c0: yb[:, kc * T + c0: kc * T + c0 + TT],
                TT, q_out, w_ready=wl_marks["l0a_Wq"],
                src_ready=lambda oc, tt=tt: ("ve", p1_marks[tt]))
        q_mm_done = C.count("mm")
        qln_ve = fm_ln(C, S,
                       src_chunk=lambda kc: S["slotA"][:, kc * TT:(kc + 1) * TT],
                       ttw=TT,
                       dst_chunk=lambda kc: S["slotA"][:, kc * TT:(kc + 1) * TT],
                       gamma_beta=(PV_GQ, PV_BETAQ),
                       src_ready=lambda kc, em=list(evac_marks): em[kc])

        @C.vec
        def _(v, tt=tt):
            for kc in range(DC):
                v.reduce_sum(smf[:, 0:1], S["slotA"][:, kc * TT:(kc + 1) * TT], axis=AX.X)
                if tt == 0:
                    v.tensor_copy(S["probe"][:, kc:kc + 1], smf[:, 0:1])
                else:
                    v.tensor_tensor(S["probe"][:, kc:kc + 1], S["probe"][:, kc:kc + 1],
                                    smf[:, 0:1], ALU.add)

        pq_marks = []

        def pq_out(oc, bk, mmk, pq_marks=pq_marks, sbf=slotB_free):
            @C.scal
            def _(sc):
                C.wait_at(sc, "mm", mmk)
                C.wmark(sc, sbf)
                sc.activation(S["slotB"][:, oc * TT:oc * TT + TT], C.bh(bk)[:, 0:TT],
                              AF.Tanh, bias=pv[:, PV_BFQ + oc:PV_BFQ + oc + 1])
                C.brel(bk, "se")
            semk = C.count("se")

            @C.vec
            def _(v):
                C.wait_at(v, "se", semk)
                v.tensor_scalar(S["slotB"][:, oc * TT:oc * TT + TT],
                                S["slotB"][:, oc * TT:oc * TT + TT],
                                cst[:, C_ONE:C_ONE + 1], None, ALU.add)
            pq_marks.append(C.count("ve"))

        fm_proj(C, S, lambda kc, oc: wsl(1, kc, oc * 128, 128),
                lambda kc: S["slotA"][:, kc * TT:(kc + 1) * TT],
                TT, pq_out, banks=(0, 1), w_ready=wl_marks["l0a_Wfq"],
                src_ready=lambda oc, q=qln_ve: ("ve", q))
        pq_mm_done = C.count("mm")
        slotA_free = C.count("ve")

        @C.syn
        def _(sy, tt=tt, pqm=list(pq_marks), c0=c0):
            grp = C.dgrp(sy)
            for oc in range(DC):
                C.wait_at(sy, "ve", pqm[oc])
                C.inc(sy.dma_start(out=g["pq_spill"][oc * 128:(oc + 1) * 128, c0:c0 + TT],
                                   in_=S["slotB"][:, oc * TT:oc * TT + TT]), grp, 16)
            C.pq_spill_st = (grp, C.count(grp))
        slotB_free = C.pq_spill_st

    @C.syn
    def _(sy):
        load_wfull(sy, "l0a_Wk", 0, gate_mm=q_mm_done)

    for tt in range(NTT):
        c0 = tt * TT
        evac_marks = []
        if tt > 0:
            slotA_free = C.count("ve")

        def k_out(oc, bk, mmk, evac_marks=evac_marks, saf=slotA_free):
            @C.scal
            def _(sc):
                C.wait_at(sc, "mm", mmk)
                C.wait_at(sc, "ve", saf)
                sc.activation(S["slotA"][:, oc * TT:oc * TT + TT], C.bh(bk)[:, 0:TT],
                              AF.Identity, bias=pv[:, PV_BK + oc:PV_BK + oc + 1])
                C.brel(bk, "se")
            evac_marks.append(("se", C.count("se")))

        fm_proj(C, S, lambda kc, oc: wsl(0, kc, oc * 128, 128),
                lambda kc, c0=c0: yb[:, kc * T + c0: kc * T + c0 + TT],
                TT, k_out, w_ready=wl_marks["l0a_Wk"],
                src_ready=lambda oc, tt=tt: ("ve", p1_marks[tt]))
        fm_ln(C, S,
              src_chunk=lambda kc: S["slotA"][:, kc * TT:(kc + 1) * TT],
              ttw=TT,
              dst_chunk=lambda kc, c0=c0: kb[:, kc * T + c0: kc * T + c0 + TT],
              gamma_beta=(PV_GK, PV_BETAK),
              src_ready=lambda kc, em=list(evac_marks): em[kc])
    k_mm_done = C.count("mm")
    k_ve_done = C.count("ve")
    maybe_dump(C, g, S, "p3_K", kb)
    if C.dumped:
        return

    @C.syn
    def _(sy):
        load_wfull(sy, "l0a_Wfk", 1, gate_mm=pq_mm_done)
        load_wfull(sy, "l0a_Wv", 0, gate_mm=k_mm_done)

    probe_ve = C.count("ve")

    @C.gps
    def _(gp):
        C.wait_at(gp, "ve", probe_ve)
        C.inc(gp.dma_start(out=g["probe_buf"][:], in_=S["probe"][:]), "gc", 16)
        C.wait(gp, "gc")
        C.inc(gp.collective_compute(
            "AllReduce", ALU.add,
            replica_groups=[[0, 1], [2, 3], [4, 5], [6, 7]],
            ins=[g["probe_buf"][:, :].opt()], outs=[g["probe_red"][:, :].opt()]), "cc")

    @C.syn
    def _(sy):
        grp = C.dgrp(sy)
        C.wait(sy, "cc")
        C.inc(sy.dma_start(out=S["probe"][:], in_=g["probe_red"][:]), grp, 16)
        C.probe_al = (grp, C.count(grp))

    @C.vec
    def _(v):
        C.wmark(v, C.probe_al)
        v.tensor_scalar(S["probe"][:], S["probe"][:], cst[:, C_PROBE:C_PROBE + 1],
                        None, ALU.mult)
        v.memset(S["pbd"][:], 0.0)
        for kc in range(DC):
            v.tensor_copy(S["pbd"][0:64, 2 * kc:2 * kc + 1], S["probe"][0:64, kc:kc + 1])
            v.tensor_copy(S["pbd"][64:128, 2 * kc + 1:2 * kc + 2],
                          S["probe"][64:128, kc:kc + 1])
    pbd_ve = C.count("ve")

    MY_M, MY_ZU, MM_, WMY, E0_, RZ_, T1_, T2_, NEG_ = \
        0, 16, 32, 48, 64, 80, 96, 112, 128
    mz = rows[0:1, 512:656]
    prev_exp_se = 0
    prev_tr_mm = 0
    for h in range(H):
        c, r = h // 2, h % 2
        lg_marks = []
        for tt in range(NTT):
            bk = 4 if tt % 2 == 0 else 0

            @C.ten
            def _(t, c=c, r=r, tt=tt, bk=bk):
                p = C.bacq(t, bk)
                C.wait_at(t, "ve", pbd_ve)
                C.wait_at(t, "ve", k_ve_done)
                C.inc(t.matmul(p[0:1, 0:TT], S["pbd"][:, 2 * c + r:2 * c + r + 1],
                               kb[:, c * T + tt * TT: c * T + (tt + 1) * TT],
                               start=True, stop=True), "mm")
            mmk = C.count("mm")

            @C.vec
            def _(v, tt=tt, mmk=mmk, pse=prev_exp_se, bk=bk):
                C.wait_at(v, "mm", mmk)
                C.wait_at(v, "se", pse)
                v.tensor_copy(smf[0:1, tt * TT:(tt + 1) * TT], C.bh(bk)[0:1, 0:TT])
                C.brel(bk, "ve")
            lg_marks.append(C.count("ve"))

        @C.vec
        def _(v, h=h, lg=list(lg_marks)):
            C.wait_at(v, "ve", lg[-1])
            v.reduce_max(mz[0:1, MY_M + h:MY_M + h + 1], smf[0:1, 0:T], axis=AX.X)
            v.tensor_scalar(mz[0:1, MY_M + h:MY_M + h + 1],
                            mz[0:1, MY_M + h:MY_M + h + 1],
                            cst[0:1, C_Z0:C_Z0 + 1], None, ALU.max)
            v.tensor_scalar(mz[0:1, NEG_ + h:NEG_ + h + 1],
                            mz[0:1, MY_M + h:MY_M + h + 1],
                            cst[0:1, C_NEG1:C_NEG1 + 1], None, ALU.mult)
        m_ve = C.count("ve")

        @C.scal
        def _(sc, h=h, m_ve=m_ve, ptm=prev_tr_mm):
            C.wait_at(sc, "ve", m_ve)
            C.wait_at(sc, "mm", ptm)
            if h == 0:
                C.wmark(sc, C.pq_spill_st)
            sc.activation(S["slotB"][0:1, 0:T], smf[0:1, 0:T], AF.Exp,
                          bias=mz[0:1, NEG_ + h:NEG_ + h + 1],
                          accum_out=mz[0:1, MY_ZU + h:MY_ZU + h + 1])
            sc.activation(mz[0:1, T1_ + h:T1_ + h + 1],
                          mz[0:1, NEG_ + h:NEG_ + h + 1], AF.Exp,
                          bias=cst[0:1, C_Z0:C_Z0 + 1])
        exp_se = C.count("se")
        prev_exp_se = exp_se

        @C.vec
        def _(v, h=h, exp_se=exp_se):
            C.wait_at(v, "se", exp_se)
            v.tensor_tensor(mz[0:1, MY_ZU + h:MY_ZU + h + 1],
                            mz[0:1, MY_ZU + h:MY_ZU + h + 1],
                            mz[0:1, T1_ + h:T1_ + h + 1], ALU.add)
        su_ve = C.count("ve")

        for kt in range(16):
            @C.ten
            def _(t, kt=kt, su_ve=su_ve):
                pT = C.bacq(t, "T")
                C.wait_at(t, "ve", su_ve)
                C.wait_at(t, "se", C.count("se"))
                C.inc(t.transpose(pT[:, 0:1], S["slotB"][0:1, kt * 128:(kt + 1) * 128],
                                  S["identb"][0:1, 0:1]), "mm")
            tr_mm = C.count("mm")
            prev_tr_mm = tr_mm

            @C.vec
            def _(v, h=h, kt=kt, tr_mm=tr_mm):
                C.wait_at(v, "mm", tr_mm)
                v.tensor_copy(S["scoreT"][:, kt * 16 + h: kt * 16 + h + 1],
                              C.bh("T")[:, 0:1])
                C.brel("T", "ve")
    score_ve = C.count("ve")

    for j, (pvc, dst) in enumerate([(PV_GV, 0), (PV_BETAV, 1024)]):
        for kc in range(DC):
            @C.vec
            def _(v, pvc=pvc, kc=kc):
                C.wmark(v, C.pre_al)
                v.tensor_copy(S["slotB"][:, 0:1], pv[:, pvc + kc:pvc + kc + 1])
            cp_ve = C.count("ve")

            @C.ten
            def _(t, cp_ve=cp_ve):
                C.wait_at(t, "ve", cp_ve)
                pT = C.bacq(t, "T")
                C.inc(t.transpose(pT[0:1, 0:128], S["slotB"][:, 0:1], S["identb"][:, :]),
                      "mm")
            tr_mm2 = C.count("mm")

            @C.vec
            def _(v, tr_mm2=tr_mm2):
                C.wait_at(v, "mm", tr_mm2)
                v.tensor_copy(S["slotB"][0:1, 128:256], C.bh("T")[0:1, 0:128])
                C.brel("T", "ve")
            ev_ve = C.count("ve")

            @C.ten
            def _(t, ev_ve=ev_ve):
                C.wait_at(t, "ve", ev_ve)
                p4 = C.bacq(t, 4)
                C.inc(t.matmul(p4[:, 0:128], S["oner_b"][0:1, :],
                               S["slotB"][0:1, 128:256], start=True, stop=True), "mm")
            bm = C.count("mm")

            @C.vec
            def _(v, dst=dst, kc=kc, bm=bm):
                C.wait_at(v, "mm", bm)
                v.tensor_copy(smf[:, dst + kc * 128: dst + (kc + 1) * 128],
                              C.bh(4)[:, 0:128])
                C.brel(4, "ve")
    gvb_ve = C.count("ve")

    prev_pkw_ve = C.count("ve")
    for kt in range(16):
        ktt = kt * 128
        pk_marks = []
        for ocp in range(2):
            @C.ten
            def _(t, kt=kt, ocp=ocp):
                p = C.bacq(t, ocp)
                C.wmark(t, wl_marks["l0a_Wfk"])
                C.wait_at(t, "ve", k_ve_done)
                for kc in range(DC):
                    t.matmul(p[:, 0:TT], kb[:, kc * T + ktt: kc * T + ktt + 128],
                             wsl(1, kc, ocp * 512, 512),
                             start=(kc == 0), stop=False)
                C.inc(t.matmul(p[:, 0:TT], S["oner_b"][0:1, 0:128],
                               S["rowsbf"][0:1, R_BFK + ocp * 512: R_BFK + ocp * 512 + 512],
                               start=False, stop=True), "mm")
            mmk = C.count("mm")

            @C.scal
            def _(sc, ocp=ocp, mmk=mmk, ppk=prev_pkw_ve):
                C.wait_at(sc, "mm", mmk)
                C.wait_at(sc, "ve", ppk)
                sc.activation(S["slotA"][:, ocp * 512:(ocp + 1) * 512],
                              C.bh(ocp)[:, 0:TT], AF.Tanh)
                C.brel(ocp, "se")
            pk_marks.append(C.count("se"))

        @C.vec
        def _(v, kt=kt, pk=list(pk_marks)):
            C.wait_at(v, "se", pk[-1])
            C.wait_at(v, "ve", score_ve)
            sv = expand_view(S["scoreT"][:, kt * 16: kt * 16 + 16], [[1, 16], [0, 64]])
            v.scalar_tensor_tensor(
                S["slotB"][:, 0:1024], S["slotA"][:, 0:1024],
                cst[:, C_ONE:C_ONE + 1], sv, ALU.add, ALU.mult)
        pkw_ve = C.count("ve")
        prev_pkw_ve = pkw_ve

        v_marks = []
        for ocp in range(2):
            @C.ten
            def _(t, kt=kt, ocp=ocp):
                p = C.bacq(t, 2 + ocp)
                C.wmark(t, wl_marks["l0a_Wv"])
                C.wait_at(t, "ve", p1_done)
                for kc in range(DC):
                    t.matmul(p[:, 0:TT], yb[:, kc * T + ktt: kc * T + ktt + 128],
                             wsl(0, kc, ocp * 512, 512),
                             start=(kc == 0), stop=False)
                C.inc(t.matmul(p[:, 0:TT], S["oner_b"][0:1, 0:128],
                               S["rowsbf"][0:1, R_BV + ocp * 512: R_BV + ocp * 512 + 512],
                               start=False, stop=True), "mm")
            mmk = C.count("mm")

            @C.scal
            def _(sc, ocp=ocp, mmk=mmk, kt=kt):
                C.wait_at(sc, "mm", mmk)
                if kt == 0:
                    C.wait_at(sc, "ve", score_ve)
                sc.activation(sqf[:, ocp * 512:(ocp + 1) * 512], C.bh(2 + ocp)[:, 0:TT],
                              AF.Copy)
                C.brel(2 + ocp, "se")
            v_marks.append(C.count("se"))

        @C.vec
        def _(v, vm=list(v_marks)):
            C.wait_at(v, "se", vm[-1])
            v.reduce_sum(rows[:, 0:1], sqf[:, 0:1024], axis=AX.X)
        vsum_ve = C.count("ve")

        @C.vec
        def _(v, vsum_ve=vsum_ve):
            v.tensor_tensor(sqf[:, 1024:1536], sqf[:, 0:512], sqf[:, 0:512], ALU.mult)
            v.reduce_sum(rows[:, 1:2], sqf[:, 1024:1536], axis=AX.X)
            v.tensor_tensor(sqf[:, 1024:1536], sqf[:, 512:1024], sqf[:, 512:1024],
                            ALU.mult)
            v.reduce_sum(rows[:, 2:3], sqf[:, 1024:1536], axis=AX.X)
            v.tensor_tensor(rows[:, 1:2], rows[:, 1:2], rows[:, 2:3], ALU.add)
            v.tensor_scalar(rows[:, 0:1], rows[:, 0:1], cst[:, C_I1024:C_I1024 + 1],
                            None, ALU.mult)
            v.tensor_scalar(rows[:, 1:2], rows[:, 1:2], cst[:, C_I1024:C_I1024 + 1],
                            None, ALU.mult)
            v.tensor_tensor(rows[:, 2:3], rows[:, 0:1], rows[:, 0:1], ALU.mult)
            v.tensor_tensor(rows[:, 1:2], rows[:, 1:2], rows[:, 2:3], ALU.subtract)
        vvar_ve = C.count("ve")

        @C.scal
        def _(sc, vvar_ve=vvar_ve):
            C.wait_at(sc, "ve", vvar_ve)
            sc.activation(rows[:, 1:2], rows[:, 1:2], AF.Ln,
                          bias=cst[:, C_EPS:C_EPS + 1])
            sc.activation(rows[:, 1:2], rows[:, 1:2], AF.Exp, scale=-0.5)
        vex_se = C.count("se")

        @C.vec
        def _(v, vex_se=vex_se):
            C.wait_at(v, "se", vex_se)
            v.tensor_scalar(rows[:, 0:1], rows[:, 0:1], cst[:, C_NEG1:C_NEG1 + 1],
                            None, ALU.mult)
            C.wait_at(v, "ve", gvb_ve)
            v.tensor_scalar(sqf[:, 0:1024], sqf[:, 0:1024], rows[:, 0:1],
                            rows[:, 1:2], ALU.add, ALU.mult)
            v.tensor_tensor(sqf[:, 0:1024], sqf[:, 0:1024], smf[:, 0:1024], ALU.mult)
            v.tensor_tensor(S["slotB"][:, 1024:2048], sqf[:, 0:1024],
                            smf[:, 1024:2048], ALU.add)
        vtm_ve = C.count("ve")

        @C.ten
        def _(t, kt=kt, pkw_ve=pkw_ve, vtm_ve=vtm_ve):
            C.wait_at(t, "ve", pkw_ve)
            C.wait_at(t, "ve", vtm_ve)
            if kt == 0:
                C.bacq(t, 4)
                C.bacq(t, 5)
                C.bacq(t, 6)
            ins = None
            for h in range(H):
                p = C.bh(5 if h < 8 else 6)
                cc = (h % 8) * 64
                t.matmul(p[0:64, cc:cc + 64],
                         S["slotB"][:, h * 64:(h + 1) * 64],
                         S["slotB"][:, 1024 + h * 64: 1024 + (h + 1) * 64],
                         start=(kt == 0 and h in (0, 8)),
                         stop=(kt == 15 and h in (7, 15)))
                ins = t.matmul(C.bh(4)[0:64, h: h + 1],
                               S["slotB"][:, h * 64:(h + 1) * 64],
                               S["onec_b"][:, 0:1],
                               start=(kt == 0 and h == 0),
                               stop=(kt == 15 and h == 15))
            C.inc(ins, "mm")
    kvu_mm = C.count("mm")

    S_kvu2 = rows[0:64, 960:1008]
    @C.vec
    def _(v):
        C.wait_at(v, "mm", kvu_mm)
        for bk, hb in [(5, 0), (6, 8)]:
            dst = expand_view(S["kvu"][0:64, hb * 65: hb * 65 + 1], [[65, 8], [1, 64]])
            srcb = expand_view(C.bh(bk)[0:64, 0:1], [[64, 8], [1, 64]])
            v.tensor_copy(dst, srcb)
            C.brel(bk, "ve")
        dstk = expand_view(S["kvu"][0:64, 64:65], [[65, 16]])
        v.tensor_copy(dstk, C.bh(4)[0:64, 0:16])
        C.brel(4, "ve")
    kvu_ve = C.count("ve")

    @C.gps
    def _(gp):
        C.wait(gp, "ve")
        C.inc(gp.dma_start(out=g["mmx_buf"][0:1, :], in_=mz[0:1, MY_M:MY_M + 16]),
              "gc", 16)
        C.wait(gp, "gc")
        C.inc(gp.collective_compute(
            "AllReduce", ALU.max,
            replica_groups=[[0, 1], [2, 3], [4, 5], [6, 7]],
            ins=[g["mmx_buf"][0:1, :].opt()], outs=[g["mmx_red"][0:1, :].opt()]), "cc")

    @C.syn
    def _(sy):
        grp = C.dgrp(sy)
        C.wait(sy, "cc")
        C.inc(sy.dma_start(out=mz[0:1, MM_:MM_ + 16], in_=g["mmx_red"][0:1, :]), grp, 16)
        C.mmx_al = (grp, C.count(grp))

    @C.vec
    def _(v):
        C.wmark(v, C.mmx_al)
        v.tensor_tensor(mz[0:1, WMY:WMY + 16], mz[0:1, MY_M:MY_M + 16],
                        mz[0:1, MM_:MM_ + 16], ALU.subtract)
    dm_ve = C.count("ve")

    @C.scal
    def _(sc, dm_ve=dm_ve):
        C.wait_at(sc, "ve", dm_ve)
        sc.activation(mz[0:1, WMY:WMY + 16], mz[0:1, WMY:WMY + 16], AF.Exp)
    w_se = C.count("se")

    @C.ten
    def _(t, w_se=w_se):
        C.wait_at(t, "se", w_se)
        p = C.bacq(t, 0)
        C.inc(t.matmul(p[0:64, 0:16], S["oner_f"][0:1, 0:64],
                       mz[0:1, WMY:WMY + 16], start=True, stop=True), "mm")
    wc_mm = C.count("mm")

    @C.vec
    def _(v, wc_mm=wc_mm):
        C.wait_at(v, "mm", wc_mm)
        v.tensor_copy(S_kvu2[0:64, 0:16], C.bh(0)[0:64, 0:16])
        C.brel(0, "ve")
        v.tensor_tensor(mz[0:1, MY_ZU:MY_ZU + 16], mz[0:1, MY_ZU:MY_ZU + 16],
                        mz[0:1, WMY:WMY + 16], ALU.mult)
        for h in range(H):
            v.tensor_scalar(S["kvu"][0:64, h * 65:(h + 1) * 65],
                            S["kvu"][0:64, h * 65:(h + 1) * 65],
                            S_kvu2[0:64, h:h + 1], None, ALU.mult)
        v.memset(sqf[0:1, 0:1040], 0.0)
        v.tensor_copy(sqf[0:1, 0:16], mz[0:1, MY_ZU:MY_ZU + 16])
    kvu3_ve = C.count("ve")

    @C.gps
    def _(gp, kvu3_ve=kvu3_ve):
        C.wait_at(gp, "ve", kvu3_ve)
        C.inc(gp.dma_start(out=g["kvu_buf"][0:64, :], in_=S["kvu"][:, :]), "gc", 16)
        C.inc(gp.dma_start(out=g["kvu_buf"][64:65, 0:1040],
                           in_=sqf[0:1, 0:1040]), "gc", 16)
        C.wait(gp, "gc")
        C.inc(gp.collective_compute(
            "AllReduce", ALU.add,
            replica_groups=[[0, 1], [2, 3], [4, 5], [6, 7]],
            ins=[g["kvu_buf"][0:65, :].opt()], outs=[g["kvu_red"][0:65, :].opt()]), "cc")

    @C.syn
    def _(sy):
        grp = C.dgrp(sy)
        C.wait(sy, "cc")
        C.inc(sy.dma_start(out=S["kvu"][:, :], in_=g["kvu_red"][0:64, :]), grp, 16)
        C.inc(sy.dma_start(out=mz[0:1, RZ_:RZ_ + 16], in_=g["kvu_red"][64:65, 0:16]),
              grp, 16)
        C.kvu_al = (grp, C.count(grp))

    @C.vec
    def _(v):
        C.wmark(v, C.kvu_al)
        v.reciprocal(mz[0:1, RZ_:RZ_ + 16], mz[0:1, RZ_:RZ_ + 16])
        v.tensor_scalar(mz[0:1, T2_:T2_ + 16], mz[0:1, MM_:MM_ + 16],
                        cst[0:1, C_NEG1:C_NEG1 + 1], None, ALU.mult)
    rz_ve = C.count("ve")

    @C.scal
    def _(sc, rz_ve=rz_ve):
        C.wait_at(sc, "ve", rz_ve)
        sc.activation(mz[0:1, T2_:T2_ + 16], mz[0:1, T2_:T2_ + 16], AF.Exp)
        C.wmark(sc, C.pre_al)
        sc.activation(rows[0:64, 16:32], pv[0:64, PV_BFKH:PV_BFKH + 16], AF.Tanh)
    e0_se = C.count("se")

    @C.vec
    def _(v, e0_se=e0_se):
        C.wait_at(v, "se", e0_se)
        v.tensor_tensor(mz[0:1, T2_:T2_ + 16], mz[0:1, T2_:T2_ + 16],
                        mz[0:1, RZ_:RZ_ + 16], ALU.mult)
        v.tensor_scalar(rows[0:64, 16:32], rows[0:64, 16:32],
                        cst[0:64, C_ONE:C_ONE + 1], None, ALU.add)
    e0r_ve = C.count("ve")

    @C.ten
    def _(t, e0r_ve=e0r_ve):
        C.wait_at(t, "ve", e0r_ve)
        p = C.bacq(t, 1)
        t.matmul(p[0:64, 0:16], S["oner_f"][0:1, 0:64],
                 mz[0:1, RZ_:RZ_ + 16], start=True, stop=True)
        C.inc(t.matmul(p[0:64, 16:32], S["oner_f"][0:1, 0:64],
                       mz[0:1, T2_:T2_ + 16], start=True, stop=True), "mm")
    bc2_mm = C.count("mm")

    @C.vec
    def _(v, bc2_mm=bc2_mm):
        C.wait_at(v, "mm", bc2_mm)
        v.tensor_copy(S_kvu2[0:64, 16:48], C.bh(1)[0:64, 0:32])
        C.brel(1, "ve")
        for h in range(H):
            v.tensor_scalar(S["kvu"][0:64, h * 65:(h + 1) * 65],
                            S["kvu"][0:64, h * 65:(h + 1) * 65],
                            S_kvu2[0:64, 16 + h:17 + h], None, ALU.mult)
            v.tensor_scalar(rows[0:64, 32:33], rows[0:64, 16 + h:17 + h],
                            S_kvu2[0:64, 32 + h:33 + h], None, ALU.mult)
            v.tensor_tensor(S["kvu"][0:64, h * 65 + 64: h * 65 + 65],
                            S["kvu"][0:64, h * 65 + 64: h * 65 + 65],
                            rows[0:64, 32:33], ALU.add)
        dstkv = expand_view(S["kvb"][0:64, 0:1], [[64, 16], [1, 64]])
        srckv = expand_view(S["kvu"][0:64, 0:1], [[65, 16], [1, 64]])
        v.tensor_copy(dstkv, srckv)
        dstks = expand_view(S["kvb"][0:64, 1024:1025], [[1, 16]])
        srcks = expand_view(S["kvu"][0:64, 64:65], [[65, 16]])
        v.tensor_copy(dstks, srcks)
    kvb_ve = C.count("ve")

    @C.syn
    def _(sy):
        grp = C.dgrp(sy)
        C.wait_at(sy, "ve", kvb_ve)
        C.inc(sy.dma_start(out=S["kvb"][64:128, :], in_=S["kvb"][0:64, :]), grp, 16)
        C.kvbdup = (grp, C.count(grp))

    p8_prev_mm = 0
    for tt in range(NTT):
        c0 = tt * TT

        @C.syn
        def _(sy, c0=c0, pm=p8_prev_mm):
            grp = C.dgrp(sy)
            C.wmark(sy, C.pq_spill_st)
            C.wait_at(sy, "ve", kvb_ve)
            C.wait_at(sy, "mm", pm)
            for oc in range(DC):
                C.inc(sy.dma_start(out=S["slotA"][:, oc * TT:(oc + 1) * TT],
                                   in_=g["pq_spill"][oc * 128:(oc + 1) * 128, c0:c0 + TT]),
                      grp, 16)
            C.pq_al = (grp, C.count(grp))

        for h in range(H):
            pb = (h % 2) * 64
            bk = 3 if h % 2 == 0 else 4
            dbk = 5 if h % 2 == 0 else 6
            ch = (h // 2)

            @C.ten
            def _(t, h=h, pb=pb, ch=ch, bk=bk, dbk=dbk):
                p = C.bacq(t, bk)
                pd = C.bacq(t, dbk)
                C.wmark(t, C.pq_al)
                C.wait_at(t, "ve", kvb_ve)
                C.wmark(t, C.kvbdup)
                rhs = S["slotA"][pb:pb + 64, ch * TT: ch * TT + TT]
                t.matmul(p[pb:pb + 64, 0:TT],
                         S["kvb"][pb:pb + 64, h * 64: h * 64 + 64],
                         rhs, start=True, stop=True,
                         skip_group_check=(pb == 64))
                C.inc(t.matmul(pd[0:1, 0:TT], S["kvb"][pb:pb + 64, 1024 + h:1025 + h],
                               rhs, start=True, stop=True), "mm")
            nd_mm = C.count("mm")

            @C.vec
            def _(v, nd_mm=nd_mm, dbk=dbk):
                C.wait_at(v, "mm", nd_mm)
                v.tensor_scalar(rows[0:1, 0:TT], C.bh(dbk)[0:1, 0:TT],
                                cst[0:1, C_EPS6:C_EPS6 + 1], None, ALU.add)
                C.brel(dbk, "ve")
                v.reciprocal(rows[0:1, 0:TT], rows[0:1, 0:TT])
            rc_ve = C.count("ve")

            @C.ten
            def _(t, rc_ve=rc_ve, pb=pb):
                C.wait_at(t, "ve", rc_ve)
                p2 = C.bacq(t, 0)
                C.inc(t.matmul(p2[pb:pb + 64, 0:TT], S["oner_f"][0:1, 0:64],
                               rows[0:1, 0:TT], start=True, stop=True,
                               skip_group_check=(pb == 64)), "mm")
            rb_mm = C.count("mm")

            @C.vec
            def _(v, h=h, pb=pb, ch=ch, rb_mm=rb_mm, bk=bk):
                C.wait_at(v, "mm", rb_mm)
                v.tensor_copy(sqf[pb:pb + 64, 0:TT], C.bh(0)[pb:pb + 64, 0:TT])
                C.brel(0, "ve")
                v.tensor_tensor(S["slotB"][pb:pb + 64, ch * TT: ch * TT + TT],
                                C.bh(bk)[pb:pb + 64, 0:TT], sqf[pb:pb + 64, 0:TT],
                                ALU.mult)
                C.brel(bk, "ve")
        a_done = C.count("ve")
        p8_prev_mm = C.count("mm")

        fm_ln(C, S,
              src_chunk=lambda kc: S["slotB"][:, kc * TT:(kc + 1) * TT],
              ttw=TT,
              dst_chunk=lambda kc, c0=c0: kb[:, kc * T + c0: kc * T + c0 + TT],
              gamma_beta=(PV_GA, PV_BA),
              src_ready=lambda kc, am=a_done: ("ve", am))
    a_ve_done = C.count("ve")
    maybe_dump(C, g, S, "p8_a", kb)
    if C.dumped:
        return

    TB = 256
    w_par_marks = [None, None]
    g_use_marks = [C.count("mm"), C.count("mm")]
    p9_prev_ln = C.count("ve")
    for tb in range(8):
        b0 = tb * TB
        for gi in range(8):
            par = gi % 2
            wc0 = WS0 if par == 0 else WS1

            @C.syn
            def _(sy, tb=tb, gi=gi, par=par, wc0=wc0):
                grp = C.dgrp(sy)
                C.wait_at(sy, "mm", g_use_marks[par])
                if tb == 0 and gi < 2:
                    C.wait_at(sy, "pc", C.precast_pc)
                C.inc(sy.dma_start(out=vb[:, wc0:wc0 + 4096], in_=g["w1_bf"][gi]),
                      grp, 16)
                C.inc(sy.dma_start(out=vb[:, wc0 + 4096:wc0 + 8192], in_=g["w2_bf"][gi]),
                      grp, 16)
                w_par_marks[par] = (grp, C.count(grp))

            for j in range(4):
                hbk = 5 if j % 2 == 0 else 6

                @C.ten
                def _(t, gi=gi, j=j, par=par, wc0=wc0, b0=b0, hbk=hbk):
                    p = C.bacq(t, hbk)
                    C.wmark(t, w_par_marks[par])
                    C.wait_at(t, "ve", a_ve_done)
                    for kc in range(DC):
                        ins = t.matmul(
                            p[:, 0:TB],
                            vb[:, wc0 + (j * 8 + kc) * 128: wc0 + (j * 8 + kc) * 128 + 128],
                            kb[:, kc * T + b0: kc * T + b0 + TB],
                            start=(kc == 0), stop=(kc == DC - 1))
                    C.inc(ins, "mm")
                h_mm = C.count("mm")

                @C.scal
                def _(sc, gi=gi, j=j, h_mm=h_mm, hbk=hbk):
                    C.wait_at(sc, "mm", h_mm)
                    sc.activation(S["slotA"][:, j * TB:(j + 1) * TB], C.bh(hbk)[:, 0:TB],
                                  AF.Gelu, bias=S["b1fm"][:, gi * 4 + j: gi * 4 + j + 1])
                    C.brel(hbk, "se")
                hb_se = C.count("se")

                @C.ten
                def _(t, gi=gi, j=j, wc0=wc0, hb_se=hb_se):
                    C.wait_at(t, "se", hb_se)
                    ins = None
                    for oc in range(DC):
                        obk = oc // 2
                        chh = (oc % 2) * TB
                        p = C.bacq(t, obk) if (gi == 0 and j == 0 and oc % 2 == 0) \
                            else C.bh(obk)
                        ins = t.matmul(p[:, chh:chh + TB],
                                       vb[:, wc0 + 4096 + j * 1024 + oc * 128:
                                          wc0 + 4096 + j * 1024 + oc * 128 + 128],
                                       S["slotA"][:, j * TB:(j + 1) * TB],
                                       start=(gi == 0 and j == 0 and oc % 2 == 0),
                                       stop=(gi == 7 and j == 3 and oc % 2 == 1))
                    C.inc(ins, "mm")
                g_use_marks[par] = C.count("mm")
        mlp_mm = C.count("mm")

        a0_marks = []
        for oc in range(DC):
            obk = oc // 2
            chh = (oc % 2) * TB

            @C.scal
            def _(sc, oc=oc, obk=obk, chh=chh, mlp_mm=mlp_mm, plv=p9_prev_ln):
                C.wait_at(sc, "mm", mlp_mm)
                C.wait_at(sc, "ve", plv)
                sc.activation(smf[:, oc * TB:(oc + 1) * TB], C.bh(obk)[:, chh:chh + TB],
                              AF.Identity, bias=pv[:, PV_B2 + oc:PV_B2 + oc + 1])
                if oc % 2 == 1:
                    C.brel(obk, "se")
            a0_se = C.count("se")

            @C.vec
            def _(v, oc=oc, b0=b0, a0_se=a0_se):
                C.wait_at(v, "se", a0_se)
                v.tensor_tensor(smf[:, oc * TB:(oc + 1) * TB],
                                smf[:, oc * TB:(oc + 1) * TB],
                                kb[:, oc * T + b0: oc * T + b0 + TB], ALU.add)
                v.tensor_tensor(smf[:, oc * TB:(oc + 1) * TB],
                                smf[:, oc * TB:(oc + 1) * TB],
                                yb[:, oc * T + b0: oc * T + b0 + TB], ALU.add)
            a0_marks.append(C.count("ve"))

        fm_ln(C, S,
              src_chunk=lambda kc: smf[:, kc * TB:(kc + 1) * TB],
              ttw=TB,
              residual_into=lambda kc, b0=b0: xs[:, kc * T + b0: kc * T + b0 + TB],
              src_f32=True,
              src_ready=lambda kc, am=list(a0_marks): ("ve", am[kc]))
        p9_prev_ln = C.count("ve")
    maybe_dump(C, g, S, "p9_x1", xs)



def emit_phase2(C, g, S):
    xs, yb, kb, vb = S["xs"], S["yb"], S["kb"], S["vb"]
    slotA, slotB, sqf, rows, smf = S["slotA"], S["slotB"], S["sqf"], S["rows"], S["smf"]
    cst, pv2 = S["consts"], S["pv2"]
    WS0, WS1 = 0, 8192

    def wsl(slot, kc, c0, cn):
        base = WS0 if slot == 0 else WS1
        return vb[:, base + kc * 1024 + c0: base + kc * 1024 + c0 + cn]

    def load_wcols(sy, dram_bf, col0, slot, gate_mm=0):
        grp = C.dgrp(sy)
        if gate_mm:
            C.wait_at(sy, "mm", gate_mm)
        C.wait_at(sy, "pc", C.precast_pc)
        for kc in range(DC):
            C.inc(sy.dma_start(out=wsl(slot, kc, 0, 1024),
                               in_=dram_bf[kc * 128:(kc + 1) * 128, col0:col0 + 1024]),
                  grp, 16)
        return (grp, C.count(grp))

    ph_mm0 = C.count("mm")
    wl2 = {}

    @C.syn
    def _(sy):
        wl2["q"] = load_wcols(sy, g["wqkv_bf"], 0, 0, gate_mm=ph_mm0)
        wl2["k"] = load_wcols(sy, g["wqkv_bf"], 1024, 1, gate_mm=ph_mm0)

    p2_marks = []
    for tt in range(NTT):
        c0 = tt * TT
        m = fm_ln(C, S,
                  src_chunk=lambda kc, c0=c0: xs[:, kc * T + c0: kc * T + c0 + TT],
                  ttw=TT,
                  dst_chunk=lambda kc, c0=c0: yb[:, kc * T + c0: kc * T + c0 + TT],
                  src_f32=True)
        p2_marks.append(m)
    maybe_dump(C, g, S, "p2_y3", yb)
    if C.dumped:
        return

    spill_marks = []

    def proj_spill(wready, pvcol, dram_rows_fn):
        for tt in range(NTT):
            c0 = tt * TT
            ev_marks = []

            def sp_out(oc, bk, mmk, ev_marks=ev_marks, pvcol=pvcol,
                       prev=(spill_marks[-1] if spill_marks else None)):
                @C.scal
                def _(sc, oc=oc, bk=bk, mmk=mmk):
                    C.wait_at(sc, "mm", mmk)
                    if prev is not None:
                        C.wmark(sc, prev)
                    sc.activation(S["slotB"][:, oc * TT:oc * TT + TT], C.bh(bk)[:, 0:TT],
                                  AF.Identity, bias=pv2[:, pvcol + oc:pvcol + oc + 1])
                    C.brel(bk, "se")
                ev_marks.append(C.count("se"))

            fm_proj(C, S, lambda kc, oc: wsl(wready[2], kc, oc * 128, 128),
                    lambda kc, c0=c0: yb[:, kc * T + c0: kc * T + c0 + TT],
                    TT, sp_out, w_ready=wready[0:2],
                    src_ready=lambda oc, tt=tt: ("ve", p2_marks[tt]))

            @C.syn
            def _(sy, c0=c0, em=list(ev_marks)):
                grp = C.dgrp(sy)
                for oc in range(DC):
                    C.wait_at(sy, "se", em[oc])
                    r0, r1 = dram_rows_fn(oc)
                    C.inc(sy.dma_start(out=r0[r1 + oc * 128: r1 + (oc + 1) * 128, c0:c0 + TT],
                                       in_=S["slotB"][:, oc * TT:oc * TT + TT]), grp, 16)
                spill_marks.append((grp, C.count(grp)))

    proj_spill(wl2["q"] + (0,), PV2_BQ, lambda oc: (g["q_buf"], 0))
    q2_mm = C.count("mm")
    proj_spill(wl2["k"] + (1,), PV2_BK, lambda oc: (g["kv_buf"], 0))
    k2_mm = C.count("mm")

    @C.syn
    def _(sy):
        wl2["v"] = load_wcols(sy, g["wqkv_bf"], 2048, 0, gate_mm=q2_mm)
    proj_spill(wl2["v"] + (0,), PV2_BV, lambda oc: (g["kv_buf"], 1024))
    qkv_mm = C.count("mm")
    qkv_se = C.count("se")

    @C.gps
    def _(gp):
        for mk in spill_marks:
            C.wmark(gp, mk)
        C.inc(gp.collective_compute(
            "AllGather", ALU.bypass,
            replica_groups=[[0, 1], [2, 3], [4, 5], [6, 7]],
            ins=[g["kv_buf"][:, :].opt()], outs=[g["kv_gath"][:, :].opt()]), "cc")

    ones_ve = C.count("ve")

    head_mm_prev = {0: qkv_mm, 1: qkv_mm}
    head_mm_all = qkv_mm
    prev_head_ve = C.count("ve")
    prev_exp_se = C.count("se")
    for h in range(H):
        par = h % 2
        kbase = 0 if par == 0 else 8192
        vbase = 4096 if par == 0 else 12288
        pb = (h % 2) * 64
        ch = h // 2

        @C.syn
        def _(sy, h=h, par=par, kbase=kbase, vbase=vbase):
            grp = C.dgrp(sy)
            C.wait(sy, "cc")
            C.wait_at(sy, "ve", ones_ve)
            C.wait_at(sy, "mm", head_mm_prev[par])
            for r in range(2):
                C.inc(sy.dma_start(
                    out=vb[0:64, kbase + r * T: kbase + (r + 1) * T],
                    in_=g["kv_gath"][r * 2048 + h * 64: r * 2048 + (h + 1) * 64, :]),
                    grp, 16)
                C.inc(sy.dma_start(
                    out=vb[0:64, vbase + r * T: vbase + (r + 1) * T],
                    in_=g["kv_gath"][r * 2048 + 1024 + h * 64: r * 2048 + 1024 + (h + 1) * 64, :]),
                    grp, 16)
            C.khm = (grp, C.count(grp))
        kh_mark = C.khm

        @C.syn
        def _(sy, h=h, hma=head_mm_all):
            grp = C.dgrp(sy)
            for mk in spill_marks[0:4]:
                C.wmark(sy, mk)
            C.wait_at(sy, "ve", prev_head_ve)
            C.wait_at(sy, "mm", hma)
            C.inc(sy.dma_start(out=slotA[0:64, 2048:4096],
                               in_=g["q_buf"][h * 64:(h + 1) * 64, :]), grp, 16)
            C.qm = (grp, C.count(grp))
        q_mark = C.qm

        bm_marks = []
        for j in range(12):
            if j < 8:
                src_ap = vb[0:64, kbase + j * 512: kbase + (j + 1) * 512]
                rdy = kh_mark
            else:
                src_ap = slotA[0:64, 2048 + (j - 8) * 512: 2048 + (j - 7) * 512]
                rdy = q_mark

            @C.vec
            def _(sc, j=j, src_ap=src_ap, rdy=rdy, phv=prev_head_ve):
                C.wmark(sc, rdy)
                if j > 0:
                    C.wait_at(sc, "mm", bm_marks[j - 1])
                sc.tensor_tensor(sqf[0:64, 0:512], src_ap, src_ap, ALU.mult)
            sq_se = C.count("ve")

            @C.ten
            def _(t, j=j, sq_se=sq_se):
                C.wait_at(t, "ve", sq_se)
                p = C.bacq(t, 6)
                C.inc(t.matmul(p[0:1, 0:512], S["onec_f"][0:64, 0:1], sqf[0:64, 0:512],
                               start=True, stop=True), "mm")
            bm_marks.append(C.count("mm"))

            @C.vec
            def _(v, j=j, mk=bm_marks[-1]):
                C.wait_at(v, "mm", mk)
                v.reduce_max(rows[0:1, 34 + (0 if j < 8 else 1):35 + (0 if j < 8 else 1)],
                             C.bh(6)[0:1, 0:512], axis=AX.X)
                C.brel(6, "ve")
                dst = 32 if j < 8 else 33
                if j in (0, 8):
                    v.tensor_copy(rows[0:1, dst:dst + 1], rows[0:1, 34 + (0 if j < 8 else 1):35 + (0 if j < 8 else 1)])
                else:
                    v.tensor_tensor(rows[0:1, dst:dst + 1], rows[0:1, dst:dst + 1],
                                    rows[0:1, 34 + (0 if j < 8 else 1):35 + (0 if j < 8 else 1)],
                                    ALU.max)
        bnd_ve = C.count("ve")

        @C.scal
        def _(sc, bnd_ve=bnd_ve):
            C.wait_at(sc, "ve", bnd_ve)
            sc.activation(rows[0:1, 32:33], rows[0:1, 32:33], AF.Ln)
            sc.activation(rows[0:1, 33:34], rows[0:1, 33:34], AF.Ln)
        bnds_se = C.count("se")

        @C.vec
        def _(v, bnds_se=bnds_se):
            C.wait_at(v, "se", bnds_se)
            v.tensor_tensor(rows[0:1, 32:33], rows[0:1, 32:33], rows[0:1, 33:34], ALU.add)
        bnh_ve = C.count("ve")

        @C.scal
        def _(sc, bnh_ve=bnh_ve):
            C.wait_at(sc, "ve", bnh_ve)
            sc.activation(rows[0:1, 32:33], rows[0:1, 32:33], AF.Exp, scale=0.5)
        bex_se = C.count("se")

        @C.vec
        def _(v, bex_se=bex_se):
            C.wait_at(v, "se", bex_se)
            v.tensor_scalar(rows[0:1, 32:33], rows[0:1, 32:33],
                            cst[0:1, C_NEG1:C_NEG1 + 1], cst[0:1, C_I8:C_I8 + 1],
                            ALU.mult, ALU.mult)
            v.tensor_scalar(slotA[0:64, 2048:4096], slotA[0:64, 2048:4096],
                            cst[0:64, C_I8:C_I8 + 1], None, ALU.mult)
        nb8_ve = C.count("ve")

        @C.ten
        def _(t, nb8_ve=nb8_ve):
            C.wait_at(t, "ve", nb8_ve)
            p6 = C.bacq(t, 6)
            C.inc(t.matmul(p6[:, 0:1], S["oner_f"][0:1, 0:128],
                           rows[0:1, 32:33], start=True, stop=True), "mm")
        nbm = C.count("mm")

        @C.vec
        def _(v, nbm=nbm, pes=prev_exp_se):
            C.wait_at(v, "mm", nbm)
            C.wait_at(v, "se", pes)
            v.tensor_copy(smf[:, 2047:2048], C.bh(6)[:, 0:1])
            C.brel(6, "ve")
        qaug_ve = C.count("ve")

        vt_marks = []
        for kc in range(32):
            @C.ten
            def _(t, kc=kc, vbase=vbase, qaug_ve=qaug_ve):
                pT = C.bacq(t, "T")
                C.wmark(t, kh_mark)
                C.wait_at(t, "ve", qaug_ve)
                C.inc(t.transpose(pT[:, 0:64], vb[0:64, vbase + kc * 128: vbase + (kc + 1) * 128],
                                  S["identb"][0:64, 0:64]), "mm")
            trm = C.count("mm")

            @C.vec
            def _(v, kc=kc, trm=trm):
                C.wait_at(v, "mm", trm)
                v.tensor_copy(slotA[:, kc * 64:(kc + 1) * 64], C.bh("T")[:, 0:64])
                C.brel("T", "ve")
            vt_marks.append(C.count("ve"))
        vtm_ve = C.count("ve")

        for qh in range(2):
            for kc in range(32):
                s_marks = []
                for q2 in range(2):
                    qt = qh * 2 + q2
                    sbk = 4 if (kc * 2 + q2) % 2 == 0 else 5

                    @C.ten
                    def _(t, h=h, kc=kc, qt=qt, sbk=sbk, kbase=kbase, qaug_ve=qaug_ve):
                        p = C.bacq(t, sbk)
                        C.wait_at(t, "ve", qaug_ve)
                        C.inc(t.matmul(p[:, 0:TT],
                                       vb[0:64, kbase + kc * 128: kbase + (kc + 1) * 128],
                                       slotA[0:64, 2048 + qt * TT: 2048 + (qt + 1) * TT],
                                       start=True, stop=True), "mm")
                    smm = C.count("mm")

                    @C.scal
                    def _(sc, q2=q2, sbk=sbk, smm=smm, pes=prev_exp_se):
                        C.wait_at(sc, "mm", smm)
                        C.wait_at(sc, "se", pes)
                        sc.activation(slotB[:, q2 * TT:(q2 + 1) * TT], C.bh(sbk)[:, 0:TT],
                                      AF.Exp, bias=smf[:, 2047:2048])
                        C.brel(sbk, "se")
                    s_marks.append(C.count("se"))
                prev_exp_se = C.count("se")

                @C.ten
                def _(t, h=h, qh=qh, kc=kc, pb=pb, sm=list(s_marks), vtm_ve=vtm_ve):
                    C.wait_at(t, "se", sm[-1])
                    C.wait_at(t, "ve", vtm_ve)
                    ins = None
                    for q2 in range(2):
                        nbk = q2
                        dbk = 2 + q2
                        pn = C.bacq(t, nbk) if kc == 0 else C.bh(nbk)
                        pd = C.bacq(t, dbk) if kc == 0 else C.bh(dbk)
                        t.matmul(pn[pb:pb + 64, 0:TT], slotA[:, kc * 64:(kc + 1) * 64],
                                 slotB[:, q2 * TT:(q2 + 1) * TT],
                                 start=(kc == 0), stop=(kc == 31),
                                 skip_group_check=(pb == 64))
                        ins = t.matmul(pd[0:1, 0:TT], S["onec_b"][:, 0:1],
                                       slotB[:, q2 * TT:(q2 + 1) * TT],
                                       start=(kc == 0), stop=(kc == 31))
                    C.inc(ins, "mm")
            pv_mm = C.count("mm")

            for q2 in range(2):
                qt = qh * 2 + q2
                nbk, dbk = q2, 2 + q2

                @C.vec
                def _(v, dbk=dbk, pv_mm=pv_mm):
                    C.wait_at(v, "mm", pv_mm)
                    v.reciprocal(rows[0:1, 0:TT], C.bh(dbk)[0:1, 0:TT])
                    C.brel(dbk, "ve")
                rcv = C.count("ve")

                @C.ten
                def _(t, pb=pb, rcv=rcv):
                    C.wait_at(t, "ve", rcv)
                    p2 = C.bacq(t, 6)
                    C.inc(t.matmul(p2[pb:pb + 64, 0:TT], S["oner_f"][0:1, 0:64],
                                   rows[0:1, 0:TT], start=True, stop=True,
                                   skip_group_check=(pb == 64)), "mm")
                rbm = C.count("mm")

                @C.vec
                def _(v, h=h, pb=pb, ch=ch, qt=qt, nbk=nbk, rbm=rbm):
                    C.wait_at(v, "mm", rbm)
                    v.tensor_copy(sqf[pb:pb + 64, 0:TT], C.bh(6)[pb:pb + 64, 0:TT])
                    C.brel(6, "ve")
                    v.tensor_tensor(kb[pb:pb + 64, ch * T + qt * TT: ch * T + (qt + 1) * TT],
                                    C.bh(nbk)[pb:pb + 64, 0:TT], sqf[pb:pb + 64, 0:TT],
                                    ALU.mult)
                    C.brel(nbk, "ve")
        head_mm_prev[par] = C.count("mm")
        head_mm_all = C.count("mm")
        prev_head_ve = C.count("ve")
    heads_mm = C.count("mm")
    heads_ve = C.count("ve")
    maybe_dump(C, g, S, "p2_o", kb)
    if C.dumped:
        return

    @C.syn
    def _(sy):
        C.wl_out = load_wcols(sy, g["wout_bf"], 0, 0, gate_mm=heads_mm)

    TB = 256
    prev_ln_ve = C.count("ve")
    for tb in range(8):
        b0 = tb * TB
        a0_marks = []

        def o_out(oc, bk, mmk, b0=b0, a0_marks=a0_marks, plv=prev_ln_ve):
            @C.scal
            def _(sc, oc=oc, bk=bk, mmk=mmk):
                C.wait_at(sc, "mm", mmk)
                C.wait_at(sc, "ve", plv)
                sc.activation(smf[:, oc * TB:(oc + 1) * TB], C.bh(bk)[:, 0:TB],
                              AF.Identity, bias=pv2[:, PV2_BO + oc:PV2_BO + oc + 1])
                C.brel(bk, "se")
            a0se = C.count("se")

            @C.vec
            def _(v, oc=oc, b0=b0, a0se=a0se):
                C.wait_at(v, "se", a0se)
                v.tensor_tensor(smf[:, oc * TB:(oc + 1) * TB],
                                smf[:, oc * TB:(oc + 1) * TB],
                                yb[:, oc * T + b0: oc * T + b0 + TB], ALU.add)
            a0_marks.append(C.count("ve"))

        fm_proj(C, S, lambda kc, oc: wsl(0, kc, oc * 128, 128),
                lambda kc, b0=b0: kb[:, kc * T + b0: kc * T + b0 + TB],
                TB, o_out, w_ready=C.wl_out,
                src_ready=lambda oc: ("ve", heads_ve))
        fm_ln(C, S,
              src_chunk=lambda kc: smf[:, kc * TB:(kc + 1) * TB],
              ttw=TB,
              dst_chunk=lambda kc, b0=b0: yb[:, kc * T + b0: kc * T + b0 + TB],
              gamma_beta=(PV2_LN1G, PV2_LN1B),
              src_f32=True, pv_src="pv2",
              src_ready=lambda kc, am=list(a0_marks): ("ve", am[kc]))
        prev_ln_ve = C.count("ve")
    maybe_dump(C, g, S, "p2_y1", yb)


def emit_phase3(C, g, S):
    xs, yb, kb, vb = S["xs"], S["yb"], S["kb"], S["vb"]
    slotA, slotB, sqf, rows, smf = S["slotA"], S["slotB"], S["sqf"], S["rows"], S["smf"]
    cst, pv3 = S["consts"], S["pv3"]
    cbT8 = slotB[0:8, 2048:4096]
    b2sb = S["kvb"][0:8, 0:1024]
    wge = S["scoreT"][:, 0:96]
    TB = 256
    ph_mm0 = C.count("mm")
    ph_ve0 = C.count("ve")

    @C.gps
    def _(gp):
        C.wait_at(gp, "mm", ph_mm0)
        C.inc(gp.dma_start(out=b2sb, in_=g["l0f_b2"][:]), "gc", 16)
        C.b2_gc = C.count("gc")

    @C.syn
    def _(sy):
        grp = C.dgrp(sy)
        C.wait_at(sy, "pc", C.precast_pc)
        for kc in range(DC):
            C.inc(sy.dma_start(out=wge[:, kc * 12:(kc + 1) * 12],
                               in_=g["wge_bf"][kc * 128:(kc + 1) * 128, :]), grp, 16)
        C.wge_mk = (grp, C.count(grp))

    prev_rt_mm = ph_mm0
    for gr in range(16):
        t0 = gr * 128

        @C.ten
        def _(t, gr=gr, t0=t0):
            p = C.bacq(t, 4)
            C.wmark(t, C.wge_mk)
            C.wait_at(t, "ve", ph_ve0)
            ins = None
            for kc in range(DC):
                ins = t.matmul(p[:, 0:12], yb[:, kc * T + t0: kc * T + t0 + 128],
                               wge[:, kc * 12:(kc + 1) * 12],
                               start=(kc == 0), stop=False)
            C.inc(t.matmul(p[:, 0:12], S["oner_b"][0:1, 0:128], S["bge"][0:1, :],
                           start=False, stop=True), "mm")
        rt_mm = C.count("mm")

        @C.vec
        def _(v, gr=gr, rt_mm=rt_mm):
            C.wait_at(v, "mm", rt_mm)
            R = rows
            v.tensor_copy(R[:, 0:12], C.bh(4)[:, 0:12])
            C.brel(4, "ve")
            v.reduce_max(R[:, 12:13], R[:, 0:4], axis=AX.X)
            v.tensor_scalar(R[:, 13:14], R[:, 12:13], cst[:, C_NEG1:C_NEG1 + 1],
                            None, ALU.mult)
        gm_ve = C.count("ve")

        @C.scal
        def _(sc, gm_ve=gm_ve):
            C.wait_at(sc, "ve", gm_ve)
            sc.activation(rows[:, 0:4], rows[:, 0:4], AF.Exp, bias=rows[:, 13:14],
                          accum_out=rows[:, 14:15])
        gexp_se = C.count("se")

        @C.vec
        def _(v, gr=gr, gexp_se=gexp_se):
            R = rows
            C.wait_at(v, "se", gexp_se)
            v.reciprocal(R[:, 14:15], R[:, 14:15])
            v.tensor_scalar(R[:, 0:4], R[:, 0:4], R[:, 14:15], None, ALU.mult)
            ev = expand_view(R[:, 4:5], [[2, 4]])
            od = expand_view(R[:, 5:6], [[2, 4]])
            v.tensor_tensor(R[:, 16:20], ev, od, ALU.max)
            m2x = expand_view(R[:, 16:17], [[1, 4], [0, 2]])
            v.tensor_tensor(R[:, 20:28], R[:, 4:12], m2x, ALU.subtract)
        ep_ve = C.count("ve")

        @C.scal
        def _(sc, ep_ve=ep_ve):
            C.wait_at(sc, "ve", ep_ve)
            sc.activation(rows[:, 20:28], rows[:, 20:28], AF.Exp)
        ep_se = C.count("se")

        @C.vec
        def _(v, gr=gr, t0=t0, ep_se=ep_se):
            R = rows
            C.wait_at(v, "se", ep_se)
            ev = expand_view(R[:, 20:21], [[2, 4]])
            od = expand_view(R[:, 21:22], [[2, 4]])
            v.tensor_tensor(R[:, 16:20], ev, od, ALU.add)
            v.reciprocal(R[:, 16:20], R[:, 16:20])
            sx = expand_view(R[:, 16:17], [[1, 4], [0, 2]])
            v.tensor_tensor(R[:, 20:28], R[:, 20:28], sx, ALU.mult)
            v.reduce_max(R[:, 12:13], R[:, 0:4], axis=AX.X)
            for gi in range(4):
                v.tensor_scalar(R[:, 28 + gi:29 + gi], R[:, gi:gi + 1],
                                R[:, 12:13], None, ALU.is_ge)
            v.tensor_copy(R[:, 32:33], R[:, 28:29])
            v.tensor_scalar(R[:, 36:37], R[:, 32:33], cst[:, C_NEG1:C_NEG1 + 1],
                            cst[:, C_ONE:C_ONE + 1], ALU.mult, ALU.add)
            for gi in range(1, 4):
                v.tensor_tensor(R[:, 32 + gi:33 + gi], R[:, 28 + gi:29 + gi],
                                R[:, 36:37], ALU.mult)
                if gi < 3:
                    v.tensor_tensor(R[:, 36:37], R[:, 36:37],
                                    R[:, 32 + gi:33 + gi], ALU.subtract)
            ev2 = expand_view(R[:, 20:21], [[2, 4]])
            od2 = expand_view(R[:, 21:22], [[2, 4]])
            v.tensor_tensor(R[:, 40:44], ev2, od2, ALU.is_ge)
            v.tensor_scalar(R[:, 44:48], R[:, 40:44], cst[:, C_NEG1:C_NEG1 + 1],
                            cst[:, C_ONE:C_ONE + 1], ALU.mult, ALU.add)
            v.tensor_tensor(R[:, 48:52], ev2, od2, ALU.max)
            v.tensor_tensor(R[:, 52:56], R[:, 32:36], R[:, 48:52], ALU.mult)
            v.tensor_scalar(R[:, 52:56], R[:, 52:56], R[:, 12:13], None, ALU.mult)
            cbe = expand_view(S["slotB"][:, 0:1], [[2, 4]])
            cbo = expand_view(S["slotB"][:, 1:2], [[2, 4]])
            v.tensor_tensor(cbe, R[:, 52:56], R[:, 40:44], ALU.mult)
            v.tensor_tensor(cbo, R[:, 52:56], R[:, 44:48], ALU.mult)
        cb_ve = C.count("ve")

        @C.ten
        def _(t, gr=gr, cb_ve=cb_ve):
            pT = C.bacq(t, "T")
            C.wait_at(t, "ve", cb_ve)
            C.inc(t.transpose(pT[0:8, 0:128], S["slotB"][:, 0:8], S["identb"][:, :]), "mm")
        tr_mm = C.count("mm")

        @C.vec
        def _(v, gr=gr, tr_mm=tr_mm):
            C.wait_at(v, "mm", tr_mm)
            v.tensor_copy(cbT8[0:8, gr * 128:(gr + 1) * 128], C.bh("T")[0:8, 0:128])
            C.brel("T", "ve")
    cb_done = C.count("ve")

    WS0, WS1 = 0, 8192
    w_marks = [None, None]
    g_use = [C.count("mm"), C.count("mm")]
    prev_ln_ve = C.count("ve")
    for tb in range(8):
        b0 = tb * TB
        for e in range(8):
            @C.ten
            def _(t, e=e, tb=tb, cb_done=cb_done):
                p = C.bacq(t, 4)
                C.wait_at(t, "ve", cb_done)
                basis = expand_view(S["identb"][0:8, e:e + 1], [[0, 128]])
                C.inc(t.matmul(p[:, 0:TB], basis,
                               cbT8[0:8, tb * TB:(tb + 1) * TB],
                               start=True, stop=True), "mm")
            cbb_mm = C.count("mm")

            @C.vec
            def _(v, cbb_mm=cbb_mm):
                C.wait_at(v, "mm", cbb_mm)
                v.tensor_copy(sqf[:, 0:TB], C.bh(4)[:, 0:TB])
                C.brel(4, "ve")
            cbb_ve = C.count("ve")

            for gq in range(4):
                par = (e * 4 + gq) % 2
                wc0 = WS0 if par == 0 else WS1

                @C.syn
                def _(sy, e=e, gq=gq, par=par, wc0=wc0):
                    grp = C.dgrp(sy)
                    C.wait_at(sy, "mm", g_use[par])
                    C.wait_at(sy, "pc2", C.precast_pc2)
                    C.inc(sy.dma_start(out=vb[:, wc0:wc0 + 4096],
                                       in_=g["moew1_bf"][e, gq]), grp, 16)
                    C.inc(sy.dma_start(out=vb[:, wc0 + 4096:wc0 + 8192],
                                       in_=g["moew2_bf"][e, gq]), grp, 16)
                    w_marks[par] = (grp, C.count(grp))

                for j in range(4):
                    hc = gq * 4 + j
                    hbk = 5 if j % 2 == 0 else 6

                    @C.ten
                    def _(t, e=e, j=j, par=par, wc0=wc0, b0=b0, hbk=hbk):
                        p = C.bacq(t, hbk)
                        C.wmark(t, w_marks[par])
                        C.wait_at(t, "ve", cbb_ve)
                        for kc in range(DC):
                            ins = t.matmul(
                                p[:, 0:TB],
                                vb[:, wc0 + (j * 8 + kc) * 128: wc0 + (j * 8 + kc) * 128 + 128],
                                yb[:, kc * T + b0: kc * T + b0 + TB],
                                start=(kc == 0), stop=(kc == DC - 1))
                        C.inc(ins, "mm")
                    h_mm = C.count("mm")

                    @C.scal
                    def _(sc, e=e, hc=hc, h_mm=h_mm, hbk=hbk):
                        C.wait_at(sc, "mm", h_mm)
                        sc.activation(S["slotA"][:, (hc % 4) * TB:(hc % 4 + 1) * TB],
                                      C.bh(hbk)[:, 0:TB], AF.Relu,
                                      bias=S["b1fm3"][:, e * 16 + hc:e * 16 + hc + 1])
                        C.brel(hbk, "se")
                    hr_se = C.count("se")

                    @C.vec
                    def _(v, hc=hc, hr_se=hr_se):
                        C.wait_at(v, "se", hr_se)
                        v.tensor_tensor(S["slotA"][:, (hc % 4) * TB:(hc % 4 + 1) * TB],
                                        S["slotA"][:, (hc % 4) * TB:(hc % 4 + 1) * TB],
                                        sqf[:, 0:TB], ALU.mult)
                    hb_ve = C.count("ve")

                    @C.ten
                    def _(t, e=e, j=j, gq=gq, hc=hc, wc0=wc0, hb_ve=hb_ve):
                        C.wait_at(t, "ve", hb_ve)
                        ins = None
                        for oc in range(DC):
                            obk = oc // 2
                            chh = (oc % 2) * TB
                            p = C.bacq(t, obk) if (e == 0 and hc == 0 and oc % 2 == 0) \
                                else C.bh(obk)
                            ins = t.matmul(p[:, chh:chh + TB],
                                           vb[:, wc0 + 4096 + j * 1024 + oc * 128:
                                              wc0 + 4096 + j * 1024 + oc * 128 + 128],
                                           S["slotA"][:, (hc % 4) * TB:(hc % 4 + 1) * TB],
                                           start=(e == 0 and hc == 0 and oc % 2 == 0),
                                           stop=False)
                        C.inc(ins, "mm")
                    g_use[par] = C.count("mm")
        @C.ten
        def _(t, tb=tb, cb_done=cb_done):
            C.wait_at(t, "ve", cb_done)
            C.wait_at(t, "gc", C.b2_gc)
            ins = None
            for oc in range(DC):
                obk = oc // 2
                chh = (oc % 2) * TB
                ins = t.matmul(C.bh(obk)[:, chh:chh + TB],
                               b2sb[0:8, oc * 128:(oc + 1) * 128],
                               cbT8[0:8, tb * TB:(tb + 1) * TB],
                               start=False, stop=(oc % 2 == 1))
            C.inc(ins, "mm")
        moe_mm = C.count("mm")

        a0_marks = []
        for oc in range(DC):
            obk = oc // 2
            chh = (oc % 2) * TB

            @C.scal
            def _(sc, oc=oc, obk=obk, chh=chh, moe_mm=moe_mm, plv=prev_ln_ve):
                C.wait_at(sc, "mm", moe_mm)
                C.wait_at(sc, "ve", plv)
                sc.activation(smf[:, oc * TB:(oc + 1) * TB], C.bh(obk)[:, chh:chh + TB],
                              AF.Copy)
                if oc % 2 == 1:
                    C.brel(obk, "se")
            a0se = C.count("se")

            @C.vec
            def _(v, oc=oc, b0=b0, a0se=a0se):
                C.wait_at(v, "se", a0se)
                v.tensor_tensor(smf[:, oc * TB:(oc + 1) * TB],
                                smf[:, oc * TB:(oc + 1) * TB],
                                yb[:, oc * T + b0: oc * T + b0 + TB], ALU.add)
            a0_marks.append(C.count("ve"))

        fm_ln(C, S,
              src_chunk=lambda kc: smf[:, kc * TB:(kc + 1) * TB],
              ttw=TB,
              dst_chunk=lambda kc: S["slotA"][:, 1024 + kc * TB: 1024 + (kc + 1) * TB],
              gamma_beta=(PV3_LN2G, PV3_LN2B),
              src_f32=True, pv_src="pv3",
              src_ready=lambda kc, am=list(a0_marks): ("ve", am[kc]))
        y2_ve = C.count("ve")
        fm_ln(C, S,
              src_chunk=lambda kc: S["slotA"][:, 1024 + kc * TB: 1024 + (kc + 1) * TB],
              ttw=TB,
              residual_into=lambda kc, b0=b0: xs[:, kc * T + b0: kc * T + b0 + TB],
              src_ready=lambda kc, ym=y2_ve: ("ve", ym))
        prev_ln_ve = C.count("ve")
    maybe_dump(C, g, S, "p3_x2", xs)



def emit_phase4(C, g, S):
    xs, yb, kb, vb = S["xs"], S["yb"], S["kb"], S["vb"]
    slotA, slotB, sqf, rows, smf = S["slotA"], S["slotB"], S["sqf"], S["rows"], S["smf"]
    cst = S["consts"]
    WS0, WS1 = 0, 8192

    def wsl(slot, kc, c0, cn):
        base = WS0 if slot == 0 else WS1
        return vb[:, base + kc * 1024 + c0: base + kc * 1024 + c0 + cn]

    def load_wcols(sy, dram_bf, col0, slot, gate_mm=0):
        grp = C.dgrp(sy)
        if gate_mm:
            C.wait_at(sy, "mm", gate_mm)
        C.wait_at(sy, "pc2", C.precast_pc2)
        for kc in range(DC):
            C.inc(sy.dma_start(out=wsl(slot, kc, 0, 1024),
                               in_=dram_bf[kc * 128:(kc + 1) * 128, col0:col0 + 1024]),
                  grp, 16)
        return (grp, C.count(grp))

    ph_mm0 = C.count("mm")
    ph_ve0 = C.count("ve")
    tril = S["scoreT"][:, 96:224]
    wl4 = {}

    @C.syn
    def _(sy):
        grp = C.dgrp(sy)
        C.wait_at(sy, "ve", ph_ve0)
        C.inc(sy.dma_start(out=tril, in_=g["tril"][:]), grp, 16)
        C.tril_mk = (grp, C.count(grp))
        wl4["q"] = load_wcols(sy, g["l1wqkv_bf"], 0, 0, gate_mm=ph_mm0)
        wl4["k"] = load_wcols(sy, g["l1wqkv_bf"], 1024, 1, gate_mm=ph_mm0)

    p4_marks = []
    for tt in range(NTT):
        c0 = tt * TT
        m = fm_ln(C, S,
                  src_chunk=lambda kc, c0=c0: xs[:, kc * T + c0: kc * T + c0 + TT],
                  ttw=TT,
                  dst_chunk=lambda kc, c0=c0: yb[:, kc * T + c0: kc * T + c0 + TT],
                  src_f32=True)
        p4_marks.append(m)

    def elu_evac(bk, mmk, dst_ap, pre=None):
        @C.vec
        def _(v, bk=bk, mmk=mmk):
            C.wait_at(v, "mm", mmk)
            C.wait_at(v, "se", C.count("se"))
            C.wmark(v, pre)
            v.tensor_scalar(sqf[:, 0:TT], C.bh(bk)[:, 0:TT], cst[:, C_ZERO:C_ZERO + 1],
                            None, ALU.min)
        mn_ve = C.count("ve")

        @C.scal
        def _(sc, mn_ve=mn_ve):
            C.wait_at(sc, "ve", mn_ve)
            sc.activation(sqf[:, 512:512 + TT], sqf[:, 0:TT], AF.Exp)
        ex_se = C.count("se")

        @C.vec
        def _(v, bk=bk, ex_se=ex_se, dst_ap=dst_ap):
            C.wait_at(v, "se", ex_se)
            v.tensor_scalar(sqf[:, 0:TT], C.bh(bk)[:, 0:TT], cst[:, C_ZERO:C_ZERO + 1],
                            None, ALU.max)
            C.brel(bk, "ve")
            v.tensor_tensor(dst_ap, sqf[:, 0:TT], sqf[:, 512:512 + TT], ALU.add)
        return C.count("ve")

    for tt in range(NTT):
        c0 = tt * TT

        def pq_out(oc, bk, mmk, c0=c0):
            elu_evac(bk, mmk, kb[:, oc * T + c0: oc * T + c0 + TT])

        fm_proj(C, S, lambda kc, oc: wsl(0, kc, oc * 128, 128),
                lambda kc, c0=c0: yb[:, kc * T + c0: kc * T + c0 + TT],
                TT, pq_out, w_ready=wl4["q"],
                src_ready=lambda oc, tt=tt: ("ve", p4_marks[tt]))
    pq4_mm = C.count("mm")

    sp_marks = []
    for tt in range(NTT):
        c0 = tt * TT
        ev_marks = []

        def pkf_out(oc, bk, mmk, ev_marks=ev_marks,
                    prev=(sp_marks[-1] if sp_marks else None)):
            m = elu_evac(bk, mmk, S["slotB"][:, oc * TT:oc * TT + TT], pre=prev)
            ev_marks.append(m)

        fm_proj(C, S, lambda kc, oc: wsl(1, kc, oc * 128, 128),
                lambda kc, c0=c0: yb[:, kc * T + c0: kc * T + c0 + TT],
                TT, pkf_out, w_ready=wl4["k"],
                src_ready=lambda oc, tt=tt: ("ve", p4_marks[tt]))

        @C.syn
        def _(sy, c0=c0, em=list(ev_marks)):
            grp = C.dgrp(sy)
            for oc in range(DC):
                C.wait_at(sy, "ve", em[oc])
                C.inc(sy.dma_start(out=g["pkfm_buf"][oc * 128:(oc + 1) * 128, c0:c0 + TT],
                                   in_=S["slotB"][:, oc * TT:oc * TT + TT]), grp, 16)
            sp_marks.append((grp, C.count(grp)))

    def tm_proj(dram_w, col0, wslot, dst_buf, do_elu, gate_mm):
        @C.syn
        def _(sy):
            wl4["tm"] = load_wcols(sy, dram_w, col0, wslot, gate_mm=gate_mm)
        for kt in range(16):
            ktt = kt * 128
            ev2 = []
            for ocp in range(2):
                @C.ten
                def _(t, kt=kt, ocp=ocp, wslot=wslot):
                    p = C.bacq(t, ocp)
                    C.wmark(t, wl4["tm"])
                    C.wait_at(t, "ve", p4_marks[-1])
                    ins = None
                    for kc in range(DC):
                        ins = t.matmul(p[:, 0:TT], yb[:, kc * T + ktt: kc * T + ktt + 128],
                                       wsl(wslot, kc, ocp * 512, 512),
                                       start=(kc == 0), stop=(kc == DC - 1))
                    C.inc(ins, "mm")
                mmk = C.count("mm")
                prevs = sp_marks[-1] if sp_marks else None
                if do_elu:
                    m = elu_evac(ocp, mmk, S["slotB"][:, ocp * 512:(ocp + 1) * 512],
                                 pre=prevs)
                    ev2.append(m)
                else:
                    @C.vec
                    def _(v, ocp=ocp, mmk=mmk, prevs=prevs):
                        C.wait_at(v, "mm", mmk)
                        C.wmark(v, prevs)
                        v.tensor_copy(S["slotB"][:, ocp * 512:(ocp + 1) * 512],
                                      C.bh(ocp)[:, 0:TT])
                        C.brel(ocp, "ve")
                    ev2.append(C.count("ve"))

            @C.syn
            def _(sy, kt=kt, ev2=list(ev2), dst_buf=dst_buf):
                grp = C.dgrp(sy)
                C.wait_at(sy, "ve", ev2[-1])
                C.inc(sy.dma_start(out=dst_buf[kt * 128:(kt + 1) * 128, :],
                                   in_=S["slotB"][:, 0:1024]), grp, 16)
                sp_marks.append((grp, C.count(grp)))

    tm_proj(g["l1wqkv_bf"], 1024, 0, g["pktm_buf"], True, pq4_mm)
    pktm_mm = C.count("mm")
    tm_proj(g["l1wqkv_bf"], 2048, 1, g["v_buf"], False, C.count("mm"))
    proj4_mm = C.count("mm")
    maybe_dump(C, g, S, "p4_pq", kb)
    if C.dumped:
        return

    tileA = [0, 0]

    def load_tile(sy, buf, kt, dstc):
        grp = C.dgrp(sy)
        C.inc(sy.dma_start(out=slotA[:, dstc:dstc + 1024],
                           in_=buf[kt * 128:(kt + 1) * 128, :]), grp, 16)
        return (grp, C.count(grp))

    @C.vec
    def _(v):
        v.memset(S["kvu"][:, :], 0.0)
    stA_mm = [C.count("mm")]
    for kt in range(16):
        par = kt % 2

        @C.syn
        def _(sy, kt=kt, par=par):
            if kt < 2:
                for mk in sp_marks:
                    C.wmark(sy, mk)
                C.wait_at(sy, "ve", C.count("ve"))
            C.wait_at(sy, "mm", stA_mm[0] if kt < 2 else tileA[par])
            C.tp1 = load_tile(sy, g["pktm_buf"], kt, par * 2048)
            C.tp2 = load_tile(sy, g["v_buf"], kt, par * 2048 + 1024)
        t1, t2 = C.tp1, C.tp2

        @C.ten
        def _(t, kt=kt, par=par, t1=t1, t2=t2):
            C.wmark(t, t1)
            C.wmark(t, t2)
            C.bacq(t, 4)
            C.bacq(t, 5)
            C.bacq(t, 6)
            ins = None
            for h in range(H):
                p = C.bh(5 if h < 8 else 6)
                cc = (h % 8) * 64
                t.matmul(p[0:64, cc:cc + 64],
                         slotA[:, par * 2048 + h * 64: par * 2048 + (h + 1) * 64],
                         slotA[:, par * 2048 + 1024 + h * 64: par * 2048 + 1024 + (h + 1) * 64],
                         start=(h in (0, 8)), stop=(h in (7, 15)))
                ins = t.matmul(C.bh(4)[0:64, h:h + 1],
                               slotA[:, par * 2048 + h * 64: par * 2048 + (h + 1) * 64],
                               S["onec_b"][:, 0:1],
                               start=(h == 0), stop=(h == 15))
            C.inc(ins, "mm")
        tileA[par] = C.count("mm")

        @C.vec
        def _(v, kt=kt, par=par):
            C.wait_at(v, "mm", tileA[par])
            for bk, hb in [(5, 0), (6, 8)]:
                dst = expand_view(sqf[0:64, hb * 65: hb * 65 + 1], [[65, 8], [1, 64]])
                srcb = expand_view(C.bh(bk)[0:64, 0:1], [[64, 8], [1, 64]])
                v.tensor_copy(dst, srcb)
                C.brel(bk, "ve")
            dstk = expand_view(sqf[0:64, 64:65], [[65, 16]])
            v.tensor_copy(dstk, C.bh(4)[0:64, 0:16])
            C.brel(4, "ve")
            v.tensor_tensor(S["kvu"][:, :], S["kvu"][:, :], sqf[0:64, 0:1040], ALU.add)
    stA_ve = C.count("ve")

    @C.gps
    def _(gp):
        C.wait_at(gp, "ve", stA_ve)
        C.inc(gp.dma_start(out=g["carry_buf"][:, :], in_=S["kvu"][:, :]), "gc", 16)
        C.wait(gp, "gc")
        C.inc(gp.collective_compute(
            "AllGather", ALU.bypass,
            replica_groups=[[0, 1], [2, 3], [4, 5], [6, 7]],
            ins=[g["carry_buf"][:, :].opt()], outs=[g["carry_gath"][:, :].opt()]), "cc")

    @C.syn
    def _(sy):
        grp = C.dgrp(sy)
        C.wait(sy, "cc")
        C.inc(sy.dma_start(out=S["kvu"][:, :], in_=g["carry_gath"][0:64, :]), grp, 16)
        C.carry_al = (grp, C.count(grp))

    @C.vec
    def _(v):
        C.wmark(v, C.carry_al)
        v.tensor_scalar(S["kvu"][:, :], S["kvu"][:, :], cst[0:64, C_CARRY:C_CARRY + 1],
                        None, ALU.mult)
    carry_ve = C.count("ve")

    init_mm = C.count("mm")
    tileB = [init_mm, init_mm]
    for kt in range(16):
        par = kt % 2
        ktt = kt * 128

        @C.syn
        def _(sy, kt=kt, par=par):
            C.wait_at(sy, "mm", tileB[par])
            grp = C.dgrp(sy)
            C.inc(sy.dma_start(out=slotA[:, par * 2048:par * 2048 + 1024],
                               in_=g["pktm_buf"][kt * 128:(kt + 1) * 128, :]), grp, 16)
            C.inc(sy.dma_start(out=slotA[:, par * 2048 + 1024:par * 2048 + 2048],
                               in_=g["v_buf"][kt * 128:(kt + 1) * 128, :]), grp, 16)
            C.tpB = (grp, C.count(grp))
        tB = C.tpB

        @C.syn
        def _(sy, kt=kt):
            grp = C.dgrp(sy)
            C.wait_at(sy, "mm", tileB[0])
            C.wait_at(sy, "mm", tileB[1])
            for oc in range(DC):
                C.inc(sy.dma_start(out=S["slotB"][:, oc * 128:(oc + 1) * 128],
                                   in_=g["pkfm_buf"][oc * 128:(oc + 1) * 128,
                                                     kt * 128:(kt + 1) * 128]), grp, 16)
            C.tpF = (grp, C.count(grp))
        tF = C.tpF

        @C.vec
        def _(v, kt=kt):
            dstkv = expand_view(S["kvb"][0:64, 0:1], [[64, 16], [1, 64]])
            srckv = expand_view(S["kvu"][0:64, 0:1], [[65, 16], [1, 64]])
            v.tensor_copy(dstkv, srckv)
            dstks = expand_view(S["kvb"][0:64, 1024:1025], [[1, 16]])
            srcks = expand_view(S["kvu"][0:64, 64:65], [[65, 16]])
            v.tensor_copy(dstks, srcks)
        snap_ve = C.count("ve")

        @C.syn
        def _(sy, snap_ve=snap_ve):
            grp = C.dgrp(sy)
            C.wait_at(sy, "ve", snap_ve)
            C.inc(sy.dma_start(out=S["kvb"][64:128, :], in_=S["kvb"][0:64, :]), grp, 16)
            C.dupm = (grp, C.count(grp))
        dupm = C.dupm

        @C.ten
        def _(t, kt=kt, par=par, tB=tB, snap_ve=snap_ve):
            C.wmark(t, tB)
            C.wait_at(t, "ve", snap_ve)
            C.bacq(t, 4)
            C.bacq(t, 5)
            C.bacq(t, 6)
            ins = None
            for h in range(H):
                p = C.bh(5 if h < 8 else 6)
                cc = (h % 8) * 64
                t.matmul(p[0:64, cc:cc + 64],
                         slotA[:, par * 2048 + h * 64: par * 2048 + (h + 1) * 64],
                         slotA[:, par * 2048 + 1024 + h * 64: par * 2048 + 1024 + (h + 1) * 64],
                         start=(h in (0, 8)), stop=(h in (7, 15)))
                ins = t.matmul(C.bh(4)[0:64, h:h + 1],
                               slotA[:, par * 2048 + h * 64: par * 2048 + (h + 1) * 64],
                               S["onec_b"][:, 0:1],
                               start=(h == 0), stop=(h == 15))
            C.inc(ins, "mm")
        st_mm = C.count("mm")

        @C.vec
        def _(v, st_mm=st_mm):
            C.wait_at(v, "mm", st_mm)
            for bk, hb in [(5, 0), (6, 8)]:
                dst = expand_view(sqf[0:64, hb * 65: hb * 65 + 1], [[65, 8], [1, 64]])
                srcb = expand_view(C.bh(bk)[0:64, 0:1], [[64, 8], [1, 64]])
                v.tensor_copy(dst, srcb)
                C.brel(bk, "ve")
            dstk = expand_view(sqf[0:64, 64:65], [[65, 16]])
            v.tensor_copy(dstk, C.bh(4)[0:64, 0:16])
            C.brel(4, "ve")
            v.tensor_tensor(S["kvu"][:, :], S["kvu"][:, :], sqf[0:64, 0:1040], ALU.add)

        for h in range(H):
            pb = (h % 2) * 64
            ch = h // 2
            nbk = 2 if h % 2 == 0 else 3

            @C.ten
            def _(t, h=h, pb=pb, ch=ch, kt=kt, ktt=ktt, par=par, tF=tF, dupm=dupm, nbk=nbk):
                p0 = C.bacq(t, 0)
                C.wmark(t, tF)
                C.inc(t.matmul(p0[:, 0:128],
                               S["slotB"][pb:pb + 64, ch * 128:(ch + 1) * 128],
                               kb[pb:pb + 64, ch * T + ktt: ch * T + ktt + 128],
                               start=True, stop=True), "mm")
            pt_mm = C.count("mm")

            @C.vec
            def _(v, pt_mm=pt_mm):
                C.wait_at(v, "mm", pt_mm)
                C.wmark(v, C.tril_mk)
                v.tensor_tensor(S["slotB"][0:128, 1024:1152], C.bh(0)[:, 0:128],
                                tril, ALU.mult)
                C.brel(0, "ve")
            pm_ve = C.count("ve")

            @C.ten
            def _(t, h=h, pb=pb, ch=ch, kt=kt, ktt=ktt, par=par, dupm=dupm,
                  nbk=nbk, pm_ve=pm_ve):
                C.wmark(t, dupm)
                C.wait_at(t, "ve", pm_ve)
                pn = C.bacq(t, nbk)
                pd = C.bacq(t, 1)
                t.matmul(pn[pb:pb + 64, 0:128],
                         S["kvb"][pb:pb + 64, h * 64:(h + 1) * 64],
                         kb[pb:pb + 64, ch * T + ktt: ch * T + ktt + 128],
                         start=True, stop=False, skip_group_check=(pb == 64))
                t.matmul(pn[pb:pb + 64, 0:128],
                         slotA[:, par * 2048 + 1024 + h * 64: par * 2048 + 1024 + (h + 1) * 64],
                         S["slotB"][0:128, 1024:1152],
                         start=False, stop=True, skip_group_check=(pb == 64))
                t.matmul(pd[0:1, 0:128], S["kvb"][pb:pb + 64, 1024 + h:1025 + h],
                         kb[pb:pb + 64, ch * T + ktt: ch * T + ktt + 128],
                         start=True, stop=False)
                C.inc(t.matmul(pd[0:1, 0:128], S["onec_b"][:, 0:1],
                               S["slotB"][0:128, 1024:1152],
                               start=False, stop=True), "mm")
            nd_mm = C.count("mm")

            @C.vec
            def _(v, nd_mm=nd_mm):
                C.wait_at(v, "mm", nd_mm)
                v.tensor_scalar(rows[0:1, 0:128], C.bh(1)[0:1, 0:128],
                                cst[0:1, C_EPS6:C_EPS6 + 1], None, ALU.add)
                C.brel(1, "ve")
                v.reciprocal(rows[0:1, 0:128], rows[0:1, 0:128])
            rc_ve = C.count("ve")

            @C.ten
            def _(t, pb=pb, rc_ve=rc_ve):
                C.wait_at(t, "ve", rc_ve)
                p0 = C.bacq(t, 0)
                C.inc(t.matmul(p0[pb:pb + 64, 0:128], S["oner_f"][0:1, 0:64],
                               rows[0:1, 0:128], start=True, stop=True,
                               skip_group_check=(pb == 64)), "mm")
            rb_mm = C.count("mm")

            @C.vec
            def _(v, h=h, pb=pb, ch=ch, ktt=ktt, rb_mm=rb_mm, nbk=nbk):
                C.wait_at(v, "mm", rb_mm)
                v.tensor_copy(sqf[pb:pb + 64, 0:128], C.bh(0)[pb:pb + 64, 0:128])
                C.brel(0, "ve")
                v.tensor_tensor(vb[pb:pb + 64, ch * T + ktt: ch * T + ktt + 128],
                                C.bh(nbk)[pb:pb + 64, 0:128], sqf[pb:pb + 64, 0:128],
                                ALU.mult)
                C.brel(nbk, "ve")
        tileB[par] = C.count("mm")
    of_ve = C.count("ve")
    maybe_dump(C, g, S, "p4_of", vb)
    if C.dumped:
        return

    @C.syn
    def _(sy):
        grp = C.dgrp(sy)
        C.wait_at(sy, "pc2", C.precast_pc2)
        C.wait_at(sy, "mm", C.count("mm"))
        for kc in range(DC):
            C.inc(sy.dma_start(out=kb[:, kc * 1024:kc * 1024 + 1024],
                               in_=g["l1wout_bf"][kc * 128:(kc + 1) * 128, :]), grp, 16)
        C.wo_mk = (grp, C.count(grp))

    TB = 256
    prev_ln_ve = C.count("ve")
    for tb in range(8):
        b0 = tb * TB
        a0_marks = []

        def o_out(oc, bk, mmk, b0=b0, a0_marks=a0_marks, plv=prev_ln_ve):
            @C.vec
            def _(v, oc=oc, bk=bk, mmk=mmk):
                C.wait_at(v, "mm", mmk)
                C.wait_at(v, "ve", plv)
                v.tensor_copy(smf[:, oc * TB:(oc + 1) * TB], C.bh(bk)[:, 0:TB])
                C.brel(bk, "ve")
            a0_marks.append(C.count("ve"))

        fm_proj(C, S, lambda kc, oc: kb[:, kc * 1024 + oc * 128: kc * 1024 + oc * 128 + 128],
                lambda kc, b0=b0: vb[:, kc * T + b0: kc * T + b0 + TB],
                TB, o_out, w_ready=C.wo_mk,
                src_ready=lambda oc: ("ve", of_ve))
        fm_ln(C, S,
              src_chunk=lambda kc: smf[:, kc * TB:(kc + 1) * TB],
              ttw=TB,
              residual_into=lambda kc, b0=b0: xs[:, kc * T + b0: kc * T + b0 + TB],
              src_f32=True,
              src_ready=lambda kc, am=list(a0_marks): ("ve", am[kc]))
        prev_ln_ve = C.count("ve")
    maybe_dump(C, g, S, "p4_x3", xs)



def emit_phase5(C, g, S):
    xs, yb, kb, vb = S["xs"], S["yb"], S["kb"], S["vb"]
    slotA, slotB, sqf, rows, smf = S["slotA"], S["slotB"], S["sqf"], S["rows"], S["smf"]
    cst, pv5 = S["consts"], S["pv5"]
    PV5_UB, PV5_BCON, PV5_BDN = 0, 32, 40
    xbuf = S["scoreT"][:, 0:128]

    ph_mm0 = C.count("mm")

    p5_marks = []
    for tt in range(NTT):
        c0 = tt * TT
        m = fm_ln(C, S,
                  src_chunk=lambda kc, c0=c0: xs[:, kc * T + c0: kc * T + c0 + TT],
                  ttw=TT,
                  dst_chunk=lambda kc, c0=c0: yb[:, kc * T + c0: kc * T + c0 + TT],
                  src_f32=True)
        p5_marks.append(m)

    for mi, (m, nst, ab0) in enumerate([(0, 7, 0), (1, 5, 7), (2, 7, 12), (3, 5, 19)]):
        @C.vec
        def _(v, m=m, mi=mi):
            C.wmark(v, C.pre_al)
            if mi == 0:
                C.wait_at(v, "mm", ph_mm0)
            for o in range(32):
                v.tensor_copy(kb[:, m * 4096 + o * 128: m * 4096 + (o + 1) * 128],
                              S["identb"][:, :])
        cur_kb = True
        for sj in range(nst):
            inst = ab0 + sj

            @C.syn
            def _(sy, inst=inst, sj=sj):
                grp = C.dgrp(sy)
                C.wait_at(sy, "ve", C.count("ve"))
                C.wait_at(sy, "mm", C.count("mm"))
                C.inc(sy.dma_start(out=rows[:, 64:128],
                                   in_=g["l1f_ab"][:, inst * 64:(inst + 1) * 64]), grp, 16)
                C.inc(sy.dma_start(out=xbuf, in_=g["l1f_xorm"][sj]), grp, 16)
                C.abm = (grp, C.count(grp))
            abm = C.abm

            @C.vec
            def _(v, abm=abm):
                C.wmark(v, abm)
                for o in range(32):
                    v.tensor_scalar(slotA[:, o * 128:(o + 1) * 128], S["identb"][:, :],
                                    rows[:, 64 + o:65 + o], None, ALU.mult)
                    v.tensor_scalar(vb[:, 4096 + o * 128:4096 + (o + 1) * 128],
                                    xbuf[:, :], rows[:, 96 + o:97 + o], None, ALU.mult)
                v.tensor_tensor(slotA[:, 0:4096], slotA[:, 0:4096],
                                vb[:, 4096:8192], ALU.add)
            s_ve = C.count("ve")

            for o in range(32):
                bk = o % 2

                @C.ten
                def _(t, m=m, o=o, bk=bk, s_ve=s_ve, cur_kb=cur_kb):
                    p = C.bacq(t, bk)
                    C.wait_at(t, "ve", s_ve)
                    src_ap = (kb[:, m * 4096 + o * 128: m * 4096 + (o + 1) * 128]
                              if cur_kb else vb[:, o * 128:(o + 1) * 128])
                    C.inc(t.matmul(p[:, 0:128], slotA[:, o * 128:(o + 1) * 128],
                                   src_ap, start=True, stop=True), "mm")
                mmk = C.count("mm")

                @C.vec
                def _(v, m=m, o=o, bk=bk, mmk=mmk, cur_kb=cur_kb):
                    C.wait_at(v, "mm", mmk)
                    dst_ap = (vb[:, o * 128:(o + 1) * 128] if cur_kb
                              else kb[:, m * 4096 + o * 128: m * 4096 + (o + 1) * 128])
                    v.tensor_copy(dst_ap, C.bh(bk)[:, 0:128])
                    C.brel(bk, "ve")
            cur_kb = not cur_kb
        assert not cur_kb
        for o in range(32):
            @C.ten
            def _(t, m=m, o=o):
                pT = C.bacq(t, "T")
                C.wait_at(t, "ve", C.count("ve"))
                C.inc(t.transpose(pT[:, 0:128], vb[:, o * 128:(o + 1) * 128],
                                  S["identb"][:, :]), "mm")
            tmk = C.count("mm")

            @C.vec
            def _(v, m=m, o=o, tmk=tmk):
                C.wait_at(v, "mm", tmk)
                v.tensor_copy(kb[:, m * 4096 + o * 128: m * 4096 + (o + 1) * 128],
                              C.bh("T")[:, 0:128])
                C.brel("T", "ve")
    dbuild_ve = C.count("ve")
    dbuild_mm = C.count("mm")
    maybe_dump(C, g, S, "p5_D", kb)
    if C.dumped:
        return

    st1_marks = []
    wexp_mk = [None]
    for og in range(8):
        @C.syn
        def _(sy, og=og):
            grp = C.dgrp(sy)
            C.wait_at(sy, "pc2", C.precast_pc2)
            C.wait_at(sy, "mm", C.count("mm"))
            C.inc(sy.dma_start(out=slotB[:, :], in_=g["wexp_bf"][og]), grp, 16)
            wexp_mk[0] = (grp, C.count(grp))

        for j in range(4):
            o = og * 4 + j
            ev1 = []
            for sb in range(4):
                @C.ten
                def _(t, og=og, j=j, sb=sb, wm=wexp_mk[0]):
                    p = C.bacq(t, 0 if sb % 2 == 0 else 1)
                    C.wmark(t, wm)
                    C.wait_at(t, "ve", p5_marks[sb])
                    ins = None
                    for kc in range(DC):
                        ins = t.matmul(
                            p[:, 0:TT],
                            slotB[:, (j * 8 + kc) * 128:(j * 8 + kc) * 128 + 128],
                            yb[:, kc * T + sb * TT: kc * T + (sb + 1) * TT],
                            start=(kc == 0), stop=(kc == DC - 1))
                    C.inc(ins, "mm")
                mk1 = C.count("mm")

                @C.vec
                def _(v, sb=sb, mk1=mk1):
                    C.wait_at(v, "mm", mk1)
                    v.tensor_copy(slotA[:, sb * TT:(sb + 1) * TT],
                                  C.bh(0 if sb % 2 == 0 else 1)[:, 0:TT])
                    C.brel(0 if sb % 2 == 0 else 1, "ve")
                ev1.append(C.count("ve"))

            ev2 = []
            for sb in range(4):
                @C.ten
                def _(t, o=o, sb=sb, ev1=list(ev1)):
                    p = C.bacq(t, 2 if sb % 2 == 0 else 3)
                    C.wait_at(t, "ve", ev1[sb])
                    C.wait_at(t, "ve", dbuild_ve)
                    C.inc(t.matmul(p[:, 0:TT], kb[:, o * 128:(o + 1) * 128],
                                   slotA[:, sb * TT:(sb + 1) * TT],
                                   start=True, stop=True), "mm")
                mk2 = C.count("mm")
                prev_sig = st1_marks[-1] if (st1_marks and sb == 0) else None

                @C.vec
                def _(v, sb=sb, mk2=mk2, prev_sig=prev_sig):
                    C.wait_at(v, "mm", mk2)
                    C.wmark(v, prev_sig)
                    v.tensor_copy(slotA[:, 2048 + sb * TT: 2048 + (sb + 1) * TT],
                                  C.bh(2 if sb % 2 == 0 else 3)[:, 0:TT])
                    C.brel(2 if sb % 2 == 0 else 3, "ve")
            h1_ve = C.count("ve")

            @C.syn
            def _(sy, o=o, h1_ve=h1_ve):
                grp = C.dgrp(sy)
                C.wait_at(sy, "ve", h1_ve)
                dst = g["scr"].rearrange("(a r o) t -> a r o t", a=32, r=4)[:, :, o, :]
                C.inc(sy.dma_start(out=dst, in_=slotA[:, 2048:4096]), grp, 16)
                st1_marks.append((grp, C.count(grp)))
    st1_mm = C.count("mm")

    def mid_stage(mcol, src_buf, dst_buf, sigma_out, gelu, in_marks, out_marks):
        for gg in range(32):
            @C.syn
            def _(sy, gg=gg):
                grp = C.dgrp(sy)
                if gg < 2:
                    for mk in in_marks:
                        C.wmark(sy, mk)
                C.wait_at(sy, "mm", C.count("mm"))
                C.inc(sy.dma_start(out=slotA[:, (gg % 2) * 2048:(gg % 2) * 2048 + 2048],
                                   in_=src_buf[gg * 128:(gg + 1) * 128, :]), grp, 16)
                C.ld5 = (grp, C.count(grp))
            ldm = C.ld5

            ev = []
            for sb in range(4):
                @C.ten
                def _(t, gg=gg, sb=sb, ldm=ldm, mcol=mcol):
                    p = C.bacq(t, 0 if sb % 2 == 0 else 1)
                    C.wmark(t, ldm)
                    C.wait_at(t, "ve", dbuild_ve)
                    C.inc(t.matmul(p[:, 0:TT],
                                   kb[:, mcol * 4096 + gg * 128: mcol * 4096 + (gg + 1) * 128],
                                   slotA[:, (gg % 2) * 2048 + sb * TT:
                                         (gg % 2) * 2048 + (sb + 1) * TT],
                                   start=True, stop=True), "mm")
                mk = C.count("mm")

                prev_out = out_marks[-1] if (out_marks and sb == 0) else None
                if gelu:
                    @C.scal
                    def _(sc, gg=gg, sb=sb, mk=mk, prev_out=prev_out):
                        C.wait_at(sc, "mm", mk)
                        C.wmark(sc, prev_out)
                        sc.activation(S["slotB"][:, sb * TT:(sb + 1) * TT],
                                      C.bh(0 if sb % 2 == 0 else 1)[:, 0:TT],
                                      AF.Gelu, bias=pv5[:, PV5_UB + gg:PV5_UB + gg + 1])
                        C.brel(0 if sb % 2 == 0 else 1, "se")
                    ev.append(("se", C.count("se")))
                else:
                    @C.vec
                    def _(v, gg=gg, sb=sb, mk=mk, prev_out=prev_out):
                        C.wait_at(v, "mm", mk)
                        C.wmark(v, prev_out)
                        v.tensor_copy(S["slotB"][:, sb * TT:(sb + 1) * TT],
                                      C.bh(0 if sb % 2 == 0 else 1)[:, 0:TT])
                        C.brel(0 if sb % 2 == 0 else 1, "ve")
                    ev.append(("ve", C.count("ve")))

            @C.syn
            def _(sy, gg=gg, ev=list(ev)):
                grp = C.dgrp(sy)
                C.wmark(sy, ev[-1])
                if sigma_out == "inv":
                    dst = dst_buf.rearrange("(o g2 r) t -> r o g2 t",
                                            o=32, g2=32, r=4)[:, :, gg, :]
                elif sigma_out == "fwd":
                    dst = dst_buf.rearrange("(a r o) t -> a r o t",
                                            a=32, r=4)[:, :, gg, :]
                else:
                    dst = dst_buf[gg * 128:(gg + 1) * 128, :]
                C.inc(sy.dma_start(out=dst, in_=S["slotB"][:, 0:2048]), grp, 16)
                out_marks.append((grp, C.count(grp)))

    st2_marks = []
    mid_stage(1, g["scr"], g["scr2"], "inv", True, st1_marks, st2_marks)
    st3_marks = []
    mid_stage(2, g["scr2"], g["scr3"], "fwd", False, st2_marks, st3_marks)
    st4_marks = []
    mid_stage(3, g["scr3"], g["scr"], "none", False, st3_marks, st4_marks)

    TB = 256
    prev_ln_ve = C.count("ve")
    wcon_mk = [None]
    for tb in range(8):
        b0 = tb * TB

        @C.syn
        def _(sy, tb=tb, b0=b0):
            grp = C.dgrp(sy)
            if tb < 1:
                for mk in st4_marks:
                    C.wmark(sy, mk)
            C.wait_at(sy, "mm", C.count("mm"))
            for fc in range(32):
                C.inc(sy.dma_start(out=vb[:, fc * TB:(fc + 1) * TB],
                                   in_=g["scr"][fc * 128:(fc + 1) * 128, b0:b0 + TB]),
                      grp, 16)
            C.h8m = (grp, C.count(grp))
        h8m = C.h8m

        a0_marks = []
        for oc in range(DC):
            @C.syn
            def _(sy, oc=oc, tb=tb):
                grp = C.dgrp(sy)
                C.wait_at(sy, "pc2", C.precast_pc2)
                C.wait_at(sy, "mm", C.count("mm"))
                C.inc(sy.dma_start(out=slotB[:, :], in_=g["wcon_bf"][oc]), grp, 16)
                C.wcm = (grp, C.count(grp))
            wcm = C.wcm

            @C.ten
            def _(t, oc=oc, h8m=h8m, wcm=wcm):
                p = C.bacq(t, 4)
                C.wmark(t, h8m)
                C.wmark(t, wcm)
                ins = None
                for fc in range(32):
                    ins = t.matmul(p[:, 0:TB], slotB[:, fc * 128:(fc + 1) * 128],
                                   vb[:, fc * TB:(fc + 1) * TB],
                                   start=(fc == 0), stop=(fc == 31))
                C.inc(ins, "mm")
            wc_mm = C.count("mm")

            @C.scal
            def _(sc, oc=oc, wc_mm=wc_mm, plv=prev_ln_ve):
                C.wait_at(sc, "mm", wc_mm)
                C.wait_at(sc, "ve", plv)
                sc.activation(smf[:, oc * TB:(oc + 1) * TB], C.bh(4)[:, 0:TB],
                              AF.Identity, bias=pv5[:, PV5_BCON + oc:PV5_BCON + oc + 1])
                C.brel(4, "se")
            ase = C.count("se")

            @C.vec
            def _(v, oc=oc, ase=ase):
                C.wait_at(v, "se", ase)
                v.tensor_scalar(smf[:, oc * TB:(oc + 1) * TB],
                                smf[:, oc * TB:(oc + 1) * TB],
                                pv5[:, PV5_BDN + oc:PV5_BDN + oc + 1], None, ALU.add)
            a0_marks.append(C.count("ve"))

        fm_ln(C, S,
              src_chunk=lambda kc: smf[:, kc * TB:(kc + 1) * TB],
              ttw=TB,
              residual_into=lambda kc, b0=b0: xs[:, kc * T + b0: kc * T + b0 + TB],
              src_f32=True,
              src_ready=lambda kc, am=list(a0_marks): ("ve", am[kc]))
        prev_ln_ve = C.count("ve")
    maybe_dump(C, g, S, "p5_x4", xs)



def kernel(**inputs):
    cores = list(range(8))
    nc, _ = build_program(0)
    in_maps = [prep_core_inputs(inputs, c) for c in cores]
    res = run_bass_kernel_spmd(nc, in_maps, core_ids=cores)
    out = np.zeros((4, 4096, 1024), dtype=np.float32)
    for c in cores:
        b, half = c // 2, c % 2
        out[b, half * T:(half + 1) * T, :] = res.results[c]["out_fm"].T
    return out


# version 07
# speedup vs baseline: 1.0741x; 1.0741x over previous
import numpy as np
import ml_dtypes
from contextlib import ExitStack
import concourse.bass as bass
import concourse.mybir as mybir
from concourse.bass_utils import run_bass_kernel_spmd

F32 = mybir.dt.float32
BF16 = mybir.dt.bfloat16
AF = mybir.ActivationFunctionType
ALU = mybir.AluOpType
AX = mybir.AxisListType

D = 1024
DC = 8
T = 2048
H = 16
HD = 64
NTT = 4
TT = 512
EPS_LN = 1e-5
BF = ml_dtypes.bfloat16

(C_EPS, C_EPS6, C_ONE, C_NEG1, C_I1024, C_PROBE, C_ZERO, C_I8, C_Z0,
 C_CARRY, C_SCRA, C_SCRB, C_NHALF, C_HALF) = range(14)

DBG = None


def f32(x):
    return np.ascontiguousarray(x, dtype=np.float32)


def _pack_pv(vecs):
    out = np.zeros((128, 8 * len(vecs)), dtype=np.float32)
    for j, v in enumerate(vecs):
        out[:, j * 8:(j + 1) * 8] = f32(v).reshape(8, 128).T
    return out


def _rows(vecs, dtype):
    return np.concatenate([np.asarray(v).reshape(-1) for v in vecs])[None, :].astype(dtype)


def prep_core_inputs(ins, core):
    b, half = core // 2, core % 2
    s0 = half * T
    x = np.asarray(ins["L0_attn_x"])
    d = {}
    d["x_fm"] = f32(x[b, s0:s0 + T, :].T)
    d["ident_bf"] = np.eye(128, dtype=BF)
    d["ones_col_f32"] = np.ones((128, 1), dtype=np.float32)
    d["ones_col_bf"] = np.ones((128, 1), dtype=BF)
    d["ones_row_f32"] = np.ones((1, 128), dtype=np.float32)
    d["ones_row_bf"] = np.ones((1, 128), dtype=BF)
    z0 = 0.0 if half == 0 else -1e30
    carry = 0.0 if half == 0 else 1.0
    d["consts"] = np.tile(np.array([[EPS_LN, 1e-6, 1.0, -1.0, 1.0 / 1024,
                                     1.0 / (4096 * 8.0), 0.0, 0.125, z0, carry,
                                     0.0, 0.0, -0.5, 0.5, 0.0, 0.0]],
                                   dtype=np.float32), (128, 1))
    for nm in ["Wq", "Wk", "Wv", "Wfq", "Wfk"]:
        d["l0a_" + nm] = f32(ins["L0_attn_" + nm])
    W1 = f32(ins["L0_attn_W1"])
    d["l0a_W1sw"] = np.ascontiguousarray(
        W1.reshape(8, 128, 8, 4, 128).transpose(2, 1, 3, 0, 4)).reshape(8, 128, 4096)
    W2 = f32(ins["L0_attn_W2"])
    d["l0a_W2sw"] = np.ascontiguousarray(
        W2.reshape(8, 4, 128, 1024).transpose(0, 2, 1, 3)).reshape(8, 128, 4096)
    pvm = _pack_pv([ins["L0_attn_" + n] for n in
                    ["gq", "betaq", "gk", "betak", "bfk", "ga", "ba",
                     "bq", "bk", "bfq", "b2", "gv", "betav"]])
    bfkh = np.zeros((128, 16), dtype=np.float32)
    bfk_ = f32(ins["L0_attn_bfk"]).reshape(16, 64)
    bfkh[0:64, :] = bfk_.T
    d["l0a_pv"] = np.concatenate([pvm, bfkh], axis=1)
    d["l0a_b1fm"] = f32(ins["L0_attn_b1"]).reshape(32, 128).T.copy()
    d["l0a_rows"] = _rows([ins["L0_attn_" + n] for n in ["bv", "bfk"]], BF)
    d["l0f_wqkv"] = f32(np.asarray(ins["L0_ffn_in_proj_w"]).T)
    d["l0f_wout"] = f32(np.asarray(ins["L0_ffn_out_proj_w"]).T)
    bi = f32(ins["L0_ffn_in_proj_b"])
    d["l0f_pv"] = _pack_pv([bi[0:1024], bi[1024:2048], bi[2048:3072],
                            ins["L0_ffn_out_proj_b"], ins["L0_ffn_ln1_g"],
                            ins["L0_ffn_ln1_b"]])
    w1 = f32(ins["L0_ffn_w1"])
    d["l0f_w1sw"] = np.ascontiguousarray(
        w1.reshape(8, 8, 128, 4, 4, 128).transpose(0, 3, 2, 4, 1, 5)
    ).reshape(8, 4, 128, 4096)
    w2m = f32(ins["L0_ffn_w2"])
    d["l0f_w2sw"] = np.ascontiguousarray(
        w2m.reshape(8, 4, 4, 128, 1024).transpose(0, 1, 3, 2, 4)
    ).reshape(8, 4, 128, 4096)
    d["l0f_wge"] = np.concatenate(
        [f32(ins["L0_ffn_w_group"]),
         f32(ins["L0_ffn_w_expert"]).transpose(1, 0, 2).reshape(1024, 8)], axis=1)
    d["l0f_bge"] = _rows([ins["L0_ffn_b_group"],
                          f32(ins["L0_ffn_b_expert"]).reshape(8)], BF)
    d["l0f_b2"] = f32(ins["L0_ffn_b2"])
    d["l0f_b1fm"] = np.ascontiguousarray(
        f32(ins["L0_ffn_b1"]).reshape(8, 16, 128).transpose(2, 0, 1)).reshape(128, 128)
    d["l0f_pv3"] = _pack_pv([ins["L0_ffn_ln2_g"], ins["L0_ffn_ln2_b"]])
    basis8 = np.zeros((8, 1024), dtype=BF)
    for e in range(8):
        basis8[e, e * 128:(e + 1) * 128] = 1
    d["basis8"] = basis8
    d["l1_wqkv"] = f32(ins["L1_attn_W_qkv"])
    d["l1_wout"] = f32(ins["L1_attn_W_out"])
    d["tril"] = np.triu(np.ones((128, 128), dtype=np.float32)).astype(BF)
    ff = np.arange(4096)
    oo, ii = ff // 128, ff % 128
    sigma = (ii // 4) * 128 + (ii % 4) * 32 + oo
    inv = np.empty(4096, np.int64)
    inv[sigma] = ff
    WEXP = f32(ins["L1_ffn_W_exp"])
    d["l1f_wexpsw"] = np.ascontiguousarray(
        WEXP.reshape(8, 128, 8, 4, 128).transpose(2, 1, 3, 0, 4)).reshape(8, 128, 4096)
    WCON = f32(ins["L1_ffn_W_con"])[inv, :]
    d["l1f_wconsw"] = np.ascontiguousarray(
        WCON.reshape(32, 128, 8, 128).transpose(2, 1, 0, 3)).reshape(8, 128, 4096)
    def stage_ab(wst, s2):
        al = np.empty(4096, np.float32)
        bt = np.empty(4096, np.float32)
        w = f32(wst).reshape(4096 // (2 * s2), s2, 2, 2)
        for f in range(4096):
            nb_i = f // (2 * s2)
            s_i = f % s2
            if (f // s2) % 2 == 0:
                al[f] = w[nb_i, s_i, 0, 0]
                bt[f] = w[nb_i, s_i, 0, 1]
            else:
                al[f] = w[nb_i, s_i, 1, 1]
                bt[f] = w[nb_i, s_i, 1, 0]
        btp = bt[np.bitwise_xor(np.arange(4096), s2)]
        return al, btp
    ab_cols = []
    for wname in ["up_weights", "down_weights"]:
        wt = f32(ins["L1_ffn_" + wname])
        for st in range(7):
            al, btp = stage_ab(wt[st], 2 ** st)
            ab_cols.append((al, btp))
        for st in range(7, 12):
            al, btp = stage_ab(wt[st], 2 ** st)
            ab_cols.append((al[inv], btp[inv]))
    n_inst = len(ab_cols)
    ab = np.zeros((128, n_inst * 64), dtype=np.float32)
    for j, (al, btp) in enumerate(ab_cols):
        ab[:, j * 64: j * 64 + 32] = al.reshape(32, 128).T
        ab[:, j * 64 + 32: j * 64 + 64] = btp.reshape(32, 128).T
    d["l1f_ab"] = ab
    xorm = np.zeros((7, 128, 128), dtype=BF)
    for sj, s2 in enumerate([1, 2, 4, 8, 16, 32, 64]):
        idx = np.arange(128)
        xorm[sj, idx, np.bitwise_xor(idx, s2)] = 1
    d["l1f_xorm"] = xorm
    ub = f32(ins["L1_ffn_up_bias"])[sigma.argsort()] if False else f32(ins["L1_ffn_up_bias"])[inv]
    d["l1f_pv"] = np.concatenate(
        [ub.reshape(32, 128).T,
         f32(ins["L1_ffn_b_con"]).reshape(8, 128).T,
         f32(ins["L1_ffn_down_bias"]).reshape(8, 128).T], axis=1)
    return d


(PV_GQ, PV_BETAQ, PV_GK, PV_BETAK, PV_BFK, PV_GA, PV_BA,
 PV_BQ, PV_BK, PV_BFQ, PV_B2, PV_GV, PV_BETAV) = [j * 8 for j in range(13)]
PV_BFKH = 104
(PV2_BQ, PV2_BK, PV2_BV, PV2_BO, PV2_LN1G, PV2_LN1B) = [j * 8 for j in range(6)]
PV3_LN2G, PV3_LN2B = 0, 8
R_BV, R_BFK = 0, 1024


_PASSTHRU = {"wait_ge", "nop", "dma_start", "dma_start_transpose", "then_inc",
             "collective_compute", "indirect_dma_start"}


class SerialEng:

    def __init__(self, C, eng, ev):
        self._C = C
        self._e = eng
        self._ev = ev

    def __getattr__(self, name):
        f = getattr(self._e, name)
        if name in _PASSTHRU or name.startswith("_"):
            return f
        C, eng, ev = self._C, self._e, self._ev

        def call(*a, **k):
            C.wait(eng, ev)
            ins = f(*a, **k)
            C.inc(ins, ev, 1)
            return ins
        return call


class RecIns:
    __slots__ = ("calls",)

    def __init__(self):
        self.calls = []

    def then_inc(self, sem, k):
        self.calls.append(("then_inc", (sem, k), {}))
        return self


class RecEng:

    def __init__(self, log):
        self._log = log

    def __getattr__(self, name):
        if name.startswith("_"):
            raise AttributeError(name)
        log = self._log

        def call(*a, **k):
            ri = RecIns()
            log.append((name, a, k, ri))
            return ri
        return call


class Ctx:
    def __init__(self, nc, block):
        self.nc = nc
        self.block = block
        self.evs = {}
        self.banks = {}
        self.dumped = False
        self.logs = {n: [] for n in ("vector", "scalar", "gpsimd",
                                     "tensor", "sync")}

    def sem(self, name, handle):
        self.evs[name] = [handle, 0]

    def dgrp(self, q):
        name = self._dsems[self._dptr % len(self._dsems)]
        self._dptr += 1
        if self.count(name) > 0:
            self.wait(q, name)
        return name

    def add_bank(self, key, ten):
        b = type("B", (), {})()
        b.h, b.mark = ten, None
        self.banks[key] = b

    def inc(self, instr, name, k=1):
        ev = self.evs[name]
        instr.then_inc(ev[0], k)
        ev[1] += k
        return instr

    def wait(self, eng, name):
        ev = self.evs[name]
        if ev[1] > 0:
            eng.wait_ge(ev[0], ev[1])

    def wait_at(self, eng, name, n):
        if n > 0:
            eng.wait_ge(self.evs[name][0], n)

    def count(self, name):
        return self.evs[name][1]

    def mark(self, name):
        return (name, self.evs[name][1])

    def wmark(self, eng, mk):
        if mk is not None:
            self.wait_at(eng, mk[0], mk[1])

    def bacq(self, eng, key):
        b = self.banks[key]
        self.wmark(eng, b.mark)
        return b.h

    def bh(self, key):
        return self.banks[key].h

    def brel(self, key, ev):
        self.banks[key].mark = (ev, self.evs[ev][1])

    def vec(self, fn):
        fn(SerialEng(self, RecEng(self.logs["vector"]), "ve"))

    def scal(self, fn):
        fn(SerialEng(self, RecEng(self.logs["scalar"]), "se"))

    def gps(self, fn):
        fn(SerialEng(self, RecEng(self.logs["gpsimd"]), "gc"))

    def ten(self, fn):
        fn(RecEng(self.logs["tensor"]))

    def syn(self, fn):
        fn(RecEng(self.logs["sync"]))

    def flush(self):
        for eng_attr in ("vector", "scalar", "gpsimd", "tensor", "sync"):
            log = self.logs[eng_attr]

            @getattr(self.block, eng_attr)
            def _(eng, log=log):
                for name, a, k, ri in log:
                    ins = getattr(eng, name)(*a, **k)
                    for m, aa, kk in ri.calls:
                        getattr(ins, m)(*aa, **kk)


def expand_view(ap, dims):
    return bass.AP(ap.tensor, offset=ap.offset,
                   ap=[list(ap.ap[0])] + [list(dd) for dd in dims])


def fm_ln(C, S, src_chunk, ttw, dst_chunk=None, gamma_beta=None,
          src_f32=False, residual_into=None, src_ready=None, nchunk=DC,
          inv_n_col=C_I1024, pv_src="pv"):
    cst = S["consts"]
    sqf = S["sqf"]
    rows = S["rows"]
    mm_entry = C.count("mm")
    se_entry = C.count("se")
    ve_entry = C.count("ve")
    RB = 3
    nround = (nchunk + RB - 1) // RB

    @C.vec
    def _(v):
        C.wait_at(v, "se", se_entry)
        for r in range(nround):
            if r > 0:
                C.wait_at(v, "mm", mm_entry + RB * r)
            for kc in range(RB * r, min(RB * r + RB, nchunk)):
                if src_ready is not None:
                    C.wmark(v, src_ready(kc))
                v.tensor_tensor(sqf[:, (kc % RB) * 512:(kc % RB) * 512 + ttw],
                                src_chunk(kc), src_chunk(kc), ALU.mult)

    @C.ten
    def _(t):
        for r in range(nround):
            hi = min(RB * r + RB, nchunk)
            C.wait_at(t, "ve", ve_entry + hi)
            for kc in range(RB * r, hi):
                pb2 = C.bacq(t, 2) if kc == 0 else C.bh(2)
                pb3 = C.bacq(t, 3) if kc == 0 else C.bh(3)
                if src_ready is not None:
                    C.wmark(t, src_ready(kc))
                ones = S["onec_f"] if src_f32 else S["onec_b"]
                t.matmul(pb2[0:1, 0:ttw], ones[:, 0:1], src_chunk(kc),
                         start=(kc == 0), stop=(kc == nchunk - 1))
                C.inc(t.matmul(pb3[0:1, 0:ttw], S["onec_f"][:, 0:1],
                               sqf[:, (kc % RB) * 512:(kc % RB) * 512 + ttw],
                               start=(kc == 0), stop=(kc == nchunk - 1)), "mm")
    stats_mm = C.count("mm")

    @C.vec
    def _(v):
        C.wait_at(v, "mm", stats_mm)
        v.tensor_scalar(rows[0:1, 0:ttw], C.bh(2)[0:1, 0:ttw],
                        cst[0:1, inv_n_col:inv_n_col + 1], None, ALU.mult)
        v.tensor_scalar(rows[0:1, 512:512 + ttw], C.bh(3)[0:1, 0:ttw],
                        cst[0:1, inv_n_col:inv_n_col + 1], None, ALU.mult)
        C.brel(3, "ve")
        v.tensor_tensor(C.bh(2)[0:1, 0:ttw], rows[0:1, 0:ttw],
                        rows[0:1, 0:ttw], ALU.mult)
        v.tensor_tensor(rows[0:1, 512:512 + ttw], rows[0:1, 512:512 + ttw],
                        C.bh(2)[0:1, 0:ttw], ALU.subtract)
        C.brel(2, "ve")
    ve_var = C.count("ve")

    @C.scal
    def _(sc):
        C.wait_at(sc, "ve", ve_var)
        sc.activation(rows[0:1, 512:512 + ttw], rows[0:1, 512:512 + ttw],
                      AF.Ln, bias=cst[0:1, C_EPS:C_EPS + 1])
        sc.activation(rows[0:1, 512:512 + ttw], rows[0:1, 512:512 + ttw],
                      AF.Exp, scale=-0.5)
    se_std = C.count("se")

    @C.vec
    def _(v):
        C.wait_at(v, "se", se_std)
        v.tensor_tensor(rows[0:1, 0:ttw], rows[0:1, 0:ttw],
                        rows[0:1, 512:512 + ttw], ALU.mult)
    ve_rows = C.count("ve")

    @C.ten
    def _(t):
        C.wait_at(t, "ve", ve_rows)
        pb2 = C.bacq(t, 2)
        t.matmul(pb2[:, 0:ttw], S["oner_f"][0:1, :], rows[0:1, 512:512 + ttw],
                 start=True, stop=True)
        pb3 = C.bacq(t, 3)
        C.inc(t.matmul(pb3[:, 0:ttw], S["oner_f"][0:1, :], rows[0:1, 0:ttw],
                       start=True, stop=True), "mm")
    bc_mm = C.count("mm")

    @C.vec
    def _(v):
        C.wait_at(v, "mm", bc_mm)
        for kc in range(nchunk):
            v.tensor_tensor(sqf[:, 0:ttw], src_chunk(kc), C.bh(2)[:, 0:ttw], ALU.mult)
            if kc == nchunk - 1:
                C.brel(2, "ve")
            v.tensor_tensor(sqf[:, 0:ttw], sqf[:, 0:ttw],
                            C.bh(3)[:, 0:ttw], ALU.subtract)
            if kc == nchunk - 1:
                C.brel(3, "ve")
            if residual_into is not None:
                v.tensor_tensor(residual_into(kc), residual_into(kc),
                                sqf[:, 0:ttw], ALU.add)
            elif gamma_beta is not None:
                gcol, bcol = gamma_beta
                v.tensor_scalar(dst_chunk(kc), sqf[:, 0:ttw],
                                S[pv_src][:, gcol + kc:gcol + kc + 1],
                                S[pv_src][:, bcol + kc:bcol + kc + 1],
                                ALU.mult, ALU.add)
            else:
                v.tensor_copy(dst_chunk(kc), sqf[:, 0:ttw])
    return C.count("ve")


def fm_proj(C, S, wslot_fn, src_chunk, ttw, out_cb, banks=(0, 1), n_oc=DC,
            src_ready=None, w_ready=None, nkc=DC):
    for oc in range(n_oc):
        bk = banks[oc % len(banks)]

        @C.ten
        def _(t, oc=oc, bk=bk):
            p = C.bacq(t, bk)
            if w_ready is not None:
                C.wait_at(t, *w_ready)
            if src_ready is not None:
                C.wmark(t, src_ready(oc))
            for kc in range(nkc):
                ins = t.matmul(p[:, 0:ttw], wslot_fn(kc, oc), src_chunk(kc),
                               start=(kc == 0), stop=(kc == nkc - 1))
            C.inc(ins, "mm")
        out_cb(oc, bk, C.count("mm"))


def build_program(core):
    half = core % 2
    nc = bass.Bass("TRN2", target_bir_lowering=False)
    g = {}

    def din(name, shape, dt=F32):
        g[name] = nc.dram_tensor(name, shape, dt, kind="ExternalInput").ap()

    def dout(name, shape, dt=F32):
        g[name] = nc.dram_tensor(name, shape, dt, kind="ExternalOutput").ap()

    def dint(name, shape, dt=F32):
        g[name] = nc.dram_tensor(name, shape, dt).ap()

    din("x_fm", [D, T])
    dout("out_fm", [D, T])
    din("ident_bf", [128, 128], BF16)
    din("ones_col_f32", [128, 1]); din("ones_col_bf", [128, 1], BF16)
    din("ones_row_f32", [1, 128]); din("ones_row_bf", [1, 128], BF16)
    din("consts", [128, 16])
    for nm in ["Wq", "Wk", "Wv", "Wfq", "Wfk"]:
        din("l0a_" + nm, [D, D])
    din("l0a_W1sw", [8, 128, 4096]); din("l0a_W2sw", [8, 128, 4096])
    din("l0a_pv", [128, 120]); din("l0a_b1fm", [128, 32])
    din("l0a_rows", [1, 2 * 1024], BF16)
    dint("pq_spill", [D, T], BF16)
    dint("w1_bf", [8, 128, 4096], BF16)
    dint("w2_bf", [8, 128, 4096], BF16)
    for nm in ["Wq", "Wk", "Wv", "Wfq", "Wfk"]:
        dint(nm + "_bf", [D, D], BF16)
    din("l0f_wqkv", [D, 3 * D]); din("l0f_wout", [D, D])
    din("l0f_pv", [128, 48])
    dint("wqkv_bf", [D, 3 * D], BF16)
    dint("wout_bf", [D, D], BF16)
    dint("kv_buf", [2048, T], BF16)
    dint("q_buf", [D, T], BF16)
    dint("kv_gath", [4096, T], BF16)
    din("l0f_w1sw", [8, 4, 128, 4096]); din("l0f_w2sw", [8, 4, 128, 4096])
    din("l0f_wge", [D, 12]); din("l0f_bge", [1, 12], BF16)
    din("l0f_b2", [8, D]); din("l0f_b1fm", [128, 128])
    din("l0f_pv3", [128, 16]); din("basis8", [8, 1024], BF16)
    dint("moew1_bf", [8, 4, 128, 4096], BF16)
    dint("moew2_bf", [8, 4, 128, 4096], BF16)
    dint("wge_bf", [D, 12], BF16)
    din("l1_wqkv", [D, 3 * D]); din("l1_wout", [D, D])
    din("tril", [128, 128], BF16)
    dint("l1wqkv_bf", [D, 3 * D], BF16)
    dint("l1wout_bf", [D, D], BF16)
    dint("pkfm_buf", [D, T], BF16)
    dint("pktm_buf", [T, D], BF16)
    dint("v_buf", [T, D], BF16)
    dint("carry_buf", [64, 1040])
    dint("carry_gath", [128, 1040])
    din("l1f_wexpsw", [8, 128, 4096]); din("l1f_wconsw", [8, 128, 4096])
    din("l1f_ab", [128, 1536]); din("l1f_xorm", [7, 128, 128], BF16)
    din("l1f_pv", [128, 48])
    dint("wexp_bf", [8, 128, 4096], BF16)
    dint("wcon_bf", [8, 128, 4096], BF16)
    dint("scr", [4096, T], BF16)
    dint("scr2", [4096, T], BF16)
    dint("scr3", [4096, T], BF16)
    dint("probe_buf", [128, 8])
    dint("probe_red", [128, 8])
    dint("kvu_buf", [66, 1040])
    dint("kvu_red", [66, 1040])
    dint("mmx_buf", [1, 16])
    dint("mmx_red", [1, 16])

    es = ExitStack()
    sbt = lambda n, s, dt: es.enter_context(nc.sbuf_tensor(n, s, dt))
    pst = lambda n, s, dt: es.enter_context(nc.psum_tensor(n, s, dt))
    sem = lambda n: es.enter_context(nc.semaphore(n))

    S = {}
    S["xs"] = sbt("xs", [128, DC * T], F32)
    S["yb"] = sbt("yb", [128, DC * T], BF16)
    S["kb"] = sbt("kb", [128, DC * T], BF16)
    S["vb"] = sbt("vb", [128, DC * T], BF16)
    S["slotA"] = sbt("slotA", [128, DC * TT], BF16)
    S["slotB"] = sbt("slotB", [128, DC * TT], BF16)
    S["sqf"] = sbt("sqf", [128, 1536], F32)
    S["rows"] = sbt("rows", [128, 1024], F32)
    S["smf"] = sbt("smf", [128, 2048], F32)
    S["kvu"] = sbt("kvu", [64, 1040], F32)
    S["kvb"] = sbt("kvb", [128, 1040], BF16)
    S["scoreT"] = sbt("scoreT", [128, 256], BF16)
    S["probe"] = sbt("probe", [128, 8], F32)
    S["pbd"] = sbt("pbd", [128, 16], BF16)
    S["consts"] = sbt("consts_sb", [128, 16], F32)
    S["pv"] = sbt("pv", [128, 120], F32)
    S["pv2"] = sbt("pv2", [128, 48], F32)
    S["pv3"] = sbt("pv3", [128, 16], F32)
    S["pv5"] = sbt("pv5", [128, 48], F32)
    S["b1fm3"] = sbt("b1fm3", [128, 128], F32)
    S["bge"] = sbt("bge", [1, 12], BF16)
    S["b1fm"] = sbt("b1fm", [128, 32], F32)
    S["rowsbf"] = sbt("rowsbf", [1, 2 * 1024], BF16)
    S["onec_f"] = sbt("onec_f", [128, 1], F32)
    S["onec_b"] = sbt("onec_b", [128, 1], BF16)
    S["oner_f"] = sbt("oner_f", [1, 128], F32)
    S["oner_b"] = sbt("oner_b", [1, 128], BF16)
    S["identb"] = sbt("identb", [128, 128], BF16)

    block = es.enter_context(nc.Block())
    C = Ctx(nc, block)
    for n in ["wl", "al", "mm", "ve", "se", "gc", "st", "cc", "pc", "pc2"]:
        C.sem(n, sem("s_" + n))
    C._dsems = []
    C._dptr = 0
    for i in range(16):
        nm = f"d{i}"
        C.sem(nm, sem("s_" + nm))
        C._dsems.append(nm)
    for i in range(7):
        C.add_bank(i, pst(f"p{i}", [128, 512], F32))
    C.add_bank("T", pst("pT", [128, 1024], BF16))
    C.scrA = S["consts"][0:1, 10:11]
    C.scrB = S["consts"][0:1, 11:12]

    emit_preamble(C, g, S)
    emit_phase1(C, g, S, half)
    if not C.dumped:
        emit_phase2(C, g, S)
    if not C.dumped:
        emit_phase3(C, g, S)
    if not C.dumped:
        emit_phase4(C, g, S)
    if not C.dumped:
        emit_phase5(C, g, S)
    if not C.dumped:
        emit_epilogue(C, g, S)
    C.flush()
    es.close()
    return nc, g


def emit_preamble(C, g, S):
    @C.syn
    def _(sy):
        grp = C.dgrp(sy)
        for kc in range(DC):
            C.inc(sy.dma_start(out=S["xs"][:, kc * T:(kc + 1) * T],
                               in_=g["x_fm"][kc * 128:(kc + 1) * 128, :]), grp, 16)
        for nm, src in [("consts", "consts"), ("pv", "l0a_pv"), ("b1fm", "l0a_b1fm"),
                        ("pv2", "l0f_pv"), ("pv3", "l0f_pv3"), ("b1fm3", "l0f_b1fm"),
                        ("bge", "l0f_bge"), ("pv5", "l1f_pv"),
                        ("rowsbf", "l0a_rows"), ("onec_f", "ones_col_f32"),
                        ("onec_b", "ones_col_bf"), ("oner_f", "ones_row_f32"),
                        ("oner_b", "ones_row_bf"), ("identb", "ident_bf")]:
            C.inc(sy.dma_start(out=S[nm][:], in_=g[src][:]), grp, 16)
        C.pre_al = (grp, C.count(grp))

    @C.gps
    def _(gp):
        for nm in ["Wq", "Wfq", "Wk", "Wv", "Wfk"]:
            C.inc(gp.dma_start(out=g[nm + "_bf"][:], in_=g["l0a_" + nm][:]), "pc", 16)
        C.inc(gp.dma_start(out=g["w1_bf"][:], in_=g["l0a_W1sw"][:]), "pc", 16)
        C.inc(gp.dma_start(out=g["w2_bf"][:], in_=g["l0a_W2sw"][:]), "pc", 16)
        C.inc(gp.dma_start(out=g["wqkv_bf"][:], in_=g["l0f_wqkv"][:]), "pc", 16)
        C.inc(gp.dma_start(out=g["wout_bf"][:], in_=g["l0f_wout"][:]), "pc", 16)
        C.inc(gp.dma_start(out=g["wge_bf"][:], in_=g["l0f_wge"][:]), "pc", 16)
        C.inc(gp.dma_start(out=g["moew1_bf"][:], in_=g["l0f_w1sw"][:]), "pc2", 16)
        C.inc(gp.dma_start(out=g["moew2_bf"][:], in_=g["l0f_w2sw"][:]), "pc2", 16)
        C.inc(gp.dma_start(out=g["l1wqkv_bf"][:], in_=g["l1_wqkv"][:]), "pc2", 16)
        C.inc(gp.dma_start(out=g["l1wout_bf"][:], in_=g["l1_wout"][:]), "pc2", 16)
        C.inc(gp.dma_start(out=g["wexp_bf"][:], in_=g["l1f_wexpsw"][:]), "pc2", 16)
        C.inc(gp.dma_start(out=g["wcon_bf"][:], in_=g["l1f_wconsw"][:]), "pc2", 16)
    C.precast_pc = C.count("pc")
    C.precast_pc2 = C.count("pc2")


def emit_epilogue(C, g, S, src=None):
    if src is None:
        src = S["xs"]

    @C.syn
    def _(sy):
        for nm in ["ve", "se", "mm", "gc"]:
            C.wait(sy, nm)
        for kc in range(DC):
            C.inc(sy.dma_start(out=g["out_fm"][kc * 128:(kc + 1) * 128, :],
                               in_=src[:, kc * T:(kc + 1) * T]), "st", 16)
        C.wait(sy, "st")


def maybe_dump(C, g, S, tag, buf):
    if DBG != tag:
        return False

    @C.gps
    def _(gp):
        for nm in ["ve", "se", "mm", "al", "wl", "st", "cc"]:
            C.wait(gp, nm)
        for kc in range(DC):
            C.inc(gp.dma_start(out=g["out_fm"][kc * 128:(kc + 1) * 128, :],
                               in_=buf[:, kc * T:(kc + 1) * T]), "st", 16)
        C.wait(gp, "st")
    C.dumped = True
    return True


def emit_phase1(C, g, S, half):
    xs, yb, kb, vb = S["xs"], S["yb"], S["kb"], S["vb"]
    slotA, slotB, sqf, rows, smf = S["slotA"], S["slotB"], S["sqf"], S["rows"], S["smf"]
    cst, pv = S["consts"], S["pv"]
    WS0, WS1 = 0, 8192

    def wsl(slot, kc, c0, cn):
        base = WS0 if slot == 0 else WS1
        return vb[:, base + kc * 1024 + c0: base + kc * 1024 + c0 + cn]

    wl_marks = {}

    def load_wfull(sy, wname, slot, gate_mm=0):
        grp = C.dgrp(sy)
        if gate_mm:
            C.wait_at(sy, "mm", gate_mm)
        C.wait_at(sy, "pc", C.precast_pc)
        src_bf = g[wname.replace("l0a_", "") + "_bf"]
        for kc in range(DC):
            C.inc(sy.dma_start(out=wsl(slot, kc, 0, 1024),
                               in_=src_bf[kc * 128:(kc + 1) * 128, :]), grp, 16)
        wl_marks[wname] = (grp, C.count(grp))

    @C.syn
    def _(sy):
        load_wfull(sy, "l0a_Wq", 0)
        load_wfull(sy, "l0a_Wfq", 1)

    p1_marks = []
    for tt in range(NTT):
        c0 = tt * TT
        m = fm_ln(C, S,
                  src_chunk=lambda kc, c0=c0: xs[:, kc * T + c0: kc * T + c0 + TT],
                  ttw=TT,
                  dst_chunk=lambda kc, c0=c0: yb[:, kc * T + c0: kc * T + c0 + TT],
                  src_f32=True,
                  src_ready=(lambda kc: C.pre_al) if tt == 0 else None)
        p1_marks.append(m)
    p1_done = p1_marks[-1]
    maybe_dump(C, g, S, "p1_y", yb)
    if C.dumped:
        return

    q_mm_done = 0
    pq_mm_done = 0
    slotA_free = 0
    slotB_free = None
    for tt in range(NTT):
        c0 = tt * TT
        evac_marks = []
        saf_now = slotA_free

        def q_out(oc, bk, mmk, evac_marks=evac_marks, saf=saf_now):
            @C.scal
            def _(sc):
                C.wait_at(sc, "mm", mmk)
                C.wait_at(sc, "ve", saf)
                sc.activation(S["slotA"][:, oc * TT:oc * TT + TT], C.bh(bk)[:, 0:TT],
                              AF.Identity, bias=pv[:, PV_BQ + oc:PV_BQ + oc + 1])
                C.brel(bk, "se")
            evac_marks.append(("se", C.count("se")))

        fm_proj(C, S, lambda kc, oc: wsl(0, kc, oc * 128, 128),
                lambda kc, c0=c0: yb[:, kc * T + c0: kc * T + c0 + TT],
                TT, q_out, w_ready=wl_marks["l0a_Wq"],
                src_ready=lambda oc, tt=tt: ("ve", p1_marks[tt]))
        q_mm_done = C.count("mm")
        qln_ve = fm_ln(C, S,
                       src_chunk=lambda kc: S["slotA"][:, kc * TT:(kc + 1) * TT],
                       ttw=TT,
                       dst_chunk=lambda kc: S["slotA"][:, kc * TT:(kc + 1) * TT],
                       gamma_beta=(PV_GQ, PV_BETAQ),
                       src_ready=lambda kc, em=list(evac_marks): em[kc])

        @C.vec
        def _(v, tt=tt):
            for kc in range(DC):
                v.reduce_sum(smf[:, 0:1], S["slotA"][:, kc * TT:(kc + 1) * TT], axis=AX.X)
                if tt == 0:
                    v.tensor_copy(S["probe"][:, kc:kc + 1], smf[:, 0:1])
                else:
                    v.tensor_tensor(S["probe"][:, kc:kc + 1], S["probe"][:, kc:kc + 1],
                                    smf[:, 0:1], ALU.add)

        pq_marks = []

        def pq_out(oc, bk, mmk, pq_marks=pq_marks, sbf=slotB_free):
            @C.scal
            def _(sc):
                C.wait_at(sc, "mm", mmk)
                C.wmark(sc, sbf)
                sc.activation(S["slotB"][:, oc * TT:oc * TT + TT], C.bh(bk)[:, 0:TT],
                              AF.Tanh, bias=pv[:, PV_BFQ + oc:PV_BFQ + oc + 1])
                C.brel(bk, "se")
            semk = C.count("se")

            @C.vec
            def _(v):
                C.wait_at(v, "se", semk)
                v.tensor_scalar(S["slotB"][:, oc * TT:oc * TT + TT],
                                S["slotB"][:, oc * TT:oc * TT + TT],
                                cst[:, C_ONE:C_ONE + 1], None, ALU.add)
            pq_marks.append(C.count("ve"))

        fm_proj(C, S, lambda kc, oc: wsl(1, kc, oc * 128, 128),
                lambda kc: S["slotA"][:, kc * TT:(kc + 1) * TT],
                TT, pq_out, banks=(0, 1), w_ready=wl_marks["l0a_Wfq"],
                src_ready=lambda oc, q=qln_ve: ("ve", q))
        pq_mm_done = C.count("mm")
        slotA_free = C.count("ve")

        @C.syn
        def _(sy, tt=tt, pqm=list(pq_marks), c0=c0):
            grp = C.dgrp(sy)
            for oc in range(DC):
                C.wait_at(sy, "ve", pqm[oc])
                C.inc(sy.dma_start(out=g["pq_spill"][oc * 128:(oc + 1) * 128, c0:c0 + TT],
                                   in_=S["slotB"][:, oc * TT:oc * TT + TT]), grp, 16)
            C.pq_spill_st = (grp, C.count(grp))
        slotB_free = C.pq_spill_st

    @C.syn
    def _(sy):
        load_wfull(sy, "l0a_Wk", 0, gate_mm=q_mm_done)

    for tt in range(NTT):
        c0 = tt * TT
        evac_marks = []
        if tt > 0:
            slotA_free = C.count("ve")

        def k_out(oc, bk, mmk, evac_marks=evac_marks, saf=slotA_free):
            @C.scal
            def _(sc):
                C.wait_at(sc, "mm", mmk)
                C.wait_at(sc, "ve", saf)
                sc.activation(S["slotA"][:, oc * TT:oc * TT + TT], C.bh(bk)[:, 0:TT],
                              AF.Identity, bias=pv[:, PV_BK + oc:PV_BK + oc + 1])
                C.brel(bk, "se")
            evac_marks.append(("se", C.count("se")))

        fm_proj(C, S, lambda kc, oc: wsl(0, kc, oc * 128, 128),
                lambda kc, c0=c0: yb[:, kc * T + c0: kc * T + c0 + TT],
                TT, k_out, w_ready=wl_marks["l0a_Wk"],
                src_ready=lambda oc, tt=tt: ("ve", p1_marks[tt]))
        fm_ln(C, S,
              src_chunk=lambda kc: S["slotA"][:, kc * TT:(kc + 1) * TT],
              ttw=TT,
              dst_chunk=lambda kc, c0=c0: kb[:, kc * T + c0: kc * T + c0 + TT],
              gamma_beta=(PV_GK, PV_BETAK),
              src_ready=lambda kc, em=list(evac_marks): em[kc])
    k_mm_done = C.count("mm")
    k_ve_done = C.count("ve")
    maybe_dump(C, g, S, "p3_K", kb)
    if C.dumped:
        return

    @C.syn
    def _(sy):
        load_wfull(sy, "l0a_Wfk", 1, gate_mm=pq_mm_done)
        load_wfull(sy, "l0a_Wv", 0, gate_mm=k_mm_done)

    probe_ve = C.count("ve")

    @C.gps
    def _(gp):
        C.wait_at(gp, "ve", probe_ve)
        C.inc(gp.dma_start(out=g["probe_buf"][:], in_=S["probe"][:]), "gc", 16)
        C.wait(gp, "gc")
        C.inc(gp.collective_compute(
            "AllReduce", ALU.add,
            replica_groups=[[0, 1], [2, 3], [4, 5], [6, 7]],
            ins=[g["probe_buf"][:, :].opt()], outs=[g["probe_red"][:, :].opt()]), "cc")

    @C.syn
    def _(sy):
        grp = C.dgrp(sy)
        C.wait(sy, "cc")
        C.inc(sy.dma_start(out=S["probe"][:], in_=g["probe_red"][:]), grp, 16)
        C.probe_al = (grp, C.count(grp))

    @C.vec
    def _(v):
        C.wmark(v, C.probe_al)
        v.tensor_scalar(S["probe"][:], S["probe"][:], cst[:, C_PROBE:C_PROBE + 1],
                        None, ALU.mult)
        v.memset(S["pbd"][:], 0.0)
        for kc in range(DC):
            v.tensor_copy(S["pbd"][0:64, 2 * kc:2 * kc + 1], S["probe"][0:64, kc:kc + 1])
            v.tensor_copy(S["pbd"][64:128, 2 * kc + 1:2 * kc + 2],
                          S["probe"][64:128, kc:kc + 1])
    pbd_ve = C.count("ve")

    MY_M, MY_ZU, MM_, WMY, E0_, RZ_, T1_, T2_, NEG_ = \
        0, 16, 32, 48, 64, 80, 96, 112, 128
    mz = rows[0:1, 512:656]
    prev_exp_se = 0
    prev_tr_mm = 0
    for h in range(H):
        c, r = h // 2, h % 2
        lg_marks = []
        for tt in range(NTT):
            bk = 4 if tt % 2 == 0 else 0

            @C.ten
            def _(t, c=c, r=r, tt=tt, bk=bk):
                p = C.bacq(t, bk)
                C.wait_at(t, "ve", pbd_ve)
                C.wait_at(t, "ve", k_ve_done)
                C.inc(t.matmul(p[0:1, 0:TT], S["pbd"][:, 2 * c + r:2 * c + r + 1],
                               kb[:, c * T + tt * TT: c * T + (tt + 1) * TT],
                               start=True, stop=True), "mm")
            mmk = C.count("mm")

            @C.vec
            def _(v, tt=tt, mmk=mmk, pse=prev_exp_se, bk=bk):
                C.wait_at(v, "mm", mmk)
                C.wait_at(v, "se", pse)
                v.tensor_copy(smf[0:1, tt * TT:(tt + 1) * TT], C.bh(bk)[0:1, 0:TT])
                C.brel(bk, "ve")
            lg_marks.append(C.count("ve"))

        @C.vec
        def _(v, h=h, lg=list(lg_marks)):
            C.wait_at(v, "ve", lg[-1])
            v.reduce_max(mz[0:1, MY_M + h:MY_M + h + 1], smf[0:1, 0:T], axis=AX.X)
            v.tensor_scalar(mz[0:1, MY_M + h:MY_M + h + 1],
                            mz[0:1, MY_M + h:MY_M + h + 1],
                            cst[0:1, C_Z0:C_Z0 + 1], None, ALU.max)
            v.tensor_scalar(mz[0:1, NEG_ + h:NEG_ + h + 1],
                            mz[0:1, MY_M + h:MY_M + h + 1],
                            cst[0:1, C_NEG1:C_NEG1 + 1], None, ALU.mult)
        m_ve = C.count("ve")

        @C.scal
        def _(sc, h=h, m_ve=m_ve, ptm=prev_tr_mm):
            C.wait_at(sc, "ve", m_ve)
            C.wait_at(sc, "mm", ptm)
            if h == 0:
                C.wmark(sc, C.pq_spill_st)
            sc.activation(S["slotB"][0:1, 0:T], smf[0:1, 0:T], AF.Exp,
                          bias=mz[0:1, NEG_ + h:NEG_ + h + 1],
                          accum_out=mz[0:1, MY_ZU + h:MY_ZU + h + 1])
            sc.activation(mz[0:1, T1_ + h:T1_ + h + 1],
                          mz[0:1, NEG_ + h:NEG_ + h + 1], AF.Exp,
                          bias=cst[0:1, C_Z0:C_Z0 + 1])
        exp_se = C.count("se")
        prev_exp_se = exp_se

        @C.vec
        def _(v, h=h, exp_se=exp_se):
            C.wait_at(v, "se", exp_se)
            v.tensor_tensor(mz[0:1, MY_ZU + h:MY_ZU + h + 1],
                            mz[0:1, MY_ZU + h:MY_ZU + h + 1],
                            mz[0:1, T1_ + h:T1_ + h + 1], ALU.add)
        su_ve = C.count("ve")

        for kt in range(16):
            @C.ten
            def _(t, kt=kt, su_ve=su_ve):
                pT = C.bacq(t, "T")
                C.wait_at(t, "ve", su_ve)
                C.wait_at(t, "se", C.count("se"))
                C.inc(t.transpose(pT[:, 0:1], S["slotB"][0:1, kt * 128:(kt + 1) * 128],
                                  S["identb"][0:1, 0:1]), "mm")
            tr_mm = C.count("mm")
            prev_tr_mm = tr_mm

            @C.vec
            def _(v, h=h, kt=kt, tr_mm=tr_mm):
                C.wait_at(v, "mm", tr_mm)
                v.tensor_copy(S["scoreT"][:, kt * 16 + h: kt * 16 + h + 1],
                              C.bh("T")[:, 0:1])
                C.brel("T", "ve")
    score_ve = C.count("ve")

    for j, (pvc, dst) in enumerate([(PV_GV, 0), (PV_BETAV, 1024)]):
        for kc in range(DC):
            @C.vec
            def _(v, pvc=pvc, kc=kc):
                C.wmark(v, C.pre_al)
                v.tensor_copy(S["slotB"][:, 0:1], pv[:, pvc + kc:pvc + kc + 1])
            cp_ve = C.count("ve")

            @C.ten
            def _(t, cp_ve=cp_ve):
                C.wait_at(t, "ve", cp_ve)
                pT = C.bacq(t, "T")
                C.inc(t.transpose(pT[0:1, 0:128], S["slotB"][:, 0:1], S["identb"][:, :]),
                      "mm")
            tr_mm2 = C.count("mm")

            @C.vec
            def _(v, tr_mm2=tr_mm2):
                C.wait_at(v, "mm", tr_mm2)
                v.tensor_copy(S["slotB"][0:1, 128:256], C.bh("T")[0:1, 0:128])
                C.brel("T", "ve")
            ev_ve = C.count("ve")

            @C.ten
            def _(t, ev_ve=ev_ve):
                C.wait_at(t, "ve", ev_ve)
                p4 = C.bacq(t, 4)
                C.inc(t.matmul(p4[:, 0:128], S["oner_b"][0:1, :],
                               S["slotB"][0:1, 128:256], start=True, stop=True), "mm")
            bm = C.count("mm")

            @C.vec
            def _(v, dst=dst, kc=kc, bm=bm):
                C.wait_at(v, "mm", bm)
                v.tensor_copy(smf[:, dst + kc * 128: dst + (kc + 1) * 128],
                              C.bh(4)[:, 0:128])
                C.brel(4, "ve")
    gvb_ve = C.count("ve")

    prev_pkw_ve = C.count("ve")
    for kt in range(16):
        ktt = kt * 128
        pk_marks = []
        for ocp in range(2):
            @C.ten
            def _(t, kt=kt, ocp=ocp):
                p = C.bacq(t, ocp)
                C.wmark(t, wl_marks["l0a_Wfk"])
                C.wait_at(t, "ve", k_ve_done)
                for kc in range(DC):
                    t.matmul(p[:, 0:TT], kb[:, kc * T + ktt: kc * T + ktt + 128],
                             wsl(1, kc, ocp * 512, 512),
                             start=(kc == 0), stop=False)
                C.inc(t.matmul(p[:, 0:TT], S["oner_b"][0:1, 0:128],
                               S["rowsbf"][0:1, R_BFK + ocp * 512: R_BFK + ocp * 512 + 512],
                               start=False, stop=True), "mm")
            mmk = C.count("mm")

            @C.scal
            def _(sc, ocp=ocp, mmk=mmk, ppk=prev_pkw_ve):
                C.wait_at(sc, "mm", mmk)
                C.wait_at(sc, "ve", ppk)
                sc.activation(S["slotA"][:, ocp * 512:(ocp + 1) * 512],
                              C.bh(ocp)[:, 0:TT], AF.Tanh)
                C.brel(ocp, "se")
            pk_marks.append(C.count("se"))

        @C.vec
        def _(v, kt=kt, pk=list(pk_marks)):
            C.wait_at(v, "se", pk[-1])
            C.wait_at(v, "ve", score_ve)
            sv = expand_view(S["scoreT"][:, kt * 16: kt * 16 + 16], [[1, 16], [0, 64]])
            v.scalar_tensor_tensor(
                S["slotB"][:, 0:1024], S["slotA"][:, 0:1024],
                cst[:, C_ONE:C_ONE + 1], sv, ALU.add, ALU.mult)
        pkw_ve = C.count("ve")
        prev_pkw_ve = pkw_ve

        v_marks = []
        for ocp in range(2):
            @C.ten
            def _(t, kt=kt, ocp=ocp):
                p = C.bacq(t, 2 + ocp)
                C.wmark(t, wl_marks["l0a_Wv"])
                C.wait_at(t, "ve", p1_done)
                for kc in range(DC):
                    t.matmul(p[:, 0:TT], yb[:, kc * T + ktt: kc * T + ktt + 128],
                             wsl(0, kc, ocp * 512, 512),
                             start=(kc == 0), stop=False)
                C.inc(t.matmul(p[:, 0:TT], S["oner_b"][0:1, 0:128],
                               S["rowsbf"][0:1, R_BV + ocp * 512: R_BV + ocp * 512 + 512],
                               start=False, stop=True), "mm")
            mmk = C.count("mm")

            @C.scal
            def _(sc, ocp=ocp, mmk=mmk, kt=kt):
                C.wait_at(sc, "mm", mmk)
                if kt == 0:
                    C.wait_at(sc, "ve", score_ve)
                sc.activation(sqf[:, ocp * 512:(ocp + 1) * 512], C.bh(2 + ocp)[:, 0:TT],
                              AF.Copy)
                C.brel(2 + ocp, "se")
            v_marks.append(C.count("se"))

        @C.vec
        def _(v, vm=list(v_marks)):
            C.wait_at(v, "se", vm[-1])
            v.reduce_sum(rows[:, 0:1], sqf[:, 0:1024], axis=AX.X)
        vsum_ve = C.count("ve")

        @C.vec
        def _(v, vsum_ve=vsum_ve):
            v.tensor_tensor(sqf[:, 1024:1536], sqf[:, 0:512], sqf[:, 0:512], ALU.mult)
            v.reduce_sum(rows[:, 1:2], sqf[:, 1024:1536], axis=AX.X)
            v.tensor_tensor(sqf[:, 1024:1536], sqf[:, 512:1024], sqf[:, 512:1024],
                            ALU.mult)
            v.reduce_sum(rows[:, 2:3], sqf[:, 1024:1536], axis=AX.X)
            v.tensor_tensor(rows[:, 1:2], rows[:, 1:2], rows[:, 2:3], ALU.add)
            v.tensor_scalar(rows[:, 0:1], rows[:, 0:1], cst[:, C_I1024:C_I1024 + 1],
                            None, ALU.mult)
            v.tensor_scalar(rows[:, 1:2], rows[:, 1:2], cst[:, C_I1024:C_I1024 + 1],
                            None, ALU.mult)
            v.tensor_tensor(rows[:, 2:3], rows[:, 0:1], rows[:, 0:1], ALU.mult)
            v.tensor_tensor(rows[:, 1:2], rows[:, 1:2], rows[:, 2:3], ALU.subtract)
        vvar_ve = C.count("ve")

        @C.scal
        def _(sc, vvar_ve=vvar_ve):
            C.wait_at(sc, "ve", vvar_ve)
            sc.activation(rows[:, 1:2], rows[:, 1:2], AF.Ln,
                          bias=cst[:, C_EPS:C_EPS + 1])
            sc.activation(rows[:, 1:2], rows[:, 1:2], AF.Exp, scale=-0.5)
        vex_se = C.count("se")

        @C.vec
        def _(v, vex_se=vex_se):
            C.wait_at(v, "se", vex_se)
            v.tensor_scalar(rows[:, 0:1], rows[:, 0:1], cst[:, C_NEG1:C_NEG1 + 1],
                            None, ALU.mult)
            C.wait_at(v, "ve", gvb_ve)
            v.tensor_scalar(sqf[:, 0:1024], sqf[:, 0:1024], rows[:, 0:1],
                            rows[:, 1:2], ALU.add, ALU.mult)
            v.tensor_tensor(sqf[:, 0:1024], sqf[:, 0:1024], smf[:, 0:1024], ALU.mult)
            v.tensor_tensor(S["slotB"][:, 1024:2048], sqf[:, 0:1024],
                            smf[:, 1024:2048], ALU.add)
        vtm_ve = C.count("ve")

        @C.ten
        def _(t, kt=kt, pkw_ve=pkw_ve, vtm_ve=vtm_ve):
            C.wait_at(t, "ve", pkw_ve)
            C.wait_at(t, "ve", vtm_ve)
            if kt == 0:
                C.bacq(t, 4)
                C.bacq(t, 5)
                C.bacq(t, 6)
            ins = None
            for h in range(H):
                p = C.bh(5 if h < 8 else 6)
                cc = (h % 8) * 64
                t.matmul(p[0:64, cc:cc + 64],
                         S["slotB"][:, h * 64:(h + 1) * 64],
                         S["slotB"][:, 1024 + h * 64: 1024 + (h + 1) * 64],
                         start=(kt == 0 and h in (0, 8)),
                         stop=(kt == 15 and h in (7, 15)))
                ins = t.matmul(C.bh(4)[0:64, h: h + 1],
                               S["slotB"][:, h * 64:(h + 1) * 64],
                               S["onec_b"][:, 0:1],
                               start=(kt == 0 and h == 0),
                               stop=(kt == 15 and h == 15))
            C.inc(ins, "mm")
    kvu_mm = C.count("mm")

    S_kvu2 = rows[0:64, 960:1008]
    @C.vec
    def _(v):
        C.wait_at(v, "mm", kvu_mm)
        for bk, hb in [(5, 0), (6, 8)]:
            dst = expand_view(S["kvu"][0:64, hb * 65: hb * 65 + 1], [[65, 8], [1, 64]])
            srcb = expand_view(C.bh(bk)[0:64, 0:1], [[64, 8], [1, 64]])
            v.tensor_copy(dst, srcb)
            C.brel(bk, "ve")
        dstk = expand_view(S["kvu"][0:64, 64:65], [[65, 16]])
        v.tensor_copy(dstk, C.bh(4)[0:64, 0:16])
        C.brel(4, "ve")
    kvu_ve = C.count("ve")

    @C.gps
    def _(gp):
        C.wait(gp, "ve")
        C.inc(gp.dma_start(out=g["mmx_buf"][0:1, :], in_=mz[0:1, MY_M:MY_M + 16]),
              "gc", 16)
        C.wait(gp, "gc")
        C.inc(gp.collective_compute(
            "AllReduce", ALU.max,
            replica_groups=[[0, 1], [2, 3], [4, 5], [6, 7]],
            ins=[g["mmx_buf"][0:1, :].opt()], outs=[g["mmx_red"][0:1, :].opt()]), "cc")

    @C.syn
    def _(sy):
        grp = C.dgrp(sy)
        C.wait(sy, "cc")
        C.inc(sy.dma_start(out=mz[0:1, MM_:MM_ + 16], in_=g["mmx_red"][0:1, :]), grp, 16)
        C.mmx_al = (grp, C.count(grp))

    @C.vec
    def _(v):
        C.wmark(v, C.mmx_al)
        v.tensor_tensor(mz[0:1, WMY:WMY + 16], mz[0:1, MY_M:MY_M + 16],
                        mz[0:1, MM_:MM_ + 16], ALU.subtract)
    dm_ve = C.count("ve")

    @C.scal
    def _(sc, dm_ve=dm_ve):
        C.wait_at(sc, "ve", dm_ve)
        sc.activation(mz[0:1, WMY:WMY + 16], mz[0:1, WMY:WMY + 16], AF.Exp)
    w_se = C.count("se")

    @C.ten
    def _(t, w_se=w_se):
        C.wait_at(t, "se", w_se)
        p = C.bacq(t, 0)
        C.inc(t.matmul(p[0:64, 0:16], S["oner_f"][0:1, 0:64],
                       mz[0:1, WMY:WMY + 16], start=True, stop=True), "mm")
    wc_mm = C.count("mm")

    @C.vec
    def _(v, wc_mm=wc_mm):
        C.wait_at(v, "mm", wc_mm)
        v.tensor_copy(S_kvu2[0:64, 0:16], C.bh(0)[0:64, 0:16])
        C.brel(0, "ve")
        v.tensor_tensor(mz[0:1, MY_ZU:MY_ZU + 16], mz[0:1, MY_ZU:MY_ZU + 16],
                        mz[0:1, WMY:WMY + 16], ALU.mult)
        for h in range(H):
            v.tensor_scalar(S["kvu"][0:64, h * 65:(h + 1) * 65],
                            S["kvu"][0:64, h * 65:(h + 1) * 65],
                            S_kvu2[0:64, h:h + 1], None, ALU.mult)
        v.memset(sqf[0:1, 0:1040], 0.0)
        v.tensor_copy(sqf[0:1, 0:16], mz[0:1, MY_ZU:MY_ZU + 16])
    kvu3_ve = C.count("ve")

    @C.gps
    def _(gp, kvu3_ve=kvu3_ve):
        C.wait_at(gp, "ve", kvu3_ve)
        C.inc(gp.dma_start(out=g["kvu_buf"][0:64, :], in_=S["kvu"][:, :]), "gc", 16)
        C.inc(gp.dma_start(out=g["kvu_buf"][64:65, 0:1040],
                           in_=sqf[0:1, 0:1040]), "gc", 16)
        C.wait(gp, "gc")
        C.inc(gp.collective_compute(
            "AllReduce", ALU.add,
            replica_groups=[[0, 1], [2, 3], [4, 5], [6, 7]],
            ins=[g["kvu_buf"][0:65, :].opt()], outs=[g["kvu_red"][0:65, :].opt()]), "cc")

    @C.syn
    def _(sy):
        grp = C.dgrp(sy)
        C.wait(sy, "cc")
        C.inc(sy.dma_start(out=S["kvu"][:, :], in_=g["kvu_red"][0:64, :]), grp, 16)
        C.inc(sy.dma_start(out=mz[0:1, RZ_:RZ_ + 16], in_=g["kvu_red"][64:65, 0:16]),
              grp, 16)
        C.kvu_al = (grp, C.count(grp))

    @C.vec
    def _(v):
        C.wmark(v, C.kvu_al)
        v.reciprocal(mz[0:1, RZ_:RZ_ + 16], mz[0:1, RZ_:RZ_ + 16])
        v.tensor_scalar(mz[0:1, T2_:T2_ + 16], mz[0:1, MM_:MM_ + 16],
                        cst[0:1, C_NEG1:C_NEG1 + 1], None, ALU.mult)
    rz_ve = C.count("ve")

    @C.scal
    def _(sc, rz_ve=rz_ve):
        C.wait_at(sc, "ve", rz_ve)
        sc.activation(mz[0:1, T2_:T2_ + 16], mz[0:1, T2_:T2_ + 16], AF.Exp)
        C.wmark(sc, C.pre_al)
        sc.activation(rows[0:64, 16:32], pv[0:64, PV_BFKH:PV_BFKH + 16], AF.Tanh)
    e0_se = C.count("se")

    @C.vec
    def _(v, e0_se=e0_se):
        C.wait_at(v, "se", e0_se)
        v.tensor_tensor(mz[0:1, T2_:T2_ + 16], mz[0:1, T2_:T2_ + 16],
                        mz[0:1, RZ_:RZ_ + 16], ALU.mult)
        v.tensor_scalar(rows[0:64, 16:32], rows[0:64, 16:32],
                        cst[0:64, C_ONE:C_ONE + 1], None, ALU.add)
    e0r_ve = C.count("ve")

    @C.ten
    def _(t, e0r_ve=e0r_ve):
        C.wait_at(t, "ve", e0r_ve)
        p = C.bacq(t, 1)
        t.matmul(p[0:64, 0:16], S["oner_f"][0:1, 0:64],
                 mz[0:1, RZ_:RZ_ + 16], start=True, stop=True)
        C.inc(t.matmul(p[0:64, 16:32], S["oner_f"][0:1, 0:64],
                       mz[0:1, T2_:T2_ + 16], start=True, stop=True), "mm")
    bc2_mm = C.count("mm")

    @C.vec
    def _(v, bc2_mm=bc2_mm):
        C.wait_at(v, "mm", bc2_mm)
        v.tensor_copy(S_kvu2[0:64, 16:48], C.bh(1)[0:64, 0:32])
        C.brel(1, "ve")
        for h in range(H):
            v.tensor_scalar(S["kvu"][0:64, h * 65:(h + 1) * 65],
                            S["kvu"][0:64, h * 65:(h + 1) * 65],
                            S_kvu2[0:64, 16 + h:17 + h], None, ALU.mult)
            v.tensor_scalar(rows[0:64, 32:33], rows[0:64, 16 + h:17 + h],
                            S_kvu2[0:64, 32 + h:33 + h], None, ALU.mult)
            v.tensor_tensor(S["kvu"][0:64, h * 65 + 64: h * 65 + 65],
                            S["kvu"][0:64, h * 65 + 64: h * 65 + 65],
                            rows[0:64, 32:33], ALU.add)
        dstkv = expand_view(S["kvb"][0:64, 0:1], [[64, 16], [1, 64]])
        srckv = expand_view(S["kvu"][0:64, 0:1], [[65, 16], [1, 64]])
        v.tensor_copy(dstkv, srckv)
        dstks = expand_view(S["kvb"][0:64, 1024:1025], [[1, 16]])
        srcks = expand_view(S["kvu"][0:64, 64:65], [[65, 16]])
        v.tensor_copy(dstks, srcks)
    kvb_ve = C.count("ve")

    @C.syn
    def _(sy):
        grp = C.dgrp(sy)
        C.wait_at(sy, "ve", kvb_ve)
        C.inc(sy.dma_start(out=S["kvb"][64:128, :], in_=S["kvb"][0:64, :]), grp, 16)
        C.kvbdup = (grp, C.count(grp))

    p8_prev_mm = 0
    for tt in range(NTT):
        c0 = tt * TT

        @C.syn
        def _(sy, c0=c0, pm=p8_prev_mm):
            grp = C.dgrp(sy)
            C.wmark(sy, C.pq_spill_st)
            C.wait_at(sy, "ve", kvb_ve)
            C.wait_at(sy, "mm", pm)
            for oc in range(DC):
                C.inc(sy.dma_start(out=S["slotA"][:, oc * TT:(oc + 1) * TT],
                                   in_=g["pq_spill"][oc * 128:(oc + 1) * 128, c0:c0 + TT]),
                      grp, 16)
            C.pq_al = (grp, C.count(grp))

        for h in range(H):
            pb = (h % 2) * 64
            bk = 3 if h % 2 == 0 else 4
            dbk = 5 if h % 2 == 0 else 6
            ch = (h // 2)

            @C.ten
            def _(t, h=h, pb=pb, ch=ch, bk=bk, dbk=dbk):
                p = C.bacq(t, bk)
                pd = C.bacq(t, dbk)
                C.wmark(t, C.pq_al)
                C.wait_at(t, "ve", kvb_ve)
                C.wmark(t, C.kvbdup)
                rhs = S["slotA"][pb:pb + 64, ch * TT: ch * TT + TT]
                t.matmul(p[pb:pb + 64, 0:TT],
                         S["kvb"][pb:pb + 64, h * 64: h * 64 + 64],
                         rhs, start=True, stop=True,
                         skip_group_check=(pb == 64))
                C.inc(t.matmul(pd[0:1, 0:TT], S["kvb"][pb:pb + 64, 1024 + h:1025 + h],
                               rhs, start=True, stop=True), "mm")
            nd_mm = C.count("mm")

            @C.vec
            def _(v, nd_mm=nd_mm, dbk=dbk):
                C.wait_at(v, "mm", nd_mm)
                v.tensor_scalar(rows[0:1, 0:TT], C.bh(dbk)[0:1, 0:TT],
                                cst[0:1, C_EPS6:C_EPS6 + 1], None, ALU.add)
                C.brel(dbk, "ve")
                v.reciprocal(rows[0:1, 0:TT], rows[0:1, 0:TT])
            rc_ve = C.count("ve")

            @C.ten
            def _(t, rc_ve=rc_ve, pb=pb):
                C.wait_at(t, "ve", rc_ve)
                p2 = C.bacq(t, 0)
                C.inc(t.matmul(p2[pb:pb + 64, 0:TT], S["oner_f"][0:1, 0:64],
                               rows[0:1, 0:TT], start=True, stop=True,
                               skip_group_check=(pb == 64)), "mm")
            rb_mm = C.count("mm")

            @C.vec
            def _(v, h=h, pb=pb, ch=ch, rb_mm=rb_mm, bk=bk):
                C.wait_at(v, "mm", rb_mm)
                v.tensor_copy(sqf[pb:pb + 64, 0:TT], C.bh(0)[pb:pb + 64, 0:TT])
                C.brel(0, "ve")
                v.tensor_tensor(S["slotB"][pb:pb + 64, ch * TT: ch * TT + TT],
                                C.bh(bk)[pb:pb + 64, 0:TT], sqf[pb:pb + 64, 0:TT],
                                ALU.mult)
                C.brel(bk, "ve")
        a_done = C.count("ve")
        p8_prev_mm = C.count("mm")

        fm_ln(C, S,
              src_chunk=lambda kc: S["slotB"][:, kc * TT:(kc + 1) * TT],
              ttw=TT,
              dst_chunk=lambda kc, c0=c0: kb[:, kc * T + c0: kc * T + c0 + TT],
              gamma_beta=(PV_GA, PV_BA),
              src_ready=lambda kc, am=a_done: ("ve", am))
    a_ve_done = C.count("ve")
    maybe_dump(C, g, S, "p8_a", kb)
    if C.dumped:
        return

    TB = 256
    w_par_marks = [None, None]
    g_use_marks = [C.count("mm"), C.count("mm")]
    p9_prev_ln = C.count("ve")
    for tb in range(8):
        b0 = tb * TB
        for gi in range(8):
            par = gi % 2
            wc0 = WS0 if par == 0 else WS1

            @C.syn
            def _(sy, tb=tb, gi=gi, par=par, wc0=wc0):
                grp = C.dgrp(sy)
                C.wait_at(sy, "mm", g_use_marks[par])
                if tb == 0 and gi < 2:
                    C.wait_at(sy, "pc", C.precast_pc)
                C.inc(sy.dma_start(out=vb[:, wc0:wc0 + 4096], in_=g["w1_bf"][gi]),
                      grp, 16)
                C.inc(sy.dma_start(out=vb[:, wc0 + 4096:wc0 + 8192], in_=g["w2_bf"][gi]),
                      grp, 16)
                w_par_marks[par] = (grp, C.count(grp))

            for j in range(4):
                hbk = 5 if j % 2 == 0 else 6

                @C.ten
                def _(t, gi=gi, j=j, par=par, wc0=wc0, b0=b0, hbk=hbk):
                    p = C.bacq(t, hbk)
                    C.wmark(t, w_par_marks[par])
                    C.wait_at(t, "ve", a_ve_done)
                    for kc in range(DC):
                        ins = t.matmul(
                            p[:, 0:TB],
                            vb[:, wc0 + (j * 8 + kc) * 128: wc0 + (j * 8 + kc) * 128 + 128],
                            kb[:, kc * T + b0: kc * T + b0 + TB],
                            start=(kc == 0), stop=(kc == DC - 1))
                    C.inc(ins, "mm")
                h_mm = C.count("mm")

                @C.scal
                def _(sc, gi=gi, j=j, h_mm=h_mm, hbk=hbk):
                    C.wait_at(sc, "mm", h_mm)
                    sc.activation(S["slotA"][:, j * TB:(j + 1) * TB], C.bh(hbk)[:, 0:TB],
                                  AF.Gelu, bias=S["b1fm"][:, gi * 4 + j: gi * 4 + j + 1])
                    C.brel(hbk, "se")
                hb_se = C.count("se")

                @C.ten
                def _(t, gi=gi, j=j, wc0=wc0, hb_se=hb_se):
                    C.wait_at(t, "se", hb_se)
                    ins = None
                    for oc in range(DC):
                        obk = oc // 2
                        chh = (oc % 2) * TB
                        p = C.bacq(t, obk) if (gi == 0 and j == 0 and oc % 2 == 0) \
                            else C.bh(obk)
                        ins = t.matmul(p[:, chh:chh + TB],
                                       vb[:, wc0 + 4096 + j * 1024 + oc * 128:
                                          wc0 + 4096 + j * 1024 + oc * 128 + 128],
                                       S["slotA"][:, j * TB:(j + 1) * TB],
                                       start=(gi == 0 and j == 0 and oc % 2 == 0),
                                       stop=(gi == 7 and j == 3 and oc % 2 == 1))
                    C.inc(ins, "mm")
                g_use_marks[par] = C.count("mm")
        mlp_mm = C.count("mm")

        a0_marks = []
        for oc in range(DC):
            obk = oc // 2
            chh = (oc % 2) * TB

            @C.scal
            def _(sc, oc=oc, obk=obk, chh=chh, mlp_mm=mlp_mm, plv=p9_prev_ln):
                C.wait_at(sc, "mm", mlp_mm)
                C.wait_at(sc, "ve", plv)
                sc.activation(smf[:, oc * TB:(oc + 1) * TB], C.bh(obk)[:, chh:chh + TB],
                              AF.Identity, bias=pv[:, PV_B2 + oc:PV_B2 + oc + 1])
                if oc % 2 == 1:
                    C.brel(obk, "se")
            a0_se = C.count("se")

            @C.vec
            def _(v, oc=oc, b0=b0, a0_se=a0_se):
                C.wait_at(v, "se", a0_se)
                v.tensor_tensor(smf[:, oc * TB:(oc + 1) * TB],
                                smf[:, oc * TB:(oc + 1) * TB],
                                kb[:, oc * T + b0: oc * T + b0 + TB], ALU.add)
                v.tensor_tensor(smf[:, oc * TB:(oc + 1) * TB],
                                smf[:, oc * TB:(oc + 1) * TB],
                                yb[:, oc * T + b0: oc * T + b0 + TB], ALU.add)
            a0_marks.append(C.count("ve"))

        fm_ln(C, S,
              src_chunk=lambda kc: smf[:, kc * TB:(kc + 1) * TB],
              ttw=TB,
              residual_into=lambda kc, b0=b0: xs[:, kc * T + b0: kc * T + b0 + TB],
              src_f32=True,
              src_ready=lambda kc, am=list(a0_marks): ("ve", am[kc]))
        p9_prev_ln = C.count("ve")
    maybe_dump(C, g, S, "p9_x1", xs)



def emit_phase2(C, g, S):
    xs, yb, kb, vb = S["xs"], S["yb"], S["kb"], S["vb"]
    slotA, slotB, sqf, rows, smf = S["slotA"], S["slotB"], S["sqf"], S["rows"], S["smf"]
    cst, pv2 = S["consts"], S["pv2"]
    WS0, WS1 = 0, 8192

    def wsl(slot, kc, c0, cn):
        base = WS0 if slot == 0 else WS1
        return vb[:, base + kc * 1024 + c0: base + kc * 1024 + c0 + cn]

    def load_wcols(sy, dram_bf, col0, slot, gate_mm=0):
        grp = C.dgrp(sy)
        if gate_mm:
            C.wait_at(sy, "mm", gate_mm)
        C.wait_at(sy, "pc", C.precast_pc)
        for kc in range(DC):
            C.inc(sy.dma_start(out=wsl(slot, kc, 0, 1024),
                               in_=dram_bf[kc * 128:(kc + 1) * 128, col0:col0 + 1024]),
                  grp, 16)
        return (grp, C.count(grp))

    ph_mm0 = C.count("mm")
    wl2 = {}

    @C.syn
    def _(sy):
        wl2["q"] = load_wcols(sy, g["wqkv_bf"], 0, 0, gate_mm=ph_mm0)
        wl2["k"] = load_wcols(sy, g["wqkv_bf"], 1024, 1, gate_mm=ph_mm0)

    p2_marks = []
    for tt in range(NTT):
        c0 = tt * TT
        m = fm_ln(C, S,
                  src_chunk=lambda kc, c0=c0: xs[:, kc * T + c0: kc * T + c0 + TT],
                  ttw=TT,
                  dst_chunk=lambda kc, c0=c0: yb[:, kc * T + c0: kc * T + c0 + TT],
                  src_f32=True)
        p2_marks.append(m)
    maybe_dump(C, g, S, "p2_y3", yb)
    if C.dumped:
        return

    spill_marks = []

    def proj_spill(wready, pvcol, dram_rows_fn):
        for tt in range(NTT):
            c0 = tt * TT
            ev_marks = []

            def sp_out(oc, bk, mmk, ev_marks=ev_marks, pvcol=pvcol,
                       prev=(spill_marks[-1] if spill_marks else None)):
                @C.scal
                def _(sc, oc=oc, bk=bk, mmk=mmk):
                    C.wait_at(sc, "mm", mmk)
                    if prev is not None:
                        C.wmark(sc, prev)
                    sc.activation(S["slotB"][:, oc * TT:oc * TT + TT], C.bh(bk)[:, 0:TT],
                                  AF.Identity, bias=pv2[:, pvcol + oc:pvcol + oc + 1])
                    C.brel(bk, "se")
                ev_marks.append(C.count("se"))

            fm_proj(C, S, lambda kc, oc: wsl(wready[2], kc, oc * 128, 128),
                    lambda kc, c0=c0: yb[:, kc * T + c0: kc * T + c0 + TT],
                    TT, sp_out, w_ready=wready[0:2],
                    src_ready=lambda oc, tt=tt: ("ve", p2_marks[tt]))

            @C.syn
            def _(sy, c0=c0, em=list(ev_marks)):
                grp = C.dgrp(sy)
                for oc in range(DC):
                    C.wait_at(sy, "se", em[oc])
                    r0, r1 = dram_rows_fn(oc)
                    C.inc(sy.dma_start(out=r0[r1 + oc * 128: r1 + (oc + 1) * 128, c0:c0 + TT],
                                       in_=S["slotB"][:, oc * TT:oc * TT + TT]), grp, 16)
                spill_marks.append((grp, C.count(grp)))

    proj_spill(wl2["q"] + (0,), PV2_BQ, lambda oc: (g["q_buf"], 0))
    q2_mm = C.count("mm")
    proj_spill(wl2["k"] + (1,), PV2_BK, lambda oc: (g["kv_buf"], 0))
    k2_mm = C.count("mm")

    @C.syn
    def _(sy):
        wl2["v"] = load_wcols(sy, g["wqkv_bf"], 2048, 0, gate_mm=q2_mm)
    proj_spill(wl2["v"] + (0,), PV2_BV, lambda oc: (g["kv_buf"], 1024))
    qkv_mm = C.count("mm")
    qkv_se = C.count("se")

    @C.gps
    def _(gp):
        for mk in spill_marks:
            C.wmark(gp, mk)
        C.inc(gp.collective_compute(
            "AllGather", ALU.bypass,
            replica_groups=[[0, 1], [2, 3], [4, 5], [6, 7]],
            ins=[g["kv_buf"][:, :].opt()], outs=[g["kv_gath"][:, :].opt()]), "cc")

    ones_ve = C.count("ve")

    head_mm_prev = {0: qkv_mm, 1: qkv_mm}
    head_mm_all = qkv_mm
    prev_head_ve = C.count("ve")
    prev_exp_se = C.count("se")
    for h in range(H):
        par = h % 2
        kbase = 0 if par == 0 else 8192
        vbase = 4096 if par == 0 else 12288
        pb = (h % 2) * 64
        ch = h // 2

        @C.syn
        def _(sy, h=h, par=par, kbase=kbase, vbase=vbase):
            grp = C.dgrp(sy)
            C.wait(sy, "cc")
            C.wait_at(sy, "ve", ones_ve)
            C.wait_at(sy, "mm", head_mm_prev[par])
            for r in range(2):
                C.inc(sy.dma_start(
                    out=vb[0:64, kbase + r * T: kbase + (r + 1) * T],
                    in_=g["kv_gath"][r * 2048 + h * 64: r * 2048 + (h + 1) * 64, :]),
                    grp, 16)
                C.inc(sy.dma_start(
                    out=vb[0:64, vbase + r * T: vbase + (r + 1) * T],
                    in_=g["kv_gath"][r * 2048 + 1024 + h * 64: r * 2048 + 1024 + (h + 1) * 64, :]),
                    grp, 16)
            C.khm = (grp, C.count(grp))
        kh_mark = C.khm

        @C.syn
        def _(sy, h=h, hma=head_mm_all):
            grp = C.dgrp(sy)
            for mk in spill_marks[0:4]:
                C.wmark(sy, mk)
            C.wait_at(sy, "ve", prev_head_ve)
            C.wait_at(sy, "mm", hma)
            C.inc(sy.dma_start(out=slotA[0:64, 2048:4096],
                               in_=g["q_buf"][h * 64:(h + 1) * 64, :]), grp, 16)
            C.qm = (grp, C.count(grp))
        q_mark = C.qm

        bm_marks = []
        for j in range(12):
            if j < 8:
                src_ap = vb[0:64, kbase + j * 512: kbase + (j + 1) * 512]
                rdy = kh_mark
            else:
                src_ap = slotA[0:64, 2048 + (j - 8) * 512: 2048 + (j - 7) * 512]
                rdy = q_mark

            @C.vec
            def _(sc, j=j, src_ap=src_ap, rdy=rdy, phv=prev_head_ve):
                C.wmark(sc, rdy)
                if j > 0:
                    C.wait_at(sc, "mm", bm_marks[j - 1])
                sc.tensor_tensor(sqf[0:64, 0:512], src_ap, src_ap, ALU.mult)
            sq_se = C.count("ve")

            @C.ten
            def _(t, j=j, sq_se=sq_se):
                C.wait_at(t, "ve", sq_se)
                p = C.bacq(t, 6)
                C.inc(t.matmul(p[0:1, 0:512], S["onec_f"][0:64, 0:1], sqf[0:64, 0:512],
                               start=True, stop=True), "mm")
            bm_marks.append(C.count("mm"))

            @C.vec
            def _(v, j=j, mk=bm_marks[-1]):
                C.wait_at(v, "mm", mk)
                v.reduce_max(rows[0:1, 34 + (0 if j < 8 else 1):35 + (0 if j < 8 else 1)],
                             C.bh(6)[0:1, 0:512], axis=AX.X)
                C.brel(6, "ve")
                dst = 32 if j < 8 else 33
                if j in (0, 8):
                    v.tensor_copy(rows[0:1, dst:dst + 1], rows[0:1, 34 + (0 if j < 8 else 1):35 + (0 if j < 8 else 1)])
                else:
                    v.tensor_tensor(rows[0:1, dst:dst + 1], rows[0:1, dst:dst + 1],
                                    rows[0:1, 34 + (0 if j < 8 else 1):35 + (0 if j < 8 else 1)],
                                    ALU.max)
        bnd_ve = C.count("ve")

        @C.scal
        def _(sc, bnd_ve=bnd_ve):
            C.wait_at(sc, "ve", bnd_ve)
            sc.activation(rows[0:1, 32:33], rows[0:1, 32:33], AF.Ln)
            sc.activation(rows[0:1, 33:34], rows[0:1, 33:34], AF.Ln)
        bnds_se = C.count("se")

        @C.vec
        def _(v, bnds_se=bnds_se):
            C.wait_at(v, "se", bnds_se)
            v.tensor_tensor(rows[0:1, 32:33], rows[0:1, 32:33], rows[0:1, 33:34], ALU.add)
        bnh_ve = C.count("ve")

        @C.scal
        def _(sc, bnh_ve=bnh_ve):
            C.wait_at(sc, "ve", bnh_ve)
            sc.activation(rows[0:1, 32:33], rows[0:1, 32:33], AF.Exp, scale=0.5)
        bex_se = C.count("se")

        @C.vec
        def _(v, bex_se=bex_se):
            C.wait_at(v, "se", bex_se)
            v.tensor_scalar(rows[0:1, 32:33], rows[0:1, 32:33],
                            cst[0:1, C_NEG1:C_NEG1 + 1], cst[0:1, C_I8:C_I8 + 1],
                            ALU.mult, ALU.mult)
            v.tensor_scalar(slotA[0:64, 2048:4096], slotA[0:64, 2048:4096],
                            cst[0:64, C_I8:C_I8 + 1], None, ALU.mult)
        nb8_ve = C.count("ve")

        @C.ten
        def _(t, nb8_ve=nb8_ve):
            C.wait_at(t, "ve", nb8_ve)
            p6 = C.bacq(t, 6)
            C.inc(t.matmul(p6[:, 0:1], S["oner_f"][0:1, 0:128],
                           rows[0:1, 32:33], start=True, stop=True), "mm")
        nbm = C.count("mm")

        @C.vec
        def _(v, nbm=nbm, pes=prev_exp_se):
            C.wait_at(v, "mm", nbm)
            C.wait_at(v, "se", pes)
            v.tensor_copy(smf[:, 2047:2048], C.bh(6)[:, 0:1])
            C.brel(6, "ve")
        qaug_ve = C.count("ve")

        vt_marks = []
        for kc in range(32):
            @C.ten
            def _(t, kc=kc, vbase=vbase, qaug_ve=qaug_ve):
                pT = C.bacq(t, "T")
                C.wmark(t, kh_mark)
                C.wait_at(t, "ve", qaug_ve)
                C.inc(t.transpose(pT[:, 0:64], vb[0:64, vbase + kc * 128: vbase + (kc + 1) * 128],
                                  S["identb"][0:64, 0:64]), "mm")
            trm = C.count("mm")

            @C.vec
            def _(v, kc=kc, trm=trm):
                C.wait_at(v, "mm", trm)
                v.tensor_copy(slotA[:, kc * 64:(kc + 1) * 64], C.bh("T")[:, 0:64])
                C.brel("T", "ve")
            vt_marks.append(C.count("ve"))
        vtm_ve = C.count("ve")

        for qt in range(4):
            nbk, dbk = qt % 2, 2 + qt % 2
            smm_of = {}
            pvm_of = {}

            def emit_S(kc, qt=qt, kbase=kbase, qaug_ve=qaug_ve,
                       smm_of=smm_of):
                sbk = 4 + (kc % 2)

                @C.ten
                def _(t, kc=kc, sbk=sbk, qt=qt, kbase=kbase, qaug_ve=qaug_ve):
                    p = C.bacq(t, sbk)
                    C.wait_at(t, "ve", qaug_ve)
                    C.inc(t.matmul(p[:, 0:TT],
                                   vb[0:64, kbase + kc * 128: kbase + (kc + 1) * 128],
                                   slotA[0:64, 2048 + qt * TT: 2048 + (qt + 1) * TT],
                                   start=True, stop=True), "mm")
                smm_of[kc] = C.count("mm")

            emit_S(0)
            for kc in range(32):
                sbk = 4 + (kc % 2)
                st_ = kc % 2

                @C.scal
                def _(sc, st_=st_, sbk=sbk, smm=smm_of[kc],
                      pvm=pvm_of.get(kc - 2)):
                    C.wait_at(sc, "mm", smm)
                    if pvm is not None:
                        C.wait_at(sc, "mm", pvm)
                    sc.activation(slotB[:, st_ * TT:(st_ + 1) * TT],
                                  C.bh(sbk)[:, 0:TT], AF.Exp,
                                  bias=smf[:, 2047:2048])
                    C.brel(sbk, "se")
                prev_exp_se = C.count("se")
                if kc + 1 < 32:
                    emit_S(kc + 1)

                @C.ten
                def _(t, kc=kc, st_=st_, pb=pb, em=prev_exp_se, vtm_ve=vtm_ve,
                      nbk=nbk, dbk=dbk):
                    C.wait_at(t, "se", em)
                    C.wait_at(t, "ve", vtm_ve)
                    pn = C.bacq(t, nbk) if kc == 0 else C.bh(nbk)
                    pd = C.bacq(t, dbk) if kc == 0 else C.bh(dbk)
                    t.matmul(pn[pb:pb + 64, 0:TT], slotA[:, kc * 64:(kc + 1) * 64],
                             slotB[:, st_ * TT:(st_ + 1) * TT],
                             start=(kc == 0), stop=(kc == 31),
                             skip_group_check=(pb == 64))
                    C.inc(t.matmul(pd[0:1, 0:TT], S["onec_b"][:, 0:1],
                                   slotB[:, st_ * TT:(st_ + 1) * TT],
                                   start=(kc == 0), stop=(kc == 31)), "mm")
                pvm_of[kc] = C.count("mm")
            pv_mm = C.count("mm")

            @C.vec
            def _(v, dbk=dbk, pv_mm=pv_mm):
                C.wait_at(v, "mm", pv_mm)
                v.reciprocal(rows[0:1, 0:TT], C.bh(dbk)[0:1, 0:TT])
                C.brel(dbk, "ve")
            rcv = C.count("ve")

            @C.ten
            def _(t, pb=pb, rcv=rcv):
                C.wait_at(t, "ve", rcv)
                p2 = C.bacq(t, 6)
                C.inc(t.matmul(p2[pb:pb + 64, 0:TT], S["oner_f"][0:1, 0:64],
                               rows[0:1, 0:TT], start=True, stop=True,
                               skip_group_check=(pb == 64)), "mm")
            rbm = C.count("mm")

            @C.vec
            def _(v, h=h, pb=pb, ch=ch, qt=qt, nbk=nbk, rbm=rbm):
                C.wait_at(v, "mm", rbm)
                v.tensor_copy(sqf[pb:pb + 64, 0:TT], C.bh(6)[pb:pb + 64, 0:TT])
                C.brel(6, "ve")
                v.tensor_tensor(kb[pb:pb + 64, ch * T + qt * TT: ch * T + (qt + 1) * TT],
                                C.bh(nbk)[pb:pb + 64, 0:TT], sqf[pb:pb + 64, 0:TT],
                                ALU.mult)
                C.brel(nbk, "ve")
        head_mm_prev[par] = C.count("mm")
        head_mm_all = C.count("mm")
        prev_head_ve = C.count("ve")
    heads_mm = C.count("mm")
    heads_ve = C.count("ve")
    maybe_dump(C, g, S, "p2_o", kb)
    if C.dumped:
        return

    @C.syn
    def _(sy):
        C.wl_out = load_wcols(sy, g["wout_bf"], 0, 0, gate_mm=heads_mm)

    TB = 256
    prev_ln_ve = C.count("ve")
    for tb in range(8):
        b0 = tb * TB
        a0_marks = []

        def o_out(oc, bk, mmk, b0=b0, a0_marks=a0_marks, plv=prev_ln_ve):
            @C.scal
            def _(sc, oc=oc, bk=bk, mmk=mmk):
                C.wait_at(sc, "mm", mmk)
                C.wait_at(sc, "ve", plv)
                sc.activation(smf[:, oc * TB:(oc + 1) * TB], C.bh(bk)[:, 0:TB],
                              AF.Identity, bias=pv2[:, PV2_BO + oc:PV2_BO + oc + 1])
                C.brel(bk, "se")
            a0se = C.count("se")

            @C.vec
            def _(v, oc=oc, b0=b0, a0se=a0se):
                C.wait_at(v, "se", a0se)
                v.tensor_tensor(smf[:, oc * TB:(oc + 1) * TB],
                                smf[:, oc * TB:(oc + 1) * TB],
                                yb[:, oc * T + b0: oc * T + b0 + TB], ALU.add)
            a0_marks.append(C.count("ve"))

        fm_proj(C, S, lambda kc, oc: wsl(0, kc, oc * 128, 128),
                lambda kc, b0=b0: kb[:, kc * T + b0: kc * T + b0 + TB],
                TB, o_out, w_ready=C.wl_out,
                src_ready=lambda oc: ("ve", heads_ve))
        fm_ln(C, S,
              src_chunk=lambda kc: smf[:, kc * TB:(kc + 1) * TB],
              ttw=TB,
              dst_chunk=lambda kc, b0=b0: yb[:, kc * T + b0: kc * T + b0 + TB],
              gamma_beta=(PV2_LN1G, PV2_LN1B),
              src_f32=True, pv_src="pv2",
              src_ready=lambda kc, am=list(a0_marks): ("ve", am[kc]))
        prev_ln_ve = C.count("ve")
    maybe_dump(C, g, S, "p2_y1", yb)


def emit_phase3(C, g, S):
    xs, yb, kb, vb = S["xs"], S["yb"], S["kb"], S["vb"]
    slotA, slotB, sqf, rows, smf = S["slotA"], S["slotB"], S["sqf"], S["rows"], S["smf"]
    cst, pv3 = S["consts"], S["pv3"]
    cbT8 = slotB[0:8, 2048:4096]
    b2sb = S["kvb"][0:8, 0:1024]
    wge = S["scoreT"][:, 0:96]
    TB = 256
    ph_mm0 = C.count("mm")
    ph_ve0 = C.count("ve")

    @C.gps
    def _(gp):
        C.wait_at(gp, "mm", ph_mm0)
        C.inc(gp.dma_start(out=b2sb, in_=g["l0f_b2"][:]), "gc", 16)
        C.b2_gc = C.count("gc")

    @C.syn
    def _(sy):
        grp = C.dgrp(sy)
        C.wait_at(sy, "pc", C.precast_pc)
        for kc in range(DC):
            C.inc(sy.dma_start(out=wge[:, kc * 12:(kc + 1) * 12],
                               in_=g["wge_bf"][kc * 128:(kc + 1) * 128, :]), grp, 16)
        C.wge_mk = (grp, C.count(grp))

    prev_rt_mm = ph_mm0
    for gr in range(16):
        t0 = gr * 128

        @C.ten
        def _(t, gr=gr, t0=t0):
            p = C.bacq(t, 4)
            C.wmark(t, C.wge_mk)
            C.wait_at(t, "ve", ph_ve0)
            ins = None
            for kc in range(DC):
                ins = t.matmul(p[:, 0:12], yb[:, kc * T + t0: kc * T + t0 + 128],
                               wge[:, kc * 12:(kc + 1) * 12],
                               start=(kc == 0), stop=False)
            C.inc(t.matmul(p[:, 0:12], S["oner_b"][0:1, 0:128], S["bge"][0:1, :],
                           start=False, stop=True), "mm")
        rt_mm = C.count("mm")

        @C.vec
        def _(v, gr=gr, rt_mm=rt_mm):
            C.wait_at(v, "mm", rt_mm)
            R = rows
            v.tensor_copy(R[:, 0:12], C.bh(4)[:, 0:12])
            C.brel(4, "ve")
            v.reduce_max(R[:, 12:13], R[:, 0:4], axis=AX.X)
            v.tensor_scalar(R[:, 13:14], R[:, 12:13], cst[:, C_NEG1:C_NEG1 + 1],
                            None, ALU.mult)
        gm_ve = C.count("ve")

        @C.scal
        def _(sc, gm_ve=gm_ve):
            C.wait_at(sc, "ve", gm_ve)
            sc.activation(rows[:, 0:4], rows[:, 0:4], AF.Exp, bias=rows[:, 13:14],
                          accum_out=rows[:, 14:15])
        gexp_se = C.count("se")

        @C.vec
        def _(v, gr=gr, gexp_se=gexp_se):
            R = rows
            C.wait_at(v, "se", gexp_se)
            v.reciprocal(R[:, 14:15], R[:, 14:15])
            v.tensor_scalar(R[:, 0:4], R[:, 0:4], R[:, 14:15], None, ALU.mult)
            ev = expand_view(R[:, 4:5], [[2, 4]])
            od = expand_view(R[:, 5:6], [[2, 4]])
            v.tensor_tensor(R[:, 16:20], ev, od, ALU.max)
            m2x = expand_view(R[:, 16:17], [[1, 4], [0, 2]])
            v.tensor_tensor(R[:, 20:28], R[:, 4:12], m2x, ALU.subtract)
        ep_ve = C.count("ve")

        @C.scal
        def _(sc, ep_ve=ep_ve):
            C.wait_at(sc, "ve", ep_ve)
            sc.activation(rows[:, 20:28], rows[:, 20:28], AF.Exp)
        ep_se = C.count("se")

        @C.vec
        def _(v, gr=gr, t0=t0, ep_se=ep_se):
            R = rows
            C.wait_at(v, "se", ep_se)
            ev = expand_view(R[:, 20:21], [[2, 4]])
            od = expand_view(R[:, 21:22], [[2, 4]])
            v.tensor_tensor(R[:, 16:20], ev, od, ALU.add)
            v.reciprocal(R[:, 16:20], R[:, 16:20])
            sx = expand_view(R[:, 16:17], [[1, 4], [0, 2]])
            v.tensor_tensor(R[:, 20:28], R[:, 20:28], sx, ALU.mult)
            v.reduce_max(R[:, 12:13], R[:, 0:4], axis=AX.X)
            for gi in range(4):
                v.tensor_scalar(R[:, 28 + gi:29 + gi], R[:, gi:gi + 1],
                                R[:, 12:13], None, ALU.is_ge)
            v.tensor_copy(R[:, 32:33], R[:, 28:29])
            v.tensor_scalar(R[:, 36:37], R[:, 32:33], cst[:, C_NEG1:C_NEG1 + 1],
                            cst[:, C_ONE:C_ONE + 1], ALU.mult, ALU.add)
            for gi in range(1, 4):
                v.tensor_tensor(R[:, 32 + gi:33 + gi], R[:, 28 + gi:29 + gi],
                                R[:, 36:37], ALU.mult)
                if gi < 3:
                    v.tensor_tensor(R[:, 36:37], R[:, 36:37],
                                    R[:, 32 + gi:33 + gi], ALU.subtract)
            ev2 = expand_view(R[:, 20:21], [[2, 4]])
            od2 = expand_view(R[:, 21:22], [[2, 4]])
            v.tensor_tensor(R[:, 40:44], ev2, od2, ALU.is_ge)
            v.tensor_scalar(R[:, 44:48], R[:, 40:44], cst[:, C_NEG1:C_NEG1 + 1],
                            cst[:, C_ONE:C_ONE + 1], ALU.mult, ALU.add)
            v.tensor_tensor(R[:, 48:52], ev2, od2, ALU.max)
            v.tensor_tensor(R[:, 52:56], R[:, 32:36], R[:, 48:52], ALU.mult)
            v.tensor_scalar(R[:, 52:56], R[:, 52:56], R[:, 12:13], None, ALU.mult)
            cbe = expand_view(S["slotB"][:, 0:1], [[2, 4]])
            cbo = expand_view(S["slotB"][:, 1:2], [[2, 4]])
            v.tensor_tensor(cbe, R[:, 52:56], R[:, 40:44], ALU.mult)
            v.tensor_tensor(cbo, R[:, 52:56], R[:, 44:48], ALU.mult)
        cb_ve = C.count("ve")

        @C.ten
        def _(t, gr=gr, cb_ve=cb_ve):
            pT = C.bacq(t, "T")
            C.wait_at(t, "ve", cb_ve)
            C.inc(t.transpose(pT[0:8, 0:128], S["slotB"][:, 0:8], S["identb"][:, :]), "mm")
        tr_mm = C.count("mm")

        @C.vec
        def _(v, gr=gr, tr_mm=tr_mm):
            C.wait_at(v, "mm", tr_mm)
            v.tensor_copy(cbT8[0:8, gr * 128:(gr + 1) * 128], C.bh("T")[0:8, 0:128])
            C.brel("T", "ve")
    cb_done = C.count("ve")

    WS0, WS1 = 0, 8192
    w_marks = [None, None]
    g_use = [C.count("mm"), C.count("mm")]
    prev_ln_ve = C.count("ve")
    for tb in range(8):
        b0 = tb * TB
        for e in range(8):
            @C.ten
            def _(t, e=e, tb=tb, cb_done=cb_done):
                p = C.bacq(t, 4)
                C.wait_at(t, "ve", cb_done)
                basis = expand_view(S["identb"][0:8, e:e + 1], [[0, 128]])
                C.inc(t.matmul(p[:, 0:TB], basis,
                               cbT8[0:8, tb * TB:(tb + 1) * TB],
                               start=True, stop=True), "mm")
            cbb_mm = C.count("mm")

            @C.vec
            def _(v, cbb_mm=cbb_mm):
                C.wait_at(v, "mm", cbb_mm)
                v.tensor_copy(sqf[:, 0:TB], C.bh(4)[:, 0:TB])
                C.brel(4, "ve")
            cbb_ve = C.count("ve")

            for gq in range(4):
                par = (e * 4 + gq) % 2
                wc0 = WS0 if par == 0 else WS1

                @C.syn
                def _(sy, e=e, gq=gq, par=par, wc0=wc0):
                    grp = C.dgrp(sy)
                    C.wait_at(sy, "mm", g_use[par])
                    C.wait_at(sy, "pc2", C.precast_pc2)
                    C.inc(sy.dma_start(out=vb[:, wc0:wc0 + 4096],
                                       in_=g["moew1_bf"][e, gq]), grp, 16)
                    C.inc(sy.dma_start(out=vb[:, wc0 + 4096:wc0 + 8192],
                                       in_=g["moew2_bf"][e, gq]), grp, 16)
                    w_marks[par] = (grp, C.count(grp))

                for j in range(4):
                    hc = gq * 4 + j
                    hbk = 5 if j % 2 == 0 else 6

                    @C.ten
                    def _(t, e=e, j=j, par=par, wc0=wc0, b0=b0, hbk=hbk):
                        p = C.bacq(t, hbk)
                        C.wmark(t, w_marks[par])
                        C.wait_at(t, "ve", cbb_ve)
                        for kc in range(DC):
                            ins = t.matmul(
                                p[:, 0:TB],
                                vb[:, wc0 + (j * 8 + kc) * 128: wc0 + (j * 8 + kc) * 128 + 128],
                                yb[:, kc * T + b0: kc * T + b0 + TB],
                                start=(kc == 0), stop=(kc == DC - 1))
                        C.inc(ins, "mm")
                    h_mm = C.count("mm")

                    @C.scal
                    def _(sc, e=e, hc=hc, h_mm=h_mm, hbk=hbk):
                        C.wait_at(sc, "mm", h_mm)
                        sc.activation(S["slotA"][:, (hc % 4) * TB:(hc % 4 + 1) * TB],
                                      C.bh(hbk)[:, 0:TB], AF.Relu,
                                      bias=S["b1fm3"][:, e * 16 + hc:e * 16 + hc + 1])
                        C.brel(hbk, "se")
                    hr_se = C.count("se")

                    @C.vec
                    def _(v, hc=hc, hr_se=hr_se):
                        C.wait_at(v, "se", hr_se)
                        v.tensor_tensor(S["slotA"][:, (hc % 4) * TB:(hc % 4 + 1) * TB],
                                        S["slotA"][:, (hc % 4) * TB:(hc % 4 + 1) * TB],
                                        sqf[:, 0:TB], ALU.mult)
                    hb_ve = C.count("ve")

                    @C.ten
                    def _(t, e=e, j=j, gq=gq, hc=hc, wc0=wc0, hb_ve=hb_ve):
                        C.wait_at(t, "ve", hb_ve)
                        ins = None
                        for oc in range(DC):
                            obk = oc // 2
                            chh = (oc % 2) * TB
                            p = C.bacq(t, obk) if (e == 0 and hc == 0 and oc % 2 == 0) \
                                else C.bh(obk)
                            ins = t.matmul(p[:, chh:chh + TB],
                                           vb[:, wc0 + 4096 + j * 1024 + oc * 128:
                                              wc0 + 4096 + j * 1024 + oc * 128 + 128],
                                           S["slotA"][:, (hc % 4) * TB:(hc % 4 + 1) * TB],
                                           start=(e == 0 and hc == 0 and oc % 2 == 0),
                                           stop=False)
                        C.inc(ins, "mm")
                    g_use[par] = C.count("mm")
        @C.ten
        def _(t, tb=tb, cb_done=cb_done):
            C.wait_at(t, "ve", cb_done)
            C.wait_at(t, "gc", C.b2_gc)
            ins = None
            for oc in range(DC):
                obk = oc // 2
                chh = (oc % 2) * TB
                ins = t.matmul(C.bh(obk)[:, chh:chh + TB],
                               b2sb[0:8, oc * 128:(oc + 1) * 128],
                               cbT8[0:8, tb * TB:(tb + 1) * TB],
                               start=False, stop=(oc % 2 == 1))
            C.inc(ins, "mm")
        moe_mm = C.count("mm")

        a0_marks = []
        for oc in range(DC):
            obk = oc // 2
            chh = (oc % 2) * TB

            @C.scal
            def _(sc, oc=oc, obk=obk, chh=chh, moe_mm=moe_mm, plv=prev_ln_ve):
                C.wait_at(sc, "mm", moe_mm)
                C.wait_at(sc, "ve", plv)
                sc.activation(smf[:, oc * TB:(oc + 1) * TB], C.bh(obk)[:, chh:chh + TB],
                              AF.Copy)
                if oc % 2 == 1:
                    C.brel(obk, "se")
            a0se = C.count("se")

            @C.vec
            def _(v, oc=oc, b0=b0, a0se=a0se):
                C.wait_at(v, "se", a0se)
                v.tensor_tensor(smf[:, oc * TB:(oc + 1) * TB],
                                smf[:, oc * TB:(oc + 1) * TB],
                                yb[:, oc * T + b0: oc * T + b0 + TB], ALU.add)
            a0_marks.append(C.count("ve"))

        fm_ln(C, S,
              src_chunk=lambda kc: smf[:, kc * TB:(kc + 1) * TB],
              ttw=TB,
              dst_chunk=lambda kc: S["slotA"][:, 1024 + kc * TB: 1024 + (kc + 1) * TB],
              gamma_beta=(PV3_LN2G, PV3_LN2B),
              src_f32=True, pv_src="pv3",
              src_ready=lambda kc, am=list(a0_marks): ("ve", am[kc]))
        y2_ve = C.count("ve")
        fm_ln(C, S,
              src_chunk=lambda kc: S["slotA"][:, 1024 + kc * TB: 1024 + (kc + 1) * TB],
              ttw=TB,
              residual_into=lambda kc, b0=b0: xs[:, kc * T + b0: kc * T + b0 + TB],
              src_ready=lambda kc, ym=y2_ve: ("ve", ym))
        prev_ln_ve = C.count("ve")
    maybe_dump(C, g, S, "p3_x2", xs)



def emit_phase4(C, g, S):
    xs, yb, kb, vb = S["xs"], S["yb"], S["kb"], S["vb"]
    slotA, slotB, sqf, rows, smf = S["slotA"], S["slotB"], S["sqf"], S["rows"], S["smf"]
    cst = S["consts"]
    WS0, WS1 = 0, 8192

    def wsl(slot, kc, c0, cn):
        base = WS0 if slot == 0 else WS1
        return vb[:, base + kc * 1024 + c0: base + kc * 1024 + c0 + cn]

    def load_wcols(sy, dram_bf, col0, slot, gate_mm=0):
        grp = C.dgrp(sy)
        if gate_mm:
            C.wait_at(sy, "mm", gate_mm)
        C.wait_at(sy, "pc2", C.precast_pc2)
        for kc in range(DC):
            C.inc(sy.dma_start(out=wsl(slot, kc, 0, 1024),
                               in_=dram_bf[kc * 128:(kc + 1) * 128, col0:col0 + 1024]),
                  grp, 16)
        return (grp, C.count(grp))

    ph_mm0 = C.count("mm")
    ph_ve0 = C.count("ve")
    tril = S["scoreT"][:, 96:224]
    wl4 = {}

    @C.syn
    def _(sy):
        grp = C.dgrp(sy)
        C.wait_at(sy, "ve", ph_ve0)
        C.inc(sy.dma_start(out=tril, in_=g["tril"][:]), grp, 16)
        C.tril_mk = (grp, C.count(grp))
        wl4["q"] = load_wcols(sy, g["l1wqkv_bf"], 0, 0, gate_mm=ph_mm0)
        wl4["k"] = load_wcols(sy, g["l1wqkv_bf"], 1024, 1, gate_mm=ph_mm0)

    p4_marks = []
    for tt in range(NTT):
        c0 = tt * TT
        m = fm_ln(C, S,
                  src_chunk=lambda kc, c0=c0: xs[:, kc * T + c0: kc * T + c0 + TT],
                  ttw=TT,
                  dst_chunk=lambda kc, c0=c0: yb[:, kc * T + c0: kc * T + c0 + TT],
                  src_f32=True)
        p4_marks.append(m)

    def elu_evac(bk, mmk, dst_ap, pre=None):
        @C.vec
        def _(v, bk=bk, mmk=mmk):
            C.wait_at(v, "mm", mmk)
            C.wait_at(v, "se", C.count("se"))
            C.wmark(v, pre)
            v.tensor_scalar(sqf[:, 0:TT], C.bh(bk)[:, 0:TT], cst[:, C_ZERO:C_ZERO + 1],
                            None, ALU.min)
        mn_ve = C.count("ve")

        @C.scal
        def _(sc, mn_ve=mn_ve):
            C.wait_at(sc, "ve", mn_ve)
            sc.activation(sqf[:, 512:512 + TT], sqf[:, 0:TT], AF.Exp)
        ex_se = C.count("se")

        @C.vec
        def _(v, bk=bk, ex_se=ex_se, dst_ap=dst_ap):
            C.wait_at(v, "se", ex_se)
            v.tensor_scalar(sqf[:, 0:TT], C.bh(bk)[:, 0:TT], cst[:, C_ZERO:C_ZERO + 1],
                            None, ALU.max)
            C.brel(bk, "ve")
            v.tensor_tensor(dst_ap, sqf[:, 0:TT], sqf[:, 512:512 + TT], ALU.add)
        return C.count("ve")

    for tt in range(NTT):
        c0 = tt * TT

        def pq_out(oc, bk, mmk, c0=c0):
            elu_evac(bk, mmk, kb[:, oc * T + c0: oc * T + c0 + TT])

        fm_proj(C, S, lambda kc, oc: wsl(0, kc, oc * 128, 128),
                lambda kc, c0=c0: yb[:, kc * T + c0: kc * T + c0 + TT],
                TT, pq_out, w_ready=wl4["q"],
                src_ready=lambda oc, tt=tt: ("ve", p4_marks[tt]))
    pq4_mm = C.count("mm")

    sp_marks = []
    for tt in range(NTT):
        c0 = tt * TT
        ev_marks = []

        def pkf_out(oc, bk, mmk, ev_marks=ev_marks,
                    prev=(sp_marks[-1] if sp_marks else None)):
            m = elu_evac(bk, mmk, S["slotB"][:, oc * TT:oc * TT + TT], pre=prev)
            ev_marks.append(m)

        fm_proj(C, S, lambda kc, oc: wsl(1, kc, oc * 128, 128),
                lambda kc, c0=c0: yb[:, kc * T + c0: kc * T + c0 + TT],
                TT, pkf_out, w_ready=wl4["k"],
                src_ready=lambda oc, tt=tt: ("ve", p4_marks[tt]))

        @C.syn
        def _(sy, c0=c0, em=list(ev_marks)):
            grp = C.dgrp(sy)
            for oc in range(DC):
                C.wait_at(sy, "ve", em[oc])
                C.inc(sy.dma_start(out=g["pkfm_buf"][oc * 128:(oc + 1) * 128, c0:c0 + TT],
                                   in_=S["slotB"][:, oc * TT:oc * TT + TT]), grp, 16)
            sp_marks.append((grp, C.count(grp)))

    def tm_proj(dram_w, col0, wslot, dst_buf, do_elu, gate_mm):
        @C.syn
        def _(sy):
            wl4["tm"] = load_wcols(sy, dram_w, col0, wslot, gate_mm=gate_mm)
        for kt in range(16):
            ktt = kt * 128
            ev2 = []
            for ocp in range(2):
                @C.ten
                def _(t, kt=kt, ocp=ocp, wslot=wslot):
                    p = C.bacq(t, ocp)
                    C.wmark(t, wl4["tm"])
                    C.wait_at(t, "ve", p4_marks[-1])
                    ins = None
                    for kc in range(DC):
                        ins = t.matmul(p[:, 0:TT], yb[:, kc * T + ktt: kc * T + ktt + 128],
                                       wsl(wslot, kc, ocp * 512, 512),
                                       start=(kc == 0), stop=(kc == DC - 1))
                    C.inc(ins, "mm")
                mmk = C.count("mm")
                prevs = sp_marks[-1] if sp_marks else None
                if do_elu:
                    m = elu_evac(ocp, mmk, S["slotB"][:, ocp * 512:(ocp + 1) * 512],
                                 pre=prevs)
                    ev2.append(m)
                else:
                    @C.vec
                    def _(v, ocp=ocp, mmk=mmk, prevs=prevs):
                        C.wait_at(v, "mm", mmk)
                        C.wmark(v, prevs)
                        v.tensor_copy(S["slotB"][:, ocp * 512:(ocp + 1) * 512],
                                      C.bh(ocp)[:, 0:TT])
                        C.brel(ocp, "ve")
                    ev2.append(C.count("ve"))

            @C.syn
            def _(sy, kt=kt, ev2=list(ev2), dst_buf=dst_buf):
                grp = C.dgrp(sy)
                C.wait_at(sy, "ve", ev2[-1])
                C.inc(sy.dma_start(out=dst_buf[kt * 128:(kt + 1) * 128, :],
                                   in_=S["slotB"][:, 0:1024]), grp, 16)
                sp_marks.append((grp, C.count(grp)))

    tm_proj(g["l1wqkv_bf"], 1024, 0, g["pktm_buf"], True, pq4_mm)
    pktm_mm = C.count("mm")
    tm_proj(g["l1wqkv_bf"], 2048, 1, g["v_buf"], False, C.count("mm"))
    proj4_mm = C.count("mm")
    maybe_dump(C, g, S, "p4_pq", kb)
    if C.dumped:
        return

    tileA = [0, 0]

    def load_tile(sy, buf, kt, dstc):
        grp = C.dgrp(sy)
        C.inc(sy.dma_start(out=slotA[:, dstc:dstc + 1024],
                           in_=buf[kt * 128:(kt + 1) * 128, :]), grp, 16)
        return (grp, C.count(grp))

    @C.vec
    def _(v):
        v.memset(S["kvu"][:, :], 0.0)
    stA_mm = [C.count("mm")]
    for kt in range(16):
        par = kt % 2

        @C.syn
        def _(sy, kt=kt, par=par):
            if kt < 2:
                for mk in sp_marks:
                    C.wmark(sy, mk)
                C.wait_at(sy, "ve", C.count("ve"))
            C.wait_at(sy, "mm", stA_mm[0] if kt < 2 else tileA[par])
            C.tp1 = load_tile(sy, g["pktm_buf"], kt, par * 2048)
            C.tp2 = load_tile(sy, g["v_buf"], kt, par * 2048 + 1024)
        t1, t2 = C.tp1, C.tp2

        @C.ten
        def _(t, kt=kt, par=par, t1=t1, t2=t2):
            C.wmark(t, t1)
            C.wmark(t, t2)
            C.bacq(t, 4)
            C.bacq(t, 5)
            C.bacq(t, 6)
            ins = None
            for h in range(H):
                p = C.bh(5 if h < 8 else 6)
                cc = (h % 8) * 64
                t.matmul(p[0:64, cc:cc + 64],
                         slotA[:, par * 2048 + h * 64: par * 2048 + (h + 1) * 64],
                         slotA[:, par * 2048 + 1024 + h * 64: par * 2048 + 1024 + (h + 1) * 64],
                         start=(h in (0, 8)), stop=(h in (7, 15)))
                ins = t.matmul(C.bh(4)[0:64, h:h + 1],
                               slotA[:, par * 2048 + h * 64: par * 2048 + (h + 1) * 64],
                               S["onec_b"][:, 0:1],
                               start=(h == 0), stop=(h == 15))
            C.inc(ins, "mm")
        tileA[par] = C.count("mm")

        @C.vec
        def _(v, kt=kt, par=par):
            C.wait_at(v, "mm", tileA[par])
            for bk, hb in [(5, 0), (6, 8)]:
                dst = expand_view(sqf[0:64, hb * 65: hb * 65 + 1], [[65, 8], [1, 64]])
                srcb = expand_view(C.bh(bk)[0:64, 0:1], [[64, 8], [1, 64]])
                v.tensor_copy(dst, srcb)
                C.brel(bk, "ve")
            dstk = expand_view(sqf[0:64, 64:65], [[65, 16]])
            v.tensor_copy(dstk, C.bh(4)[0:64, 0:16])
            C.brel(4, "ve")
            v.tensor_tensor(S["kvu"][:, :], S["kvu"][:, :], sqf[0:64, 0:1040], ALU.add)
    stA_ve = C.count("ve")

    @C.gps
    def _(gp):
        C.wait_at(gp, "ve", stA_ve)
        C.inc(gp.dma_start(out=g["carry_buf"][:, :], in_=S["kvu"][:, :]), "gc", 16)
        C.wait(gp, "gc")
        C.inc(gp.collective_compute(
            "AllGather", ALU.bypass,
            replica_groups=[[0, 1], [2, 3], [4, 5], [6, 7]],
            ins=[g["carry_buf"][:, :].opt()], outs=[g["carry_gath"][:, :].opt()]), "cc")

    @C.syn
    def _(sy):
        grp = C.dgrp(sy)
        C.wait(sy, "cc")
        C.inc(sy.dma_start(out=S["kvu"][:, :], in_=g["carry_gath"][0:64, :]), grp, 16)
        C.carry_al = (grp, C.count(grp))

    @C.vec
    def _(v):
        C.wmark(v, C.carry_al)
        v.tensor_scalar(S["kvu"][:, :], S["kvu"][:, :], cst[0:64, C_CARRY:C_CARRY + 1],
                        None, ALU.mult)
    carry_ve = C.count("ve")

    init_mm = C.count("mm")
    tileB = [init_mm, init_mm]
    for kt in range(16):
        par = kt % 2
        ktt = kt * 128

        @C.syn
        def _(sy, kt=kt, par=par):
            C.wait_at(sy, "mm", tileB[par])
            grp = C.dgrp(sy)
            C.inc(sy.dma_start(out=slotA[:, par * 2048:par * 2048 + 1024],
                               in_=g["pktm_buf"][kt * 128:(kt + 1) * 128, :]), grp, 16)
            C.inc(sy.dma_start(out=slotA[:, par * 2048 + 1024:par * 2048 + 2048],
                               in_=g["v_buf"][kt * 128:(kt + 1) * 128, :]), grp, 16)
            C.tpB = (grp, C.count(grp))
        tB = C.tpB

        @C.syn
        def _(sy, kt=kt):
            grp = C.dgrp(sy)
            C.wait_at(sy, "mm", tileB[0])
            C.wait_at(sy, "mm", tileB[1])
            for oc in range(DC):
                C.inc(sy.dma_start(out=S["slotB"][:, oc * 128:(oc + 1) * 128],
                                   in_=g["pkfm_buf"][oc * 128:(oc + 1) * 128,
                                                     kt * 128:(kt + 1) * 128]), grp, 16)
            C.tpF = (grp, C.count(grp))
        tF = C.tpF

        @C.vec
        def _(v, kt=kt):
            dstkv = expand_view(S["kvb"][0:64, 0:1], [[64, 16], [1, 64]])
            srckv = expand_view(S["kvu"][0:64, 0:1], [[65, 16], [1, 64]])
            v.tensor_copy(dstkv, srckv)
            dstks = expand_view(S["kvb"][0:64, 1024:1025], [[1, 16]])
            srcks = expand_view(S["kvu"][0:64, 64:65], [[65, 16]])
            v.tensor_copy(dstks, srcks)
        snap_ve = C.count("ve")

        @C.syn
        def _(sy, snap_ve=snap_ve):
            grp = C.dgrp(sy)
            C.wait_at(sy, "ve", snap_ve)
            C.inc(sy.dma_start(out=S["kvb"][64:128, :], in_=S["kvb"][0:64, :]), grp, 16)
            C.dupm = (grp, C.count(grp))
        dupm = C.dupm

        @C.ten
        def _(t, kt=kt, par=par, tB=tB, snap_ve=snap_ve):
            C.wmark(t, tB)
            C.wait_at(t, "ve", snap_ve)
            C.bacq(t, 4)
            C.bacq(t, 5)
            C.bacq(t, 6)
            ins = None
            for h in range(H):
                p = C.bh(5 if h < 8 else 6)
                cc = (h % 8) * 64
                t.matmul(p[0:64, cc:cc + 64],
                         slotA[:, par * 2048 + h * 64: par * 2048 + (h + 1) * 64],
                         slotA[:, par * 2048 + 1024 + h * 64: par * 2048 + 1024 + (h + 1) * 64],
                         start=(h in (0, 8)), stop=(h in (7, 15)))
                ins = t.matmul(C.bh(4)[0:64, h:h + 1],
                               slotA[:, par * 2048 + h * 64: par * 2048 + (h + 1) * 64],
                               S["onec_b"][:, 0:1],
                               start=(h == 0), stop=(h == 15))
            C.inc(ins, "mm")
        st_mm = C.count("mm")

        @C.vec
        def _(v, st_mm=st_mm):
            C.wait_at(v, "mm", st_mm)
            for bk, hb in [(5, 0), (6, 8)]:
                dst = expand_view(sqf[0:64, hb * 65: hb * 65 + 1], [[65, 8], [1, 64]])
                srcb = expand_view(C.bh(bk)[0:64, 0:1], [[64, 8], [1, 64]])
                v.tensor_copy(dst, srcb)
                C.brel(bk, "ve")
            dstk = expand_view(sqf[0:64, 64:65], [[65, 16]])
            v.tensor_copy(dstk, C.bh(4)[0:64, 0:16])
            C.brel(4, "ve")
            v.tensor_tensor(S["kvu"][:, :], S["kvu"][:, :], sqf[0:64, 0:1040], ALU.add)

        for h in range(H):
            pb = (h % 2) * 64
            ch = h // 2
            nbk = 2 if h % 2 == 0 else 3

            @C.ten
            def _(t, h=h, pb=pb, ch=ch, kt=kt, ktt=ktt, par=par, tF=tF, dupm=dupm, nbk=nbk):
                p0 = C.bacq(t, 0)
                C.wmark(t, tF)
                C.inc(t.matmul(p0[:, 0:128],
                               S["slotB"][pb:pb + 64, ch * 128:(ch + 1) * 128],
                               kb[pb:pb + 64, ch * T + ktt: ch * T + ktt + 128],
                               start=True, stop=True), "mm")
            pt_mm = C.count("mm")

            @C.vec
            def _(v, pt_mm=pt_mm):
                C.wait_at(v, "mm", pt_mm)
                C.wmark(v, C.tril_mk)
                v.tensor_tensor(S["slotB"][0:128, 1024:1152], C.bh(0)[:, 0:128],
                                tril, ALU.mult)
                C.brel(0, "ve")
            pm_ve = C.count("ve")

            @C.ten
            def _(t, h=h, pb=pb, ch=ch, kt=kt, ktt=ktt, par=par, dupm=dupm,
                  nbk=nbk, pm_ve=pm_ve):
                C.wmark(t, dupm)
                C.wait_at(t, "ve", pm_ve)
                pn = C.bacq(t, nbk)
                pd = C.bacq(t, 1)
                t.matmul(pn[pb:pb + 64, 0:128],
                         S["kvb"][pb:pb + 64, h * 64:(h + 1) * 64],
                         kb[pb:pb + 64, ch * T + ktt: ch * T + ktt + 128],
                         start=True, stop=False, skip_group_check=(pb == 64))
                t.matmul(pn[pb:pb + 64, 0:128],
                         slotA[:, par * 2048 + 1024 + h * 64: par * 2048 + 1024 + (h + 1) * 64],
                         S["slotB"][0:128, 1024:1152],
                         start=False, stop=True, skip_group_check=(pb == 64))
                t.matmul(pd[0:1, 0:128], S["kvb"][pb:pb + 64, 1024 + h:1025 + h],
                         kb[pb:pb + 64, ch * T + ktt: ch * T + ktt + 128],
                         start=True, stop=False)
                C.inc(t.matmul(pd[0:1, 0:128], S["onec_b"][:, 0:1],
                               S["slotB"][0:128, 1024:1152],
                               start=False, stop=True), "mm")
            nd_mm = C.count("mm")

            @C.vec
            def _(v, nd_mm=nd_mm):
                C.wait_at(v, "mm", nd_mm)
                v.tensor_scalar(rows[0:1, 0:128], C.bh(1)[0:1, 0:128],
                                cst[0:1, C_EPS6:C_EPS6 + 1], None, ALU.add)
                C.brel(1, "ve")
                v.reciprocal(rows[0:1, 0:128], rows[0:1, 0:128])
            rc_ve = C.count("ve")

            @C.ten
            def _(t, pb=pb, rc_ve=rc_ve):
                C.wait_at(t, "ve", rc_ve)
                p0 = C.bacq(t, 0)
                C.inc(t.matmul(p0[pb:pb + 64, 0:128], S["oner_f"][0:1, 0:64],
                               rows[0:1, 0:128], start=True, stop=True,
                               skip_group_check=(pb == 64)), "mm")
            rb_mm = C.count("mm")

            @C.vec
            def _(v, h=h, pb=pb, ch=ch, ktt=ktt, rb_mm=rb_mm, nbk=nbk):
                C.wait_at(v, "mm", rb_mm)
                v.tensor_copy(sqf[pb:pb + 64, 0:128], C.bh(0)[pb:pb + 64, 0:128])
                C.brel(0, "ve")
                v.tensor_tensor(vb[pb:pb + 64, ch * T + ktt: ch * T + ktt + 128],
                                C.bh(nbk)[pb:pb + 64, 0:128], sqf[pb:pb + 64, 0:128],
                                ALU.mult)
                C.brel(nbk, "ve")
        tileB[par] = C.count("mm")
    of_ve = C.count("ve")
    maybe_dump(C, g, S, "p4_of", vb)
    if C.dumped:
        return

    @C.syn
    def _(sy):
        grp = C.dgrp(sy)
        C.wait_at(sy, "pc2", C.precast_pc2)
        C.wait_at(sy, "mm", C.count("mm"))
        for kc in range(DC):
            C.inc(sy.dma_start(out=kb[:, kc * 1024:kc * 1024 + 1024],
                               in_=g["l1wout_bf"][kc * 128:(kc + 1) * 128, :]), grp, 16)
        C.wo_mk = (grp, C.count(grp))

    TB = 256
    prev_ln_ve = C.count("ve")
    for tb in range(8):
        b0 = tb * TB
        a0_marks = []

        def o_out(oc, bk, mmk, b0=b0, a0_marks=a0_marks, plv=prev_ln_ve):
            @C.vec
            def _(v, oc=oc, bk=bk, mmk=mmk):
                C.wait_at(v, "mm", mmk)
                C.wait_at(v, "ve", plv)
                v.tensor_copy(smf[:, oc * TB:(oc + 1) * TB], C.bh(bk)[:, 0:TB])
                C.brel(bk, "ve")
            a0_marks.append(C.count("ve"))

        fm_proj(C, S, lambda kc, oc: kb[:, kc * 1024 + oc * 128: kc * 1024 + oc * 128 + 128],
                lambda kc, b0=b0: vb[:, kc * T + b0: kc * T + b0 + TB],
                TB, o_out, w_ready=C.wo_mk,
                src_ready=lambda oc: ("ve", of_ve))
        fm_ln(C, S,
              src_chunk=lambda kc: smf[:, kc * TB:(kc + 1) * TB],
              ttw=TB,
              residual_into=lambda kc, b0=b0: xs[:, kc * T + b0: kc * T + b0 + TB],
              src_f32=True,
              src_ready=lambda kc, am=list(a0_marks): ("ve", am[kc]))
        prev_ln_ve = C.count("ve")
    maybe_dump(C, g, S, "p4_x3", xs)



def emit_phase5(C, g, S):
    xs, yb, kb, vb = S["xs"], S["yb"], S["kb"], S["vb"]
    slotA, slotB, sqf, rows, smf = S["slotA"], S["slotB"], S["sqf"], S["rows"], S["smf"]
    cst, pv5 = S["consts"], S["pv5"]
    PV5_UB, PV5_BCON, PV5_BDN = 0, 32, 40
    xbuf = S["scoreT"][:, 0:128]

    ph_mm0 = C.count("mm")

    p5_marks = []
    for tt in range(NTT):
        c0 = tt * TT
        m = fm_ln(C, S,
                  src_chunk=lambda kc, c0=c0: xs[:, kc * T + c0: kc * T + c0 + TT],
                  ttw=TT,
                  dst_chunk=lambda kc, c0=c0: yb[:, kc * T + c0: kc * T + c0 + TT],
                  src_f32=True)
        p5_marks.append(m)

    for mi, (m, nst, ab0) in enumerate([(0, 7, 0), (1, 5, 7), (2, 7, 12), (3, 5, 19)]):
        @C.vec
        def _(v, m=m, mi=mi):
            C.wmark(v, C.pre_al)
            if mi == 0:
                C.wait_at(v, "mm", ph_mm0)
            for o in range(32):
                v.tensor_copy(kb[:, m * 4096 + o * 128: m * 4096 + (o + 1) * 128],
                              S["identb"][:, :])
        cur_kb = True
        for sj in range(nst):
            inst = ab0 + sj

            @C.syn
            def _(sy, inst=inst, sj=sj):
                grp = C.dgrp(sy)
                C.wait_at(sy, "ve", C.count("ve"))
                C.wait_at(sy, "mm", C.count("mm"))
                C.inc(sy.dma_start(out=rows[:, 64:128],
                                   in_=g["l1f_ab"][:, inst * 64:(inst + 1) * 64]), grp, 16)
                C.inc(sy.dma_start(out=xbuf, in_=g["l1f_xorm"][sj]), grp, 16)
                C.abm = (grp, C.count(grp))
            abm = C.abm

            @C.vec
            def _(v, abm=abm):
                C.wmark(v, abm)
                for o in range(32):
                    v.tensor_scalar(slotA[:, o * 128:(o + 1) * 128], S["identb"][:, :],
                                    rows[:, 64 + o:65 + o], None, ALU.mult)
                    v.tensor_scalar(vb[:, 4096 + o * 128:4096 + (o + 1) * 128],
                                    xbuf[:, :], rows[:, 96 + o:97 + o], None, ALU.mult)
                v.tensor_tensor(slotA[:, 0:4096], slotA[:, 0:4096],
                                vb[:, 4096:8192], ALU.add)
            s_ve = C.count("ve")

            for o in range(32):
                bk = o % 2

                @C.ten
                def _(t, m=m, o=o, bk=bk, s_ve=s_ve, cur_kb=cur_kb):
                    p = C.bacq(t, bk)
                    C.wait_at(t, "ve", s_ve)
                    src_ap = (kb[:, m * 4096 + o * 128: m * 4096 + (o + 1) * 128]
                              if cur_kb else vb[:, o * 128:(o + 1) * 128])
                    C.inc(t.matmul(p[:, 0:128], slotA[:, o * 128:(o + 1) * 128],
                                   src_ap, start=True, stop=True), "mm")
                mmk = C.count("mm")

                @C.vec
                def _(v, m=m, o=o, bk=bk, mmk=mmk, cur_kb=cur_kb):
                    C.wait_at(v, "mm", mmk)
                    dst_ap = (vb[:, o * 128:(o + 1) * 128] if cur_kb
                              else kb[:, m * 4096 + o * 128: m * 4096 + (o + 1) * 128])
                    v.tensor_copy(dst_ap, C.bh(bk)[:, 0:128])
                    C.brel(bk, "ve")
            cur_kb = not cur_kb
        assert not cur_kb
        for o in range(32):
            @C.ten
            def _(t, m=m, o=o):
                pT = C.bacq(t, "T")
                C.wait_at(t, "ve", C.count("ve"))
                C.inc(t.transpose(pT[:, 0:128], vb[:, o * 128:(o + 1) * 128],
                                  S["identb"][:, :]), "mm")
            tmk = C.count("mm")

            @C.vec
            def _(v, m=m, o=o, tmk=tmk):
                C.wait_at(v, "mm", tmk)
                v.tensor_copy(kb[:, m * 4096 + o * 128: m * 4096 + (o + 1) * 128],
                              C.bh("T")[:, 0:128])
                C.brel("T", "ve")
    dbuild_ve = C.count("ve")
    dbuild_mm = C.count("mm")
    maybe_dump(C, g, S, "p5_D", kb)
    if C.dumped:
        return

    st1_marks = []
    wexp_mk = [None]
    for og in range(8):
        @C.syn
        def _(sy, og=og):
            grp = C.dgrp(sy)
            C.wait_at(sy, "pc2", C.precast_pc2)
            C.wait_at(sy, "mm", C.count("mm"))
            C.inc(sy.dma_start(out=slotB[:, :], in_=g["wexp_bf"][og]), grp, 16)
            wexp_mk[0] = (grp, C.count(grp))

        for j in range(4):
            o = og * 4 + j
            ev1 = []
            for sb in range(4):
                @C.ten
                def _(t, og=og, j=j, sb=sb, wm=wexp_mk[0]):
                    p = C.bacq(t, 0 if sb % 2 == 0 else 1)
                    C.wmark(t, wm)
                    C.wait_at(t, "ve", p5_marks[sb])
                    ins = None
                    for kc in range(DC):
                        ins = t.matmul(
                            p[:, 0:TT],
                            slotB[:, (j * 8 + kc) * 128:(j * 8 + kc) * 128 + 128],
                            yb[:, kc * T + sb * TT: kc * T + (sb + 1) * TT],
                            start=(kc == 0), stop=(kc == DC - 1))
                    C.inc(ins, "mm")
                mk1 = C.count("mm")

                @C.vec
                def _(v, sb=sb, mk1=mk1):
                    C.wait_at(v, "mm", mk1)
                    v.tensor_copy(slotA[:, sb * TT:(sb + 1) * TT],
                                  C.bh(0 if sb % 2 == 0 else 1)[:, 0:TT])
                    C.brel(0 if sb % 2 == 0 else 1, "ve")
                ev1.append(C.count("ve"))

            ev2 = []
            for sb in range(4):
                @C.ten
                def _(t, o=o, sb=sb, ev1=list(ev1)):
                    p = C.bacq(t, 2 if sb % 2 == 0 else 3)
                    C.wait_at(t, "ve", ev1[sb])
                    C.wait_at(t, "ve", dbuild_ve)
                    C.inc(t.matmul(p[:, 0:TT], kb[:, o * 128:(o + 1) * 128],
                                   slotA[:, sb * TT:(sb + 1) * TT],
                                   start=True, stop=True), "mm")
                mk2 = C.count("mm")
                prev_sig = st1_marks[-1] if (st1_marks and sb == 0) else None

                @C.vec
                def _(v, sb=sb, mk2=mk2, prev_sig=prev_sig):
                    C.wait_at(v, "mm", mk2)
                    C.wmark(v, prev_sig)
                    v.tensor_copy(slotA[:, 2048 + sb * TT: 2048 + (sb + 1) * TT],
                                  C.bh(2 if sb % 2 == 0 else 3)[:, 0:TT])
                    C.brel(2 if sb % 2 == 0 else 3, "ve")
            h1_ve = C.count("ve")

            @C.syn
            def _(sy, o=o, h1_ve=h1_ve):
                grp = C.dgrp(sy)
                C.wait_at(sy, "ve", h1_ve)
                dst = g["scr"].rearrange("(a r o) t -> a r o t", a=32, r=4)[:, :, o, :]
                C.inc(sy.dma_start(out=dst, in_=slotA[:, 2048:4096]), grp, 16)
                st1_marks.append((grp, C.count(grp)))
    st1_mm = C.count("mm")

    def mid_stage(mcol, src_buf, dst_buf, sigma_out, gelu, in_marks, out_marks):
        for gg in range(32):
            @C.syn
            def _(sy, gg=gg):
                grp = C.dgrp(sy)
                if gg < 2:
                    for mk in in_marks:
                        C.wmark(sy, mk)
                C.wait_at(sy, "mm", C.count("mm"))
                C.inc(sy.dma_start(out=slotA[:, (gg % 2) * 2048:(gg % 2) * 2048 + 2048],
                                   in_=src_buf[gg * 128:(gg + 1) * 128, :]), grp, 16)
                C.ld5 = (grp, C.count(grp))
            ldm = C.ld5

            ev = []
            for sb in range(4):
                @C.ten
                def _(t, gg=gg, sb=sb, ldm=ldm, mcol=mcol):
                    p = C.bacq(t, 0 if sb % 2 == 0 else 1)
                    C.wmark(t, ldm)
                    C.wait_at(t, "ve", dbuild_ve)
                    C.inc(t.matmul(p[:, 0:TT],
                                   kb[:, mcol * 4096 + gg * 128: mcol * 4096 + (gg + 1) * 128],
                                   slotA[:, (gg % 2) * 2048 + sb * TT:
                                         (gg % 2) * 2048 + (sb + 1) * TT],
                                   start=True, stop=True), "mm")
                mk = C.count("mm")

                prev_out = out_marks[-1] if (out_marks and sb == 0) else None
                if gelu:
                    @C.scal
                    def _(sc, gg=gg, sb=sb, mk=mk, prev_out=prev_out):
                        C.wait_at(sc, "mm", mk)
                        C.wmark(sc, prev_out)
                        sc.activation(S["slotB"][:, sb * TT:(sb + 1) * TT],
                                      C.bh(0 if sb % 2 == 0 else 1)[:, 0:TT],
                                      AF.Gelu, bias=pv5[:, PV5_UB + gg:PV5_UB + gg + 1])
                        C.brel(0 if sb % 2 == 0 else 1, "se")
                    ev.append(("se", C.count("se")))
                else:
                    @C.vec
                    def _(v, gg=gg, sb=sb, mk=mk, prev_out=prev_out):
                        C.wait_at(v, "mm", mk)
                        C.wmark(v, prev_out)
                        v.tensor_copy(S["slotB"][:, sb * TT:(sb + 1) * TT],
                                      C.bh(0 if sb % 2 == 0 else 1)[:, 0:TT])
                        C.brel(0 if sb % 2 == 0 else 1, "ve")
                    ev.append(("ve", C.count("ve")))

            @C.syn
            def _(sy, gg=gg, ev=list(ev)):
                grp = C.dgrp(sy)
                C.wmark(sy, ev[-1])
                if sigma_out == "inv":
                    dst = dst_buf.rearrange("(o g2 r) t -> r o g2 t",
                                            o=32, g2=32, r=4)[:, :, gg, :]
                elif sigma_out == "fwd":
                    dst = dst_buf.rearrange("(a r o) t -> a r o t",
                                            a=32, r=4)[:, :, gg, :]
                else:
                    dst = dst_buf[gg * 128:(gg + 1) * 128, :]
                C.inc(sy.dma_start(out=dst, in_=S["slotB"][:, 0:2048]), grp, 16)
                out_marks.append((grp, C.count(grp)))

    st2_marks = []
    mid_stage(1, g["scr"], g["scr2"], "inv", True, st1_marks, st2_marks)
    st3_marks = []
    mid_stage(2, g["scr2"], g["scr3"], "fwd", False, st2_marks, st3_marks)
    st4_marks = []
    mid_stage(3, g["scr3"], g["scr"], "none", False, st3_marks, st4_marks)

    TB = 256
    prev_ln_ve = C.count("ve")
    wcon_mk = [None]
    for tb in range(8):
        b0 = tb * TB

        @C.syn
        def _(sy, tb=tb, b0=b0):
            grp = C.dgrp(sy)
            if tb < 1:
                for mk in st4_marks:
                    C.wmark(sy, mk)
            C.wait_at(sy, "mm", C.count("mm"))
            for fc in range(32):
                C.inc(sy.dma_start(out=vb[:, fc * TB:(fc + 1) * TB],
                                   in_=g["scr"][fc * 128:(fc + 1) * 128, b0:b0 + TB]),
                      grp, 16)
            C.h8m = (grp, C.count(grp))
        h8m = C.h8m

        a0_marks = []
        for oc in range(DC):
            @C.syn
            def _(sy, oc=oc, tb=tb):
                grp = C.dgrp(sy)
                C.wait_at(sy, "pc2", C.precast_pc2)
                C.wait_at(sy, "mm", C.count("mm"))
                C.inc(sy.dma_start(out=slotB[:, :], in_=g["wcon_bf"][oc]), grp, 16)
                C.wcm = (grp, C.count(grp))
            wcm = C.wcm

            @C.ten
            def _(t, oc=oc, h8m=h8m, wcm=wcm):
                p = C.bacq(t, 4)
                C.wmark(t, h8m)
                C.wmark(t, wcm)
                ins = None
                for fc in range(32):
                    ins = t.matmul(p[:, 0:TB], slotB[:, fc * 128:(fc + 1) * 128],
                                   vb[:, fc * TB:(fc + 1) * TB],
                                   start=(fc == 0), stop=(fc == 31))
                C.inc(ins, "mm")
            wc_mm = C.count("mm")

            @C.scal
            def _(sc, oc=oc, wc_mm=wc_mm, plv=prev_ln_ve):
                C.wait_at(sc, "mm", wc_mm)
                C.wait_at(sc, "ve", plv)
                sc.activation(smf[:, oc * TB:(oc + 1) * TB], C.bh(4)[:, 0:TB],
                              AF.Identity, bias=pv5[:, PV5_BCON + oc:PV5_BCON + oc + 1])
                C.brel(4, "se")
            ase = C.count("se")

            @C.vec
            def _(v, oc=oc, ase=ase):
                C.wait_at(v, "se", ase)
                v.tensor_scalar(smf[:, oc * TB:(oc + 1) * TB],
                                smf[:, oc * TB:(oc + 1) * TB],
                                pv5[:, PV5_BDN + oc:PV5_BDN + oc + 1], None, ALU.add)
            a0_marks.append(C.count("ve"))

        fm_ln(C, S,
              src_chunk=lambda kc: smf[:, kc * TB:(kc + 1) * TB],
              ttw=TB,
              residual_into=lambda kc, b0=b0: xs[:, kc * T + b0: kc * T + b0 + TB],
              src_f32=True,
              src_ready=lambda kc, am=list(a0_marks): ("ve", am[kc]))
        prev_ln_ve = C.count("ve")
    maybe_dump(C, g, S, "p5_x4", xs)



def kernel(**inputs):
    cores = list(range(8))
    nc, _ = build_program(0)
    in_maps = [prep_core_inputs(inputs, c) for c in cores]
    res = run_bass_kernel_spmd(nc, in_maps, core_ids=cores)
    out = np.zeros((4, 4096, 1024), dtype=np.float32)
    for c in cores:
        b, half = c // 2, c % 2
        out[b, half * T:(half + 1) * T, :] = res.results[c]["out_fm"].T
    return out


# version 08
# speedup vs baseline: 1.0880x; 1.0130x over previous
import numpy as np
import ml_dtypes
from contextlib import ExitStack
import concourse.bass as bass
import concourse.mybir as mybir
from concourse.bass_utils import run_bass_kernel_spmd

F32 = mybir.dt.float32
BF16 = mybir.dt.bfloat16
AF = mybir.ActivationFunctionType
ALU = mybir.AluOpType
AX = mybir.AxisListType

D = 1024
DC = 8
T = 2048
H = 16
HD = 64
NTT = 4
TT = 512
EPS_LN = 1e-5
BF = ml_dtypes.bfloat16

(C_EPS, C_EPS6, C_ONE, C_NEG1, C_I1024, C_PROBE, C_ZERO, C_I8, C_Z0,
 C_CARRY, C_SCRA, C_SCRB, C_NHALF, C_HALF) = range(14)

DBG = None


def f32(x):
    return np.ascontiguousarray(x, dtype=np.float32)


def _pack_pv(vecs):
    out = np.zeros((128, 8 * len(vecs)), dtype=np.float32)
    for j, v in enumerate(vecs):
        out[:, j * 8:(j + 1) * 8] = f32(v).reshape(8, 128).T
    return out


def _rows(vecs, dtype):
    return np.concatenate([np.asarray(v).reshape(-1) for v in vecs])[None, :].astype(dtype)


def prep_core_inputs(ins, core):
    b, half = core // 2, core % 2
    s0 = half * T
    x = np.asarray(ins["L0_attn_x"])
    d = {}
    d["x_fm"] = f32(x[b, s0:s0 + T, :].T)
    d["ident_bf"] = np.eye(128, dtype=BF)
    d["ones_col_f32"] = np.ones((128, 1), dtype=np.float32)
    d["ones_col_bf"] = np.ones((128, 1), dtype=BF)
    d["ones_row_f32"] = np.ones((1, 128), dtype=np.float32)
    d["ones_row_bf"] = np.ones((1, 128), dtype=BF)
    z0 = 0.0 if half == 0 else -1e30
    carry = 0.0 if half == 0 else 1.0
    d["consts"] = np.tile(np.array([[EPS_LN, 1e-6, 1.0, -1.0, 1.0 / 1024,
                                     1.0 / (4096 * 8.0), 0.0, 0.125, z0, carry,
                                     0.0, 0.0, -0.5, 0.5, 0.0, 0.0]],
                                   dtype=np.float32), (128, 1))
    for nm in ["Wq", "Wk", "Wv", "Wfq", "Wfk"]:
        d["l0a_" + nm] = f32(ins["L0_attn_" + nm])
    W1 = f32(ins["L0_attn_W1"])
    d["l0a_W1sw"] = np.ascontiguousarray(
        W1.reshape(8, 128, 8, 4, 128).transpose(2, 1, 3, 0, 4)).reshape(8, 128, 4096)
    W2 = f32(ins["L0_attn_W2"])
    d["l0a_W2sw"] = np.ascontiguousarray(
        W2.reshape(8, 4, 128, 1024).transpose(0, 2, 1, 3)).reshape(8, 128, 4096)
    pvm = _pack_pv([ins["L0_attn_" + n] for n in
                    ["gq", "betaq", "gk", "betak", "bfk", "ga", "ba",
                     "bq", "bk", "bfq", "b2", "gv", "betav"]])
    bfkh = np.zeros((128, 16), dtype=np.float32)
    bfk_ = f32(ins["L0_attn_bfk"]).reshape(16, 64)
    bfkh[0:64, :] = bfk_.T
    d["l0a_pv"] = np.concatenate([pvm, bfkh], axis=1)
    d["l0a_b1fm"] = f32(ins["L0_attn_b1"]).reshape(32, 128).T.copy()
    d["l0a_rows"] = _rows([ins["L0_attn_" + n] for n in ["bv", "bfk"]], BF)
    d["l0f_wqkv"] = f32(np.asarray(ins["L0_ffn_in_proj_w"]).T)
    d["l0f_wout"] = f32(np.asarray(ins["L0_ffn_out_proj_w"]).T)
    bi = f32(ins["L0_ffn_in_proj_b"])
    d["l0f_pv"] = _pack_pv([bi[0:1024], bi[1024:2048], bi[2048:3072],
                            ins["L0_ffn_out_proj_b"], ins["L0_ffn_ln1_g"],
                            ins["L0_ffn_ln1_b"]])
    w1 = f32(ins["L0_ffn_w1"])
    d["l0f_w1sw"] = np.ascontiguousarray(
        w1.reshape(8, 8, 128, 4, 4, 128).transpose(0, 3, 2, 4, 1, 5)
    ).reshape(8, 4, 128, 4096)
    w2m = f32(ins["L0_ffn_w2"])
    d["l0f_w2sw"] = np.ascontiguousarray(
        w2m.reshape(8, 4, 4, 128, 1024).transpose(0, 1, 3, 2, 4)
    ).reshape(8, 4, 128, 4096)
    d["l0f_wge"] = np.concatenate(
        [f32(ins["L0_ffn_w_group"]),
         f32(ins["L0_ffn_w_expert"]).transpose(1, 0, 2).reshape(1024, 8)], axis=1)
    d["l0f_bge"] = _rows([ins["L0_ffn_b_group"],
                          f32(ins["L0_ffn_b_expert"]).reshape(8)], BF)
    d["l0f_b2"] = f32(ins["L0_ffn_b2"])
    d["l0f_b1fm"] = np.ascontiguousarray(
        f32(ins["L0_ffn_b1"]).reshape(8, 16, 128).transpose(2, 0, 1)).reshape(128, 128)
    d["l0f_pv3"] = _pack_pv([ins["L0_ffn_ln2_g"], ins["L0_ffn_ln2_b"]])
    basis8 = np.zeros((8, 1024), dtype=BF)
    for e in range(8):
        basis8[e, e * 128:(e + 1) * 128] = 1
    d["basis8"] = basis8
    d["l1_wqkv"] = f32(ins["L1_attn_W_qkv"])
    d["l1_wout"] = f32(ins["L1_attn_W_out"])
    d["tril"] = np.triu(np.ones((128, 128), dtype=np.float32)).astype(BF)
    ff = np.arange(4096)
    oo, ii = ff // 128, ff % 128
    sigma = (ii // 4) * 128 + (ii % 4) * 32 + oo
    inv = np.empty(4096, np.int64)
    inv[sigma] = ff
    WEXP = f32(ins["L1_ffn_W_exp"])
    d["l1f_wexpsw"] = np.ascontiguousarray(
        WEXP.reshape(8, 128, 8, 4, 128).transpose(2, 1, 3, 0, 4)).reshape(8, 128, 4096)
    WCON = f32(ins["L1_ffn_W_con"])[inv, :]
    d["l1f_wconsw"] = np.ascontiguousarray(
        WCON.reshape(32, 128, 8, 128).transpose(2, 1, 0, 3)).reshape(8, 128, 4096)
    def stage_ab(wst, s2):
        al = np.empty(4096, np.float32)
        bt = np.empty(4096, np.float32)
        w = f32(wst).reshape(4096 // (2 * s2), s2, 2, 2)
        for f in range(4096):
            nb_i = f // (2 * s2)
            s_i = f % s2
            if (f // s2) % 2 == 0:
                al[f] = w[nb_i, s_i, 0, 0]
                bt[f] = w[nb_i, s_i, 0, 1]
            else:
                al[f] = w[nb_i, s_i, 1, 1]
                bt[f] = w[nb_i, s_i, 1, 0]
        btp = bt[np.bitwise_xor(np.arange(4096), s2)]
        return al, btp
    ab_cols = []
    for wname in ["up_weights", "down_weights"]:
        wt = f32(ins["L1_ffn_" + wname])
        for st in range(7):
            al, btp = stage_ab(wt[st], 2 ** st)
            ab_cols.append((al, btp))
        for st in range(7, 12):
            al, btp = stage_ab(wt[st], 2 ** st)
            ab_cols.append((al[inv], btp[inv]))
    n_inst = len(ab_cols)
    ab = np.zeros((128, n_inst * 64), dtype=np.float32)
    for j, (al, btp) in enumerate(ab_cols):
        ab[:, j * 64: j * 64 + 32] = al.reshape(32, 128).T
        ab[:, j * 64 + 32: j * 64 + 64] = btp.reshape(32, 128).T
    d["l1f_ab"] = ab
    xorm = np.zeros((7, 128, 128), dtype=BF)
    for sj, s2 in enumerate([1, 2, 4, 8, 16, 32, 64]):
        idx = np.arange(128)
        xorm[sj, idx, np.bitwise_xor(idx, s2)] = 1
    d["l1f_xorm"] = xorm
    ub = f32(ins["L1_ffn_up_bias"])[sigma.argsort()] if False else f32(ins["L1_ffn_up_bias"])[inv]
    d["l1f_pv"] = np.concatenate(
        [ub.reshape(32, 128).T,
         f32(ins["L1_ffn_b_con"]).reshape(8, 128).T,
         f32(ins["L1_ffn_down_bias"]).reshape(8, 128).T], axis=1)
    return d


(PV_GQ, PV_BETAQ, PV_GK, PV_BETAK, PV_BFK, PV_GA, PV_BA,
 PV_BQ, PV_BK, PV_BFQ, PV_B2, PV_GV, PV_BETAV) = [j * 8 for j in range(13)]
PV_BFKH = 104
(PV2_BQ, PV2_BK, PV2_BV, PV2_BO, PV2_LN1G, PV2_LN1B) = [j * 8 for j in range(6)]
PV3_LN2G, PV3_LN2B = 0, 8
R_BV, R_BFK = 0, 1024


_PASSTHRU = {"wait_ge", "nop", "dma_start", "dma_start_transpose", "then_inc",
             "collective_compute", "indirect_dma_start"}


class SerialEng:

    def __init__(self, C, eng, ev):
        self._C = C
        self._e = eng
        self._ev = ev

    def __getattr__(self, name):
        f = getattr(self._e, name)
        if name in _PASSTHRU or name.startswith("_"):
            return f
        C, eng, ev = self._C, self._e, self._ev

        def call(*a, **k):
            C.wait(eng, ev)
            ins = f(*a, **k)
            C.inc(ins, ev, 1)
            return ins
        return call


class RecIns:
    __slots__ = ("calls",)

    def __init__(self):
        self.calls = []

    def then_inc(self, sem, k):
        self.calls.append(("then_inc", (sem, k), {}))
        return self


class RecEng:

    def __init__(self, log):
        self._log = log

    def __getattr__(self, name):
        if name.startswith("_"):
            raise AttributeError(name)
        log = self._log

        def call(*a, **k):
            ri = RecIns()
            log.append((name, a, k, ri))
            return ri
        return call


class Ctx:
    def __init__(self, nc, block):
        self.nc = nc
        self.block = block
        self.evs = {}
        self.banks = {}
        self.dumped = False
        self.logs = {n: [] for n in ("vector", "scalar", "gpsimd",
                                     "tensor", "sync")}

    def sem(self, name, handle):
        self.evs[name] = [handle, 0]

    def dgrp(self, q):
        name = self._dsems[self._dptr % len(self._dsems)]
        self._dptr += 1
        if self.count(name) > 0:
            self.wait(q, name)
        return name

    def add_bank(self, key, ten):
        b = type("B", (), {})()
        b.h, b.mark = ten, None
        self.banks[key] = b

    def inc(self, instr, name, k=1):
        ev = self.evs[name]
        instr.then_inc(ev[0], k)
        ev[1] += k
        return instr

    def wait(self, eng, name):
        ev = self.evs[name]
        if ev[1] > 0:
            eng.wait_ge(ev[0], ev[1])

    def wait_at(self, eng, name, n):
        if n > 0:
            eng.wait_ge(self.evs[name][0], n)

    def count(self, name):
        return self.evs[name][1]

    def mark(self, name):
        return (name, self.evs[name][1])

    def wmark(self, eng, mk):
        if mk is not None:
            self.wait_at(eng, mk[0], mk[1])

    def bacq(self, eng, key):
        b = self.banks[key]
        self.wmark(eng, b.mark)
        return b.h

    def bh(self, key):
        return self.banks[key].h

    def brel(self, key, ev):
        self.banks[key].mark = (ev, self.evs[ev][1])

    def vec(self, fn):
        fn(SerialEng(self, RecEng(self.logs["vector"]), "ve"))

    def scal(self, fn):
        fn(SerialEng(self, RecEng(self.logs["scalar"]), "se"))

    def gps(self, fn):
        fn(SerialEng(self, RecEng(self.logs["gpsimd"]), "gc"))

    def ten(self, fn):
        fn(RecEng(self.logs["tensor"]))

    def syn(self, fn):
        fn(RecEng(self.logs["sync"]))

    def flush(self):
        for eng_attr in ("vector", "scalar", "gpsimd", "tensor", "sync"):
            log = self.logs[eng_attr]

            @getattr(self.block, eng_attr)
            def _(eng, log=log):
                for name, a, k, ri in log:
                    ins = getattr(eng, name)(*a, **k)
                    for m, aa, kk in ri.calls:
                        getattr(ins, m)(*aa, **kk)


def expand_view(ap, dims):
    return bass.AP(ap.tensor, offset=ap.offset,
                   ap=[list(ap.ap[0])] + [list(dd) for dd in dims])


def fm_ln(C, S, src_chunk, ttw, dst_chunk=None, gamma_beta=None,
          src_f32=False, residual_into=None, src_ready=None, nchunk=DC,
          inv_n_col=C_I1024, pv_src="pv"):
    cst = S["consts"]
    sqf = S["sqf"]
    rows = S["rows"]
    mm_entry = C.count("mm")
    se_entry = C.count("se")
    ve_entry = C.count("ve")
    RB = 3
    nround = (nchunk + RB - 1) // RB

    @C.vec
    def _(v):
        C.wait_at(v, "se", se_entry)
        for r in range(nround):
            if r > 0:
                C.wait_at(v, "mm", mm_entry + RB * r)
            for kc in range(RB * r, min(RB * r + RB, nchunk)):
                if src_ready is not None:
                    C.wmark(v, src_ready(kc))
                v.tensor_tensor(sqf[:, (kc % RB) * 512:(kc % RB) * 512 + ttw],
                                src_chunk(kc), src_chunk(kc), ALU.mult)

    @C.ten
    def _(t):
        for r in range(nround):
            hi = min(RB * r + RB, nchunk)
            C.wait_at(t, "ve", ve_entry + hi)
            for kc in range(RB * r, hi):
                pb2 = C.bacq(t, 2) if kc == 0 else C.bh(2)
                pb3 = C.bacq(t, 3) if kc == 0 else C.bh(3)
                if src_ready is not None:
                    C.wmark(t, src_ready(kc))
                ones = S["onec_f"] if src_f32 else S["onec_b"]
                t.matmul(pb2[0:1, 0:ttw], ones[:, 0:1], src_chunk(kc),
                         start=(kc == 0), stop=(kc == nchunk - 1))
                C.inc(t.matmul(pb3[0:1, 0:ttw], S["onec_f"][:, 0:1],
                               sqf[:, (kc % RB) * 512:(kc % RB) * 512 + ttw],
                               start=(kc == 0), stop=(kc == nchunk - 1)), "mm")
    stats_mm = C.count("mm")

    @C.vec
    def _(v):
        C.wait_at(v, "mm", stats_mm)
        v.tensor_scalar(rows[0:1, 0:ttw], C.bh(2)[0:1, 0:ttw],
                        cst[0:1, inv_n_col:inv_n_col + 1], None, ALU.mult)
        v.tensor_scalar(rows[0:1, 512:512 + ttw], C.bh(3)[0:1, 0:ttw],
                        cst[0:1, inv_n_col:inv_n_col + 1], None, ALU.mult)
        C.brel(3, "ve")
        v.tensor_tensor(C.bh(2)[0:1, 0:ttw], rows[0:1, 0:ttw],
                        rows[0:1, 0:ttw], ALU.mult)
        v.tensor_tensor(rows[0:1, 512:512 + ttw], rows[0:1, 512:512 + ttw],
                        C.bh(2)[0:1, 0:ttw], ALU.subtract)
        C.brel(2, "ve")
    ve_var = C.count("ve")

    @C.scal
    def _(sc):
        C.wait_at(sc, "ve", ve_var)
        sc.activation(rows[0:1, 512:512 + ttw], rows[0:1, 512:512 + ttw],
                      AF.Ln, bias=cst[0:1, C_EPS:C_EPS + 1])
        sc.activation(rows[0:1, 512:512 + ttw], rows[0:1, 512:512 + ttw],
                      AF.Exp, scale=-0.5)
    se_std = C.count("se")

    @C.vec
    def _(v):
        C.wait_at(v, "se", se_std)
        v.tensor_tensor(rows[0:1, 0:ttw], rows[0:1, 0:ttw],
                        rows[0:1, 512:512 + ttw], ALU.mult)
    ve_rows = C.count("ve")

    @C.ten
    def _(t):
        C.wait_at(t, "ve", ve_rows)
        pb2 = C.bacq(t, 2)
        t.matmul(pb2[:, 0:ttw], S["oner_f"][0:1, :], rows[0:1, 512:512 + ttw],
                 start=True, stop=True)
        pb3 = C.bacq(t, 3)
        C.inc(t.matmul(pb3[:, 0:ttw], S["oner_f"][0:1, :], rows[0:1, 0:ttw],
                       start=True, stop=True), "mm")
    bc_mm = C.count("mm")

    @C.vec
    def _(v):
        C.wait_at(v, "mm", bc_mm)
        for kc in range(nchunk):
            v.tensor_tensor(sqf[:, 0:ttw], src_chunk(kc), C.bh(2)[:, 0:ttw], ALU.mult)
            if kc == nchunk - 1:
                C.brel(2, "ve")
            v.tensor_tensor(sqf[:, 0:ttw], sqf[:, 0:ttw],
                            C.bh(3)[:, 0:ttw], ALU.subtract)
            if kc == nchunk - 1:
                C.brel(3, "ve")
            if residual_into is not None:
                v.tensor_tensor(residual_into(kc), residual_into(kc),
                                sqf[:, 0:ttw], ALU.add)
            elif gamma_beta is not None:
                gcol, bcol = gamma_beta
                v.tensor_scalar(dst_chunk(kc), sqf[:, 0:ttw],
                                S[pv_src][:, gcol + kc:gcol + kc + 1],
                                S[pv_src][:, bcol + kc:bcol + kc + 1],
                                ALU.mult, ALU.add)
            else:
                v.tensor_copy(dst_chunk(kc), sqf[:, 0:ttw])
    return C.count("ve")


def fm_proj(C, S, wslot_fn, src_chunk, ttw, out_cb, banks=(0, 1), n_oc=DC,
            src_ready=None, w_ready=None, nkc=DC):
    for oc in range(n_oc):
        bk = banks[oc % len(banks)]

        @C.ten
        def _(t, oc=oc, bk=bk):
            p = C.bacq(t, bk)
            if w_ready is not None:
                C.wait_at(t, *w_ready)
            if src_ready is not None:
                C.wmark(t, src_ready(oc))
            for kc in range(nkc):
                ins = t.matmul(p[:, 0:ttw], wslot_fn(kc, oc), src_chunk(kc),
                               start=(kc == 0), stop=(kc == nkc - 1))
            C.inc(ins, "mm")
        out_cb(oc, bk, C.count("mm"))


def build_program(core):
    half = core % 2
    nc = bass.Bass("TRN2", target_bir_lowering=False)
    g = {}

    def din(name, shape, dt=F32):
        g[name] = nc.dram_tensor(name, shape, dt, kind="ExternalInput").ap()

    def dout(name, shape, dt=F32):
        g[name] = nc.dram_tensor(name, shape, dt, kind="ExternalOutput").ap()

    def dint(name, shape, dt=F32):
        g[name] = nc.dram_tensor(name, shape, dt).ap()

    din("x_fm", [D, T])
    dout("out_fm", [D, T])
    din("ident_bf", [128, 128], BF16)
    din("ones_col_f32", [128, 1]); din("ones_col_bf", [128, 1], BF16)
    din("ones_row_f32", [1, 128]); din("ones_row_bf", [1, 128], BF16)
    din("consts", [128, 16])
    for nm in ["Wq", "Wk", "Wv", "Wfq", "Wfk"]:
        din("l0a_" + nm, [D, D])
    din("l0a_W1sw", [8, 128, 4096]); din("l0a_W2sw", [8, 128, 4096])
    din("l0a_pv", [128, 120]); din("l0a_b1fm", [128, 32])
    din("l0a_rows", [1, 2 * 1024], BF16)
    dint("pq_spill", [D, T], BF16)
    dint("w1_bf", [8, 128, 4096], BF16)
    dint("w2_bf", [8, 128, 4096], BF16)
    for nm in ["Wq", "Wk", "Wv", "Wfq", "Wfk"]:
        dint(nm + "_bf", [D, D], BF16)
    din("l0f_wqkv", [D, 3 * D]); din("l0f_wout", [D, D])
    din("l0f_pv", [128, 48])
    dint("wqkv_bf", [D, 3 * D], BF16)
    dint("wout_bf", [D, D], BF16)
    dint("kv_buf", [2048, T], BF16)
    dint("q_buf", [D, T], BF16)
    dint("kv_gath", [4096, T], BF16)
    din("l0f_w1sw", [8, 4, 128, 4096]); din("l0f_w2sw", [8, 4, 128, 4096])
    din("l0f_wge", [D, 12]); din("l0f_bge", [1, 12], BF16)
    din("l0f_b2", [8, D]); din("l0f_b1fm", [128, 128])
    din("l0f_pv3", [128, 16]); din("basis8", [8, 1024], BF16)
    dint("moew1_bf", [8, 4, 128, 4096], BF16)
    dint("moew2_bf", [8, 4, 128, 4096], BF16)
    dint("wge_bf", [D, 12], BF16)
    din("l1_wqkv", [D, 3 * D]); din("l1_wout", [D, D])
    din("tril", [128, 128], BF16)
    dint("l1wqkv_bf", [D, 3 * D], BF16)
    dint("l1wout_bf", [D, D], BF16)
    dint("pkfm_buf", [D, T], BF16)
    dint("pktm_buf", [T, D], BF16)
    dint("v_buf", [T, D], BF16)
    dint("carry_buf", [64, 1040])
    dint("carry_gath", [128, 1040])
    din("l1f_wexpsw", [8, 128, 4096]); din("l1f_wconsw", [8, 128, 4096])
    din("l1f_ab", [128, 1536]); din("l1f_xorm", [7, 128, 128], BF16)
    din("l1f_pv", [128, 48])
    dint("wexp_bf", [8, 128, 4096], BF16)
    dint("wcon_bf", [8, 128, 4096], BF16)
    dint("scr", [4096, T], BF16)
    dint("scr2", [4096, T], BF16)
    dint("scr3", [4096, T], BF16)
    dint("probe_buf", [128, 8])
    dint("probe_red", [128, 8])
    dint("kvu_buf", [66, 1040])
    dint("kvu_red", [66, 1040])
    dint("mmx_buf", [1, 16])
    dint("mmx_red", [1, 16])

    es = ExitStack()
    sbt = lambda n, s, dt: es.enter_context(nc.sbuf_tensor(n, s, dt))
    pst = lambda n, s, dt: es.enter_context(nc.psum_tensor(n, s, dt))
    sem = lambda n: es.enter_context(nc.semaphore(n))

    S = {}
    S["xs"] = sbt("xs", [128, DC * T], F32)
    S["yb"] = sbt("yb", [128, DC * T], BF16)
    S["kb"] = sbt("kb", [128, DC * T], BF16)
    S["vb"] = sbt("vb", [128, DC * T], BF16)
    S["slotA"] = sbt("slotA", [128, DC * TT], BF16)
    S["slotB"] = sbt("slotB", [128, DC * TT], BF16)
    S["sqf"] = sbt("sqf", [128, 1536], F32)
    S["rows"] = sbt("rows", [128, 1024], F32)
    S["smf"] = sbt("smf", [128, 2048], F32)
    S["kvu"] = sbt("kvu", [64, 1040], F32)
    S["kvb"] = sbt("kvb", [128, 1040], BF16)
    S["scoreT"] = sbt("scoreT", [128, 256], BF16)
    S["probe"] = sbt("probe", [128, 8], F32)
    S["pbd"] = sbt("pbd", [128, 16], BF16)
    S["consts"] = sbt("consts_sb", [128, 16], F32)
    S["pv"] = sbt("pv", [128, 120], F32)
    S["pv2"] = sbt("pv2", [128, 48], F32)
    S["pv3"] = sbt("pv3", [128, 16], F32)
    S["pv5"] = sbt("pv5", [128, 48], F32)
    S["b1fm3"] = sbt("b1fm3", [128, 128], F32)
    S["bge"] = sbt("bge", [1, 12], BF16)
    S["b1fm"] = sbt("b1fm", [128, 32], F32)
    S["rowsbf"] = sbt("rowsbf", [1, 2 * 1024], BF16)
    S["onec_f"] = sbt("onec_f", [128, 1], F32)
    S["onec_b"] = sbt("onec_b", [128, 1], BF16)
    S["oner_f"] = sbt("oner_f", [1, 128], F32)
    S["oner_b"] = sbt("oner_b", [1, 128], BF16)
    S["identb"] = sbt("identb", [128, 128], BF16)

    block = es.enter_context(nc.Block())
    C = Ctx(nc, block)
    for n in ["wl", "al", "mm", "ve", "se", "gc", "st", "cc", "pc", "pc2"]:
        C.sem(n, sem("s_" + n))
    C._dsems = []
    C._dptr = 0
    for i in range(16):
        nm = f"d{i}"
        C.sem(nm, sem("s_" + nm))
        C._dsems.append(nm)
    for i in range(7):
        C.add_bank(i, pst(f"p{i}", [128, 512], F32))
    C.add_bank("T", pst("pT", [128, 1024], BF16))
    C.scrA = S["consts"][0:1, 10:11]
    C.scrB = S["consts"][0:1, 11:12]

    emit_preamble(C, g, S)
    emit_phase1(C, g, S, half)
    if not C.dumped:
        emit_phase2(C, g, S)
    if not C.dumped:
        emit_phase3(C, g, S)
    if not C.dumped:
        emit_phase4(C, g, S)
    if not C.dumped:
        emit_phase5(C, g, S)
    if not C.dumped:
        emit_epilogue(C, g, S)
    C.flush()
    es.close()
    return nc, g


def emit_preamble(C, g, S):
    @C.syn
    def _(sy):
        grp = C.dgrp(sy)
        for kc in range(DC):
            C.inc(sy.dma_start(out=S["xs"][:, kc * T:(kc + 1) * T],
                               in_=g["x_fm"][kc * 128:(kc + 1) * 128, :]), grp, 16)
        for nm, src in [("consts", "consts"), ("pv", "l0a_pv"), ("b1fm", "l0a_b1fm"),
                        ("pv2", "l0f_pv"), ("pv3", "l0f_pv3"), ("b1fm3", "l0f_b1fm"),
                        ("bge", "l0f_bge"), ("pv5", "l1f_pv"),
                        ("rowsbf", "l0a_rows"), ("onec_f", "ones_col_f32"),
                        ("onec_b", "ones_col_bf"), ("oner_f", "ones_row_f32"),
                        ("oner_b", "ones_row_bf"), ("identb", "ident_bf")]:
            C.inc(sy.dma_start(out=S[nm][:], in_=g[src][:]), grp, 16)
        C.pre_al = (grp, C.count(grp))

    @C.gps
    def _(gp):
        for nm in ["Wq", "Wfq", "Wk", "Wv", "Wfk"]:
            C.inc(gp.dma_start(out=g[nm + "_bf"][:], in_=g["l0a_" + nm][:]), "pc", 16)
        C.inc(gp.dma_start(out=g["w1_bf"][:], in_=g["l0a_W1sw"][:]), "pc", 16)
        C.inc(gp.dma_start(out=g["w2_bf"][:], in_=g["l0a_W2sw"][:]), "pc", 16)
        C.inc(gp.dma_start(out=g["wqkv_bf"][:], in_=g["l0f_wqkv"][:]), "pc", 16)
        C.inc(gp.dma_start(out=g["wout_bf"][:], in_=g["l0f_wout"][:]), "pc", 16)
        C.inc(gp.dma_start(out=g["wge_bf"][:], in_=g["l0f_wge"][:]), "pc", 16)
        C.inc(gp.dma_start(out=g["moew1_bf"][:], in_=g["l0f_w1sw"][:]), "pc2", 16)
        C.inc(gp.dma_start(out=g["moew2_bf"][:], in_=g["l0f_w2sw"][:]), "pc2", 16)
        C.inc(gp.dma_start(out=g["l1wqkv_bf"][:], in_=g["l1_wqkv"][:]), "pc2", 16)
        C.inc(gp.dma_start(out=g["l1wout_bf"][:], in_=g["l1_wout"][:]), "pc2", 16)
        C.inc(gp.dma_start(out=g["wexp_bf"][:], in_=g["l1f_wexpsw"][:]), "pc2", 16)
        C.inc(gp.dma_start(out=g["wcon_bf"][:], in_=g["l1f_wconsw"][:]), "pc2", 16)
    C.precast_pc = C.count("pc")
    C.precast_pc2 = C.count("pc2")


def emit_epilogue(C, g, S, src=None):
    if src is None:
        src = S["xs"]

    @C.syn
    def _(sy):
        for nm in ["ve", "se", "mm", "gc"]:
            C.wait(sy, nm)
        for kc in range(DC):
            C.inc(sy.dma_start(out=g["out_fm"][kc * 128:(kc + 1) * 128, :],
                               in_=src[:, kc * T:(kc + 1) * T]), "st", 16)
        C.wait(sy, "st")


def maybe_dump(C, g, S, tag, buf):
    if DBG != tag:
        return False

    @C.gps
    def _(gp):
        for nm in ["ve", "se", "mm", "al", "wl", "st", "cc"]:
            C.wait(gp, nm)
        for kc in range(DC):
            C.inc(gp.dma_start(out=g["out_fm"][kc * 128:(kc + 1) * 128, :],
                               in_=buf[:, kc * T:(kc + 1) * T]), "st", 16)
        C.wait(gp, "st")
    C.dumped = True
    return True


def emit_phase1(C, g, S, half):
    xs, yb, kb, vb = S["xs"], S["yb"], S["kb"], S["vb"]
    slotA, slotB, sqf, rows, smf = S["slotA"], S["slotB"], S["sqf"], S["rows"], S["smf"]
    cst, pv = S["consts"], S["pv"]
    WS0, WS1 = 0, 8192

    def wsl(slot, kc, c0, cn):
        base = WS0 if slot == 0 else WS1
        return vb[:, base + kc * 1024 + c0: base + kc * 1024 + c0 + cn]

    wl_marks = {}

    def load_wfull(sy, wname, slot, gate_mm=0):
        grp = C.dgrp(sy)
        if gate_mm:
            C.wait_at(sy, "mm", gate_mm)
        C.wait_at(sy, "pc", C.precast_pc)
        src_bf = g[wname.replace("l0a_", "") + "_bf"]
        for kc in range(DC):
            C.inc(sy.dma_start(out=wsl(slot, kc, 0, 1024),
                               in_=src_bf[kc * 128:(kc + 1) * 128, :]), grp, 16)
        wl_marks[wname] = (grp, C.count(grp))

    @C.syn
    def _(sy):
        load_wfull(sy, "l0a_Wq", 0)
        load_wfull(sy, "l0a_Wfq", 1)

    p1_marks = []
    for tt in range(NTT):
        c0 = tt * TT
        m = fm_ln(C, S,
                  src_chunk=lambda kc, c0=c0: xs[:, kc * T + c0: kc * T + c0 + TT],
                  ttw=TT,
                  dst_chunk=lambda kc, c0=c0: yb[:, kc * T + c0: kc * T + c0 + TT],
                  src_f32=True,
                  src_ready=(lambda kc: C.pre_al) if tt == 0 else None)
        p1_marks.append(m)
    p1_done = p1_marks[-1]
    maybe_dump(C, g, S, "p1_y", yb)
    if C.dumped:
        return

    q_mm_done = 0
    pq_mm_done = 0
    slotA_free = 0
    slotB_free = None
    for tt in range(NTT):
        c0 = tt * TT
        evac_marks = []
        saf_now = slotA_free

        def q_out(oc, bk, mmk, evac_marks=evac_marks, saf=saf_now):
            @C.scal
            def _(sc):
                C.wait_at(sc, "mm", mmk)
                C.wait_at(sc, "ve", saf)
                sc.activation(S["slotA"][:, oc * TT:oc * TT + TT], C.bh(bk)[:, 0:TT],
                              AF.Identity, bias=pv[:, PV_BQ + oc:PV_BQ + oc + 1])
                C.brel(bk, "se")
            evac_marks.append(("se", C.count("se")))

        fm_proj(C, S, lambda kc, oc: wsl(0, kc, oc * 128, 128),
                lambda kc, c0=c0: yb[:, kc * T + c0: kc * T + c0 + TT],
                TT, q_out, w_ready=wl_marks["l0a_Wq"],
                src_ready=lambda oc, tt=tt: ("ve", p1_marks[tt]))
        q_mm_done = C.count("mm")
        qln_ve = fm_ln(C, S,
                       src_chunk=lambda kc: S["slotA"][:, kc * TT:(kc + 1) * TT],
                       ttw=TT,
                       dst_chunk=lambda kc: S["slotA"][:, kc * TT:(kc + 1) * TT],
                       gamma_beta=(PV_GQ, PV_BETAQ),
                       src_ready=lambda kc, em=list(evac_marks): em[kc])

        @C.vec
        def _(v, tt=tt):
            for kc in range(DC):
                v.reduce_sum(smf[:, 0:1], S["slotA"][:, kc * TT:(kc + 1) * TT], axis=AX.X)
                if tt == 0:
                    v.tensor_copy(S["probe"][:, kc:kc + 1], smf[:, 0:1])
                else:
                    v.tensor_tensor(S["probe"][:, kc:kc + 1], S["probe"][:, kc:kc + 1],
                                    smf[:, 0:1], ALU.add)

        pq_marks = []

        def pq_out(oc, bk, mmk, pq_marks=pq_marks, sbf=slotB_free):
            @C.scal
            def _(sc):
                C.wait_at(sc, "mm", mmk)
                C.wmark(sc, sbf)
                sc.activation(S["slotB"][:, oc * TT:oc * TT + TT], C.bh(bk)[:, 0:TT],
                              AF.Tanh, bias=pv[:, PV_BFQ + oc:PV_BFQ + oc + 1])
                C.brel(bk, "se")
            semk = C.count("se")

            @C.vec
            def _(v):
                C.wait_at(v, "se", semk)
                v.tensor_scalar(S["slotB"][:, oc * TT:oc * TT + TT],
                                S["slotB"][:, oc * TT:oc * TT + TT],
                                cst[:, C_ONE:C_ONE + 1], None, ALU.add)
            pq_marks.append(C.count("ve"))

        fm_proj(C, S, lambda kc, oc: wsl(1, kc, oc * 128, 128),
                lambda kc: S["slotA"][:, kc * TT:(kc + 1) * TT],
                TT, pq_out, banks=(0, 1), w_ready=wl_marks["l0a_Wfq"],
                src_ready=lambda oc, q=qln_ve: ("ve", q))
        pq_mm_done = C.count("mm")
        slotA_free = C.count("ve")

        @C.syn
        def _(sy, tt=tt, pqm=list(pq_marks), c0=c0):
            grp = C.dgrp(sy)
            for oc in range(DC):
                C.wait_at(sy, "ve", pqm[oc])
                C.inc(sy.dma_start(out=g["pq_spill"][oc * 128:(oc + 1) * 128, c0:c0 + TT],
                                   in_=S["slotB"][:, oc * TT:oc * TT + TT]), grp, 16)
            C.pq_spill_st = (grp, C.count(grp))
        slotB_free = C.pq_spill_st

    @C.syn
    def _(sy):
        load_wfull(sy, "l0a_Wk", 0, gate_mm=q_mm_done)

    for tt in range(NTT):
        c0 = tt * TT
        evac_marks = []
        if tt > 0:
            slotA_free = C.count("ve")

        def k_out(oc, bk, mmk, evac_marks=evac_marks, saf=slotA_free):
            @C.scal
            def _(sc):
                C.wait_at(sc, "mm", mmk)
                C.wait_at(sc, "ve", saf)
                sc.activation(S["slotA"][:, oc * TT:oc * TT + TT], C.bh(bk)[:, 0:TT],
                              AF.Identity, bias=pv[:, PV_BK + oc:PV_BK + oc + 1])
                C.brel(bk, "se")
            evac_marks.append(("se", C.count("se")))

        fm_proj(C, S, lambda kc, oc: wsl(0, kc, oc * 128, 128),
                lambda kc, c0=c0: yb[:, kc * T + c0: kc * T + c0 + TT],
                TT, k_out, w_ready=wl_marks["l0a_Wk"],
                src_ready=lambda oc, tt=tt: ("ve", p1_marks[tt]))
        fm_ln(C, S,
              src_chunk=lambda kc: S["slotA"][:, kc * TT:(kc + 1) * TT],
              ttw=TT,
              dst_chunk=lambda kc, c0=c0: kb[:, kc * T + c0: kc * T + c0 + TT],
              gamma_beta=(PV_GK, PV_BETAK),
              src_ready=lambda kc, em=list(evac_marks): em[kc])
    k_mm_done = C.count("mm")
    k_ve_done = C.count("ve")
    maybe_dump(C, g, S, "p3_K", kb)
    if C.dumped:
        return

    @C.syn
    def _(sy):
        load_wfull(sy, "l0a_Wfk", 1, gate_mm=pq_mm_done)
        load_wfull(sy, "l0a_Wv", 0, gate_mm=k_mm_done)

    probe_ve = C.count("ve")

    @C.gps
    def _(gp):
        C.wait_at(gp, "ve", probe_ve)
        C.inc(gp.dma_start(out=g["probe_buf"][:], in_=S["probe"][:]), "gc", 16)
        C.wait(gp, "gc")
        C.inc(gp.collective_compute(
            "AllReduce", ALU.add,
            replica_groups=[[0, 1], [2, 3], [4, 5], [6, 7]],
            ins=[g["probe_buf"][:, :].opt()], outs=[g["probe_red"][:, :].opt()]), "cc")

    @C.syn
    def _(sy):
        grp = C.dgrp(sy)
        C.wait(sy, "cc")
        C.inc(sy.dma_start(out=S["probe"][:], in_=g["probe_red"][:]), grp, 16)
        C.probe_al = (grp, C.count(grp))

    @C.vec
    def _(v):
        C.wmark(v, C.probe_al)
        v.tensor_scalar(S["probe"][:], S["probe"][:], cst[:, C_PROBE:C_PROBE + 1],
                        None, ALU.mult)
        v.memset(S["pbd"][:], 0.0)
        for kc in range(DC):
            v.tensor_copy(S["pbd"][0:64, 2 * kc:2 * kc + 1], S["probe"][0:64, kc:kc + 1])
            v.tensor_copy(S["pbd"][64:128, 2 * kc + 1:2 * kc + 2],
                          S["probe"][64:128, kc:kc + 1])
    pbd_ve = C.count("ve")

    MY_M, MY_ZU, MM_, WMY, E0_, RZ_, T1_, T2_, NEG_ = \
        0, 16, 32, 48, 64, 80, 96, 112, 128
    mz = rows[0:1, 512:656]
    prev_exp_se = 0
    prev_tr_mm = 0
    for h in range(H):
        c, r = h // 2, h % 2
        lg_marks = []
        for tt in range(NTT):
            bk = 4 if tt % 2 == 0 else 0

            @C.ten
            def _(t, c=c, r=r, tt=tt, bk=bk):
                p = C.bacq(t, bk)
                C.wait_at(t, "ve", pbd_ve)
                C.wait_at(t, "ve", k_ve_done)
                C.inc(t.matmul(p[0:1, 0:TT], S["pbd"][:, 2 * c + r:2 * c + r + 1],
                               kb[:, c * T + tt * TT: c * T + (tt + 1) * TT],
                               start=True, stop=True), "mm")
            mmk = C.count("mm")

            @C.vec
            def _(v, tt=tt, mmk=mmk, pse=prev_exp_se, bk=bk):
                C.wait_at(v, "mm", mmk)
                C.wait_at(v, "se", pse)
                v.tensor_copy(smf[0:1, tt * TT:(tt + 1) * TT], C.bh(bk)[0:1, 0:TT])
                C.brel(bk, "ve")
            lg_marks.append(C.count("ve"))

        @C.vec
        def _(v, h=h, lg=list(lg_marks)):
            C.wait_at(v, "ve", lg[-1])
            v.reduce_max(mz[0:1, MY_M + h:MY_M + h + 1], smf[0:1, 0:T], axis=AX.X)
            v.tensor_scalar(mz[0:1, MY_M + h:MY_M + h + 1],
                            mz[0:1, MY_M + h:MY_M + h + 1],
                            cst[0:1, C_Z0:C_Z0 + 1], None, ALU.max)
            v.tensor_scalar(mz[0:1, NEG_ + h:NEG_ + h + 1],
                            mz[0:1, MY_M + h:MY_M + h + 1],
                            cst[0:1, C_NEG1:C_NEG1 + 1], None, ALU.mult)
        m_ve = C.count("ve")

        @C.scal
        def _(sc, h=h, m_ve=m_ve, ptm=prev_tr_mm):
            C.wait_at(sc, "ve", m_ve)
            C.wait_at(sc, "mm", ptm)
            if h == 0:
                C.wmark(sc, C.pq_spill_st)
            sc.activation(S["slotB"][0:1, 0:T], smf[0:1, 0:T], AF.Exp,
                          bias=mz[0:1, NEG_ + h:NEG_ + h + 1],
                          accum_out=mz[0:1, MY_ZU + h:MY_ZU + h + 1])
            sc.activation(mz[0:1, T1_ + h:T1_ + h + 1],
                          mz[0:1, NEG_ + h:NEG_ + h + 1], AF.Exp,
                          bias=cst[0:1, C_Z0:C_Z0 + 1])
        exp_se = C.count("se")
        prev_exp_se = exp_se

        @C.vec
        def _(v, h=h, exp_se=exp_se):
            C.wait_at(v, "se", exp_se)
            v.tensor_tensor(mz[0:1, MY_ZU + h:MY_ZU + h + 1],
                            mz[0:1, MY_ZU + h:MY_ZU + h + 1],
                            mz[0:1, T1_ + h:T1_ + h + 1], ALU.add)
        su_ve = C.count("ve")

        for kt in range(16):
            @C.ten
            def _(t, kt=kt, su_ve=su_ve):
                pT = C.bacq(t, "T")
                C.wait_at(t, "ve", su_ve)
                C.wait_at(t, "se", C.count("se"))
                C.inc(t.transpose(pT[:, 0:1], S["slotB"][0:1, kt * 128:(kt + 1) * 128],
                                  S["identb"][0:1, 0:1]), "mm")
            tr_mm = C.count("mm")
            prev_tr_mm = tr_mm

            @C.vec
            def _(v, h=h, kt=kt, tr_mm=tr_mm):
                C.wait_at(v, "mm", tr_mm)
                v.tensor_copy(S["scoreT"][:, kt * 16 + h: kt * 16 + h + 1],
                              C.bh("T")[:, 0:1])
                C.brel("T", "ve")
    score_ve = C.count("ve")

    for j, (pvc, dst) in enumerate([(PV_GV, 0), (PV_BETAV, 1024)]):
        for kc in range(DC):
            @C.vec
            def _(v, pvc=pvc, kc=kc):
                C.wmark(v, C.pre_al)
                v.tensor_copy(S["slotB"][:, 0:1], pv[:, pvc + kc:pvc + kc + 1])
            cp_ve = C.count("ve")

            @C.ten
            def _(t, cp_ve=cp_ve):
                C.wait_at(t, "ve", cp_ve)
                pT = C.bacq(t, "T")
                C.inc(t.transpose(pT[0:1, 0:128], S["slotB"][:, 0:1], S["identb"][:, :]),
                      "mm")
            tr_mm2 = C.count("mm")

            @C.vec
            def _(v, tr_mm2=tr_mm2):
                C.wait_at(v, "mm", tr_mm2)
                v.tensor_copy(S["slotB"][0:1, 128:256], C.bh("T")[0:1, 0:128])
                C.brel("T", "ve")
            ev_ve = C.count("ve")

            @C.ten
            def _(t, ev_ve=ev_ve):
                C.wait_at(t, "ve", ev_ve)
                p4 = C.bacq(t, 4)
                C.inc(t.matmul(p4[:, 0:128], S["oner_b"][0:1, :],
                               S["slotB"][0:1, 128:256], start=True, stop=True), "mm")
            bm = C.count("mm")

            @C.vec
            def _(v, dst=dst, kc=kc, bm=bm):
                C.wait_at(v, "mm", bm)
                v.tensor_copy(smf[:, dst + kc * 128: dst + (kc + 1) * 128],
                              C.bh(4)[:, 0:128])
                C.brel(4, "ve")
    gvb_ve = C.count("ve")

    prev_pkw_ve = C.count("ve")
    for kt in range(16):
        ktt = kt * 128
        pk_marks = []
        for ocp in range(2):
            @C.ten
            def _(t, kt=kt, ocp=ocp):
                p = C.bacq(t, ocp)
                C.wmark(t, wl_marks["l0a_Wfk"])
                C.wait_at(t, "ve", k_ve_done)
                for kc in range(DC):
                    t.matmul(p[:, 0:TT], kb[:, kc * T + ktt: kc * T + ktt + 128],
                             wsl(1, kc, ocp * 512, 512),
                             start=(kc == 0), stop=False)
                C.inc(t.matmul(p[:, 0:TT], S["oner_b"][0:1, 0:128],
                               S["rowsbf"][0:1, R_BFK + ocp * 512: R_BFK + ocp * 512 + 512],
                               start=False, stop=True), "mm")
            mmk = C.count("mm")

            @C.scal
            def _(sc, ocp=ocp, mmk=mmk, ppk=prev_pkw_ve):
                C.wait_at(sc, "mm", mmk)
                C.wait_at(sc, "ve", ppk)
                sc.activation(S["slotA"][:, ocp * 512:(ocp + 1) * 512],
                              C.bh(ocp)[:, 0:TT], AF.Tanh)
                C.brel(ocp, "se")
            pk_marks.append(C.count("se"))

        @C.vec
        def _(v, kt=kt, pk=list(pk_marks)):
            C.wait_at(v, "se", pk[-1])
            C.wait_at(v, "ve", score_ve)
            sv = expand_view(S["scoreT"][:, kt * 16: kt * 16 + 16], [[1, 16], [0, 64]])
            v.scalar_tensor_tensor(
                S["slotB"][:, 0:1024], S["slotA"][:, 0:1024],
                cst[:, C_ONE:C_ONE + 1], sv, ALU.add, ALU.mult)
        pkw_ve = C.count("ve")
        prev_pkw_ve = pkw_ve

        v_marks = []
        for ocp in range(2):
            @C.ten
            def _(t, kt=kt, ocp=ocp):
                p = C.bacq(t, 2 + ocp)
                C.wmark(t, wl_marks["l0a_Wv"])
                C.wait_at(t, "ve", p1_done)
                for kc in range(DC):
                    t.matmul(p[:, 0:TT], yb[:, kc * T + ktt: kc * T + ktt + 128],
                             wsl(0, kc, ocp * 512, 512),
                             start=(kc == 0), stop=False)
                C.inc(t.matmul(p[:, 0:TT], S["oner_b"][0:1, 0:128],
                               S["rowsbf"][0:1, R_BV + ocp * 512: R_BV + ocp * 512 + 512],
                               start=False, stop=True), "mm")
            mmk = C.count("mm")

            @C.scal
            def _(sc, ocp=ocp, mmk=mmk, kt=kt):
                C.wait_at(sc, "mm", mmk)
                if kt == 0:
                    C.wait_at(sc, "ve", score_ve)
                sc.activation(sqf[:, ocp * 512:(ocp + 1) * 512], C.bh(2 + ocp)[:, 0:TT],
                              AF.Copy)
                C.brel(2 + ocp, "se")
            v_marks.append(C.count("se"))

        @C.vec
        def _(v, vm=list(v_marks)):
            C.wait_at(v, "se", vm[-1])
            v.reduce_sum(rows[:, 0:1], sqf[:, 0:1024], axis=AX.X)
        vsum_ve = C.count("ve")

        @C.vec
        def _(v, vsum_ve=vsum_ve):
            v.tensor_tensor(sqf[:, 1024:1536], sqf[:, 0:512], sqf[:, 0:512], ALU.mult)
            v.reduce_sum(rows[:, 1:2], sqf[:, 1024:1536], axis=AX.X)
            v.tensor_tensor(sqf[:, 1024:1536], sqf[:, 512:1024], sqf[:, 512:1024],
                            ALU.mult)
            v.reduce_sum(rows[:, 2:3], sqf[:, 1024:1536], axis=AX.X)
            v.tensor_tensor(rows[:, 1:2], rows[:, 1:2], rows[:, 2:3], ALU.add)
            v.tensor_scalar(rows[:, 0:1], rows[:, 0:1], cst[:, C_I1024:C_I1024 + 1],
                            None, ALU.mult)
            v.tensor_scalar(rows[:, 1:2], rows[:, 1:2], cst[:, C_I1024:C_I1024 + 1],
                            None, ALU.mult)
            v.tensor_tensor(rows[:, 2:3], rows[:, 0:1], rows[:, 0:1], ALU.mult)
            v.tensor_tensor(rows[:, 1:2], rows[:, 1:2], rows[:, 2:3], ALU.subtract)
        vvar_ve = C.count("ve")

        @C.scal
        def _(sc, vvar_ve=vvar_ve):
            C.wait_at(sc, "ve", vvar_ve)
            sc.activation(rows[:, 1:2], rows[:, 1:2], AF.Ln,
                          bias=cst[:, C_EPS:C_EPS + 1])
            sc.activation(rows[:, 1:2], rows[:, 1:2], AF.Exp, scale=-0.5)
        vex_se = C.count("se")

        @C.vec
        def _(v, vex_se=vex_se):
            C.wait_at(v, "se", vex_se)
            v.tensor_scalar(rows[:, 0:1], rows[:, 0:1], cst[:, C_NEG1:C_NEG1 + 1],
                            None, ALU.mult)
            C.wait_at(v, "ve", gvb_ve)
            v.tensor_scalar(sqf[:, 0:1024], sqf[:, 0:1024], rows[:, 0:1],
                            rows[:, 1:2], ALU.add, ALU.mult)
            v.tensor_tensor(sqf[:, 0:1024], sqf[:, 0:1024], smf[:, 0:1024], ALU.mult)
            v.tensor_tensor(S["slotB"][:, 1024:2048], sqf[:, 0:1024],
                            smf[:, 1024:2048], ALU.add)
        vtm_ve = C.count("ve")

        @C.ten
        def _(t, kt=kt, pkw_ve=pkw_ve, vtm_ve=vtm_ve):
            C.wait_at(t, "ve", pkw_ve)
            C.wait_at(t, "ve", vtm_ve)
            if kt == 0:
                C.bacq(t, 4)
                C.bacq(t, 5)
                C.bacq(t, 6)
            ins = None
            for h in range(H):
                p = C.bh(5 if h < 8 else 6)
                cc = (h % 8) * 64
                t.matmul(p[0:64, cc:cc + 64],
                         S["slotB"][:, h * 64:(h + 1) * 64],
                         S["slotB"][:, 1024 + h * 64: 1024 + (h + 1) * 64],
                         start=(kt == 0 and h in (0, 8)),
                         stop=(kt == 15 and h in (7, 15)))
                ins = t.matmul(C.bh(4)[0:64, h: h + 1],
                               S["slotB"][:, h * 64:(h + 1) * 64],
                               S["onec_b"][:, 0:1],
                               start=(kt == 0 and h == 0),
                               stop=(kt == 15 and h == 15))
            C.inc(ins, "mm")
    kvu_mm = C.count("mm")

    S_kvu2 = rows[0:64, 960:1008]
    @C.vec
    def _(v):
        C.wait_at(v, "mm", kvu_mm)
        for bk, hb in [(5, 0), (6, 8)]:
            dst = expand_view(S["kvu"][0:64, hb * 65: hb * 65 + 1], [[65, 8], [1, 64]])
            srcb = expand_view(C.bh(bk)[0:64, 0:1], [[64, 8], [1, 64]])
            v.tensor_copy(dst, srcb)
            C.brel(bk, "ve")
        dstk = expand_view(S["kvu"][0:64, 64:65], [[65, 16]])
        v.tensor_copy(dstk, C.bh(4)[0:64, 0:16])
        C.brel(4, "ve")
    kvu_ve = C.count("ve")

    @C.gps
    def _(gp):
        C.wait(gp, "ve")
        C.inc(gp.dma_start(out=g["mmx_buf"][0:1, :], in_=mz[0:1, MY_M:MY_M + 16]),
              "gc", 16)
        C.wait(gp, "gc")
        C.inc(gp.collective_compute(
            "AllReduce", ALU.max,
            replica_groups=[[0, 1], [2, 3], [4, 5], [6, 7]],
            ins=[g["mmx_buf"][0:1, :].opt()], outs=[g["mmx_red"][0:1, :].opt()]), "cc")

    @C.syn
    def _(sy):
        grp = C.dgrp(sy)
        C.wait(sy, "cc")
        C.inc(sy.dma_start(out=mz[0:1, MM_:MM_ + 16], in_=g["mmx_red"][0:1, :]), grp, 16)
        C.mmx_al = (grp, C.count(grp))

    @C.vec
    def _(v):
        C.wmark(v, C.mmx_al)
        v.tensor_tensor(mz[0:1, WMY:WMY + 16], mz[0:1, MY_M:MY_M + 16],
                        mz[0:1, MM_:MM_ + 16], ALU.subtract)
    dm_ve = C.count("ve")

    @C.scal
    def _(sc, dm_ve=dm_ve):
        C.wait_at(sc, "ve", dm_ve)
        sc.activation(mz[0:1, WMY:WMY + 16], mz[0:1, WMY:WMY + 16], AF.Exp)
    w_se = C.count("se")

    @C.ten
    def _(t, w_se=w_se):
        C.wait_at(t, "se", w_se)
        p = C.bacq(t, 0)
        C.inc(t.matmul(p[0:64, 0:16], S["oner_f"][0:1, 0:64],
                       mz[0:1, WMY:WMY + 16], start=True, stop=True), "mm")
    wc_mm = C.count("mm")

    @C.vec
    def _(v, wc_mm=wc_mm):
        C.wait_at(v, "mm", wc_mm)
        v.tensor_copy(S_kvu2[0:64, 0:16], C.bh(0)[0:64, 0:16])
        C.brel(0, "ve")
        v.tensor_tensor(mz[0:1, MY_ZU:MY_ZU + 16], mz[0:1, MY_ZU:MY_ZU + 16],
                        mz[0:1, WMY:WMY + 16], ALU.mult)
        for h in range(H):
            v.tensor_scalar(S["kvu"][0:64, h * 65:(h + 1) * 65],
                            S["kvu"][0:64, h * 65:(h + 1) * 65],
                            S_kvu2[0:64, h:h + 1], None, ALU.mult)
        v.memset(sqf[0:1, 0:1040], 0.0)
        v.tensor_copy(sqf[0:1, 0:16], mz[0:1, MY_ZU:MY_ZU + 16])
    kvu3_ve = C.count("ve")

    @C.gps
    def _(gp, kvu3_ve=kvu3_ve):
        C.wait_at(gp, "ve", kvu3_ve)
        C.inc(gp.dma_start(out=g["kvu_buf"][0:64, :], in_=S["kvu"][:, :]), "gc", 16)
        C.inc(gp.dma_start(out=g["kvu_buf"][64:65, 0:1040],
                           in_=sqf[0:1, 0:1040]), "gc", 16)
        C.wait(gp, "gc")
        C.inc(gp.collective_compute(
            "AllReduce", ALU.add,
            replica_groups=[[0, 1], [2, 3], [4, 5], [6, 7]],
            ins=[g["kvu_buf"][0:65, :].opt()], outs=[g["kvu_red"][0:65, :].opt()]), "cc")

    @C.syn
    def _(sy):
        grp = C.dgrp(sy)
        C.wait(sy, "cc")
        C.inc(sy.dma_start(out=S["kvu"][:, :], in_=g["kvu_red"][0:64, :]), grp, 16)
        C.inc(sy.dma_start(out=mz[0:1, RZ_:RZ_ + 16], in_=g["kvu_red"][64:65, 0:16]),
              grp, 16)
        C.kvu_al = (grp, C.count(grp))

    @C.vec
    def _(v):
        C.wmark(v, C.kvu_al)
        v.reciprocal(mz[0:1, RZ_:RZ_ + 16], mz[0:1, RZ_:RZ_ + 16])
        v.tensor_scalar(mz[0:1, T2_:T2_ + 16], mz[0:1, MM_:MM_ + 16],
                        cst[0:1, C_NEG1:C_NEG1 + 1], None, ALU.mult)
    rz_ve = C.count("ve")

    @C.scal
    def _(sc, rz_ve=rz_ve):
        C.wait_at(sc, "ve", rz_ve)
        sc.activation(mz[0:1, T2_:T2_ + 16], mz[0:1, T2_:T2_ + 16], AF.Exp)
        C.wmark(sc, C.pre_al)
        sc.activation(rows[0:64, 16:32], pv[0:64, PV_BFKH:PV_BFKH + 16], AF.Tanh)
    e0_se = C.count("se")

    @C.vec
    def _(v, e0_se=e0_se):
        C.wait_at(v, "se", e0_se)
        v.tensor_tensor(mz[0:1, T2_:T2_ + 16], mz[0:1, T2_:T2_ + 16],
                        mz[0:1, RZ_:RZ_ + 16], ALU.mult)
        v.tensor_scalar(rows[0:64, 16:32], rows[0:64, 16:32],
                        cst[0:64, C_ONE:C_ONE + 1], None, ALU.add)
    e0r_ve = C.count("ve")

    @C.ten
    def _(t, e0r_ve=e0r_ve):
        C.wait_at(t, "ve", e0r_ve)
        p = C.bacq(t, 1)
        t.matmul(p[0:64, 0:16], S["oner_f"][0:1, 0:64],
                 mz[0:1, RZ_:RZ_ + 16], start=True, stop=True)
        C.inc(t.matmul(p[0:64, 16:32], S["oner_f"][0:1, 0:64],
                       mz[0:1, T2_:T2_ + 16], start=True, stop=True), "mm")
    bc2_mm = C.count("mm")

    @C.vec
    def _(v, bc2_mm=bc2_mm):
        C.wait_at(v, "mm", bc2_mm)
        v.tensor_copy(S_kvu2[0:64, 16:48], C.bh(1)[0:64, 0:32])
        C.brel(1, "ve")
        for h in range(H):
            v.tensor_scalar(S["kvu"][0:64, h * 65:(h + 1) * 65],
                            S["kvu"][0:64, h * 65:(h + 1) * 65],
                            S_kvu2[0:64, 16 + h:17 + h], None, ALU.mult)
            v.tensor_scalar(rows[0:64, 32:33], rows[0:64, 16 + h:17 + h],
                            S_kvu2[0:64, 32 + h:33 + h], None, ALU.mult)
            v.tensor_tensor(S["kvu"][0:64, h * 65 + 64: h * 65 + 65],
                            S["kvu"][0:64, h * 65 + 64: h * 65 + 65],
                            rows[0:64, 32:33], ALU.add)
        dstkv = expand_view(S["kvb"][0:64, 0:1], [[64, 16], [1, 64]])
        srckv = expand_view(S["kvu"][0:64, 0:1], [[65, 16], [1, 64]])
        v.tensor_copy(dstkv, srckv)
        dstks = expand_view(S["kvb"][0:64, 1024:1025], [[1, 16]])
        srcks = expand_view(S["kvu"][0:64, 64:65], [[65, 16]])
        v.tensor_copy(dstks, srcks)
    kvb_ve = C.count("ve")

    @C.syn
    def _(sy):
        grp = C.dgrp(sy)
        C.wait_at(sy, "ve", kvb_ve)
        C.inc(sy.dma_start(out=S["kvb"][64:128, :], in_=S["kvb"][0:64, :]), grp, 16)
        C.kvbdup = (grp, C.count(grp))

    p8_prev_mm = 0
    for tt in range(NTT):
        c0 = tt * TT

        @C.syn
        def _(sy, c0=c0, pm=p8_prev_mm):
            grp = C.dgrp(sy)
            C.wmark(sy, C.pq_spill_st)
            C.wait_at(sy, "ve", kvb_ve)
            C.wait_at(sy, "mm", pm)
            for oc in range(DC):
                C.inc(sy.dma_start(out=S["slotA"][:, oc * TT:(oc + 1) * TT],
                                   in_=g["pq_spill"][oc * 128:(oc + 1) * 128, c0:c0 + TT]),
                      grp, 16)
            C.pq_al = (grp, C.count(grp))

        for h in range(H):
            pb = (h % 2) * 64
            bk = 3 if h % 2 == 0 else 4
            dbk = 5 if h % 2 == 0 else 6
            ch = (h // 2)

            @C.ten
            def _(t, h=h, pb=pb, ch=ch, bk=bk, dbk=dbk):
                p = C.bacq(t, bk)
                pd = C.bacq(t, dbk)
                C.wmark(t, C.pq_al)
                C.wait_at(t, "ve", kvb_ve)
                C.wmark(t, C.kvbdup)
                rhs = S["slotA"][pb:pb + 64, ch * TT: ch * TT + TT]
                t.matmul(p[pb:pb + 64, 0:TT],
                         S["kvb"][pb:pb + 64, h * 64: h * 64 + 64],
                         rhs, start=True, stop=True,
                         skip_group_check=(pb == 64))
                C.inc(t.matmul(pd[0:1, 0:TT], S["kvb"][pb:pb + 64, 1024 + h:1025 + h],
                               rhs, start=True, stop=True), "mm")
            nd_mm = C.count("mm")

            @C.vec
            def _(v, nd_mm=nd_mm, dbk=dbk):
                C.wait_at(v, "mm", nd_mm)
                v.tensor_scalar(rows[0:1, 0:TT], C.bh(dbk)[0:1, 0:TT],
                                cst[0:1, C_EPS6:C_EPS6 + 1], None, ALU.add)
                C.brel(dbk, "ve")
                v.reciprocal(rows[0:1, 0:TT], rows[0:1, 0:TT])
            rc_ve = C.count("ve")

            @C.ten
            def _(t, rc_ve=rc_ve, pb=pb):
                C.wait_at(t, "ve", rc_ve)
                p2 = C.bacq(t, 0)
                C.inc(t.matmul(p2[pb:pb + 64, 0:TT], S["oner_f"][0:1, 0:64],
                               rows[0:1, 0:TT], start=True, stop=True,
                               skip_group_check=(pb == 64)), "mm")
            rb_mm = C.count("mm")

            @C.vec
            def _(v, h=h, pb=pb, ch=ch, rb_mm=rb_mm, bk=bk):
                C.wait_at(v, "mm", rb_mm)
                v.tensor_copy(sqf[pb:pb + 64, 0:TT], C.bh(0)[pb:pb + 64, 0:TT])
                C.brel(0, "ve")
                v.tensor_tensor(S["slotB"][pb:pb + 64, ch * TT: ch * TT + TT],
                                C.bh(bk)[pb:pb + 64, 0:TT], sqf[pb:pb + 64, 0:TT],
                                ALU.mult)
                C.brel(bk, "ve")
        a_done = C.count("ve")
        p8_prev_mm = C.count("mm")

        fm_ln(C, S,
              src_chunk=lambda kc: S["slotB"][:, kc * TT:(kc + 1) * TT],
              ttw=TT,
              dst_chunk=lambda kc, c0=c0: kb[:, kc * T + c0: kc * T + c0 + TT],
              gamma_beta=(PV_GA, PV_BA),
              src_ready=lambda kc, am=a_done: ("ve", am))
    a_ve_done = C.count("ve")
    maybe_dump(C, g, S, "p8_a", kb)
    if C.dumped:
        return

    TB = 256
    w_par_marks = [None, None]
    g_use_marks = [C.count("mm"), C.count("mm")]
    p9_prev_ln = C.count("ve")
    for tb in range(8):
        b0 = tb * TB
        for gi in range(8):
            par = gi % 2
            wc0 = WS0 if par == 0 else WS1

            @C.syn
            def _(sy, tb=tb, gi=gi, par=par, wc0=wc0):
                grp = C.dgrp(sy)
                C.wait_at(sy, "mm", g_use_marks[par])
                if tb == 0 and gi < 2:
                    C.wait_at(sy, "pc", C.precast_pc)
                C.inc(sy.dma_start(out=vb[:, wc0:wc0 + 4096], in_=g["w1_bf"][gi]),
                      grp, 16)
                C.inc(sy.dma_start(out=vb[:, wc0 + 4096:wc0 + 8192], in_=g["w2_bf"][gi]),
                      grp, 16)
                w_par_marks[par] = (grp, C.count(grp))

            for j in range(4):
                hbk = 5 if j % 2 == 0 else 6

                @C.ten
                def _(t, gi=gi, j=j, par=par, wc0=wc0, b0=b0, hbk=hbk):
                    p = C.bacq(t, hbk)
                    C.wmark(t, w_par_marks[par])
                    C.wait_at(t, "ve", a_ve_done)
                    for kc in range(DC):
                        ins = t.matmul(
                            p[:, 0:TB],
                            vb[:, wc0 + (j * 8 + kc) * 128: wc0 + (j * 8 + kc) * 128 + 128],
                            kb[:, kc * T + b0: kc * T + b0 + TB],
                            start=(kc == 0), stop=(kc == DC - 1))
                    C.inc(ins, "mm")
                h_mm = C.count("mm")

                @C.scal
                def _(sc, gi=gi, j=j, h_mm=h_mm, hbk=hbk):
                    C.wait_at(sc, "mm", h_mm)
                    sc.activation(S["slotA"][:, j * TB:(j + 1) * TB], C.bh(hbk)[:, 0:TB],
                                  AF.Gelu, bias=S["b1fm"][:, gi * 4 + j: gi * 4 + j + 1])
                    C.brel(hbk, "se")
                hb_se = C.count("se")

                @C.ten
                def _(t, gi=gi, j=j, wc0=wc0, hb_se=hb_se):
                    C.wait_at(t, "se", hb_se)
                    ins = None
                    for oc in range(DC):
                        obk = oc // 2
                        chh = (oc % 2) * TB
                        p = C.bacq(t, obk) if (gi == 0 and j == 0 and oc % 2 == 0) \
                            else C.bh(obk)
                        ins = t.matmul(p[:, chh:chh + TB],
                                       vb[:, wc0 + 4096 + j * 1024 + oc * 128:
                                          wc0 + 4096 + j * 1024 + oc * 128 + 128],
                                       S["slotA"][:, j * TB:(j + 1) * TB],
                                       start=(gi == 0 and j == 0 and oc % 2 == 0),
                                       stop=(gi == 7 and j == 3 and oc % 2 == 1))
                    C.inc(ins, "mm")
                g_use_marks[par] = C.count("mm")
        mlp_mm = C.count("mm")

        a0_marks = []
        for oc in range(DC):
            obk = oc // 2
            chh = (oc % 2) * TB

            @C.scal
            def _(sc, oc=oc, obk=obk, chh=chh, mlp_mm=mlp_mm, plv=p9_prev_ln):
                C.wait_at(sc, "mm", mlp_mm)
                C.wait_at(sc, "ve", plv)
                sc.activation(smf[:, oc * TB:(oc + 1) * TB], C.bh(obk)[:, chh:chh + TB],
                              AF.Identity, bias=pv[:, PV_B2 + oc:PV_B2 + oc + 1])
                if oc % 2 == 1:
                    C.brel(obk, "se")
            a0_se = C.count("se")

            @C.vec
            def _(v, oc=oc, b0=b0, a0_se=a0_se):
                C.wait_at(v, "se", a0_se)
                v.tensor_tensor(smf[:, oc * TB:(oc + 1) * TB],
                                smf[:, oc * TB:(oc + 1) * TB],
                                kb[:, oc * T + b0: oc * T + b0 + TB], ALU.add)
                v.tensor_tensor(smf[:, oc * TB:(oc + 1) * TB],
                                smf[:, oc * TB:(oc + 1) * TB],
                                yb[:, oc * T + b0: oc * T + b0 + TB], ALU.add)
            a0_marks.append(C.count("ve"))

        fm_ln(C, S,
              src_chunk=lambda kc: smf[:, kc * TB:(kc + 1) * TB],
              ttw=TB,
              residual_into=lambda kc, b0=b0: xs[:, kc * T + b0: kc * T + b0 + TB],
              src_f32=True,
              src_ready=lambda kc, am=list(a0_marks): ("ve", am[kc]))
        p9_prev_ln = C.count("ve")
    maybe_dump(C, g, S, "p9_x1", xs)



def emit_phase2(C, g, S):
    xs, yb, kb, vb = S["xs"], S["yb"], S["kb"], S["vb"]
    slotA, slotB, sqf, rows, smf = S["slotA"], S["slotB"], S["sqf"], S["rows"], S["smf"]
    cst, pv2 = S["consts"], S["pv2"]
    WS0, WS1 = 0, 8192

    def wsl(slot, kc, c0, cn):
        base = WS0 if slot == 0 else WS1
        return vb[:, base + kc * 1024 + c0: base + kc * 1024 + c0 + cn]

    def load_wcols(sy, dram_bf, col0, slot, gate_mm=0):
        grp = C.dgrp(sy)
        if gate_mm:
            C.wait_at(sy, "mm", gate_mm)
        C.wait_at(sy, "pc", C.precast_pc)
        for kc in range(DC):
            C.inc(sy.dma_start(out=wsl(slot, kc, 0, 1024),
                               in_=dram_bf[kc * 128:(kc + 1) * 128, col0:col0 + 1024]),
                  grp, 16)
        return (grp, C.count(grp))

    ph_mm0 = C.count("mm")
    wl2 = {}

    @C.syn
    def _(sy):
        wl2["q"] = load_wcols(sy, g["wqkv_bf"], 0, 0, gate_mm=ph_mm0)
        wl2["k"] = load_wcols(sy, g["wqkv_bf"], 1024, 1, gate_mm=ph_mm0)

    p2_marks = []
    for tt in range(NTT):
        c0 = tt * TT
        m = fm_ln(C, S,
                  src_chunk=lambda kc, c0=c0: xs[:, kc * T + c0: kc * T + c0 + TT],
                  ttw=TT,
                  dst_chunk=lambda kc, c0=c0: yb[:, kc * T + c0: kc * T + c0 + TT],
                  src_f32=True)
        p2_marks.append(m)
    maybe_dump(C, g, S, "p2_y3", yb)
    if C.dumped:
        return

    spill_marks = []

    def proj_spill(wready, pvcol, dram_rows_fn):
        for tt in range(NTT):
            c0 = tt * TT
            ev_marks = []

            def sp_out(oc, bk, mmk, ev_marks=ev_marks, pvcol=pvcol,
                       prev=(spill_marks[-1] if spill_marks else None)):
                @C.scal
                def _(sc, oc=oc, bk=bk, mmk=mmk):
                    C.wait_at(sc, "mm", mmk)
                    if prev is not None:
                        C.wmark(sc, prev)
                    sc.activation(S["slotB"][:, oc * TT:oc * TT + TT], C.bh(bk)[:, 0:TT],
                                  AF.Identity, bias=pv2[:, pvcol + oc:pvcol + oc + 1])
                    C.brel(bk, "se")
                ev_marks.append(C.count("se"))

            fm_proj(C, S, lambda kc, oc: wsl(wready[2], kc, oc * 128, 128),
                    lambda kc, c0=c0: yb[:, kc * T + c0: kc * T + c0 + TT],
                    TT, sp_out, w_ready=wready[0:2],
                    src_ready=lambda oc, tt=tt: ("ve", p2_marks[tt]))

            @C.syn
            def _(sy, c0=c0, em=list(ev_marks)):
                grp = C.dgrp(sy)
                for oc in range(DC):
                    C.wait_at(sy, "se", em[oc])
                    r0, r1 = dram_rows_fn(oc)
                    C.inc(sy.dma_start(out=r0[r1 + oc * 128: r1 + (oc + 1) * 128, c0:c0 + TT],
                                       in_=S["slotB"][:, oc * TT:oc * TT + TT]), grp, 16)
                spill_marks.append((grp, C.count(grp)))

    proj_spill(wl2["q"] + (0,), PV2_BQ, lambda oc: (g["q_buf"], 0))
    q2_mm = C.count("mm")
    proj_spill(wl2["k"] + (1,), PV2_BK, lambda oc: (g["kv_buf"], 0))
    k2_mm = C.count("mm")

    @C.syn
    def _(sy):
        wl2["v"] = load_wcols(sy, g["wqkv_bf"], 2048, 0, gate_mm=q2_mm)
    proj_spill(wl2["v"] + (0,), PV2_BV, lambda oc: (g["kv_buf"], 1024))
    qkv_mm = C.count("mm")
    qkv_se = C.count("se")

    @C.gps
    def _(gp):
        for mk in spill_marks:
            C.wmark(gp, mk)
        C.inc(gp.collective_compute(
            "AllGather", ALU.bypass,
            replica_groups=[[0, 1], [2, 3], [4, 5], [6, 7]],
            ins=[g["kv_buf"][:, :].opt()], outs=[g["kv_gath"][:, :].opt()]), "cc")

    ones_ve = C.count("ve")

    head_mm_prev = {0: qkv_mm, 1: qkv_mm}
    head_mm_all = qkv_mm
    prev_head_ve = C.count("ve")
    prev_exp_se = C.count("se")
    for h in range(H):
        par = h % 2
        kbase = 0 if par == 0 else 8192
        vbase = 4096 if par == 0 else 12288
        pb = (h % 2) * 64
        ch = h // 2

        @C.syn
        def _(sy, h=h, par=par, kbase=kbase, vbase=vbase):
            grp = C.dgrp(sy)
            C.wait(sy, "cc")
            C.wait_at(sy, "ve", ones_ve)
            C.wait_at(sy, "mm", head_mm_prev[par])
            for r in range(2):
                C.inc(sy.dma_start(
                    out=vb[0:64, kbase + r * T: kbase + (r + 1) * T],
                    in_=g["kv_gath"][r * 2048 + h * 64: r * 2048 + (h + 1) * 64, :]),
                    grp, 16)
                C.inc(sy.dma_start(
                    out=vb[0:64, vbase + r * T: vbase + (r + 1) * T],
                    in_=g["kv_gath"][r * 2048 + 1024 + h * 64: r * 2048 + 1024 + (h + 1) * 64, :]),
                    grp, 16)
            C.khm = (grp, C.count(grp))
        kh_mark = C.khm

        @C.syn
        def _(sy, h=h, hma=head_mm_all):
            grp = C.dgrp(sy)
            for mk in spill_marks[0:4]:
                C.wmark(sy, mk)
            C.wait_at(sy, "ve", prev_head_ve)
            C.wait_at(sy, "mm", hma)
            C.inc(sy.dma_start(out=slotA[0:64, 2048:4096],
                               in_=g["q_buf"][h * 64:(h + 1) * 64, :]), grp, 16)
            C.qm = (grp, C.count(grp))
        q_mark = C.qm

        bm_marks = []
        for j in range(12):
            if j < 8:
                src_ap = vb[0:64, kbase + j * 512: kbase + (j + 1) * 512]
                rdy = kh_mark
            else:
                src_ap = slotA[0:64, 2048 + (j - 8) * 512: 2048 + (j - 7) * 512]
                rdy = q_mark

            @C.vec
            def _(sc, j=j, src_ap=src_ap, rdy=rdy, phv=prev_head_ve):
                C.wmark(sc, rdy)
                if j > 0:
                    C.wait_at(sc, "mm", bm_marks[j - 1])
                sc.tensor_tensor(sqf[0:64, 0:512], src_ap, src_ap, ALU.mult)
            sq_se = C.count("ve")

            @C.ten
            def _(t, j=j, sq_se=sq_se):
                C.wait_at(t, "ve", sq_se)
                p = C.bacq(t, 6)
                C.inc(t.matmul(p[0:1, 0:512], S["onec_f"][0:64, 0:1], sqf[0:64, 0:512],
                               start=True, stop=True), "mm")
            bm_marks.append(C.count("mm"))

            @C.vec
            def _(v, j=j, mk=bm_marks[-1]):
                C.wait_at(v, "mm", mk)
                v.reduce_max(rows[0:1, 34 + (0 if j < 8 else 1):35 + (0 if j < 8 else 1)],
                             C.bh(6)[0:1, 0:512], axis=AX.X)
                C.brel(6, "ve")
                dst = 32 if j < 8 else 33
                if j in (0, 8):
                    v.tensor_copy(rows[0:1, dst:dst + 1], rows[0:1, 34 + (0 if j < 8 else 1):35 + (0 if j < 8 else 1)])
                else:
                    v.tensor_tensor(rows[0:1, dst:dst + 1], rows[0:1, dst:dst + 1],
                                    rows[0:1, 34 + (0 if j < 8 else 1):35 + (0 if j < 8 else 1)],
                                    ALU.max)
        bnd_ve = C.count("ve")

        @C.scal
        def _(sc, bnd_ve=bnd_ve):
            C.wait_at(sc, "ve", bnd_ve)
            sc.activation(rows[0:1, 32:33], rows[0:1, 32:33], AF.Ln)
            sc.activation(rows[0:1, 33:34], rows[0:1, 33:34], AF.Ln)
        bnds_se = C.count("se")

        @C.vec
        def _(v, bnds_se=bnds_se):
            C.wait_at(v, "se", bnds_se)
            v.tensor_tensor(rows[0:1, 32:33], rows[0:1, 32:33], rows[0:1, 33:34], ALU.add)
        bnh_ve = C.count("ve")

        @C.scal
        def _(sc, bnh_ve=bnh_ve):
            C.wait_at(sc, "ve", bnh_ve)
            sc.activation(rows[0:1, 32:33], rows[0:1, 32:33], AF.Exp, scale=0.5)
        bex_se = C.count("se")

        @C.vec
        def _(v, bex_se=bex_se):
            C.wait_at(v, "se", bex_se)
            v.tensor_scalar(rows[0:1, 32:33], rows[0:1, 32:33],
                            cst[0:1, C_NEG1:C_NEG1 + 1], cst[0:1, C_I8:C_I8 + 1],
                            ALU.mult, ALU.mult)
            v.tensor_scalar(slotA[0:64, 2048:4096], slotA[0:64, 2048:4096],
                            cst[0:64, C_I8:C_I8 + 1], None, ALU.mult)
        nb8_ve = C.count("ve")

        @C.ten
        def _(t, nb8_ve=nb8_ve):
            C.wait_at(t, "ve", nb8_ve)
            p6 = C.bacq(t, 6)
            C.inc(t.matmul(p6[:, 0:1], S["oner_f"][0:1, 0:128],
                           rows[0:1, 32:33], start=True, stop=True), "mm")
        nbm = C.count("mm")

        @C.vec
        def _(v, nbm=nbm, pes=prev_exp_se):
            C.wait_at(v, "mm", nbm)
            C.wait_at(v, "se", pes)
            v.tensor_copy(smf[:, 2047:2048], C.bh(6)[:, 0:1])
            C.brel(6, "ve")
        qaug_ve = C.count("ve")

        vt_marks = []
        for kc in range(32):
            @C.ten
            def _(t, kc=kc, vbase=vbase, qaug_ve=qaug_ve):
                pT = C.bacq(t, "T")
                C.wmark(t, kh_mark)
                C.wait_at(t, "ve", qaug_ve)
                C.inc(t.transpose(pT[:, 0:64], vb[0:64, vbase + kc * 128: vbase + (kc + 1) * 128],
                                  S["identb"][0:64, 0:64]), "mm")
            trm = C.count("mm")

            @C.vec
            def _(v, kc=kc, trm=trm):
                C.wait_at(v, "mm", trm)
                v.tensor_copy(slotA[:, kc * 64:(kc + 1) * 64], C.bh("T")[:, 0:64])
                C.brel("T", "ve")
            vt_marks.append(C.count("ve"))
        vtm_ve = C.count("ve")

        for qt in range(4):
            nbk, dbk = qt % 2, 2 + qt % 2
            smm_of = {}
            pvm_of = {}

            def emit_S(kc, qt=qt, kbase=kbase, qaug_ve=qaug_ve,
                       smm_of=smm_of):
                sbk = 4 + (kc % 2)

                @C.ten
                def _(t, kc=kc, sbk=sbk, qt=qt, kbase=kbase, qaug_ve=qaug_ve):
                    p = C.bacq(t, sbk)
                    C.wait_at(t, "ve", qaug_ve)
                    C.inc(t.matmul(p[:, 0:TT],
                                   vb[0:64, kbase + kc * 128: kbase + (kc + 1) * 128],
                                   slotA[0:64, 2048 + qt * TT: 2048 + (qt + 1) * TT],
                                   start=True, stop=True), "mm")
                smm_of[kc] = C.count("mm")

            emit_S(0)
            for kc in range(32):
                sbk = 4 + (kc % 2)
                st_ = kc % 2

                @C.scal
                def _(sc, st_=st_, sbk=sbk, smm=smm_of[kc],
                      pvm=pvm_of.get(kc - 2)):
                    C.wait_at(sc, "mm", smm)
                    if pvm is not None:
                        C.wait_at(sc, "mm", pvm)
                    sc.activation(slotB[:, st_ * TT:(st_ + 1) * TT],
                                  C.bh(sbk)[:, 0:TT], AF.Exp,
                                  bias=smf[:, 2047:2048])
                    C.brel(sbk, "se")
                prev_exp_se = C.count("se")
                if kc + 1 < 32:
                    emit_S(kc + 1)

                @C.ten
                def _(t, kc=kc, st_=st_, pb=pb, em=prev_exp_se, vtm_ve=vtm_ve,
                      nbk=nbk, dbk=dbk):
                    C.wait_at(t, "se", em)
                    C.wait_at(t, "ve", vtm_ve)
                    pn = C.bacq(t, nbk) if kc == 0 else C.bh(nbk)
                    pd = C.bacq(t, dbk) if kc == 0 else C.bh(dbk)
                    t.matmul(pn[pb:pb + 64, 0:TT], slotA[:, kc * 64:(kc + 1) * 64],
                             slotB[:, st_ * TT:(st_ + 1) * TT],
                             start=(kc == 0), stop=(kc == 31),
                             skip_group_check=(pb == 64))
                    C.inc(t.matmul(pd[0:1, 0:TT], S["onec_b"][:, 0:1],
                                   slotB[:, st_ * TT:(st_ + 1) * TT],
                                   start=(kc == 0), stop=(kc == 31)), "mm")
                pvm_of[kc] = C.count("mm")
            pv_mm = C.count("mm")

            @C.vec
            def _(v, dbk=dbk, pv_mm=pv_mm):
                C.wait_at(v, "mm", pv_mm)
                v.reciprocal(rows[0:1, 0:TT], C.bh(dbk)[0:1, 0:TT])
                C.brel(dbk, "ve")
            rcv = C.count("ve")

            @C.ten
            def _(t, pb=pb, rcv=rcv):
                C.wait_at(t, "ve", rcv)
                p2 = C.bacq(t, 6)
                C.inc(t.matmul(p2[pb:pb + 64, 0:TT], S["oner_f"][0:1, 0:64],
                               rows[0:1, 0:TT], start=True, stop=True,
                               skip_group_check=(pb == 64)), "mm")
            rbm = C.count("mm")

            @C.vec
            def _(v, h=h, pb=pb, ch=ch, qt=qt, nbk=nbk, rbm=rbm):
                C.wait_at(v, "mm", rbm)
                v.tensor_copy(sqf[pb:pb + 64, 0:TT], C.bh(6)[pb:pb + 64, 0:TT])
                C.brel(6, "ve")
                v.tensor_tensor(kb[pb:pb + 64, ch * T + qt * TT: ch * T + (qt + 1) * TT],
                                C.bh(nbk)[pb:pb + 64, 0:TT], sqf[pb:pb + 64, 0:TT],
                                ALU.mult)
                C.brel(nbk, "ve")
        head_mm_prev[par] = C.count("mm")
        head_mm_all = C.count("mm")
        prev_head_ve = C.count("ve")
    heads_mm = C.count("mm")
    heads_ve = C.count("ve")
    maybe_dump(C, g, S, "p2_o", kb)
    if C.dumped:
        return

    @C.syn
    def _(sy):
        C.wl_out = load_wcols(sy, g["wout_bf"], 0, 0, gate_mm=heads_mm)

    TB = 256
    prev_ln_ve = C.count("ve")
    for tb in range(8):
        b0 = tb * TB
        a0_marks = []

        def o_out(oc, bk, mmk, b0=b0, a0_marks=a0_marks, plv=prev_ln_ve):
            @C.scal
            def _(sc, oc=oc, bk=bk, mmk=mmk):
                C.wait_at(sc, "mm", mmk)
                C.wait_at(sc, "ve", plv)
                sc.activation(smf[:, oc * TB:(oc + 1) * TB], C.bh(bk)[:, 0:TB],
                              AF.Identity, bias=pv2[:, PV2_BO + oc:PV2_BO + oc + 1])
                C.brel(bk, "se")
            a0se = C.count("se")

            @C.vec
            def _(v, oc=oc, b0=b0, a0se=a0se):
                C.wait_at(v, "se", a0se)
                v.tensor_tensor(smf[:, oc * TB:(oc + 1) * TB],
                                smf[:, oc * TB:(oc + 1) * TB],
                                yb[:, oc * T + b0: oc * T + b0 + TB], ALU.add)
            a0_marks.append(C.count("ve"))

        fm_proj(C, S, lambda kc, oc: wsl(0, kc, oc * 128, 128),
                lambda kc, b0=b0: kb[:, kc * T + b0: kc * T + b0 + TB],
                TB, o_out, w_ready=C.wl_out,
                src_ready=lambda oc: ("ve", heads_ve))
        fm_ln(C, S,
              src_chunk=lambda kc: smf[:, kc * TB:(kc + 1) * TB],
              ttw=TB,
              dst_chunk=lambda kc, b0=b0: yb[:, kc * T + b0: kc * T + b0 + TB],
              gamma_beta=(PV2_LN1G, PV2_LN1B),
              src_f32=True, pv_src="pv2",
              src_ready=lambda kc, am=list(a0_marks): ("ve", am[kc]))
        prev_ln_ve = C.count("ve")
    maybe_dump(C, g, S, "p2_y1", yb)


def emit_phase3(C, g, S):
    xs, yb, kb, vb = S["xs"], S["yb"], S["kb"], S["vb"]
    slotA, slotB, sqf, rows, smf = S["slotA"], S["slotB"], S["sqf"], S["rows"], S["smf"]
    cst, pv3 = S["consts"], S["pv3"]
    cbT8 = slotB[0:8, 2048:4096]
    b2sb = S["kvb"][0:8, 0:1024]
    wge = S["scoreT"][:, 0:96]
    TB = 256
    ph_mm0 = C.count("mm")
    ph_ve0 = C.count("ve")

    @C.gps
    def _(gp):
        C.wait_at(gp, "mm", ph_mm0)
        C.inc(gp.dma_start(out=b2sb, in_=g["l0f_b2"][:]), "gc", 16)
        C.b2_gc = C.count("gc")

    @C.syn
    def _(sy):
        grp = C.dgrp(sy)
        C.wait_at(sy, "pc", C.precast_pc)
        for kc in range(DC):
            C.inc(sy.dma_start(out=wge[:, kc * 12:(kc + 1) * 12],
                               in_=g["wge_bf"][kc * 128:(kc + 1) * 128, :]), grp, 16)
        C.wge_mk = (grp, C.count(grp))

    prev_rt_mm = ph_mm0
    for gr in range(16):
        t0 = gr * 128

        @C.ten
        def _(t, gr=gr, t0=t0):
            p = C.bacq(t, 4)
            C.wmark(t, C.wge_mk)
            C.wait_at(t, "ve", ph_ve0)
            ins = None
            for kc in range(DC):
                ins = t.matmul(p[:, 0:12], yb[:, kc * T + t0: kc * T + t0 + 128],
                               wge[:, kc * 12:(kc + 1) * 12],
                               start=(kc == 0), stop=False)
            C.inc(t.matmul(p[:, 0:12], S["oner_b"][0:1, 0:128], S["bge"][0:1, :],
                           start=False, stop=True), "mm")
        rt_mm = C.count("mm")

        @C.vec
        def _(v, gr=gr, rt_mm=rt_mm):
            C.wait_at(v, "mm", rt_mm)
            R = rows
            v.tensor_copy(R[:, 0:12], C.bh(4)[:, 0:12])
            C.brel(4, "ve")
            v.reduce_max(R[:, 12:13], R[:, 0:4], axis=AX.X)
            v.tensor_scalar(R[:, 13:14], R[:, 12:13], cst[:, C_NEG1:C_NEG1 + 1],
                            None, ALU.mult)
        gm_ve = C.count("ve")

        @C.scal
        def _(sc, gm_ve=gm_ve):
            C.wait_at(sc, "ve", gm_ve)
            sc.activation(rows[:, 0:4], rows[:, 0:4], AF.Exp, bias=rows[:, 13:14],
                          accum_out=rows[:, 14:15])
        gexp_se = C.count("se")

        @C.vec
        def _(v, gr=gr, gexp_se=gexp_se):
            R = rows
            C.wait_at(v, "se", gexp_se)
            v.reciprocal(R[:, 14:15], R[:, 14:15])
            v.tensor_scalar(R[:, 0:4], R[:, 0:4], R[:, 14:15], None, ALU.mult)
            ev = expand_view(R[:, 4:5], [[2, 4]])
            od = expand_view(R[:, 5:6], [[2, 4]])
            v.tensor_tensor(R[:, 16:20], ev, od, ALU.max)
            m2x = expand_view(R[:, 16:17], [[1, 4], [0, 2]])
            v.tensor_tensor(R[:, 20:28], R[:, 4:12], m2x, ALU.subtract)
        ep_ve = C.count("ve")

        @C.scal
        def _(sc, ep_ve=ep_ve):
            C.wait_at(sc, "ve", ep_ve)
            sc.activation(rows[:, 20:28], rows[:, 20:28], AF.Exp)
        ep_se = C.count("se")

        @C.vec
        def _(v, gr=gr, t0=t0, ep_se=ep_se):
            R = rows
            C.wait_at(v, "se", ep_se)
            ev = expand_view(R[:, 20:21], [[2, 4]])
            od = expand_view(R[:, 21:22], [[2, 4]])
            v.tensor_tensor(R[:, 16:20], ev, od, ALU.add)
            v.reciprocal(R[:, 16:20], R[:, 16:20])
            sx = expand_view(R[:, 16:17], [[1, 4], [0, 2]])
            v.tensor_tensor(R[:, 20:28], R[:, 20:28], sx, ALU.mult)
            v.reduce_max(R[:, 12:13], R[:, 0:4], axis=AX.X)
            for gi in range(4):
                v.tensor_scalar(R[:, 28 + gi:29 + gi], R[:, gi:gi + 1],
                                R[:, 12:13], None, ALU.is_ge)
            v.tensor_copy(R[:, 32:33], R[:, 28:29])
            v.tensor_scalar(R[:, 36:37], R[:, 32:33], cst[:, C_NEG1:C_NEG1 + 1],
                            cst[:, C_ONE:C_ONE + 1], ALU.mult, ALU.add)
            for gi in range(1, 4):
                v.tensor_tensor(R[:, 32 + gi:33 + gi], R[:, 28 + gi:29 + gi],
                                R[:, 36:37], ALU.mult)
                if gi < 3:
                    v.tensor_tensor(R[:, 36:37], R[:, 36:37],
                                    R[:, 32 + gi:33 + gi], ALU.subtract)
            ev2 = expand_view(R[:, 20:21], [[2, 4]])
            od2 = expand_view(R[:, 21:22], [[2, 4]])
            v.tensor_tensor(R[:, 40:44], ev2, od2, ALU.is_ge)
            v.tensor_scalar(R[:, 44:48], R[:, 40:44], cst[:, C_NEG1:C_NEG1 + 1],
                            cst[:, C_ONE:C_ONE + 1], ALU.mult, ALU.add)
            v.tensor_tensor(R[:, 48:52], ev2, od2, ALU.max)
            v.tensor_tensor(R[:, 52:56], R[:, 32:36], R[:, 48:52], ALU.mult)
            v.tensor_scalar(R[:, 52:56], R[:, 52:56], R[:, 12:13], None, ALU.mult)
            cbe = expand_view(S["slotB"][:, 0:1], [[2, 4]])
            cbo = expand_view(S["slotB"][:, 1:2], [[2, 4]])
            v.tensor_tensor(cbe, R[:, 52:56], R[:, 40:44], ALU.mult)
            v.tensor_tensor(cbo, R[:, 52:56], R[:, 44:48], ALU.mult)
        cb_ve = C.count("ve")

        @C.ten
        def _(t, gr=gr, cb_ve=cb_ve):
            pT = C.bacq(t, "T")
            C.wait_at(t, "ve", cb_ve)
            C.inc(t.transpose(pT[0:8, 0:128], S["slotB"][:, 0:8], S["identb"][:, :]), "mm")
        tr_mm = C.count("mm")

        @C.vec
        def _(v, gr=gr, tr_mm=tr_mm):
            C.wait_at(v, "mm", tr_mm)
            v.tensor_copy(cbT8[0:8, gr * 128:(gr + 1) * 128], C.bh("T")[0:8, 0:128])
            C.brel("T", "ve")
    cb_done = C.count("ve")

    WS0, WS1 = 0, 8192
    w_marks = [None, None]
    g_use = [C.count("mm"), C.count("mm")]
    prev_ln_ve = C.count("ve")
    for tb in range(8):
        b0 = tb * TB
        for e in range(8):
            @C.ten
            def _(t, e=e, tb=tb, cb_done=cb_done):
                p = C.bacq(t, 4)
                C.wait_at(t, "ve", cb_done)
                basis = expand_view(S["identb"][0:8, e:e + 1], [[0, 128]])
                C.inc(t.matmul(p[:, 0:TB], basis,
                               cbT8[0:8, tb * TB:(tb + 1) * TB],
                               start=True, stop=True), "mm")
            cbb_mm = C.count("mm")

            @C.vec
            def _(v, cbb_mm=cbb_mm):
                C.wait_at(v, "mm", cbb_mm)
                v.tensor_copy(sqf[:, 0:TB], C.bh(4)[:, 0:TB])
                C.brel(4, "ve")
            cbb_ve = C.count("ve")

            for gq in range(4):
                par = (e * 4 + gq) % 2
                wc0 = WS0 if par == 0 else WS1

                @C.syn
                def _(sy, e=e, gq=gq, par=par, wc0=wc0):
                    grp = C.dgrp(sy)
                    C.wait_at(sy, "mm", g_use[par])
                    C.wait_at(sy, "pc2", C.precast_pc2)
                    C.inc(sy.dma_start(out=vb[:, wc0:wc0 + 4096],
                                       in_=g["moew1_bf"][e, gq]), grp, 16)
                    C.inc(sy.dma_start(out=vb[:, wc0 + 4096:wc0 + 8192],
                                       in_=g["moew2_bf"][e, gq]), grp, 16)
                    w_marks[par] = (grp, C.count(grp))

                for j in range(4):
                    hc = gq * 4 + j
                    hbk = 5 if j % 2 == 0 else 6

                    @C.ten
                    def _(t, e=e, j=j, par=par, wc0=wc0, b0=b0, hbk=hbk):
                        p = C.bacq(t, hbk)
                        C.wmark(t, w_marks[par])
                        C.wait_at(t, "ve", cbb_ve)
                        for kc in range(DC):
                            ins = t.matmul(
                                p[:, 0:TB],
                                vb[:, wc0 + (j * 8 + kc) * 128: wc0 + (j * 8 + kc) * 128 + 128],
                                yb[:, kc * T + b0: kc * T + b0 + TB],
                                start=(kc == 0), stop=(kc == DC - 1))
                        C.inc(ins, "mm")
                    h_mm = C.count("mm")

                    @C.scal
                    def _(sc, e=e, hc=hc, h_mm=h_mm, hbk=hbk):
                        C.wait_at(sc, "mm", h_mm)
                        sc.activation(S["slotA"][:, (hc % 4) * TB:(hc % 4 + 1) * TB],
                                      C.bh(hbk)[:, 0:TB], AF.Relu,
                                      bias=S["b1fm3"][:, e * 16 + hc:e * 16 + hc + 1])
                        C.brel(hbk, "se")
                    hr_se = C.count("se")

                    @C.vec
                    def _(v, hc=hc, hr_se=hr_se):
                        C.wait_at(v, "se", hr_se)
                        v.tensor_tensor(S["slotA"][:, (hc % 4) * TB:(hc % 4 + 1) * TB],
                                        S["slotA"][:, (hc % 4) * TB:(hc % 4 + 1) * TB],
                                        sqf[:, 0:TB], ALU.mult)
                    hb_ve = C.count("ve")

                    @C.ten
                    def _(t, e=e, j=j, gq=gq, hc=hc, wc0=wc0, hb_ve=hb_ve):
                        C.wait_at(t, "ve", hb_ve)
                        ins = None
                        for oc in range(DC):
                            obk = oc // 2
                            chh = (oc % 2) * TB
                            p = C.bacq(t, obk) if (e == 0 and hc == 0 and oc % 2 == 0) \
                                else C.bh(obk)
                            ins = t.matmul(p[:, chh:chh + TB],
                                           vb[:, wc0 + 4096 + j * 1024 + oc * 128:
                                              wc0 + 4096 + j * 1024 + oc * 128 + 128],
                                           S["slotA"][:, (hc % 4) * TB:(hc % 4 + 1) * TB],
                                           start=(e == 0 and hc == 0 and oc % 2 == 0),
                                           stop=False)
                        C.inc(ins, "mm")
                    g_use[par] = C.count("mm")
        @C.ten
        def _(t, tb=tb, cb_done=cb_done):
            C.wait_at(t, "ve", cb_done)
            C.wait_at(t, "gc", C.b2_gc)
            ins = None
            for oc in range(DC):
                obk = oc // 2
                chh = (oc % 2) * TB
                ins = t.matmul(C.bh(obk)[:, chh:chh + TB],
                               b2sb[0:8, oc * 128:(oc + 1) * 128],
                               cbT8[0:8, tb * TB:(tb + 1) * TB],
                               start=False, stop=(oc % 2 == 1))
            C.inc(ins, "mm")
        moe_mm = C.count("mm")

        a0_marks = []
        for oc in range(DC):
            obk = oc // 2
            chh = (oc % 2) * TB

            @C.scal
            def _(sc, oc=oc, obk=obk, chh=chh, moe_mm=moe_mm, plv=prev_ln_ve):
                C.wait_at(sc, "mm", moe_mm)
                C.wait_at(sc, "ve", plv)
                sc.activation(smf[:, oc * TB:(oc + 1) * TB], C.bh(obk)[:, chh:chh + TB],
                              AF.Copy)
                if oc % 2 == 1:
                    C.brel(obk, "se")
            a0se = C.count("se")

            @C.vec
            def _(v, oc=oc, b0=b0, a0se=a0se):
                C.wait_at(v, "se", a0se)
                v.tensor_tensor(smf[:, oc * TB:(oc + 1) * TB],
                                smf[:, oc * TB:(oc + 1) * TB],
                                yb[:, oc * T + b0: oc * T + b0 + TB], ALU.add)
            a0_marks.append(C.count("ve"))

        fm_ln(C, S,
              src_chunk=lambda kc: smf[:, kc * TB:(kc + 1) * TB],
              ttw=TB,
              dst_chunk=lambda kc: S["slotA"][:, 1024 + kc * TB: 1024 + (kc + 1) * TB],
              gamma_beta=(PV3_LN2G, PV3_LN2B),
              src_f32=True, pv_src="pv3",
              src_ready=lambda kc, am=list(a0_marks): ("ve", am[kc]))
        y2_ve = C.count("ve")
        fm_ln(C, S,
              src_chunk=lambda kc: S["slotA"][:, 1024 + kc * TB: 1024 + (kc + 1) * TB],
              ttw=TB,
              residual_into=lambda kc, b0=b0: xs[:, kc * T + b0: kc * T + b0 + TB],
              src_ready=lambda kc, ym=y2_ve: ("ve", ym))
        prev_ln_ve = C.count("ve")
    maybe_dump(C, g, S, "p3_x2", xs)



def emit_phase4(C, g, S):
    xs, yb, kb, vb = S["xs"], S["yb"], S["kb"], S["vb"]
    slotA, slotB, sqf, rows, smf = S["slotA"], S["slotB"], S["sqf"], S["rows"], S["smf"]
    cst = S["consts"]
    WS0, WS1 = 0, 8192

    def wsl(slot, kc, c0, cn):
        base = WS0 if slot == 0 else WS1
        return vb[:, base + kc * 1024 + c0: base + kc * 1024 + c0 + cn]

    def load_wcols(sy, dram_bf, col0, slot, gate_mm=0):
        grp = C.dgrp(sy)
        if gate_mm:
            C.wait_at(sy, "mm", gate_mm)
        C.wait_at(sy, "pc2", C.precast_pc2)
        for kc in range(DC):
            C.inc(sy.dma_start(out=wsl(slot, kc, 0, 1024),
                               in_=dram_bf[kc * 128:(kc + 1) * 128, col0:col0 + 1024]),
                  grp, 16)
        return (grp, C.count(grp))

    ph_mm0 = C.count("mm")
    ph_ve0 = C.count("ve")
    tril = S["scoreT"][:, 96:224]
    wl4 = {}

    @C.syn
    def _(sy):
        grp = C.dgrp(sy)
        C.wait_at(sy, "ve", ph_ve0)
        C.inc(sy.dma_start(out=tril, in_=g["tril"][:]), grp, 16)
        C.tril_mk = (grp, C.count(grp))
        wl4["q"] = load_wcols(sy, g["l1wqkv_bf"], 0, 0, gate_mm=ph_mm0)
        wl4["k"] = load_wcols(sy, g["l1wqkv_bf"], 1024, 1, gate_mm=ph_mm0)

    p4_marks = []
    for tt in range(NTT):
        c0 = tt * TT
        m = fm_ln(C, S,
                  src_chunk=lambda kc, c0=c0: xs[:, kc * T + c0: kc * T + c0 + TT],
                  ttw=TT,
                  dst_chunk=lambda kc, c0=c0: yb[:, kc * T + c0: kc * T + c0 + TT],
                  src_f32=True)
        p4_marks.append(m)

    def elu_evac(bk, mmk, dst_ap, pre=None):
        @C.vec
        def _(v, bk=bk, mmk=mmk):
            C.wait_at(v, "mm", mmk)
            C.wait_at(v, "se", C.count("se"))
            C.wmark(v, pre)
            v.tensor_scalar(sqf[:, 0:TT], C.bh(bk)[:, 0:TT], cst[:, C_ZERO:C_ZERO + 1],
                            None, ALU.min)
        mn_ve = C.count("ve")

        @C.scal
        def _(sc, mn_ve=mn_ve):
            C.wait_at(sc, "ve", mn_ve)
            sc.activation(sqf[:, 512:512 + TT], sqf[:, 0:TT], AF.Exp)
        ex_se = C.count("se")

        @C.vec
        def _(v, bk=bk, ex_se=ex_se, dst_ap=dst_ap):
            C.wait_at(v, "se", ex_se)
            v.tensor_scalar(sqf[:, 0:TT], C.bh(bk)[:, 0:TT], cst[:, C_ZERO:C_ZERO + 1],
                            None, ALU.max)
            C.brel(bk, "ve")
            v.tensor_tensor(dst_ap, sqf[:, 0:TT], sqf[:, 512:512 + TT], ALU.add)
        return C.count("ve")

    for tt in range(NTT):
        c0 = tt * TT

        def pq_out(oc, bk, mmk, c0=c0):
            elu_evac(bk, mmk, kb[:, oc * T + c0: oc * T + c0 + TT])

        fm_proj(C, S, lambda kc, oc: wsl(0, kc, oc * 128, 128),
                lambda kc, c0=c0: yb[:, kc * T + c0: kc * T + c0 + TT],
                TT, pq_out, w_ready=wl4["q"],
                src_ready=lambda oc, tt=tt: ("ve", p4_marks[tt]))
    pq4_mm = C.count("mm")

    sp_marks = []
    for tt in range(NTT):
        c0 = tt * TT
        ev_marks = []

        def pkf_out(oc, bk, mmk, ev_marks=ev_marks,
                    prev=(sp_marks[-1] if sp_marks else None)):
            m = elu_evac(bk, mmk, S["slotB"][:, oc * TT:oc * TT + TT], pre=prev)
            ev_marks.append(m)

        fm_proj(C, S, lambda kc, oc: wsl(1, kc, oc * 128, 128),
                lambda kc, c0=c0: yb[:, kc * T + c0: kc * T + c0 + TT],
                TT, pkf_out, w_ready=wl4["k"],
                src_ready=lambda oc, tt=tt: ("ve", p4_marks[tt]))

        @C.syn
        def _(sy, c0=c0, em=list(ev_marks)):
            grp = C.dgrp(sy)
            for oc in range(DC):
                C.wait_at(sy, "ve", em[oc])
                C.inc(sy.dma_start(out=g["pkfm_buf"][oc * 128:(oc + 1) * 128, c0:c0 + TT],
                                   in_=S["slotB"][:, oc * TT:oc * TT + TT]), grp, 16)
            sp_marks.append((grp, C.count(grp)))

    def tm_proj(dram_w, col0, wslot, dst_buf, do_elu, gate_mm):
        @C.syn
        def _(sy):
            wl4["tm"] = load_wcols(sy, dram_w, col0, wslot, gate_mm=gate_mm)
        for kt in range(16):
            ktt = kt * 128
            ev2 = []
            for ocp in range(2):
                @C.ten
                def _(t, kt=kt, ocp=ocp, wslot=wslot):
                    p = C.bacq(t, ocp)
                    C.wmark(t, wl4["tm"])
                    C.wait_at(t, "ve", p4_marks[-1])
                    ins = None
                    for kc in range(DC):
                        ins = t.matmul(p[:, 0:TT], yb[:, kc * T + ktt: kc * T + ktt + 128],
                                       wsl(wslot, kc, ocp * 512, 512),
                                       start=(kc == 0), stop=(kc == DC - 1))
                    C.inc(ins, "mm")
                mmk = C.count("mm")
                prevs = sp_marks[-1] if sp_marks else None
                if do_elu:
                    m = elu_evac(ocp, mmk, S["slotB"][:, ocp * 512:(ocp + 1) * 512],
                                 pre=prevs)
                    ev2.append(m)
                else:
                    @C.vec
                    def _(v, ocp=ocp, mmk=mmk, prevs=prevs):
                        C.wait_at(v, "mm", mmk)
                        C.wmark(v, prevs)
                        v.tensor_copy(S["slotB"][:, ocp * 512:(ocp + 1) * 512],
                                      C.bh(ocp)[:, 0:TT])
                        C.brel(ocp, "ve")
                    ev2.append(C.count("ve"))

            @C.syn
            def _(sy, kt=kt, ev2=list(ev2), dst_buf=dst_buf):
                grp = C.dgrp(sy)
                C.wait_at(sy, "ve", ev2[-1])
                C.inc(sy.dma_start(out=dst_buf[kt * 128:(kt + 1) * 128, :],
                                   in_=S["slotB"][:, 0:1024]), grp, 16)
                sp_marks.append((grp, C.count(grp)))

    tm_proj(g["l1wqkv_bf"], 1024, 0, g["pktm_buf"], True, pq4_mm)
    pktm_mm = C.count("mm")
    tm_proj(g["l1wqkv_bf"], 2048, 1, g["v_buf"], False, C.count("mm"))
    proj4_mm = C.count("mm")
    maybe_dump(C, g, S, "p4_pq", kb)
    if C.dumped:
        return

    tileA = [0, 0]

    def load_tile(sy, buf, kt, dstc):
        grp = C.dgrp(sy)
        C.inc(sy.dma_start(out=slotA[:, dstc:dstc + 1024],
                           in_=buf[kt * 128:(kt + 1) * 128, :]), grp, 16)
        return (grp, C.count(grp))

    @C.vec
    def _(v):
        v.memset(S["kvu"][:, :], 0.0)
    stA_mm = [C.count("mm")]
    for kt in range(16):
        par = kt % 2

        @C.syn
        def _(sy, kt=kt, par=par):
            if kt < 2:
                for mk in sp_marks:
                    C.wmark(sy, mk)
                C.wait_at(sy, "ve", C.count("ve"))
            C.wait_at(sy, "mm", stA_mm[0] if kt < 2 else tileA[par])
            C.tp1 = load_tile(sy, g["pktm_buf"], kt, par * 2048)
            C.tp2 = load_tile(sy, g["v_buf"], kt, par * 2048 + 1024)
        t1, t2 = C.tp1, C.tp2

        @C.ten
        def _(t, kt=kt, par=par, t1=t1, t2=t2):
            C.wmark(t, t1)
            C.wmark(t, t2)
            C.bacq(t, 4)
            C.bacq(t, 5)
            C.bacq(t, 6)
            ins = None
            for h in range(H):
                p = C.bh(5 if h < 8 else 6)
                cc = (h % 8) * 64
                t.matmul(p[0:64, cc:cc + 64],
                         slotA[:, par * 2048 + h * 64: par * 2048 + (h + 1) * 64],
                         slotA[:, par * 2048 + 1024 + h * 64: par * 2048 + 1024 + (h + 1) * 64],
                         start=(h in (0, 8)), stop=(h in (7, 15)))
                ins = t.matmul(C.bh(4)[0:64, h:h + 1],
                               slotA[:, par * 2048 + h * 64: par * 2048 + (h + 1) * 64],
                               S["onec_b"][:, 0:1],
                               start=(h == 0), stop=(h == 15))
            C.inc(ins, "mm")
        tileA[par] = C.count("mm")

        @C.vec
        def _(v, kt=kt, par=par):
            C.wait_at(v, "mm", tileA[par])
            for bk, hb in [(5, 0), (6, 8)]:
                dst = expand_view(sqf[0:64, hb * 65: hb * 65 + 1], [[65, 8], [1, 64]])
                srcb = expand_view(C.bh(bk)[0:64, 0:1], [[64, 8], [1, 64]])
                v.tensor_copy(dst, srcb)
                C.brel(bk, "ve")
            dstk = expand_view(sqf[0:64, 64:65], [[65, 16]])
            v.tensor_copy(dstk, C.bh(4)[0:64, 0:16])
            C.brel(4, "ve")
            v.tensor_tensor(S["kvu"][:, :], S["kvu"][:, :], sqf[0:64, 0:1040], ALU.add)
    stA_ve = C.count("ve")

    @C.gps
    def _(gp):
        C.wait_at(gp, "ve", stA_ve)
        C.inc(gp.dma_start(out=g["carry_buf"][:, :], in_=S["kvu"][:, :]), "gc", 16)
        C.wait(gp, "gc")
        C.inc(gp.collective_compute(
            "AllGather", ALU.bypass,
            replica_groups=[[0, 1], [2, 3], [4, 5], [6, 7]],
            ins=[g["carry_buf"][:, :].opt()], outs=[g["carry_gath"][:, :].opt()]), "cc")

    @C.syn
    def _(sy):
        grp = C.dgrp(sy)
        C.wait(sy, "cc")
        C.inc(sy.dma_start(out=S["kvu"][:, :], in_=g["carry_gath"][0:64, :]), grp, 16)
        C.carry_al = (grp, C.count(grp))

    @C.vec
    def _(v):
        C.wmark(v, C.carry_al)
        v.tensor_scalar(S["kvu"][:, :], S["kvu"][:, :], cst[0:64, C_CARRY:C_CARRY + 1],
                        None, ALU.mult)
    carry_ve = C.count("ve")

    init_mm = C.count("mm")
    tileB = [init_mm, init_mm]
    for kt in range(16):
        par = kt % 2
        ktt = kt * 128

        @C.syn
        def _(sy, kt=kt, par=par):
            C.wait_at(sy, "mm", tileB[par])
            grp = C.dgrp(sy)
            C.inc(sy.dma_start(out=slotA[:, par * 2048:par * 2048 + 1024],
                               in_=g["pktm_buf"][kt * 128:(kt + 1) * 128, :]), grp, 16)
            C.inc(sy.dma_start(out=slotA[:, par * 2048 + 1024:par * 2048 + 2048],
                               in_=g["v_buf"][kt * 128:(kt + 1) * 128, :]), grp, 16)
            C.tpB = (grp, C.count(grp))
        tB = C.tpB

        @C.syn
        def _(sy, kt=kt):
            grp = C.dgrp(sy)
            C.wait_at(sy, "mm", tileB[0])
            C.wait_at(sy, "mm", tileB[1])
            for oc in range(DC):
                C.inc(sy.dma_start(out=S["slotB"][:, oc * 128:(oc + 1) * 128],
                                   in_=g["pkfm_buf"][oc * 128:(oc + 1) * 128,
                                                     kt * 128:(kt + 1) * 128]), grp, 16)
            C.tpF = (grp, C.count(grp))
        tF = C.tpF

        @C.vec
        def _(v, kt=kt):
            dstkv = expand_view(S["kvb"][0:64, 0:1], [[64, 16], [1, 64]])
            srckv = expand_view(S["kvu"][0:64, 0:1], [[65, 16], [1, 64]])
            v.tensor_copy(dstkv, srckv)
            dstks = expand_view(S["kvb"][0:64, 1024:1025], [[1, 16]])
            srcks = expand_view(S["kvu"][0:64, 64:65], [[65, 16]])
            v.tensor_copy(dstks, srcks)
        snap_ve = C.count("ve")

        @C.syn
        def _(sy, snap_ve=snap_ve):
            grp = C.dgrp(sy)
            C.wait_at(sy, "ve", snap_ve)
            C.inc(sy.dma_start(out=S["kvb"][64:128, :], in_=S["kvb"][0:64, :]), grp, 16)
            C.dupm = (grp, C.count(grp))
        dupm = C.dupm

        @C.ten
        def _(t, kt=kt, par=par, tB=tB, snap_ve=snap_ve):
            C.wmark(t, tB)
            C.wait_at(t, "ve", snap_ve)
            C.bacq(t, 4)
            C.bacq(t, 5)
            C.bacq(t, 6)
            ins = None
            for h in range(H):
                p = C.bh(5 if h < 8 else 6)
                cc = (h % 8) * 64
                t.matmul(p[0:64, cc:cc + 64],
                         slotA[:, par * 2048 + h * 64: par * 2048 + (h + 1) * 64],
                         slotA[:, par * 2048 + 1024 + h * 64: par * 2048 + 1024 + (h + 1) * 64],
                         start=(h in (0, 8)), stop=(h in (7, 15)))
                ins = t.matmul(C.bh(4)[0:64, h:h + 1],
                               slotA[:, par * 2048 + h * 64: par * 2048 + (h + 1) * 64],
                               S["onec_b"][:, 0:1],
                               start=(h == 0), stop=(h == 15))
            C.inc(ins, "mm")
        st_mm = C.count("mm")

        @C.vec
        def _(v, st_mm=st_mm):
            C.wait_at(v, "mm", st_mm)
            for bk, hb in [(5, 0), (6, 8)]:
                dst = expand_view(sqf[0:64, hb * 65: hb * 65 + 1], [[65, 8], [1, 64]])
                srcb = expand_view(C.bh(bk)[0:64, 0:1], [[64, 8], [1, 64]])
                v.tensor_copy(dst, srcb)
                C.brel(bk, "ve")
            dstk = expand_view(sqf[0:64, 64:65], [[65, 16]])
            v.tensor_copy(dstk, C.bh(4)[0:64, 0:16])
            C.brel(4, "ve")
            v.tensor_tensor(S["kvu"][:, :], S["kvu"][:, :], sqf[0:64, 0:1040], ALU.add)

        ptm_of = {}
        ndm_of = {}
        rbm_of = {}
        rcv_of = {}

        def emit_PT(h, kt=kt, ktt=ktt, tF=tF, ptm_of=ptm_of, ndm_of=ndm_of):
            pb = (h % 2) * 64
            ch = h // 2
            ptbk = 0 if h % 2 == 0 else 4

            @C.ten
            def _(t, h=h, pb=pb, ch=ch, ptbk=ptbk):
                p0 = C.bacq(t, ptbk)
                C.wmark(t, tF)
                C.inc(t.matmul(p0[:, 0:128],
                               S["slotB"][pb:pb + 64, ch * 128:(ch + 1) * 128],
                               kb[pb:pb + 64, ch * T + ktt: ch * T + ktt + 128],
                               start=True, stop=True), "mm")
            ptm_of[h] = C.count("mm")

        def emit_BCfin(h, ktt=ktt, rbm_of=rbm_of, rcv_of=rcv_of):
            pb = (h % 2) * 64
            ch = h // 2
            nbk = 2 if h % 2 == 0 else 3

            @C.ten
            def _(t, h=h, pb=pb, rcv=rcv_of[h]):
                C.wait_at(t, "ve", rcv)
                p6 = C.bacq(t, 6)
                C.inc(t.matmul(p6[pb:pb + 64, 0:128], S["oner_f"][0:1, 0:64],
                               rows[0:1, (h % 2) * 128:(h % 2) * 128 + 128],
                               start=True, stop=True,
                               skip_group_check=(pb == 64)), "mm")
            rbm_of[h] = C.count("mm")

            @C.vec
            def _(v, h=h, pb=pb, ch=ch, nbk=nbk, rb_mm=rbm_of[h]):
                C.wait_at(v, "mm", rb_mm)
                v.tensor_copy(sqf[pb:pb + 64, 0:128], C.bh(6)[pb:pb + 64, 0:128])
                C.brel(6, "ve")
                v.tensor_tensor(vb[pb:pb + 64, ch * T + ktt: ch * T + ktt + 128],
                                C.bh(nbk)[pb:pb + 64, 0:128], sqf[pb:pb + 64, 0:128],
                                ALU.mult)
                C.brel(nbk, "ve")

        emit_PT(0)
        for h in range(H):
            pb = (h % 2) * 64
            ch = h // 2
            nbk = 2 if h % 2 == 0 else 3
            denbk = 1 if h % 2 == 0 else 5
            moff = 1024 + (h % 2) * 128

            @C.vec
            def _(v, h=h, moff=moff, pt_mm=ptm_of[h], ndm=ndm_of.get(h - 2)):
                C.wait_at(v, "mm", pt_mm)
                if ndm is not None:
                    C.wait_at(v, "mm", ndm)
                C.wmark(v, C.tril_mk)
                v.tensor_tensor(S["slotB"][0:128, moff:moff + 128],
                                C.bh(0 if h % 2 == 0 else 4)[:, 0:128],
                                tril, ALU.mult)
                C.brel(0 if h % 2 == 0 else 4, "ve")
            pm_ve = C.count("ve")
            if h + 1 < H:
                emit_PT(h + 1)

            @C.ten
            def _(t, h=h, pb=pb, ch=ch, par=par, dupm=dupm, nbk=nbk,
                  denbk=denbk, moff=moff, pm_ve=pm_ve, ktt=ktt):
                C.wmark(t, dupm)
                C.wait_at(t, "ve", pm_ve)
                pn = C.bacq(t, nbk)
                pd = C.bacq(t, denbk)
                t.matmul(pn[pb:pb + 64, 0:128],
                         S["kvb"][pb:pb + 64, h * 64:(h + 1) * 64],
                         kb[pb:pb + 64, ch * T + ktt: ch * T + ktt + 128],
                         start=True, stop=False, skip_group_check=(pb == 64))
                t.matmul(pn[pb:pb + 64, 0:128],
                         slotA[:, par * 2048 + 1024 + h * 64: par * 2048 + 1024 + (h + 1) * 64],
                         S["slotB"][0:128, moff:moff + 128],
                         start=False, stop=True, skip_group_check=(pb == 64))
                t.matmul(pd[0:1, 0:128], S["kvb"][pb:pb + 64, 1024 + h:1025 + h],
                         kb[pb:pb + 64, ch * T + ktt: ch * T + ktt + 128],
                         start=True, stop=False)
                C.inc(t.matmul(pd[0:1, 0:128], S["onec_b"][:, 0:1],
                               S["slotB"][0:128, moff:moff + 128],
                               start=False, stop=True), "mm")
            ndm_of[h] = C.count("mm")

            @C.vec
            def _(v, h=h, denbk=denbk, nd_mm=ndm_of[h],
                  rbm=rbm_of.get(h - 2)):
                C.wait_at(v, "mm", nd_mm)
                if rbm is not None:
                    C.wait_at(v, "mm", rbm)
                v.tensor_scalar(rows[0:1, (h % 2) * 128:(h % 2) * 128 + 128],
                                C.bh(denbk)[0:1, 0:128],
                                cst[0:1, C_EPS6:C_EPS6 + 1], None, ALU.add)
                C.brel(denbk, "ve")
                v.reciprocal(rows[0:1, (h % 2) * 128:(h % 2) * 128 + 128],
                             rows[0:1, (h % 2) * 128:(h % 2) * 128 + 128])
            rcv_of[h] = C.count("ve")
            if h > 0:
                emit_BCfin(h - 1)
        emit_BCfin(H - 1)
        tileB[par] = C.count("mm")
    of_ve = C.count("ve")
    maybe_dump(C, g, S, "p4_of", vb)
    if C.dumped:
        return

    @C.syn
    def _(sy):
        grp = C.dgrp(sy)
        C.wait_at(sy, "pc2", C.precast_pc2)
        C.wait_at(sy, "mm", C.count("mm"))
        for kc in range(DC):
            C.inc(sy.dma_start(out=kb[:, kc * 1024:kc * 1024 + 1024],
                               in_=g["l1wout_bf"][kc * 128:(kc + 1) * 128, :]), grp, 16)
        C.wo_mk = (grp, C.count(grp))

    TB = 256
    prev_ln_ve = C.count("ve")
    for tb in range(8):
        b0 = tb * TB
        a0_marks = []

        def o_out(oc, bk, mmk, b0=b0, a0_marks=a0_marks, plv=prev_ln_ve):
            @C.vec
            def _(v, oc=oc, bk=bk, mmk=mmk):
                C.wait_at(v, "mm", mmk)
                C.wait_at(v, "ve", plv)
                v.tensor_copy(smf[:, oc * TB:(oc + 1) * TB], C.bh(bk)[:, 0:TB])
                C.brel(bk, "ve")
            a0_marks.append(C.count("ve"))

        fm_proj(C, S, lambda kc, oc: kb[:, kc * 1024 + oc * 128: kc * 1024 + oc * 128 + 128],
                lambda kc, b0=b0: vb[:, kc * T + b0: kc * T + b0 + TB],
                TB, o_out, w_ready=C.wo_mk,
                src_ready=lambda oc: ("ve", of_ve))
        fm_ln(C, S,
              src_chunk=lambda kc: smf[:, kc * TB:(kc + 1) * TB],
              ttw=TB,
              residual_into=lambda kc, b0=b0: xs[:, kc * T + b0: kc * T + b0 + TB],
              src_f32=True,
              src_ready=lambda kc, am=list(a0_marks): ("ve", am[kc]))
        prev_ln_ve = C.count("ve")
    maybe_dump(C, g, S, "p4_x3", xs)



def emit_phase5(C, g, S):
    xs, yb, kb, vb = S["xs"], S["yb"], S["kb"], S["vb"]
    slotA, slotB, sqf, rows, smf = S["slotA"], S["slotB"], S["sqf"], S["rows"], S["smf"]
    cst, pv5 = S["consts"], S["pv5"]
    PV5_UB, PV5_BCON, PV5_BDN = 0, 32, 40
    xbuf = S["scoreT"][:, 0:128]

    ph_mm0 = C.count("mm")

    p5_marks = []
    for tt in range(NTT):
        c0 = tt * TT
        m = fm_ln(C, S,
                  src_chunk=lambda kc, c0=c0: xs[:, kc * T + c0: kc * T + c0 + TT],
                  ttw=TT,
                  dst_chunk=lambda kc, c0=c0: yb[:, kc * T + c0: kc * T + c0 + TT],
                  src_f32=True)
        p5_marks.append(m)

    for mi, (m, nst, ab0) in enumerate([(0, 7, 0), (1, 5, 7), (2, 7, 12), (3, 5, 19)]):
        @C.vec
        def _(v, m=m, mi=mi):
            C.wmark(v, C.pre_al)
            if mi == 0:
                C.wait_at(v, "mm", ph_mm0)
            for o in range(32):
                v.tensor_copy(kb[:, m * 4096 + o * 128: m * 4096 + (o + 1) * 128],
                              S["identb"][:, :])
        cur_kb = True
        for sj in range(nst):
            inst = ab0 + sj

            @C.syn
            def _(sy, inst=inst, sj=sj):
                grp = C.dgrp(sy)
                C.wait_at(sy, "ve", C.count("ve"))
                C.wait_at(sy, "mm", C.count("mm"))
                C.inc(sy.dma_start(out=rows[:, 64:128],
                                   in_=g["l1f_ab"][:, inst * 64:(inst + 1) * 64]), grp, 16)
                C.inc(sy.dma_start(out=xbuf, in_=g["l1f_xorm"][sj]), grp, 16)
                C.abm = (grp, C.count(grp))
            abm = C.abm

            @C.vec
            def _(v, abm=abm):
                C.wmark(v, abm)
                for o in range(32):
                    v.tensor_scalar(slotA[:, o * 128:(o + 1) * 128], S["identb"][:, :],
                                    rows[:, 64 + o:65 + o], None, ALU.mult)
                    v.tensor_scalar(vb[:, 4096 + o * 128:4096 + (o + 1) * 128],
                                    xbuf[:, :], rows[:, 96 + o:97 + o], None, ALU.mult)
                v.tensor_tensor(slotA[:, 0:4096], slotA[:, 0:4096],
                                vb[:, 4096:8192], ALU.add)
            s_ve = C.count("ve")

            for o in range(32):
                bk = o % 2

                @C.ten
                def _(t, m=m, o=o, bk=bk, s_ve=s_ve, cur_kb=cur_kb):
                    p = C.bacq(t, bk)
                    C.wait_at(t, "ve", s_ve)
                    src_ap = (kb[:, m * 4096 + o * 128: m * 4096 + (o + 1) * 128]
                              if cur_kb else vb[:, o * 128:(o + 1) * 128])
                    C.inc(t.matmul(p[:, 0:128], slotA[:, o * 128:(o + 1) * 128],
                                   src_ap, start=True, stop=True), "mm")
                mmk = C.count("mm")

                @C.vec
                def _(v, m=m, o=o, bk=bk, mmk=mmk, cur_kb=cur_kb):
                    C.wait_at(v, "mm", mmk)
                    dst_ap = (vb[:, o * 128:(o + 1) * 128] if cur_kb
                              else kb[:, m * 4096 + o * 128: m * 4096 + (o + 1) * 128])
                    v.tensor_copy(dst_ap, C.bh(bk)[:, 0:128])
                    C.brel(bk, "ve")
            cur_kb = not cur_kb
        assert not cur_kb
        for o in range(32):
            @C.ten
            def _(t, m=m, o=o):
                pT = C.bacq(t, "T")
                C.wait_at(t, "ve", C.count("ve"))
                C.inc(t.transpose(pT[:, 0:128], vb[:, o * 128:(o + 1) * 128],
                                  S["identb"][:, :]), "mm")
            tmk = C.count("mm")

            @C.vec
            def _(v, m=m, o=o, tmk=tmk):
                C.wait_at(v, "mm", tmk)
                v.tensor_copy(kb[:, m * 4096 + o * 128: m * 4096 + (o + 1) * 128],
                              C.bh("T")[:, 0:128])
                C.brel("T", "ve")
    dbuild_ve = C.count("ve")
    dbuild_mm = C.count("mm")
    maybe_dump(C, g, S, "p5_D", kb)
    if C.dumped:
        return

    st1_marks = []
    wexp_mk = [None]
    for og in range(8):
        @C.syn
        def _(sy, og=og):
            grp = C.dgrp(sy)
            C.wait_at(sy, "pc2", C.precast_pc2)
            C.wait_at(sy, "mm", C.count("mm"))
            C.inc(sy.dma_start(out=slotB[:, :], in_=g["wexp_bf"][og]), grp, 16)
            wexp_mk[0] = (grp, C.count(grp))

        for j in range(4):
            o = og * 4 + j
            ev1 = []
            for sb in range(4):
                @C.ten
                def _(t, og=og, j=j, sb=sb, wm=wexp_mk[0]):
                    p = C.bacq(t, 0 if sb % 2 == 0 else 1)
                    C.wmark(t, wm)
                    C.wait_at(t, "ve", p5_marks[sb])
                    ins = None
                    for kc in range(DC):
                        ins = t.matmul(
                            p[:, 0:TT],
                            slotB[:, (j * 8 + kc) * 128:(j * 8 + kc) * 128 + 128],
                            yb[:, kc * T + sb * TT: kc * T + (sb + 1) * TT],
                            start=(kc == 0), stop=(kc == DC - 1))
                    C.inc(ins, "mm")
                mk1 = C.count("mm")

                @C.vec
                def _(v, sb=sb, mk1=mk1):
                    C.wait_at(v, "mm", mk1)
                    v.tensor_copy(slotA[:, sb * TT:(sb + 1) * TT],
                                  C.bh(0 if sb % 2 == 0 else 1)[:, 0:TT])
                    C.brel(0 if sb % 2 == 0 else 1, "ve")
                ev1.append(C.count("ve"))

            ev2 = []
            for sb in range(4):
                @C.ten
                def _(t, o=o, sb=sb, ev1=list(ev1)):
                    p = C.bacq(t, 2 if sb % 2 == 0 else 3)
                    C.wait_at(t, "ve", ev1[sb])
                    C.wait_at(t, "ve", dbuild_ve)
                    C.inc(t.matmul(p[:, 0:TT], kb[:, o * 128:(o + 1) * 128],
                                   slotA[:, sb * TT:(sb + 1) * TT],
                                   start=True, stop=True), "mm")
                mk2 = C.count("mm")
                prev_sig = st1_marks[-1] if (st1_marks and sb == 0) else None

                @C.vec
                def _(v, sb=sb, mk2=mk2, prev_sig=prev_sig):
                    C.wait_at(v, "mm", mk2)
                    C.wmark(v, prev_sig)
                    v.tensor_copy(slotA[:, 2048 + sb * TT: 2048 + (sb + 1) * TT],
                                  C.bh(2 if sb % 2 == 0 else 3)[:, 0:TT])
                    C.brel(2 if sb % 2 == 0 else 3, "ve")
            h1_ve = C.count("ve")

            @C.syn
            def _(sy, o=o, h1_ve=h1_ve):
                grp = C.dgrp(sy)
                C.wait_at(sy, "ve", h1_ve)
                dst = g["scr"].rearrange("(a r o) t -> a r o t", a=32, r=4)[:, :, o, :]
                C.inc(sy.dma_start(out=dst, in_=slotA[:, 2048:4096]), grp, 16)
                st1_marks.append((grp, C.count(grp)))
    st1_mm = C.count("mm")

    def mid_stage(mcol, src_buf, dst_buf, sigma_out, gelu, in_marks, out_marks):
        for gg in range(32):
            @C.syn
            def _(sy, gg=gg):
                grp = C.dgrp(sy)
                if gg < 2:
                    for mk in in_marks:
                        C.wmark(sy, mk)
                C.wait_at(sy, "mm", C.count("mm"))
                C.inc(sy.dma_start(out=slotA[:, (gg % 2) * 2048:(gg % 2) * 2048 + 2048],
                                   in_=src_buf[gg * 128:(gg + 1) * 128, :]), grp, 16)
                C.ld5 = (grp, C.count(grp))
            ldm = C.ld5

            ev = []
            for sb in range(4):
                @C.ten
                def _(t, gg=gg, sb=sb, ldm=ldm, mcol=mcol):
                    p = C.bacq(t, 0 if sb % 2 == 0 else 1)
                    C.wmark(t, ldm)
                    C.wait_at(t, "ve", dbuild_ve)
                    C.inc(t.matmul(p[:, 0:TT],
                                   kb[:, mcol * 4096 + gg * 128: mcol * 4096 + (gg + 1) * 128],
                                   slotA[:, (gg % 2) * 2048 + sb * TT:
                                         (gg % 2) * 2048 + (sb + 1) * TT],
                                   start=True, stop=True), "mm")
                mk = C.count("mm")

                prev_out = out_marks[-1] if (out_marks and sb == 0) else None
                if gelu:
                    @C.scal
                    def _(sc, gg=gg, sb=sb, mk=mk, prev_out=prev_out):
                        C.wait_at(sc, "mm", mk)
                        C.wmark(sc, prev_out)
                        sc.activation(S["slotB"][:, sb * TT:(sb + 1) * TT],
                                      C.bh(0 if sb % 2 == 0 else 1)[:, 0:TT],
                                      AF.Gelu, bias=pv5[:, PV5_UB + gg:PV5_UB + gg + 1])
                        C.brel(0 if sb % 2 == 0 else 1, "se")
                    ev.append(("se", C.count("se")))
                else:
                    @C.vec
                    def _(v, gg=gg, sb=sb, mk=mk, prev_out=prev_out):
                        C.wait_at(v, "mm", mk)
                        C.wmark(v, prev_out)
                        v.tensor_copy(S["slotB"][:, sb * TT:(sb + 1) * TT],
                                      C.bh(0 if sb % 2 == 0 else 1)[:, 0:TT])
                        C.brel(0 if sb % 2 == 0 else 1, "ve")
                    ev.append(("ve", C.count("ve")))

            @C.syn
            def _(sy, gg=gg, ev=list(ev)):
                grp = C.dgrp(sy)
                C.wmark(sy, ev[-1])
                if sigma_out == "inv":
                    dst = dst_buf.rearrange("(o g2 r) t -> r o g2 t",
                                            o=32, g2=32, r=4)[:, :, gg, :]
                elif sigma_out == "fwd":
                    dst = dst_buf.rearrange("(a r o) t -> a r o t",
                                            a=32, r=4)[:, :, gg, :]
                else:
                    dst = dst_buf[gg * 128:(gg + 1) * 128, :]
                C.inc(sy.dma_start(out=dst, in_=S["slotB"][:, 0:2048]), grp, 16)
                out_marks.append((grp, C.count(grp)))

    st2_marks = []
    mid_stage(1, g["scr"], g["scr2"], "inv", True, st1_marks, st2_marks)
    st3_marks = []
    mid_stage(2, g["scr2"], g["scr3"], "fwd", False, st2_marks, st3_marks)
    st4_marks = []
    mid_stage(3, g["scr3"], g["scr"], "none", False, st3_marks, st4_marks)

    TB = 256
    prev_ln_ve = C.count("ve")
    wcon_mk = [None]
    for tb in range(8):
        b0 = tb * TB

        @C.syn
        def _(sy, tb=tb, b0=b0):
            grp = C.dgrp(sy)
            if tb < 1:
                for mk in st4_marks:
                    C.wmark(sy, mk)
            C.wait_at(sy, "mm", C.count("mm"))
            for fc in range(32):
                C.inc(sy.dma_start(out=vb[:, fc * TB:(fc + 1) * TB],
                                   in_=g["scr"][fc * 128:(fc + 1) * 128, b0:b0 + TB]),
                      grp, 16)
            C.h8m = (grp, C.count(grp))
        h8m = C.h8m

        a0_marks = []
        for oc in range(DC):
            @C.syn
            def _(sy, oc=oc, tb=tb):
                grp = C.dgrp(sy)
                C.wait_at(sy, "pc2", C.precast_pc2)
                C.wait_at(sy, "mm", C.count("mm"))
                C.inc(sy.dma_start(out=slotB[:, :], in_=g["wcon_bf"][oc]), grp, 16)
                C.wcm = (grp, C.count(grp))
            wcm = C.wcm

            @C.ten
            def _(t, oc=oc, h8m=h8m, wcm=wcm):
                p = C.bacq(t, 4)
                C.wmark(t, h8m)
                C.wmark(t, wcm)
                ins = None
                for fc in range(32):
                    ins = t.matmul(p[:, 0:TB], slotB[:, fc * 128:(fc + 1) * 128],
                                   vb[:, fc * TB:(fc + 1) * TB],
                                   start=(fc == 0), stop=(fc == 31))
                C.inc(ins, "mm")
            wc_mm = C.count("mm")

            @C.scal
            def _(sc, oc=oc, wc_mm=wc_mm, plv=prev_ln_ve):
                C.wait_at(sc, "mm", wc_mm)
                C.wait_at(sc, "ve", plv)
                sc.activation(smf[:, oc * TB:(oc + 1) * TB], C.bh(4)[:, 0:TB],
                              AF.Identity, bias=pv5[:, PV5_BCON + oc:PV5_BCON + oc + 1])
                C.brel(4, "se")
            ase = C.count("se")

            @C.vec
            def _(v, oc=oc, ase=ase):
                C.wait_at(v, "se", ase)
                v.tensor_scalar(smf[:, oc * TB:(oc + 1) * TB],
                                smf[:, oc * TB:(oc + 1) * TB],
                                pv5[:, PV5_BDN + oc:PV5_BDN + oc + 1], None, ALU.add)
            a0_marks.append(C.count("ve"))

        fm_ln(C, S,
              src_chunk=lambda kc: smf[:, kc * TB:(kc + 1) * TB],
              ttw=TB,
              residual_into=lambda kc, b0=b0: xs[:, kc * T + b0: kc * T + b0 + TB],
              src_f32=True,
              src_ready=lambda kc, am=list(a0_marks): ("ve", am[kc]))
        prev_ln_ve = C.count("ve")
    maybe_dump(C, g, S, "p5_x4", xs)



def kernel(**inputs):
    cores = list(range(8))
    nc, _ = build_program(0)
    in_maps = [prep_core_inputs(inputs, c) for c in cores]
    res = run_bass_kernel_spmd(nc, in_maps, core_ids=cores)
    out = np.zeros((4, 4096, 1024), dtype=np.float32)
    for c in cores:
        b, half = c // 2, c % 2
        out[b, half * T:(half + 1) * T, :] = res.results[c]["out_fm"].T
    return out
